# version 01
# baseline (speedup 1.0000x reference)
import contextlib
import numpy as np
import concourse.bass as bass
import concourse.mybir as mybir

F32 = mybir.dt.float32
BF16 = mybir.dt.bfloat16
I32 = mybir.dt.int32
AF = mybir.ActivationFunctionType
ALU = mybir.AluOpType
AX = mybir.AxisListType

QUEUES = ("PE", "ACT", "DVE", "POOL", "SYNC")
_DT_SIZE = {F32: 4, BF16: 2, I32: 4, mybir.dt.uint8: 1, mybir.dt.int16: 2,
            mybir.dt.uint32: 4, mybir.dt.float16: 2, mybir.dt.uint16: 2}


class Access:
    __slots__ = ("p0", "p1", "b0", "b1", "deps")

    def __init__(self, p0, p1, b0, b1, deps):
        self.p0, self.p1, self.b0, self.b1 = p0, p1, b0, b1
        self.deps = deps


class Buf:
    def __init__(self, name, apobj, kind, dtype, shape):
        self.name = name
        self.apobj = apobj
        self.kind = kind
        self.dtype = dtype
        self.shape = shape
        self.writes = []
        self.reads = []
        self.stream_sem = None
        self.stream_count = 0
        self.stream_deps = None
        self.batch_open = False

    def tv(self):
        if self.kind == "dram":
            P = self.shape[0] if len(self.shape) > 1 else 1
            E = int(np.prod(self.shape[1:])) if len(self.shape) > 1 else self.shape[0]
            return TV(self, 0, P, 0, E)
        return TV(self, 0, self.shape[0], 0, int(np.prod(self.shape[1:])))


class TV:
    __slots__ = ("buf", "p0", "p1", "e0", "e1", "cast")

    def __init__(self, buf, p0, p1, e0, e1, cast=None):
        self.buf, self.p0, self.p1, self.e0, self.e1 = buf, p0, p1, e0, e1
        self.cast = cast

    @property
    def P(self):
        return self.p1 - self.p0

    @property
    def N(self):
        return self.e1 - self.e0

    def ap(self):
        h = self.buf.apobj
        if self.buf.kind == "dram":
            a = h
        elif self.cast is not None:
            a = h.bitcast(self.cast).ap()
        else:
            a = h.ap()
        return a[self.p0:self.p1, self.e0:self.e1]

    def bf16(self):
        assert self.cast is None and self.buf.dtype == F32
        return TV(self.buf, self.p0, self.p1, self.e0 * 2, self.e1 * 2, cast=BF16)

    def __getitem__(self, idx):
        ps, es = idx if isinstance(idx, tuple) else (idx, slice(None))
        if isinstance(ps, int):
            ps = slice(ps, ps + 1)
        if isinstance(es, int):
            es = slice(es, es + 1)
        p0, p1, _ = ps.indices(self.P)
        e0, e1, _ = es.indices(self.N)
        return TV(self.buf, self.p0 + p0, self.p0 + p1, self.e0 + e0,
                  self.e0 + e1, self.cast)

    def rect(self):
        if self.buf.kind != "sbuf":
            return (0, 128, 0, 1 << 40)
        s = _DT_SIZE[self.buf.dtype]
        return (self.p0, self.p1, self.e0 * s, self.e1 * s)


def _merge_into(dst: dict, deps: dict):
    for k, v in deps.items():
        if dst.get(k, -1) < v:
            dst[k] = v


def _prune(lst, maxlen=48):
    if len(lst) <= maxlen:
        return lst
    half = len(lst) // 2
    old, rest = lst[:half], lst[half:]
    p0 = min(a.p0 for a in old)
    p1 = max(a.p1 for a in old)
    b0 = min(a.b0 for a in old)
    b1 = max(a.b1 for a in old)
    deps = {}
    for a in old:
        _merge_into(deps, a.deps)
    return [Access(p0, p1, b0, b1, deps)] + rest


class OpRec:
    __slots__ = ("queue", "waits", "emit", "inc")

    def __init__(self, queue, waits, emit, inc):
        self.queue, self.waits, self.emit, self.inc = queue, waits, emit, inc


class _MMGroup:

    def __init__(self, ctx, psum_tv):
        self.ctx = ctx
        self.psum = psum_tv
        self.first = True
        self.last_val = None

    def add(self, emit_fn, reads):
        ctx = self.ctx
        deps = {}
        ctx._collect_deps(list(reads), [self.psum] if self.first else [], deps)
        waits = ctx._waits_for("PE", deps)
        ctx.qcount["PE"] += 1
        val = ctx.qcount["PE"]
        for tv in reads:
            p0, p1, b0, b1 = tv.rect()
            tv.buf.reads.append(Access(p0, p1, b0, b1, {"prog_PE": val}))
            tv.buf.reads = _prune(tv.buf.reads)
            tv.buf.batch_open = False
        ctx.ops.append(OpRec("PE", waits, emit_fn, ("prog_PE", 1)))
        self.first = False
        self.last_val = val

    def close(self):
        self.ctx._record_accesses([], [self.psum], ("prog_PE", self.last_val))


class Ctx:
    def __init__(self, nc, name="ctx"):
        self.nc = nc
        self.name = name
        self.ops = []
        self.qcount = {q: 0 for q in ("PE", "ACT", "DVE", "POOL")}
        self.clock = {q: {} for q in QUEUES}
        self.sem_names = ["prog_PE", "prog_ACT", "prog_DVE", "prog_POOL"]
        self.sems = {}
        self.bufs = {}
        self._cm = []
        self.n_waits = 0
        self.sem_pool = []

    def _new_sem(self, name):
        assert name not in self.sem_names, name
        self.sem_names.append(name)
        return name

    def sbuf(self, name, shape, dtype):
        cm = self.nc.sbuf_tensor(name, list(shape), dtype)
        t = cm.__enter__()
        self._cm.append((name, cm))
        b = Buf(name, t, "sbuf", dtype, tuple(shape))
        self.bufs[name] = b
        return b.tv()

    def psum(self, name, shape=(128, 512), dtype=F32):
        cm = self.nc.psum_tensor(name, list(shape), dtype)
        t = cm.__enter__()
        self._cm.append((name, cm))
        b = Buf(name, t, "psum", dtype, tuple(shape))
        self.bufs[name] = b
        return b.tv()

    def mark(self):
        return len(self._cm)

    def free_to(self, mark):
        self.barrier()
        while len(self._cm) > mark:
            name, cm = self._cm.pop()
            b = self.bufs.pop(name, None)
            if b is not None and b.stream_sem is not None:
                self.sem_pool.append((b.stream_sem, b.stream_count))
            cm.__exit__(None, None, None)

    def dram(self, name, shape, dtype, kind):
        t = self.nc.dram_tensor(name, list(shape), dtype, kind=kind)
        b = Buf(name, t.ap(), "dram", dtype, tuple(shape))
        self.bufs[name] = b
        return b

    def _collect_deps(self, reads, writes, deps):
        for tv in reads:
            p0, p1, b0, b1 = tv.rect()
            for a in tv.buf.writes:
                if not (a.p1 <= p0 or p1 <= a.p0 or a.b1 <= b0 or b1 <= a.b0):
                    _merge_into(deps, a.deps)
        for tv in writes:
            p0, p1, b0, b1 = tv.rect()
            for a in tv.buf.writes:
                if not (a.p1 <= p0 or p1 <= a.p0 or a.b1 <= b0 or b1 <= a.b0):
                    _merge_into(deps, a.deps)
            for a in tv.buf.reads:
                if not (a.p1 <= p0 or p1 <= a.p0 or a.b1 <= b0 or b1 <= a.b0):
                    _merge_into(deps, a.deps)

    def _waits_for(self, queue, deps):
        cl = self.clock[queue]
        waits = []
        for sem, val in deps.items():
            if cl.get(sem, 0) >= val:
                continue
            cl[sem] = val
            waits.append((sem, val))
        self.n_waits += len(waits)
        return waits

    def _record_accesses(self, reads, writes, dep, is_dma=False):
        depd = dep if isinstance(dep, dict) else {dep[0]: dep[1]}
        for tv in reads:
            p0, p1, b0, b1 = tv.rect()
            tv.buf.reads.append(Access(p0, p1, b0, b1, depd))
            tv.buf.reads = _prune(tv.buf.reads)
            tv.buf.batch_open = False
        for tv in writes:
            p0, p1, b0, b1 = tv.rect()
            tv.buf.writes = [a for a in tv.buf.writes
                             if not (p0 <= a.p0 and a.p1 <= p1 and b0 <= a.b0 and a.b1 <= b1)]
            tv.buf.reads = [a for a in tv.buf.reads
                            if not (p0 <= a.p0 and a.p1 <= p1 and b0 <= a.b0 and a.b1 <= b1)]
            tv.buf.writes.append(Access(p0, p1, b0, b1, depd))
            if not is_dma:
                tv.buf.batch_open = False

    def op(self, queue, emit, reads=(), writes=()):
        reads, writes = list(reads), list(writes)
        deps = {}
        self._collect_deps(reads, writes, deps)
        waits = self._waits_for(queue, deps)
        self.qcount[queue] += 1
        dep = (f"prog_{queue}", self.qcount[queue])
        self._record_accesses(reads, writes, dep)
        self.ops.append(OpRec(queue, waits, emit, (dep[0], 1)))

    def mm_group(self, psum_tv, items):
        deps = {}
        all_reads = []
        for _, rds in items:
            all_reads.extend(rds)
        self._collect_deps(all_reads, [psum_tv], deps)
        waits = self._waits_for("PE", deps)
        self.qcount["PE"] += 1
        dep = ("prog_PE", self.qcount["PE"])
        self._record_accesses(all_reads, [psum_tv], dep)
        n = len(items)
        for i, (fn, _) in enumerate(items):
            self.ops.append(OpRec("PE", waits if i == 0 else [],
                                  fn, (dep[0], 1) if i == n - 1 else None))

    def mm_seq(self, psums, seq):
        deps = {}
        all_reads = []
        for _, rds, _, _ in seq:
            all_reads.extend(rds)
        self._collect_deps(all_reads, list(psums), deps)
        waits = self._waits_for("PE", deps)
        incs = {}
        for fn, rds, pi, last in seq:
            if last:
                self.qcount["PE"] += 1
                incs[pi] = self.qcount["PE"]
        for tv in all_reads:
            p0, p1, b0, b1 = tv.rect()
            tv.buf.reads.append(Access(p0, p1, b0, b1, {"prog_PE": self.qcount["PE"]}))
            tv.buf.reads = _prune(tv.buf.reads)
            tv.buf.batch_open = False
        for pi, tv in enumerate(psums):
            self._record_accesses([], [tv], ("prog_PE", incs[pi]))
        first = True
        for fn, rds, pi, last in seq:
            self.ops.append(OpRec("PE", waits if first else [], fn,
                                  ("prog_PE", 1) if last else None))
            first = False

    def mm_open(self, psum_tv):
        return _MMGroup(self, psum_tv)

    def dma(self, queue, dst_tv, src_tv, emit=None):
        db = dst_tv.buf
        if db.stream_sem is None:
            if self.sem_pool:
                db.stream_sem, db.stream_count = self.sem_pool.pop()
            else:
                db.stream_sem = self._new_sem(f"dma_{db.name}")
            db.stream_deps = {}
        deps = {}
        self._collect_deps([src_tv], [dst_tv], deps)
        if not db.batch_open:
            if db.stream_count:
                deps[db.stream_sem] = max(deps.get(db.stream_sem, 0), db.stream_count)
            db.batch_open = True
        waits = self._waits_for(queue, deps)
        db.stream_count += 16
        db.stream_deps[db.stream_sem] = db.stream_count
        self._record_accesses([src_tv], [dst_tv], db.stream_deps, is_dma=True)
        if emit is None:
            d, s = dst_tv, src_tv

            def emit(eng):
                return eng.dma_start(out=d.ap(), in_=s.ap())
        self.ops.append(OpRec(queue, waits, emit, (db.stream_sem, 16)))

    def barrier(self):
        deps = {}
        for q in ("PE", "ACT", "DVE", "POOL"):
            if self.qcount[q]:
                deps[f"prog_{q}"] = self.qcount[q]
        for b in self.bufs.values():
            if b.stream_sem is not None:
                deps[b.stream_sem] = b.stream_count
        for q in QUEUES:
            waits = self._waits_for(q, dict(deps))
            if waits:
                self.ops.append(OpRec(q, waits, None, None))

    def final_wait(self, queue, bufs):
        waits = []
        for b in bufs:
            if b.stream_sem is not None:
                waits.append((b.stream_sem, b.stream_count))
        self.ops.append(OpRec(queue, waits, None, None))

    def build(self):
        st = contextlib.ExitStack()
        for name in self.sem_names:
            self.sems[name] = st.enter_context(self.nc.semaphore(name))
        per_q = {q: [] for q in QUEUES}
        for rec in self.ops:
            per_q[rec.queue].append(rec)
        sec_name = {"PE": "tensor", "ACT": "scalar", "DVE": "vector",
                    "POOL": "gpsimd", "SYNC": "sync"}
        with self.nc.Block(self.name) as block:
            for q in QUEUES:
                recs = per_q[q]
                if not recs:
                    continue

                def body(e, recs=recs):
                    for rec in recs:
                        for sem, val in rec.waits:
                            e.wait_ge(self.sems[sem], val)
                        if rec.emit is not None:
                            inst = rec.emit(e)
                            if rec.inc is not None:
                                inst.then_inc(self.sems[rec.inc[0]], rec.inc[1])

                getattr(block, sec_name[q])(body)
        st.close()

    def close(self):
        for _, cm in reversed(self._cm):
            cm.__exit__(None, None, None)


"""Common emission helpers on top of fw.Ctx for the mk_seq_preln_d4 kernel.

Conventions:
 - feature-major activation: list of d-tile TVs [128, T] (tile j = features j*128..)
 - weights host-pre-transposed where needed; streamed fp32->bf16 via POOL cast DMA.
 - LN: xm = x - mean (bf16, 1 DVE pass per tile); rstd (rb, [128,T] fp32 broadcast)
   folded into projection evacuations via scalar_tensor_tensor.
"""

PI = float(np.pi)


def f32(v):
    return np.float32(v)



def ts(ctx, q, out, in0, scalar1, op0, scalar2=None, op1=None, accum_out=None):
    s1 = scalar1.ap() if isinstance(scalar1, TV) else scalar1
    s2 = scalar2.ap() if isinstance(scalar2, TV) else scalar2
    reads = [in0] + [s for s in (scalar1, scalar2) if isinstance(s, TV)]
    writes = [out]
    if accum_out is not None:
        writes = writes + [accum_out]

    def emit(e, out=out, in0=in0, s1=s1, s2=s2, op0=op0, op1=op1, ao=accum_out):
        kw = {}
        if op1 is not None:
            kw["op1"] = op1
        if ao is not None:
            kw["accum_out"] = ao.ap()
        return e.tensor_scalar(out=out.ap(), in0=in0.ap(), scalar1=s1, scalar2=s2,
                               op0=op0, **kw)
    ctx.op(q, emit, reads=reads, writes=writes)


def tt(ctx, q, out, a, b, op):
    ctx.op(q, lambda e, out=out, a=a, b=b, op=op: e.tensor_tensor(
        out=out.ap(), in0=a.ap(), in1=b.ap(), op=op), reads=[a, b], writes=[out])


def stt(ctx, out, in0, scalar, op0, in1, op1, accum_out=None):
    s = scalar.ap() if isinstance(scalar, TV) else (0.0 if scalar is None else scalar)
    reads = [in0, in1] + ([scalar] if isinstance(scalar, TV) else [])
    writes = [out] + ([accum_out] if accum_out is not None else [])

    def emit(e, out=out, in0=in0, s=s, in1=in1, op0=op0, op1=op1, ao=accum_out):
        kw = {"accum_out": ao.ap()} if ao is not None else {}
        return e.scalar_tensor_tensor(out=out.ap(), in0=in0.ap(), scalar=s,
                                      in1=in1.ap(), op0=op0, op1=op1, **kw)
    ctx.op("DVE", emit, reads=reads, writes=writes)


def act(ctx, out, in_, func, bias=0.0, scale=1.0, accum_out=None):
    b = bias.ap() if isinstance(bias, TV) else f32(bias)
    s = scale.ap() if isinstance(scale, TV) else f32(scale)
    reads = [in_] + [v for v in (bias, scale) if isinstance(v, TV)]
    writes = [out] + ([accum_out] if accum_out is not None else [])

    def emit(e, out=out, in_=in_, func=func, b=b, s=s, ao=accum_out):
        kw = {"accum_out": ao.ap()} if ao is not None else {}
        return e.activation(out=out.ap(), in_=in_.ap(), func=func, bias=b, scale=s, **kw)
    ctx.op("ACT", emit, reads=reads, writes=writes)


def copy(ctx, q, out, in_):
    if q == "ACT":
        ctx.op(q, lambda e, o=out, i=in_: e.copy(o.ap(), i.ap()), reads=[in_], writes=[out])
    else:
        ctx.op(q, lambda e, o=out, i=in_: e.tensor_copy(o.ap(), i.ap()),
               reads=[in_], writes=[out])


def memset(ctx, q, out, val):
    ctx.op(q, lambda e, o=out, v=val: e.memset(o.ap(), v), writes=[out])


def reduce(ctx, q, out, in_, op, negate=False):
    ctx.op(q, lambda e, o=out, i=in_, op=op, n=negate: e.tensor_reduce(
        out=o.ap(), in_=i.ap(), axis=AX.X, op=op, negate=n), reads=[in_], writes=[out])


def recip(ctx, out, in_):
    ctx.op("DVE", lambda e, o=out, i=in_: e.reciprocal(o.ap(), i.ap()),
           reads=[in_], writes=[out])


def transpose(ctx, psum_out, in_, ident):
    ctx.mm_group(psum_out, [(lambda e, o=psum_out, i=in_, id=ident: e.transpose(
        o.ap(), i.ap(), id.ap()), [in_, ident])])


def _affsel_emit(e, o, i, p, c, f, b, m):
    oa = o.ap()
    ia = i.ap()
    return e.affine_select(out=oa, in_=ia, pattern=p, compare_op=c, fill=f,
                           base=b, channel_multiplier=m)


def affsel(ctx, out, in_, pattern, cmp, fill, base, chmul):
    ctx.op("POOL", lambda e, o=out, i=in_, p=pattern, c=cmp, f=fill, b=base, m=chmul:
           _affsel_emit(e, o, i, p, c, f, b, m),
           reads=[in_], writes=[out])


def mm_acc(ctx, psum_tv, pairs):
    items = []
    n = len(pairs)
    for i, (lt, rt) in enumerate(pairs):
        st, sp = i == 0, i == n - 1
        items.append((lambda e, lt=lt, rt=rt, st=st, sp=sp, o=psum_tv: e.matmul(
            o.ap(), lt.ap(), rt.ap(), start=st, stop=sp), [lt, rt]))
    ctx.mm_group(psum_tv, items)



def load_full_bf16(ctx, name, dram_buf, R, C, q="POOL"):
    nk = (R + 127) // 128
    sb = ctx.sbuf(name, [128, nk * C], BF16)
    d = dram_buf.apobj
    tvs = []
    for ch in range(nk):
        r0, r1 = ch * 128, min(R, (ch + 1) * 128)
        tv = sb[:r1 - r0, ch * C:(ch + 1) * C]

        def emit(e, tv=tv, d=d, r0=r0, r1=r1):
            return e.dma_start(out=tv.ap(), in_=d[r0:r1, :])
        ctx.dma(q, tv, dram_buf.tv(), emit=emit)
        tvs.append(tv)
    return tvs


class WSlots:

    def __init__(self, ctx, name, nslots, npart, ncols):
        self.ctx = ctx
        self.npart, self.ncols = npart, ncols
        self.slots = [ctx.sbuf(f"{name}_s{i}", [npart, ncols], BF16) for i in range(nslots)]
        self.i = 0

    def load(self, dram_buf, r0, nrows, c0, ncols, pre=None):
        tv = self.slots[self.i][:nrows, :ncols]
        self.i = (self.i + 1) % len(self.slots)
        d = dram_buf.apobj

        def emit(e, tv=tv, d=d, r0=r0, nrows=nrows, c0=c0, ncols=ncols, pre=pre):
            src = d[pre] if pre is not None else d
            return e.dma_start(out=tv.ap(), in_=src[r0:r0 + nrows, c0:c0 + ncols])
        self.ctx.dma("POOL", tv, dram_buf.tv(), emit=emit)
        return tv


def load_col_fp32(ctx, name, dram_buf, R, ln, ps, q="SYNC"):
    nk = (R + 127) // 128
    stage = ctx.sbuf(name + "_st", [nk, 128], F32)
    sb = ctx.sbuf(name, [128, nk], F32)
    d = dram_buf.apobj

    def emit(e, stage=stage, d=d, nk=nk):
        return e.dma_start(out=stage.ap(), in_=d.rearrange("(k p) -> k p", p=128))
    ctx.dma(q, stage, dram_buf.tv(), emit=emit)
    transpose(ctx, ps[:, 0:nk], stage, ln.ident[0:nk, 0:nk])
    copy(ctx, "ACT", sb, ps[:, 0:nk])
    return sb



class LNScratch:
    def __init__(self, ctx, T):
        self.T = T
        self.stat = ctx.sbuf("ln_stat", [1, 1024], F32)
        self.rb = ctx.sbuf("ln_rb", [128, T], F32)
        self.xsq = ctx.sbuf("ln_xsq", [128, 2 * 512], BF16)
        self.ones_k = ctx.sbuf("ln_ones_k", [128, 1], BF16)
        self.ones_1 = ctx.sbuf("ln_ones_1", [1, 128], F32)
        self.ident = ctx.sbuf("g_ident", [128, 128], F32)
        self.identb = ctx.sbuf("g_identb", [128, 128], BF16)
        self.cmask = ctx.sbuf("g_cmask", [128, 128], F32)
        memset(ctx, "POOL", self.cmask, 0.0)
        affsel(ctx, self.cmask, self.cmask, [[-1, 128]], ALU.is_ge, -1e30, 0, 1)
        memset(ctx, "DVE", self.ones_k, 1.0)
        memset(ctx, "DVE", self.ones_1, 1.0)
        memset(ctx, "POOL", self.ident, 1.0)
        affsel(ctx, self.ident, self.ident, [[-1, 128]], ALU.is_equal, 0.0, 0, 1)
        copy(ctx, "DVE", self.identb, self.ident)


def emit_ln(ctx, xs, tok0, T, ln, ps_a, ps_b, xm, mode="ln", pre_col=None, eps=1e-5):
    nd = len(xs)
    D = nd * 128
    for c in range((T + 511) // 512):
        c0, c1 = c * 512, min(T, (c + 1) * 512)
        W = c1 - c0
        mean = ln.stat[:, 0:W]
        sd = ln.stat[:, 512:512 + W]
        g2 = ctx.mm_open(ps_b[0:1, 0:W])
        for j in range(nd):
            xc = xs[j][:, tok0 + c0:tok0 + c1]
            xq = ln.xsq[:, (j % 2) * 512:(j % 2) * 512 + W]
            if mode == "rms":
                if pre_col is not None:
                    ts(ctx, "DVE", xm[j][:, c0:c1], xc, pre_col[j], ALU.mult)
                else:
                    copy(ctx, "DVE", xm[j][:, c0:c1], xc)
                act(ctx, xq, xc, AF.Square)
            else:
                copy(ctx, "DVE", xm[j][:, c0:c1], xc)
                act(ctx, xq, xm[j][:, c0:c1], AF.Square)
            g2.add(lambda e, o=ps_b[0:1, 0:W], l=ln.ones_k, r=xq, st=(j == 0), sp=(j == nd - 1):
                   e.matmul(o.ap(), l.ap(), r.ap(), start=st, stop=sp), [ln.ones_k, xq])
        g2.close()
        ts(ctx, "DVE", sd, ps_b[0:1, 0:W], 1.0 / D, ALU.mult)
        if mode == "ln":
            mm_acc(ctx, ps_a[0:1, 0:W], [(ln.ones_k, xm[j][:, c0:c1]) for j in range(nd)])
            ts(ctx, "DVE", mean, ps_a[0:1, 0:W], 1.0 / D, ALU.mult)
            mm_acc(ctx, ps_a[:, 0:W], [(ln.ones_1, mean)])
            for j in range(nd):
                xc = xs[j][:, tok0 + c0:tok0 + c1]
                stt(ctx, xm[j][:, c0:c1], xc, None, ALU.bypass,
                    ps_a[:, 0:W], ALU.subtract)
            tt(ctx, "DVE", mean, mean, mean, ALU.mult)
            tt(ctx, "DVE", sd, sd, mean, ALU.subtract)
        act(ctx, sd, sd, AF.Sqrt, bias=eps)
        recip(ctx, sd, sd)
        mm_acc(ctx, ps_b[:, 0:W], [(ln.ones_1, sd)])
        copy(ctx, "ACT", ln.rb[:, c0:c1], ps_b[:, 0:W])


def emit_ln_inplace(ctx, xs, ln, ps_a, ps_b, T, eps=1e-5):
    nd = len(xs)
    D = nd * 128
    for c in range((T + 511) // 512):
        c0, c1 = c * 512, min(T, (c + 1) * 512)
        W = c1 - c0
        mean = ln.stat[:, 0:W]
        sd = ln.stat[:, 512:512 + W]
        g1 = ctx.mm_open(ps_a[0:1, 0:W])
        g2 = ctx.mm_open(ps_b[0:1, 0:W])
        for j in range(nd):
            xc = xs[j][:, c0:c1]
            xb = ln.xsq[:, (j % 2) * 512:(j % 2) * 512 + W]
            copy(ctx, "DVE", xb, xc)
            g1.add(lambda e, o=ps_a[0:1, 0:W], l=ln.ones_k, r=xb, st=(j == 0),
                   sp=(j == nd - 1): e.matmul(o.ap(), l.ap(), r.ap(), start=st, stop=sp),
                   [ln.ones_k, xb])
            act(ctx, xb, xb, AF.Square)
            g2.add(lambda e, o=ps_b[0:1, 0:W], l=ln.ones_k, r=xb, st=(j == 0),
                   sp=(j == nd - 1): e.matmul(o.ap(), l.ap(), r.ap(), start=st, stop=sp),
                   [ln.ones_k, xb])
        g1.close()
        g2.close()
        ts(ctx, "DVE", mean, ps_a[0:1, 0:W], 1.0 / D, ALU.mult)
        ts(ctx, "DVE", sd, ps_b[0:1, 0:W], 1.0 / D, ALU.mult)
        mm_acc(ctx, ps_a[:, 0:W], [(ln.ones_1, mean)])
        for j in range(nd):
            stt(ctx, xs[j][:, c0:c1], xs[j][:, c0:c1], None, ALU.bypass,
                ps_a[:, 0:W], ALU.subtract)
        tt(ctx, "DVE", mean, mean, mean, ALU.mult)
        tt(ctx, "DVE", sd, sd, mean, ALU.subtract)
        act(ctx, sd, sd, AF.Sqrt, bias=eps)
        recip(ctx, sd, sd)
        mm_acc(ctx, ps_b[:, 0:W], [(ln.ones_1, sd)])
        copy(ctx, "ACT", ln.rb[:, c0:c1], ps_b[:, 0:W])
        for j in range(nd):
            tt(ctx, "DVE", xs[j][:, c0:c1], xs[j][:, c0:c1], ln.rb[:, c0:c1], ALU.mult)



def proj_feat(ctx, out_tiles, wT_tiles, xm, rb, ps_banks, T, n_out,
              bias_col=None, func=None, out_chunks=512, extra=None):
    nf = n_out // 128
    nk = len(xm)
    bi = 0
    for f in range(nf):
        for c0 in range(0, T, out_chunks):
            c1 = min(T, c0 + out_chunks)
            ps = ps_banks[bi % len(ps_banks)][:, 0:c1 - c0]
            bi += 1
            mm_acc(ctx, ps, [(wT_tiles[j][:, f * 128:(f + 1) * 128], xm[j][:, c0:c1])
                             for j in range(nk)])
            dst = out_tiles[f][:, c0:c1]
            if rb is not None:
                stt(ctx, dst, ps, None, ALU.bypass, rb[:, c0:c1], ALU.mult)
            else:
                copy(ctx, "DVE", dst, ps)
            if func is not None:
                act(ctx, dst, dst, func,
                    bias=bias_col[:, f:f + 1] if bias_col is not None else 0.0)
            elif bias_col is not None:
                ts(ctx, "DVE", dst, dst, bias_col[:, f:f + 1], ALU.add)
            if extra is not None:
                extra(ctx, f, c0, c1, dst)



def softmax_rows(ctx, s_tv, p_out, scale, tmp_col, do_recip_mult=True):
    mx = tmp_col[:, 0:1]
    bias = tmp_col[:, 1:2]
    sm = tmp_col[:, 2:3]
    reduce(ctx, "DVE", mx, s_tv, ALU.max)
    ts(ctx, "DVE", bias, mx, -scale, ALU.mult)
    act(ctx, p_out, s_tv, AF.Exp, bias=bias, scale=scale, accum_out=sm)
    recip(ctx, sm, sm)
    if do_recip_mult:
        ts(ctx, "DVE", p_out, p_out, sm, ALU.mult)
    return sm


"""Block emitters for mk_seq_preln_d4. Each operates on one batch (T=2048 tokens)
of the feature-major fp32 residual stream xs (8 TVs [128, 2048])."""

T = 2048


class L0AttnScratch:
    def __init__(self, ctx, tag):
        self.xm = [ctx.sbuf(f"l0a_xm{j}_{tag}", [128, T], BF16) for j in range(8)]
        self.V = [ctx.sbuf(f"l0a_v{i}_{tag}", [128, 1024], BF16) for i in range(16)]
        self.A = ctx.sbuf(f"l0a_A_{tag}", [64, 8 * 256], BF16)
        self.B = ctx.sbuf(f"l0a_B_{tag}", [64, 8 * 256], BF16)
        self.s = ctx.sbuf(f"l0a_s_{tag}", [128, T], F32)
        self.p = ctx.sbuf(f"l0a_p_{tag}", [128, T], BF16)
        self.rtok = ctx.sbuf(f"l0a_rtok_{tag}", [128, 16], F32)
        self.col = ctx.sbuf(f"l0a_col_{tag}", [128, 8], F32)


def emit_l0_attn(ctx, xs, w, ln, ps, tag):
    sc = L0AttnScratch(ctx, tag)
    emit_ln(ctx, xs, 0, T, ln, ps[0], ps[1], sc.xm, mode="ln")
    for i in range(16):
        transpose(ctx, ps[2][:, 0:1], ln.rb[0:1, i * 128:(i + 1) * 128], ln.ones_1[0:1, 0:1])
        copy(ctx, "ACT", sc.rtok[:, i:i + 1], ps[2][:, 0:1])
    mq = load_full_bf16(ctx, f"l0_mq_{tag}", w["Mq"], 1024, 512)
    mk = load_full_bf16(ctx, f"l0_mk_{tag}", w["Mk"], 1024, 512)
    mv = load_full_bf16(ctx, f"l0_mv_{tag}", w["Mv"], 1024, 1024)
    for i in range(16):
        for n in range(2):
            pp = ps[2 + (i * 2 + n) % 2][:, 0:512]
            mm_acc(ctx, pp, [(sc.xm[j][:, i * 128:(i + 1) * 128], mv[j][:, n * 512:(n + 1) * 512])
                             for j in range(8)])
            ts(ctx, "DVE", sc.V[i][:, n * 512:(n + 1) * 512], pp, sc.rtok[:, i:i + 1], ALU.mult)
    for m in range(8):
        t0 = m * 256
        for (dst, wt) in ((sc.A, mq), (sc.B, mk)):
            for jb in range(8):
                pp = ps[2 + jb % 2][0:64, 0:256]
                mm_acc(ctx, pp, [(wt[j][:, jb * 64:(jb + 1) * 64],
                                  sc.xm[j][:, t0:t0 + 256]) for j in range(8)])
                d = dst[:, jb * 256:(jb + 1) * 256]
                stt(ctx, d, pp, None, ALU.bypass, ln.rb[0:64, t0:t0 + 256], ALU.mult)
        outg = [ctx.mm_open(ps[4 + gi]) for gi in range(4)]
        started = [False] * 4
        for jp in range(8):
            for tc in range(2):
                for jj in range(4):
                    pp = ps[2 + jj % 2]
                    g = ctx.mm_open(pp)
                    for j2 in range(2):
                        j = jj * 2 + j2
                        lt = sc.B[:, jp * 256 + tc * 128:jp * 256 + (tc + 1) * 128]
                        rt = sc.A[:, j * 256:(j + 1) * 256]
                        g.add(lambda e, o=pp, l=lt, r=rt, st=(j2 == 0), sp=(j2 == 1), j2=j2:
                              e.matmul(o[:, j2 * 256:(j2 + 1) * 256].ap(), l.ap(), r.ap(),
                                       start=st, stop=sp), [lt, rt])
                    g.close()
                    copy(ctx, "DVE", sc.s[:, jj * 512:(jj + 1) * 512], pp[:, 0:512])
                softmax_rows(ctx, sc.s, sc.p, 1.0 / 8.0, sc.col[:, 0:4])
                for hp in range(8):
                    lt = sc.V[hp * 2 + tc][:, jp * 128:(jp + 1) * 128]
                    rt = sc.p[:, hp * 256:(hp + 1) * 256]
                    gi = hp // 2
                    st = not started[gi]
                    outg[gi].add(lambda e, o=ps[4 + gi], l=lt, r=rt, st=st, c0=(hp % 2) * 256:
                                 e.matmul(o[:, c0:c0 + 256].ap(), l.ap(), r.ap(),
                                          start=st, stop=False), [lt, rt])
                    started[gi] = True
        for gi in range(4):
            outg[gi].close()
        for hp in range(8):
            xsl = xs[hp][:, t0:t0 + 256]
            stt(ctx, xsl, ps[4 + hp // 2][:, (hp % 2) * 256:(hp % 2) * 256 + 256],
                None, ALU.bypass, xsl, ALU.add)


def emit_l0_ffn(ctx, xs, w, ln, ps, tag):
    TWO_PI = 2 * PI
    xm = [ctx.sbuf(f"l0f_xm{j}_{tag}", [128, T], BF16) for j in range(8)]
    emit_ln(ctx, xs, 0, T, ln, ps[0], ps[1], xm, mode="ln")
    b1c = load_col_fp32(ctx, f"l0f_b1_{tag}", w["b1"], 4096, ln, ps[0])
    b2c = load_col_fp32(ctx, f"l0f_b2_{tag}", w["b2"], 1024, ln, ps[1])
    thc = load_col_fp32(ctx, f"l0f_th_{tag}", w["qp"], 4096, ln, ps[2])
    wrc = load_col_fp32(ctx, f"l0f_wr_{tag}", w["qwr"], 4096, ln, ps[3])
    wic = load_col_fp32(ctx, f"l0f_wi_{tag}", w["qwi"], 4096, ln, ps[4])
    cc = ctx.sbuf(f"l0f_cc_{tag}", [128, 6 * 32], F32)
    sth, cth, c1c = cc[:, 0:32], cc[:, 32:64], cc[:, 64:96]
    b110, b110p, tmpc = cc[:, 96:128], cc[:, 128:160], cc[:, 160:192]
    copy(ctx, "DVE", tmpc, thc)
    for _ in range(8):
        ts(ctx, "DVE", sth, tmpc, PI, ALU.is_gt)
        stt(ctx, tmpc, sth, -TWO_PI, ALU.mult, tmpc, ALU.add)
        ts(ctx, "DVE", sth, tmpc, -PI, ALU.is_lt)
        stt(ctx, tmpc, sth, TWO_PI, ALU.mult, tmpc, ALU.add)
    ts(ctx, "DVE", cth, tmpc, PI / 2, ALU.add)
    ts(ctx, "DVE", sth, cth, PI, ALU.is_gt)
    stt(ctx, cth, sth, -TWO_PI, ALU.mult, cth, ALU.add)
    act(ctx, cth, cth, AF.Sin)
    act(ctx, sth, tmpc, AF.Sin)
    tt(ctx, "DVE", c1c, cth, wrc, ALU.mult)
    tt(ctx, "DVE", tmpc, sth, wic, ALU.mult)
    tt(ctx, "DVE", c1c, c1c, tmpc, ALU.add)
    tt(ctx, "DVE", c1c, c1c, wic, ALU.mult)
    ts(ctx, "DVE", c1c, c1c, 0.1, ALU.mult)
    ts(ctx, "DVE", b110, b1c, 0.1, ALU.mult)
    ts(ctx, "DVE", b110p, b110, PI / 2, ALU.add)
    w1s = [ctx.sbuf(f"l0f_w1s{i}_{tag}", [128, 1024], BF16) for i in range(3)]
    w2s = [ctx.sbuf(f"l0f_w2s{i}_{tag}", [128, 1024], BF16) for i in range(3)]
    hsl = [ctx.sbuf(f"l0f_h{i}_{tag}", [128, 512], F32) for i in range(2)]
    ssl = [ctx.sbuf(f"l0f_s{i}_{tag}", [128, 512], F32) for i in range(2)]
    csl = [ctx.sbuf(f"l0f_c{i}_{tag}", [128, 512], F32) for i in range(2)]
    asb = [ctx.sbuf(f"l0f_a{i}_{tag}", [128, 512], BF16) for i in range(32)]
    d1 = w["W1"].apobj
    for c in range(4):
        t0 = c * 512
        rbch = ln.rb[:, t0:t0 + 512]
        for f in range(32):
            slot = w1s[f % 3]
            def em(e, slot=slot, d1=d1, f=f):
                return e.dma_start(out=slot.ap(),
                                   in_=d1[:, f * 128:(f + 1) * 128].rearrange(
                                       "(k p) c -> p k c", p=128))
            ctx.dma("POOL", slot, w["W1"].tv(), emit=em)
            pp = ps[f % 2]
            mm_acc(ctx, pp, [(slot[:, j * 128:(j + 1) * 128], xm[j][:, t0:t0 + 512])
                             for j in range(8)])
            h0 = hsl[f % 2]
            sg = ssl[f % 2]
            cg = csl[f % 2]
            fc = slice(f, f + 1)
            stt(ctx, h0, pp, None, ALU.bypass, rbch, ALU.mult)
            act(ctx, sg, h0, AF.Sin, bias=b110[:, fc], scale=0.1)
            act(ctx, cg, h0, AF.Sin, bias=b110p[:, fc], scale=0.1)
            ts(ctx, "DVE", sg, sg, cth[:, fc], ALU.mult)
            stt(ctx, sg, cg, sth[:, fc], ALU.mult, sg, ALU.add)
            stt(ctx, h0, sg, c1c[:, fc], ALU.mult, h0, ALU.add)
            act(ctx, asb[f], h0, AF.Relu, bias=b1c[:, fc])
        groups = [ctx.mm_open(ps[fo]) for fo in range(8)]
        d2 = w["W2"].apobj
        for k in range(32):
            slot = w2s[k % 3]
            def em2(e, slot=slot, d2=d2, k=k):
                return e.dma_start(out=slot.ap(), in_=d2[k * 128:(k + 1) * 128, :])
            ctx.dma("POOL", slot, w["W2"].tv(), emit=em2)
            for fo in range(8):
                lt = slot[:, fo * 128:(fo + 1) * 128]
                rt = asb[k]
                groups[fo].add(lambda e, o=ps[fo], l=lt, r=rt, st=(k == 0):
                               e.matmul(o.ap(), l.ap(), r.ap(), start=st, stop=False),
                               [lt, rt])
        for fo in range(8):
            groups[fo].close()
            xsl = xs[fo][:, t0:t0 + 512]
            stt(ctx, xsl, ps[fo], b2c[:, fo:fo + 1], ALU.add, xsl, ALU.add)


def tr_mm(ctx, g, out_tv, in_tv, ident_tv, st, sp):
    g.add(lambda e, o=out_tv, i=in_tv, id=ident_tv, st=st, sp=sp:
          e.matmul(o.ap(), i.ap(), id.ap(), start=st, stop=sp, is_transpose=True),
          [in_tv, ident_tv])


def emit_l1_attn(ctx, xs, w, ln, ps, tag, dbg=None):
    H = 16
    xm = [ctx.sbuf(f"l1a_xm{j}_{tag}", [128, T], BF16) for j in range(8)]
    emit_ln(ctx, xs, 0, T, ln, ps[0], ps[1], xm, mode="ln")
    rtok = ctx.sbuf(f"l1a_rtok_{tag}", [128, 16], F32)
    for i in range(16):
        transpose(ctx, ps[2][:, 0:1], ln.rb[0:1, i * 128:(i + 1) * 128], ln.ones_1[0:1, 0:1])
        copy(ctx, "ACT", rtok[:, i:i + 1], ps[2][:, 0:1])
    mval = ctx.sbuf(f"l1a_mval_{tag}", [1, 4], F32)
    mcol = ctx.sbuf(f"l1a_mcol_{tag}", [128, 8], F32)
    c_1m, c_mqs = mcol[:, 0:1], mcol[:, 1:2]
    maxc, sumc, biasc = mcol[:, 2:3], mcol[:, 3:4], mcol[:, 4:5]
    ctx.dma("SYNC", mval[:, 0:1], w["mixture"].tv())
    ctx.dma("SYNC", mval[:, 1:2], w["qs"].tv())
    act(ctx, mval[:, 2:3], mval[:, 0:1], AF.Sigmoid)
    tt(ctx, "DVE", mval[:, 3:4], mval[:, 2:3], mval[:, 1:2], ALU.mult)
    ts(ctx, "DVE", mval[:, 2:3], mval[:, 2:3], -1.0, ALU.mult, 1.0, ALU.add)
    mm_acc(ctx, ps[2][:, 0:2], [(ln.ones_1, mval[:, 2:4])])
    copy(ctx, "ACT", mcol[:, 0:2], ps[2][:, 0:2])
    ao = [ctx.sbuf(f"l1a_ao{j}_{tag}", [128, T], BF16) for j in range(8)]
    pt = ctx.sbuf(f"l1a_pt_{tag}", [128, 16 * 512], BF16)
    p = ctx.sbuf(f"l1a_p_{tag}", [128, T], BF16)
    u = ctx.sbuf(f"l1a_u_{tag}", [128, T], BF16)
    tq = ctx.sbuf(f"l1a_tq_{tag}", [128, T], F32)
    mk0 = ctx.mark()
    qh = ctx.sbuf(f"l1a_qh_{tag}", [64, T], BF16)
    kh = ctx.sbuf(f"l1a_kh_{tag}", [64, T], BF16)
    q2h = ctx.sbuf(f"l1a_q2h_{tag}", [64, T], BF16)
    k2h = ctx.sbuf(f"l1a_k2h_{tag}", [64, T], BF16)
    vtok = ctx.sbuf(f"l1a_vtok_{tag}", [128, 16 * 64], BF16)
    ktok = ctx.sbuf(f"l1a_ktok_{tag}", [128, 16 * 64], BF16)
    k2tok = ctx.sbuf(f"l1a_k2tok_{tag}", [128, 16 * 64], BF16)
    gsb = ctx.sbuf(f"l1a_g_{tag}", [64, 128], BF16)
    hq = ctx.sbuf(f"l1a_hq_{tag}", [64, 512], BF16)
    kcol = ctx.sbuf(f"l1a_kcol_{tag}", [64, 4], F32)
    kcolb = ctx.sbuf(f"l1a_kcolb_{tag}", [64, 2], BF16)
    stg = ctx.sbuf(f"l1a_stg_{tag}", [1, 512], F32)
    cols = ctx.sbuf(f"l1a_cols_{tag}", [128, 8 * 16], F32)
    nmu1, ss1 = cols[:, 0:16], cols[:, 16:32]
    nmu2, ss2 = cols[:, 32:48], cols[:, 48:64]
    rs1, a2c = cols[:, 64:80], cols[:, 80:96]
    b2c, ctmp = cols[:, 96:112], cols[:, 112:128]
    wsl = [ctx.sbuf(f"l1a_ws{i}_{tag}", [128, 512], BF16) for i in range(5)]
    SC8 = 1.0 / 2048.0

    def stat_rows(qsrc, gview, kb, nmucols, sscols):
        for half, dst in ((0, nmucols), (1, sscols)):
            gcol = ctx.mm_open(ps[5][:, 0:16])
            first = True
            for c in range(4):
                if half == 0:
                    mm_acc(ctx, ps[4][0:1, 0:512], [(kb, qsrc[:, c * 512:(c + 1) * 512])])
                    ts(ctx, "DVE", stg, ps[4][0:1, 0:512], -SC8, ALU.mult)
                else:
                    mm_acc(ctx, ps[4][:64, 0:512], [(gview, qsrc[:, c * 512:(c + 1) * 512])])
                    tt(ctx, "DVE", hq, ps[4][:64, 0:512], qsrc[:, c * 512:(c + 1) * 512], ALU.mult)
                    mm_acc(ctx, ps[3][0:1, 0:512], [(ln.ones_k[0:64], hq)])
                    ts(ctx, "DVE", stg, ps[3][0:1, 0:512], 1.0 / 64.0, ALU.mult)
                for i in range(4):
                    tr_mm(ctx, gcol, ps[5][:, c * 4 + i:c * 4 + i + 1],
                          stg[:, i * 128:(i + 1) * 128], ln.ones_1[0:1, 0:1],
                          st=first, sp=(c == 3 and i == 3))
                    first = False
            gcol.close()
            copy(ctx, "ACT", dst, ps[5][:, 0:16])

    def rsig_from(nmu, ss, dst):
        tt(ctx, "DVE", ctmp, nmu, nmu, ALU.mult)
        stt(ctx, dst, ctmp, -32.0, ALU.mult, ss, ALU.add)
        ts(ctx, "DVE", dst, dst, 1.0 / 2047.0, ALU.mult)
        act(ctx, dst, dst, AF.Sqrt)
        ts(ctx, "DVE", dst, dst, 1e-5, ALU.add)
        recip(ctx, dst, dst)

    for h in range(H):
        hb = (h % 2) * 64
        for i, nm in enumerate(("WqT", "WkT", "Wq2T", "Wk2T", "WvT")):
            d = w[nm].apobj

            def em(e, s=wsl[i], d=d, h=h):
                return e.dma_start(out=s.ap(), in_=d[:, h * 64:(h + 1) * 64].rearrange(
                    "(k p) c -> p k c", p=128))
            ctx.dma("POOL", wsl[i], w[nm].tv(), emit=em)
        for mi, dst in ((0, qh), (1, kh), (2, q2h), (3, k2h)):
            for n in range(4):
                pp = ps[n % 2][0:64, 0:512]
                mm_acc(ctx, pp, [(wsl[mi][:, j * 64:(j + 1) * 64],
                                  xm[j][:, n * 512:(n + 1) * 512]) for j in range(8)])
                stt(ctx, dst[:, n * 512:(n + 1) * 512], pp, None, ALU.bypass,
                    ln.rb[0:64, n * 512:(n + 1) * 512], ALU.mult)
        for tt_i in range(16):
            pp = ps[2 + tt_i % 2][:, 0:64]
            mm_acc(ctx, pp, [(xm[j][:, tt_i * 128:(tt_i + 1) * 128],
                              wsl[4][:, j * 64:(j + 1) * 64]) for j in range(8)])
            ts(ctx, "DVE", vtok[:, tt_i * 64:(tt_i + 1) * 64], pp, rtok[:, tt_i:tt_i + 1], ALU.mult)
        for src, dst in ((kh, ktok), (k2h, k2tok)):
            for gq in range(2):
                bank = ps[6].bf16()[:, 0:512]
                gg = ctx.mm_open(ps[6])
                for i2 in range(8):
                    kc = gq * 8 + i2
                    tr_mm(ctx, gg, bank[:, i2 * 64:(i2 + 1) * 64],
                          src[:, kc * 128:(kc + 1) * 128], ln.identb[0:64, 0:64],
                          st=(i2 == 0), sp=(i2 == 7))
                gg.close()
                copy(ctx, "DVE", dst[:, gq * 512:(gq + 1) * 512], bank)
        for gi, (ktk, gslice) in enumerate(((ktok, gsb[:, 0:64]), (k2tok, gsb[:, 64:128]))):
            mm_acc(ctx, ps[3][0:64, 0:64], [(ktk[:, kc * 64:(kc + 1) * 64],
                                             ktk[:, kc * 64:(kc + 1) * 64]) for kc in range(16)])
            copy(ctx, "DVE", gslice, ps[3][0:64, 0:64])
        reduce(ctx, "DVE", kcol[:, 0:1], kh, ALU.add)
        reduce(ctx, "DVE", kcol[:, 1:2], k2h, ALU.add)
        copy(ctx, "DVE", kcolb, kcol[:, 0:2])
        stat_rows(qh, gsb[:, 0:64], kcolb[:, 0:1], nmu1, ss1)
        stat_rows(q2h, gsb[:, 64:128], kcolb[:, 1:2], nmu2, ss2)
        rsig_from(nmu1, ss1, rs1)
        ts(ctx, "DVE", rs1, rs1, 0.125, ALU.mult)
        rsig_from(nmu2, ss2, a2c)
        ts(ctx, "DVE", a2c, a2c, c_mqs, ALU.mult, 0.125, ALU.mult)
        tt(ctx, "DVE", ctmp, nmu2, a2c, ALU.mult)
        ts(ctx, "DVE", b2c, ctmp, c_1m, ALU.add)
        if dbg is not None and h == 0:
            for nm, tvv in (("qh", qh), ("kh", kh), ("q2h", q2h), ("k2h", k2h)):
                ctx.dma("SYNC", dbg[nm].tv(), tvv)
            ctx.dma("SYNC", dbg["vtok"].tv(), vtok)
            ctx.dma("SYNC", dbg["ktok"].tv(), ktok)
            ctx.dma("SYNC", dbg["gsb"].tv(), gsb)
            ctx.dma("SYNC", dbg["cols"].tv(), cols)
        for qc in range(16):
            W = (qc + 1) * 128
            if qc % 4 == 0:
                for kc in range(qc, min(qc + 4, 16)):
                    memset(ctx, "POOL", pt[:, kc * 512:(kc + 1) * 512], 0.0)
            nch = (W + 511) // 512
            for c in range(nch):
                c0, c1 = c * 512, min(W, (c + 1) * 512)
                pp = ps[c % 2][:, 0:c1 - c0]
                mm_acc(ctx, pp, [(q2h[:, qc * 128:(qc + 1) * 128], k2h[:, c0:c1])])
                ts(ctx, "DVE", u[:, c0:c1], pp, a2c[:, qc:qc + 1], ALU.mult,
                   b2c[:, qc:qc + 1], ALU.add)
            for c in range(nch):
                c0, c1 = c * 512, min(W, (c + 1) * 512)
                pp = ps[c % 2][:, 0:c1 - c0]
                mm_acc(ctx, pp, [(qh[:, qc * 128:(qc + 1) * 128], kh[:, c0:c1])])
                stt(ctx, tq[:, c0:c1], pp, nmu1[:, qc:qc + 1], ALU.add, u[:, c0:c1], ALU.mult)
            tt(ctx, "DVE", tq[:, qc * 128:W], tq[:, qc * 128:W], ln.cmask, ALU.add)
            reduce(ctx, "DVE", maxc, tq[:, 0:W], ALU.max)
            ts(ctx, "DVE", biasc, maxc, rs1[:, qc:qc + 1], ALU.mult, -1.0, ALU.mult)
            act(ctx, p[:, 0:W], tq[:, 0:W], AF.Exp, bias=biasc, scale=rs1[:, qc:qc + 1],
                accum_out=sumc)
            recip(ctx, sumc, sumc)
            ts(ctx, "POOL", p[:, 0:W], p[:, 0:W], sumc, ALU.mult)
            for kc in range(qc + 1):
                bank = ps[6 + kc % 2]
                gg = ctx.mm_open(bank)
                tr_mm(ctx, gg, bank.bf16()[:, 0:128], p[:, kc * 128:(kc + 1) * 128],
                      ln.identb, st=True, sp=True)
                gg.close()
                dst = pt[:, kc * 512 + (qc % 4) * 128:kc * 512 + (qc % 4) * 128 + 128]
                copy(ctx, "DVE" if kc % 2 == 0 else "ACT", dst, bank.bf16()[:, 0:128])
            if dbg is not None and h == 0 and qc == 9:
                ctx.dma("SYNC", dbg["u"].tv(), u)
                ctx.dma("SYNC", dbg["tq"].tv(), tq)
                ctx.dma("SYNC", dbg["p"].tv(), p)
            if qc % 4 == 3:
                sp_i = qc // 4
                po = ps[2]
                out_tv = po[hb:hb + 64, 0:512]
                gpv = ctx.mm_open(po)
                for kc in range(qc + 1):
                    lt = vtok[:, kc * 64:(kc + 1) * 64]
                    rt = pt[:, kc * 512:(kc + 1) * 512]
                    tp = (0, hb)
                    gpv.add(lambda e, o=out_tv, l=lt, r=rt, st=(kc == 0), sp2=(kc == qc),
                            tp=tp: e.matmul(o.ap(), l.ap(), r.ap(), start=st, stop=sp2,
                                            tile_position=tp), [lt, rt])
                gpv.close()
                copy(ctx, "DVE", ao[h // 2][hb:hb + 64, sp_i * 512:(sp_i + 1) * 512], out_tv)
    if dbg is not None:
        for j in range(8):
            ctx.dma("SYNC", dbg[f"ao{j}"].tv(), ao[j])
    ctx.free_to(mk0)
    wot = load_full_bf16(ctx, f"l1a_wot_{tag}", w["WoT"], 1024, 1024)
    for f in range(8):
        for n in range(4):
            pp = ps[f % 2][:, 0:512]
            mm_acc(ctx, pp, [(wot[j][:, f * 128:(f + 1) * 128],
                              ao[j][:, n * 512:(n + 1) * 512]) for j in range(8)])
            xsl = xs[f][:, n * 512:(n + 1) * 512]
            stt(ctx, xsl, pp, None, ALU.bypass, xsl, ALU.add)


def emit_l1_ffn(ctx, xs, w, ln, ps, tag):
    xm = [ctx.sbuf(f"l1f_xm{j}_{tag}", [128, T], BF16) for j in range(8)]
    emit_ln(ctx, xs, 0, T, ln, ps[0], ps[1], xm, mode="ln")
    trash = ctx.sbuf(f"l1f_trash_{tag}", [128, T], BF16)
    gcol = ctx.sbuf(f"l1f_gcol_{tag}", [128, 20], F32)
    gcolb = ctx.sbuf(f"l1f_gcolb_{tag}", [128, 16], BF16)
    sval = ctx.sbuf(f"l1f_sval_{tag}", [1, 8], F32)
    for j in range(8):
        tt(ctx, "DVE", trash, xm[j], ln.rb, ALU.mult)
        reduce(ctx, "DVE", gcol[:, j:j + 1], trash, ALU.add)
    ts(ctx, "DVE", gcolb[:, 0:8], gcol[:, 0:8], 1.0 / 2048.0, ALU.mult)
    gw1 = load_full_bf16(ctx, f"l1f_gw1_{tag}", w["gc_w1"], 1024, 512)
    gb1 = load_col_fp32(ctx, f"l1f_gb1_{tag}", w["gc_b1"], 512, ln, ps[2])
    gw2 = load_full_bf16(ctx, f"l1f_gw2_{tag}", w["gc_w2"], 512, 256)
    gb2 = load_col_fp32(ctx, f"l1f_gb2_{tag}", w["gc_b2"], 256, ln, ps[3])
    gw3 = load_full_bf16(ctx, f"l1f_gw3_{tag}", w["gc_w3"], 256, 1)
    ctx.dma("SYNC", sval[:, 1:2], w["ar"].tv())
    ctx.dma("SYNC", sval[:, 5:6], w["gc_b3"].tv())
    ctx.dma("SYNC", sval[:, 6:7], w["cn_b4"].tv())
    for ft in range(4):
        mm_acc(ctx, ps[0][:, 0:1], [(gw1[j][:, ft * 128:(ft + 1) * 128], gcolb[:, j:j + 1])
                                    for j in range(8)])
        act(ctx, gcolb[:, 8 + ft:9 + ft], ps[0][:, 0:1], AF.Gelu, bias=gb1[:, ft:ft + 1])
    for ft in range(2):
        mm_acc(ctx, ps[0][:, 0:1], [(gw2[k][:, ft * 128:(ft + 1) * 128], gcolb[:, 8 + k:9 + k])
                                    for k in range(4)])
        act(ctx, gcolb[:, 12 + ft:13 + ft], ps[0][:, 0:1], AF.Gelu, bias=gb2[:, ft:ft + 1])
    mm_acc(ctx, ps[0][0:1, 0:1], [(gw3[k][:, 0:1], gcolb[:, 12 + k:13 + k]) for k in range(2)])
    act(ctx, sval[:, 0:1], ps[0][0:1, 0:1], AF.Sigmoid, bias=sval[:, 5:6])
    ts(ctx, "DVE", sval[:, 2:3], sval[:, 1:2], 0.6, ALU.mult)
    ts(ctx, "DVE", sval[:, 4:5], sval[:, 0:1], 1.4, ALU.mult, -1.0, ALU.add)
    tt(ctx, "DVE", sval[:, 4:5], sval[:, 4:5], sval[:, 1:2], ALU.mult)
    ts(ctx, "DVE", sval[:, 3:4], sval[:, 4:5], 0.1, ALU.add)
    w1t = load_full_bf16(ctx, f"l1f_w1_{tag}", w["cn_w1"], 1024, 1024)
    b1c = load_col_fp32(ctx, f"l1f_b1_{tag}", w["cn_b1"], 1024, ln, ps[2])
    w2t = load_full_bf16(ctx, f"l1f_w2_{tag}", w["cn_w2"], 1024, 512)
    b2c = load_col_fp32(ctx, f"l1f_b2_{tag}", w["cn_b2"], 512, ln, ps[3])
    w3t = load_full_bf16(ctx, f"l1f_w3_{tag}", w["cn_w3"], 512, 256)
    b3c = load_col_fp32(ctx, f"l1f_b3_{tag}", w["cn_b3"], 256, ln, ps[4])
    w4t = load_full_bf16(ctx, f"l1f_w4_{tag}", w["cn_w4"], 256, 1)
    out = [ctx.sbuf(f"l1f_out{j}_{tag}", [128, 512], F32) for j in range(8)]
    outb = [ctx.sbuf(f"l1f_outb{j}_{tag}", [128, 512], BF16) for j in range(8)]
    h1b = ctx.sbuf(f"l1f_h1b_{tag}", [128, 8 * 512], BF16)
    h2b = ctx.sbuf(f"l1f_h2b_{tag}", [128, 4 * 512], BF16)
    h3b = ctx.sbuf(f"l1f_h3b_{tag}", [128, 2 * 512], BF16)
    tfr = ctx.sbuf(f"l1f_tfr_{tag}", [1, 512], F32)
    cplb = ctx.sbuf(f"l1f_cplb_{tag}", [128, 2 * 512], F32)
    tmp = ctx.sbuf(f"l1f_tmp_{tag}", [128, 512], F32)
    PERM = [4, 5, 6, 7, 0, 1, 2, 3]
    SGN = [1.0, 1.0, -1.0, -1.0, -1.0, -1.0, 1.0, 1.0]
    for c in range(4):
        t0 = c * 512
        for j in range(8):
            stt(ctx, out[j], xs[j][:, t0:t0 + 512] if False else xm[j][:, t0:t0 + 512],
                None, ALU.bypass, ln.rb[:, t0:t0 + 512], ALU.mult)
        for cyc in range(3):
            for j in range(8):
                copy(ctx, "DVE" if j % 2 == 0 else "POOL", outb[j], out[j])
            for f in range(8):
                mm_acc(ctx, ps[f % 2][:, 0:512], [(w1t[j][:, f * 128:(f + 1) * 128], outb[j])
                                                  for j in range(8)])
                act(ctx, h1b[:, f * 512:(f + 1) * 512], ps[f % 2][:, 0:512], AF.Gelu,
                    bias=b1c[:, f:f + 1])
            for f in range(4):
                mm_acc(ctx, ps[f % 2][:, 0:512], [(w2t[j][:, f * 128:(f + 1) * 128],
                                                   h1b[:, j * 512:(j + 1) * 512]) for j in range(8)])
                act(ctx, h2b[:, f * 512:(f + 1) * 512], ps[f % 2][:, 0:512], AF.Gelu,
                    bias=b2c[:, f:f + 1])
            for f in range(2):
                mm_acc(ctx, ps[f % 2][:, 0:512], [(w3t[k][:, f * 128:(f + 1) * 128],
                                                   h2b[:, k * 512:(k + 1) * 512]) for k in range(4)])
                act(ctx, h3b[:, f * 512:(f + 1) * 512], ps[f % 2][:, 0:512], AF.Gelu,
                    bias=b3c[:, f:f + 1])
            mm_acc(ctx, ps[2][0:1, 0:512], [(w4t[k][:, 0:1], h3b[:, k * 512:(k + 1) * 512])
                                            for k in range(2)])
            act(ctx, tfr, ps[2][0:1, 0:512], AF.Sigmoid, bias=sval[:, 6:7])
            ts(ctx, "DVE", tfr, tfr, sval[:, 2:3], ALU.mult, sval[:, 3:4], ALU.add)
            mm_acc(ctx, ps[3][:, 0:512], [(ln.ones_1, tfr)])
            copy(ctx, "ACT", cplb[:, 0:512], ps[3][:, 0:512])
            ts(ctx, "DVE", cplb[:, 512:1024], cplb[:, 0:512], -1.0, ALU.mult)
            for j in range(8):
                cp = cplb[:, 0:512] if SGN[j] > 0 else cplb[:, 512:1024]
                stt(ctx, tmp, outb[PERM[j]], None, ALU.bypass, cp, ALU.mult)
                tt(ctx, "DVE", out[j], out[j], tmp, ALU.add)
        for j in range(8):
            tt(ctx, "DVE", xs[j][:, t0:t0 + 512], xs[j][:, t0:t0 + 512], out[j], ALU.add)


def emit_l2_attn(ctx, xs, w, ln, ps, tag):
    H = 16
    C = 128
    xm = [ctx.sbuf(f"l2a_xm{j}_{tag}", [128, T], BF16) for j in range(8)]
    emit_ln(ctx, xs, 0, T, ln, ps[0], ps[1], xm, mode="ln")
    rtok = ctx.sbuf(f"l2a_rtok_{tag}", [128, 16], F32)
    for i in range(16):
        transpose(ctx, ps[2][:, 0:1], ln.rb[0:1, i * 128:(i + 1) * 128], ln.ones_1[0:1, 0:1])
        copy(ctx, "ACT", rtok[:, i:i + 1], ps[2][:, 0:1])
    bqk = ctx.sbuf(f"l2a_bqk_{tag}", [64, 32], F32)
    stg = ctx.sbuf(f"l2a_bstg_{tag}", [16, 64], F32)
    for i, nm in enumerate(("bq", "bk")):
        ctx.dma("SYNC", stg, w[nm].tv())
        transpose(ctx, ps[2][0:64, 0:16], stg, ln.ident[0:16, 0:16])
        copy(ctx, "ACT", bqk[:, i * 16:(i + 1) * 16], ps[2][0:64, 0:16])
    bvrow = ctx.sbuf(f"l2a_bv_{tag}", [1, 1024], F32)
    ctx.dma("SYNC", bvrow, w["bv"].tv())
    tmulT = ctx.sbuf(f"l2a_tmulT_{tag}", [128, 128], F32)
    memset(ctx, "POOL", tmulT, 1.0)
    affsel(ctx, tmulT, tmulT, [[1, 128]], ALU.is_ge, 0.0, 0, -1)
    ao = [ctx.sbuf(f"l2a_ao{j}_{tag}", [128, T], BF16) for j in range(8)]
    NH = 2
    qf = [ctx.sbuf(f"l2a_q{i}_{tag}", [64, T], BF16) for i in range(NH)]
    kf = [ctx.sbuf(f"l2a_k{i}_{tag}", [64, T], BF16) for i in range(NH)]
    vt = [ctx.sbuf(f"l2a_v{i}_{tag}", [128, 16 * 64], BF16) for i in range(NH)]
    ssb = [ctx.sbuf(f"l2a_S{i}_{tag}", [64, 64], F32) for i in range(NH)]
    ssbb = [ctx.sbuf(f"l2a_Sb{i}_{tag}", [64, 64], BF16) for i in range(NH)]
    zcol = [ctx.sbuf(f"l2a_z{i}_{tag}", [64, 2], F32) for i in range(NH)]
    zcolb = [ctx.sbuf(f"l2a_zb{i}_{tag}", [64, 1], BF16) for i in range(NH)]
    ktc = [ctx.sbuf(f"l2a_kt{i}_{tag}", [128, 64], BF16) for i in range(NH)]
    sct = [ctx.sbuf(f"l2a_sct{i}_{tag}", [128, 128], BF16) for i in range(NH)]
    denr = [ctx.sbuf(f"l2a_den{i}_{tag}", [1, 128], F32) for i in range(NH)]
    denb = [ctx.sbuf(f"l2a_denb{i}_{tag}", [64, 128], F32) for i in range(NH)]
    bvb = [ctx.sbuf(f"l2a_bvb{i}_{tag}", [128, 64], F32) for i in range(NH)]
    zq = [ctx.sbuf(f"l2a_zq{i}_{tag}", [64, 512], F32) for i in range(NH)]
    zq2 = [ctx.sbuf(f"l2a_zq2{i}_{tag}", [64, 512], F32) for i in range(NH)]
    wsl = [ctx.sbuf(f"l2a_ws{i}_{tag}", [128, 512], BF16) for i in range(3 * NH)]
    ones1b = ctx.sbuf(f"l2a_ones1b_{tag}", [128, 1], BF16)
    memset(ctx, "DVE", ones1b, 1.0)
    for hg in range(H // NH):
        heads = [hg * NH + i for i in range(NH)]
        for ii, h in enumerate(heads):
            hb = (h % 2) * 64
            for mi, nm in enumerate(("WqT", "WkT", "WvT")):
                d = w[nm].apobj

                def em(e, s=wsl[3 * ii + mi], d=d, h=h):
                    return e.dma_start(out=s.ap(), in_=d[:, h * 64:(h + 1) * 64].rearrange(
                        "(k p) c -> p k c", p=128))
                ctx.dma("POOL", wsl[3 * ii + mi], w[nm].tv(), emit=em)
            for mi, dst in ((0, qf[ii]), (1, kf[ii])):
                bcol = bqk[:, mi * 16 + h:mi * 16 + h + 1]
                for n in range(4):
                    pp = ps[n % 2][0:64, 0:512]
                    mm_acc(ctx, pp, [(wsl[3 * ii + mi][:, j * 64:(j + 1) * 64],
                                      xm[j][:, n * 512:(n + 1) * 512]) for j in range(8)])
                    z0 = zq[ii]
                    stt(ctx, z0, pp, None, ALU.bypass, ln.rb[0:64, n * 512:(n + 1) * 512], ALU.mult)
                    ts(ctx, "DVE", zq2[ii], z0, bcol, ALU.add, 0.0, ALU.min)
                    act(ctx, zq2[ii], zq2[ii], AF.Exp)
                    ts(ctx, "DVE", z0, z0, bcol, ALU.add, 0.0, ALU.max)
                    tt(ctx, "DVE", dst[:, n * 512:(n + 1) * 512], zq2[ii], z0, ALU.add)
            mm_acc(ctx, ps[2][:, 0:64], [(ln.ones_1, bvrow[:, h * 64:(h + 1) * 64])])
            copy(ctx, "ACT", bvb[ii], ps[2][:, 0:64])
            for tt_i in range(16):
                pp = ps[2 + tt_i % 2][:, 0:64]
                mm_acc(ctx, pp, [(xm[j][:, tt_i * 128:(tt_i + 1) * 128],
                                  wsl[3 * ii + 2][:, j * 64:(j + 1) * 64]) for j in range(8)])
                stt(ctx, vt[ii][:, tt_i * 64:(tt_i + 1) * 64], pp, rtok[:, tt_i:tt_i + 1],
                    ALU.mult, bvb[ii], ALU.add)
            memset(ctx, "DVE", ssb[ii], 0.0)
            memset(ctx, "DVE", ssbb[ii], 0.0)
            memset(ctx, "DVE", zcol[ii], 0.0)
            memset(ctx, "DVE", zcolb[ii], 0.0)
        for i in range(16):
            for ii, h in enumerate(heads):
                hb = (h % 2) * 64
                qi = qf[ii][:, i * 128:(i + 1) * 128]
                ki = kf[ii][:, i * 128:(i + 1) * 128]
                vi = vt[ii][:, i * 64:(i + 1) * 64]
                pp = ps[ii]
                mm_acc(ctx, pp[:, 0:128], [(ki, qi)])
                stt(ctx, sct[ii], pp[:, 0:128], None, ALU.bypass, tmulT, ALU.mult)
                pd = ps[2 + ii]
                gd = ctx.mm_open(pd)
                gd.add(lambda e, o=pd[0:1, 0:128], l=ones1b, r=sct[ii]: e.matmul(
                    o.ap(), l.ap(), r.ap(), start=True, stop=False), [ones1b, sct[ii]])
                gd.add(lambda e, o=pd[0:1, 0:128], l=zcolb[ii], r=qi: e.matmul(
                    o.ap(), l.ap(), r.ap(), start=False, stop=True), [zcolb[ii], qi])
                gd.close()
                copy(ctx, "DVE", denr[ii], pd[0:1, 0:128])
                recip(ctx, denr[ii], denr[ii])
                mm_acc(ctx, ps[4 + ii][0:64, 0:128], [(ln.ones_1[:, 0:64], denr[ii])])
                copy(ctx, "ACT", denb[ii], ps[4 + ii][0:64, 0:128])
                pn = ps[6]
                gn = ctx.mm_open(pn)
                out_tv = pn[hb:hb + 64, 0:128]
                gn.add(lambda e, o=out_tv, l=vi, r=sct[ii], tp=(0, hb): e.matmul(
                    o.ap(), l.ap(), r.ap(), start=True, stop=False, tile_position=tp),
                    [vi, sct[ii]])
                gn.add(lambda e, o=out_tv, l=ssbb[ii], r=qi, tp=(0, hb): e.matmul(
                    o.ap(), l.ap(), r.ap(), start=False, stop=True, tile_position=tp),
                    [ssbb[ii], qi])
                gn.close()
                stt(ctx, ao[h // 2][hb:hb + 64, i * 128:(i + 1) * 128], out_tv,
                    None, ALU.bypass, denb[ii], ALU.mult)
                pk = ps[7]
                gk = ctx.mm_open(pk)
                tr_mm(ctx, gk, pk.bf16()[:, 0:64], ki, ln.identb[0:64, 0:64], True, True)
                gk.close()
                copy(ctx, "DVE", ktc[ii], pk.bf16()[:, 0:64])
                mm_acc(ctx, ps[4 + ii][0:64, 0:64], [(ktc[ii], vi)])
                tt(ctx, "DVE", ssb[ii], ssb[ii], ps[4 + ii][0:64, 0:64], ALU.add)
                copy(ctx, "DVE", ssbb[ii], ssb[ii])
                reduce(ctx, "DVE", zcol[ii][:, 1:2], ki, ALU.add)
                tt(ctx, "DVE", zcol[ii][:, 0:1], zcol[ii][:, 0:1], zcol[ii][:, 1:2], ALU.add)
                copy(ctx, "DVE", zcolb[ii], zcol[ii][:, 0:1])
    wot = load_full_bf16(ctx, f"l2a_wot_{tag}", w["WoT"], 1024, 1024)
    boc = load_col_fp32(ctx, f"l2a_bo_{tag}", w["bo"], 1024, ln, ps[2])
    for f in range(8):
        for n in range(4):
            pp = ps[f % 2][:, 0:512]
            mm_acc(ctx, pp, [(wot[j][:, f * 128:(f + 1) * 128],
                              ao[j][:, n * 512:(n + 1) * 512]) for j in range(8)])
            xsl = xs[f][:, n * 512:(n + 1) * 512]
            stt(ctx, xsl, pp, boc[:, f:f + 1], ALU.add, xsl, ALU.add)


def emit_l2_ffn(ctx, xs, w, ln, ps, tag):
    E, F = 8, 3584
    xm = [ctx.sbuf(f"l2f_xm{j}_{tag}", [128, T], BF16) for j in range(8)]
    emit_ln(ctx, xs, 0, T, ln, ps[0], ps[1], xm, mode="ln")
    for j in range(8):
        stt(ctx, xm[j], xm[j], None, ALU.bypass, ln.rb, ALU.mult)
    gwt = load_full_bf16(ctx, f"l2f_gw_{tag}", w["gwT"], 1024, 8)
    gtok = ctx.sbuf(f"l2f_gtok_{tag}", [128, 16 * 8], F32)
    rsc = ctx.sbuf(f"l2f_rsc_{tag}", [128, 6 * 8], F32)
    rcol = ctx.sbuf(f"l2f_rcol_{tag}", [128, 8], F32)
    sc_, ab_, nu_ = rsc[:, 0:8], rsc[:, 8:16], rsc[:, 16:24]
    ke_, ee_, g1_ = rsc[:, 24:32], rsc[:, 32:40], rsc[:, 40:48]
    t1c, s1c, t2c, s2c = rcol[:, 0:1], rcol[:, 1:2], rcol[:, 2:3], rcol[:, 3:4]
    for ti in range(16):
        pp = ps[2][:, 0:8]
        mm_acc(ctx, pp, [(xm[j][:, ti * 128:(ti + 1) * 128], gwt[j]) for j in range(8)])
        copy(ctx, "DVE", sc_, pp)
        g = gtok[:, ti * 8:(ti + 1) * 8]
        reduce(ctx, "DVE", t1c, sc_, ALU.max)
        act(ctx, ab_, sc_, AF.Abs)
        ts(ctx, "DVE", nu_, sc_, -1.0, ALU.mult, t1c, ALU.add)
        ts(ctx, "DVE", ee_, ab_, t1c, ALU.max, 0.02, ALU.mult)
        tt(ctx, "DVE", ke_, nu_, ee_, ALU.is_le)
        ts(ctx, "DVE", t2c, t1c, -1.0, ALU.mult)
        act(ctx, ee_, sc_, AF.Exp, bias=t2c)
        tt(ctx, "DVE", ee_, ee_, ke_, ALU.mult)
        reduce(ctx, "DVE", s1c, ee_, ALU.add)
        recip(ctx, s1c, s1c)
        ts(ctx, "DVE", g1_, sc_, t1c, ALU.is_equal)
        ts(ctx, "DVE", g1_, g1_, s1c, ALU.mult)
        ts(ctx, "DVE", ke_, sc_, t1c, ALU.is_equal)
        stt(ctx, ee_, ke_, -1e30, ALU.mult, sc_, ALU.add)
        reduce(ctx, "DVE", t2c, ee_, ALU.max)
        ts(ctx, "DVE", nu_, sc_, -1.0, ALU.mult, t2c, ALU.add)
        ts(ctx, "DVE", ab_, ab_, t2c, ALU.max, 0.02, ALU.mult)
        tt(ctx, "DVE", ke_, nu_, ab_, ALU.is_le)
        ts(ctx, "DVE", s2c, t2c, -1.0, ALU.mult)
        act(ctx, ee_, ee_, AF.Exp, bias=s2c)
        tt(ctx, "DVE", ee_, ee_, ke_, ALU.mult)
        reduce(ctx, "DVE", s2c, ee_, ALU.add)
        recip(ctx, s2c, s2c)
        ts(ctx, "DVE", nu_, sc_, t1c, ALU.is_equal)
        stt(ctx, ee_, nu_, -1e30, ALU.mult, sc_, ALU.add)
        ts(ctx, "DVE", ee_, ee_, t2c, ALU.is_equal)
        ts(ctx, "DVE", ee_, ee_, s2c, ALU.mult)
        tt(ctx, "DVE", g, g1_, ee_, ALU.add)
    grow = ctx.sbuf(f"l2f_grow_{tag}", [1, 512], F32)
    w1s = [ctx.sbuf(f"l2f_w1s{i}_{tag}", [128, 8 * 512], BF16) for i in range(2)]
    w3s = [ctx.sbuf(f"l2f_w3s{i}_{tag}", [128, 8 * 512], BF16) for i in range(2)]
    w2s = [ctx.sbuf(f"l2f_w2s{i}_{tag}", [128, 4 * 1024], BF16) for i in range(2)]
    s1b = ctx.sbuf(f"l2f_s1b_{tag}", [128, 4 * 512], BF16)
    h3b = ctx.sbuf(f"l2f_h3b_{tag}", [128, 4 * 512], BF16)
    gbeb = [ctx.sbuf(f"l2f_gbe{i}_{tag}", [128, 8 * 512], BF16) for i in range(4)]
    gbet = [[gbeb[tc][:, e * 512:(e + 1) * 512] for tc in range(4)] for e in range(8)]
    d1, d2, d3 = w["w1T"].apobj, w["w2T"].apobj, w["w3T"].apobj
    si = 0
    for e in range(E):
        for fg in range(7):
            sl1, sl3, sl2 = w1s[si % 2], w3s[si % 2], w2s[si % 2]
            si += 1

            def em1(e_, s=sl1, d=d1, e=e, fg=fg):
                return e_.dma_start(out=s.ap(), in_=d[e][:, fg * 512:(fg + 1) * 512]
                                    .rearrange("(k p) c -> p k c", p=128))
            ctx.dma("POOL", sl1, w["w1T"].tv(), emit=em1)

            def em3(e_, s=sl3, d=d3, e=e, fg=fg):
                return e_.dma_start(out=s.ap(), in_=d[e][:, fg * 512:(fg + 1) * 512]
                                    .rearrange("(k p) c -> p k c", p=128))
            ctx.dma("POOL", sl3, w["w3T"].tv(), emit=em3)

            def em2(e_, s=sl2, d=d2, e=e, fg=fg):
                return e_.dma_start(out=s.ap(), in_=d[e][fg * 512:(fg + 1) * 512, :]
                                    .rearrange("(k p) c -> p k c", p=128))
            ctx.dma("POOL", sl2, w["w2T"].tv(), emit=em2)
            for tc in range(4):
                t0 = tc * 512
                if fg == 0:
                    gp = ctx.mm_open(ps[3])
                    for c4 in range(4):
                        gcolv = gtok[:, (tc * 4 + c4) * 8 + e:(tc * 4 + c4) * 8 + e + 1]
                        gp.add(lambda e_, o=ps[3][0:1, 0:512], i=gcolv, id=ln.ident,
                               st=(c4 == 0), sp2=(c4 == 3), c4=c4: e_.matmul(
                                   o[:, c4 * 128:(c4 + 1) * 128].ap(), i.ap(), id.ap(),
                                   start=st, stop=sp2, is_transpose=True), [gcolv, ln.ident])
                    gp.close()
                    copy(ctx, "ACT", grow, ps[3][0:1, 0:512])
                    mm_acc(ctx, ps[3][:, 0:512], [(ln.ones_1, grow)])
                    copy(ctx, "ACT", gbet[e][tc], ps[3][:, 0:512])
                for fsub in range(4):
                    pp1 = ps[4 + fsub % 2]
                    mm_acc(ctx, pp1, [(sl1[:, k * 512 + fsub * 128:k * 512 + (fsub + 1) * 128],
                                       xm[k][:, t0:t0 + 512]) for k in range(8)])
                    act(ctx, s1b[:, fsub * 512:(fsub + 1) * 512], pp1, AF.Silu)
                    pp3 = ps[6 + fsub % 2]
                    mm_acc(ctx, pp3, [(sl3[:, k * 512 + fsub * 128:k * 512 + (fsub + 1) * 128],
                                       xm[k][:, t0:t0 + 512]) for k in range(8)])
                    stt(ctx, h3b[:, fsub * 512:(fsub + 1) * 512], pp3, None, ALU.bypass,
                        gbet[e][tc], ALU.mult)
                    tt(ctx, "DVE", s1b[:, fsub * 512:(fsub + 1) * 512],
                       s1b[:, fsub * 512:(fsub + 1) * 512],
                       h3b[:, fsub * 512:(fsub + 1) * 512], ALU.mult)
                for ho in range(8):
                    pp = ps[ho % 2]
                    mm_acc(ctx, pp, [(sl2[:, k * 1024 + ho * 128:k * 1024 + (ho + 1) * 128],
                                      s1b[:, k * 512:(k + 1) * 512]) for k in range(4)])
                    yv = xs[ho][:, t0:t0 + 512]
                    stt(ctx, yv, pp, None, ALU.bypass, yv, ALU.add)


def emit_l3_attn(ctx, xs, w, ln, ps, tag):
    H = 16
    xm = [ctx.sbuf(f"l3a_xm{j}_{tag}", [128, T], BF16) for j in range(8)]
    emit_ln(ctx, xs, 0, T, ln, ps[0], ps[1], xm, mode="ln")
    t1s = ctx.sbuf(f"l3a_t1_{tag}", [128, T], F32)
    mkr = ctx.mark()
    cosb = ctx.sbuf(f"l3a_cosb_{tag}", [128, T], F32)
    sinb = ctx.sbuf(f"l3a_sinb_{tag}", [128, T], F32)
    ctx.dma("SYNC", cosb, w["cosb"].tv())
    ctx.dma("SYNC", sinb, w["sinb2"].tv())
    permb = ctx.sbuf(f"l3a_perm_{tag}", [128, 128], BF16)
    ctx.dma("POOL", permb, w["permT"].tv())
    t2s = ctx.sbuf(f"l3a_t2_{tag}", [128, T], F32)
    for j in range(8):
        for n in range(4):
            pp = ps[2 + n % 2][:, 0:512]
            sl = slice(n * 512, (n + 1) * 512)
            mm_acc(ctx, pp, [(permb, xm[j][:, sl])])
            stt(ctx, t1s[:, sl], pp, None, ALU.bypass, sinb[:, sl], ALU.mult)
        tt(ctx, "POOL", t2s, xm[j], cosb, ALU.mult)
        tt(ctx, "DVE", xm[j], t1s, t2s, ALU.add)
    ctx.free_to(mkr)
    ao = [ctx.sbuf(f"l3a_ao{j}_{tag}", [128, T], BF16) for j in range(8)]
    mkh = ctx.mark()
    qstage = ctx.sbuf(f"l3a_qstg_{tag}", [64, 512], F32)
    bqk = ctx.sbuf(f"l3a_bqk_{tag}", [64, 48], F32)
    stg = ctx.sbuf(f"l3a_bstg_{tag}", [48, 64], F32)
    ctx.dma("SYNC", stg, w["ipb48"].tv())
    transpose(ctx, ps[2][0:64, 0:48], stg, ln.ident[0:48, 0:48])
    copy(ctx, "ACT", bqk, ps[2][0:64, 0:48])
    bvrow = ctx.sbuf(f"l3a_bvrow_{tag}", [1, 3072], F32)
    ctx.dma("SYNC", bvrow, w["ipbrow"].tv())
    rtok = ctx.sbuf(f"l3a_rtok_{tag}", [128, 16], F32)
    for i in range(16):
        transpose(ctx, ps[2][:, 0:1], ln.rb[0:1, i * 128:(i + 1) * 128], ln.ones_1[0:1, 0:1])
        copy(ctx, "ACT", rtok[:, i:i + 1], ps[2][:, 0:1])
    pt = ctx.sbuf(f"l3a_pt_{tag}", [128, 16 * 512], BF16)
    p = ctx.sbuf(f"l3a_p_{tag}", [128, T], BF16)
    tq = t1s
    mcol = ctx.sbuf(f"l3a_mcol_{tag}", [128, 4], F32)
    maxc, sumc, biasc = mcol[:, 0:1], mcol[:, 1:2], mcol[:, 2:3]
    qh = ctx.sbuf(f"l3a_qh_{tag}", [64, T], BF16)
    kh = ctx.sbuf(f"l3a_kh_{tag}", [64, T], BF16)
    vtok = ctx.sbuf(f"l3a_vtok_{tag}", [128, 16 * 64], BF16)
    bvb = ctx.sbuf(f"l3a_bvb_{tag}", [128, 64], F32)
    msk = [ctx.sbuf(f"l3a_msk{i}_{tag}", [128, T], BF16) for i in range(2)]
    wsl = [ctx.sbuf(f"l3a_ws{i}_{tag}", [128, 512], BF16) for i in range(3)]
    dw = w["ipwT"].apobj
    dm = w["maskadd"].apobj
    for h in range(H):
        hb = (h % 2) * 64
        for mi, col0 in ((0, h * 64), (1, 1024 + h * 64), (2, 2048 + h * 64)):
            def em(e, s=wsl[mi], d=dw, col0=col0):
                return e.dma_start(out=s.ap(), in_=d[:, col0:col0 + 64].rearrange(
                    "(k p) c -> p k c", p=128))
            ctx.dma("POOL", wsl[mi], w["ipwT"].tv(), emit=em)
        for mi, dst in ((0, qh), (1, kh)):
            bcol = bqk[:, (mi * 16 + h):(mi * 16 + h) + 1]
            for n in range(4):
                pp = ps[n % 2][0:64, 0:512]
                sl = slice(n * 512, (n + 1) * 512)
                mm_acc(ctx, pp, [(wsl[mi][:, j * 64:(j + 1) * 64], xm[j][:, sl])
                                 for j in range(8)])
                stt(ctx, qstage, pp, None, ALU.bypass, ln.rb[0:64, sl], ALU.mult)
                ts(ctx, "DVE", dst[:, sl], qstage, bcol, ALU.add)
        mm_acc(ctx, ps[2][:, 0:64], [(ln.ones_1, bvrow[:, 2048 + h * 64:2048 + (h + 1) * 64])])
        copy(ctx, "ACT", bvb, ps[2][:, 0:64])
        for tt_i in range(16):
            pp = ps[2 + tt_i % 2][:, 0:64]
            mm_acc(ctx, pp, [(xm[j][:, tt_i * 128:(tt_i + 1) * 128],
                              wsl[2][:, j * 64:(j + 1) * 64]) for j in range(8)])
            stt(ctx, vtok[:, tt_i * 64:(tt_i + 1) * 64], pp, rtok[:, tt_i:tt_i + 1],
                ALU.mult, bvb, ALU.add)
        for qc in range(16):
            mslot = msk[qc % 2]

            def emm(e, s=mslot, d=dm, qc=qc):
                return e.dma_start(out=s.ap(), in_=d[qc * 128:(qc + 1) * 128, :])
            ctx.dma("SYNC", mslot, w["maskadd"].tv(), emit=emm)
            for c in range(4):
                c0 = c * 512
                pp = ps[c % 2][:, 0:512]
                mm_acc(ctx, pp, [(qh[:, qc * 128:(qc + 1) * 128], kh[:, c0:c0 + 512])])
                stt(ctx, tq[:, c0:c0 + 512], pp, 0.125, ALU.mult, mslot[:, c0:c0 + 512], ALU.add)
            reduce(ctx, "DVE", maxc, tq, ALU.max)
            ts(ctx, "DVE", biasc, maxc, -1.0, ALU.mult)
            act(ctx, p, tq, AF.Exp, bias=biasc, accum_out=sumc)
            recip(ctx, sumc, sumc)
            ts(ctx, "POOL", p, p, sumc, ALU.mult)
            for kc in range(16):
                bank = ps[6 + kc % 2]
                gg = ctx.mm_open(bank)
                tr_mm(ctx, gg, bank.bf16()[:, 0:128], p[:, kc * 128:(kc + 1) * 128],
                      ln.identb, st=True, sp=True)
                gg.close()
                dst = pt[:, kc * 512 + (qc % 4) * 128:kc * 512 + (qc % 4) * 128 + 128]
                copy(ctx, "DVE" if kc % 2 == 0 else "ACT", dst, bank.bf16()[:, 0:128])
            if qc % 4 == 3:
                sp_i = qc // 4
                po = ps[2]
                out_tv = po[hb:hb + 64, 0:512]
                gpv = ctx.mm_open(po)
                for kc in range(16):
                    lt = vtok[:, kc * 64:(kc + 1) * 64]
                    rt = pt[:, kc * 512:(kc + 1) * 512]
                    gpv.add(lambda e, o=out_tv, l=lt, r=rt, st=(kc == 0), sp2=(kc == 15),
                            tp=(0, hb): e.matmul(o.ap(), l.ap(), r.ap(), start=st, stop=sp2,
                                                 tile_position=tp), [lt, rt])
                gpv.close()
                copy(ctx, "DVE", ao[h // 2][hb:hb + 64, sp_i * 512:(sp_i + 1) * 512], out_tv)
    ctx.free_to(mkh)
    wot = load_full_bf16(ctx, f"l3a_owt_{tag}", w["owT"], 1024, 1024)
    obc = load_col_fp32(ctx, f"l3a_ob_{tag}", w["ob"], 1024, ln, ps[2])
    for f in range(8):
        for n in range(4):
            pp = ps[f % 2][:, 0:512]
            mm_acc(ctx, pp, [(wot[j][:, f * 128:(f + 1) * 128],
                              ao[j][:, n * 512:(n + 1) * 512]) for j in range(8)])
            xsl = xs[f][:, n * 512:(n + 1) * 512]
            stt(ctx, xsl, pp, obc[:, f:f + 1], ALU.add, xsl, ALU.add)


def emit_l3_ffn(ctx, xs, w, ln, ps, tag, load_x_tile=None, dbg=None):
    xsave = ctx.dram(f"l3f_xsave_{tag}", [8, 128, T], F32, "Internal")
    dxs = xsave.apobj
    for j in range(8):
        def emS(e, j=j, d=dxs, s=xs[j]):
            return e.dma_start(out=d[j], in_=s.ap())
        ctx.dma("SYNC", xsave.tv(), xs[j], emit=emS)
    """MLA-ish attn + shared+top2 MoE. w: g1c/g2c via cols, WqT [1024,1024],
    WkcT/WvcT [1024,256], WoT [1024,1024], WrT [1024,8], ebrow [1,8],
    sW1T/sW2T/sW3T [1024,1024], eW1T/eW2T/eW3T [8,1024,1024]."""
    emit_ln_inplace(ctx, xs, ln, ps[0], ps[1], T)
    g1c = load_col_fp32(ctx, f"l3f_g1_{tag}", w["g1"], 1024, ln, ps[0])
    g2c = load_col_fp32(ctx, f"l3f_g2_{tag}", w["g2"], 1024, ln, ps[1])
    g1cols = [g1c[:, j:j + 1] for j in range(8)]
    g2cols = [g2c[:, j:j + 1] for j in range(8)]
    xm = [ctx.sbuf(f"l3f_xm{j}_{tag}", [128, T], BF16) for j in range(8)]
    emit_ln(ctx, xs, 0, T, ln, ps[0], ps[1], xm, mode="rms", pre_col=g1cols, eps=1e-6)
    rtok = ctx.sbuf(f"l3f_rtok_{tag}", [128, 16], F32)
    for i in range(16):
        transpose(ctx, ps[2][:, 0:1], ln.rb[0:1, i * 128:(i + 1) * 128], ln.ones_1[0:1, 0:1])
        copy(ctx, "ACT", rtok[:, i:i + 1], ps[2][:, 0:1])
    mk_ao = ctx.mark()
    ao = [ctx.sbuf(f"l3f_ao{j}_{tag}", [128, T], BF16) for j in range(8)]
    mka = ctx.mark()
    wqsl = [ctx.sbuf(f"l3f_wqs{i}_{tag}", [128, 8 * 256], BF16) for i in range(2)]
    wkt = load_full_bf16(ctx, f"l3f_wkc_{tag}", w["WkcT"], 1024, 256)
    wvt = load_full_bf16(ctx, f"l3f_wvc_{tag}", w["WvcT"], 1024, 256)
    qf = [ctx.sbuf(f"l3f_q{i}_{tag}", [128, T], BF16) for i in range(2)]
    kf = [ctx.sbuf(f"l3f_k{i}_{tag}", [128, T], BF16) for i in range(2)]
    vtok = ctx.sbuf(f"l3f_vtok_{tag}", [128, 16 * 256], BF16)
    for f in range(2):
        for n in range(4):
            pp = ps[f % 2][:, 0:512]
            sl = slice(n * 512, (n + 1) * 512)
            mm_acc(ctx, pp, [(wkt[j][:, f * 128:(f + 1) * 128], xm[j][:, sl]) for j in range(8)])
            stt(ctx, kf[f][:, sl], pp, None, ALU.bypass, ln.rb[:, sl], ALU.mult)
    for ti in range(16):
        for f2 in range(2):
            pp = ps[2 + f2][:, 0:128]
            mm_acc(ctx, pp, [(xm[j][:, ti * 128:(ti + 1) * 128],
                              wvt[j][:, f2 * 128:(f2 + 1) * 128]) for j in range(8)])
            ts(ctx, "DVE", vtok[:, ti * 256 + f2 * 128:ti * 256 + (f2 + 1) * 128], pp,
               rtok[:, ti:ti + 1], ALU.mult)
    pt = ctx.sbuf(f"l3f_pt_{tag}", [128, 16 * 512], BF16)
    p = ctx.sbuf(f"l3f_p_{tag}", [128, T], BF16)
    mcol = ctx.sbuf(f"l3f_mcol_{tag}", [128, 12], F32)
    maxc, sumc, biasc = mcol[:, 0:1], mcol[:, 1:2], mcol[:, 2:3]
    dwq = w["WqT"].apobj
    for h in range(4):
        wq = wqsl[h % 2]

        def emq(e, s=wq, d=dwq, h=h):
            return e.dma_start(out=s.ap(), in_=d[:, h * 256:(h + 1) * 256].rearrange(
                "(k p) c -> p k c", p=128))
        ctx.dma("POOL", wq, w["WqT"].tv(), emit=emq)
        for f in range(2):
            for n in range(4):
                pp = ps[f % 2][:, 0:512]
                sl = slice(n * 512, (n + 1) * 512)
                mm_acc(ctx, pp, [(wq[:, j * 256 + f * 128:j * 256 + (f + 1) * 128],
                                  xm[j][:, sl]) for j in range(8)])
                stt(ctx, qf[f][:, sl], pp, None, ALU.bypass, ln.rb[:, sl], ALU.mult)
        for qc in range(16):
            for c in range(4):
                c0 = c * 512
                pp = ps[c][:, 0:512]
                mm_acc(ctx, pp, [(qf[f2][:, qc * 128:(qc + 1) * 128],
                                  kf[f2][:, c0:c0 + 512]) for f2 in range(2)])
                reduce(ctx, "DVE", mcol[:, 4 + c:5 + c], pp, ALU.max)
            tt(ctx, "DVE", mcol[:, 8:9], mcol[:, 4:5], mcol[:, 5:6], ALU.max)
            tt(ctx, "DVE", mcol[:, 9:10], mcol[:, 6:7], mcol[:, 7:8], ALU.max)
            tt(ctx, "DVE", maxc, mcol[:, 8:9], mcol[:, 9:10], ALU.max)
            ts(ctx, "DVE", biasc, maxc, -1.0 / 16.0, ALU.mult)
            for c in range(4):
                c0 = c * 512
                act(ctx, p[:, c0:c0 + 512], ps[c][:, 0:512], AF.Exp, bias=biasc,
                    scale=1.0 / 16.0, accum_out=mcol[:, 4 + c:5 + c])
            tt(ctx, "DVE", mcol[:, 8:9], mcol[:, 4:5], mcol[:, 5:6], ALU.add)
            tt(ctx, "DVE", mcol[:, 9:10], mcol[:, 6:7], mcol[:, 7:8], ALU.add)
            tt(ctx, "DVE", sumc, mcol[:, 8:9], mcol[:, 9:10], ALU.add)
            recip(ctx, sumc, sumc)
            ts(ctx, "POOL", p, p, sumc, ALU.mult)
            for kc in range(16):
                bank = ps[6 + kc % 2]
                gg = ctx.mm_open(bank)
                tr_mm(ctx, gg, bank.bf16()[:, 0:128], p[:, kc * 128:(kc + 1) * 128],
                      ln.identb, st=True, sp=True)
                gg.close()
                dst = pt[:, kc * 512 + (qc % 4) * 128:kc * 512 + (qc % 4) * 128 + 128]
                copy(ctx, "DVE" if kc % 2 == 0 else "ACT", dst, bank.bf16()[:, 0:128])
            if qc % 4 == 3:
                sp_i = qc // 4
                for f2 in range(2):
                    po = ps[2 + f2]
                    gpv = ctx.mm_open(po)
                    for kc in range(16):
                        lt = vtok[:, kc * 256 + f2 * 128:kc * 256 + (f2 + 1) * 128]
                        rt = pt[:, kc * 512:(kc + 1) * 512]
                        gpv.add(lambda e, o=po[:, 0:512], l=lt, r=rt, st=(kc == 0),
                                sp2=(kc == 15): e.matmul(o.ap(), l.ap(), r.ap(),
                                                         start=st, stop=sp2), [lt, rt])
                    gpv.close()
                    copy(ctx, "DVE", ao[2 * h + f2][:, sp_i * 512:(sp_i + 1) * 512], po[:, 0:512])
    ctx.free_to(mka)
    wot = load_full_bf16(ctx, f"l3f_wo_{tag}", w["WoT"], 1024, 1024)
    for f in range(8):
        for n in range(4):
            pp = ps[f % 2][:, 0:512]
            mm_acc(ctx, pp, [(wot[j][:, f * 128:(f + 1) * 128],
                              ao[j][:, n * 512:(n + 1) * 512]) for j in range(8)])
            xsl = xs[f][:, n * 512:(n + 1) * 512]
            stt(ctx, xsl, pp, None, ALU.bypass, xsl, ALU.add)
    if dbg is not None:
        for j in range(8):
            ctx.dma("SYNC", dbg[f"xa{j}"].tv(), xs[j])
    ctx.free_to(mk_ao)
    ctx.barrier()
    emit_ln(ctx, xs, 0, T, ln, ps[0], ps[1], xm, mode="rms", pre_col=g2cols, eps=1e-6)
    for j in range(8):
        stt(ctx, xm[j], xm[j], None, ALU.bypass, ln.rb, ALU.mult)
    mk_moe = ctx.mark()
    wrt = load_full_bf16(ctx, f"l3f_wr_{tag}", w["WrT"], 1024, 8)
    ebr = ctx.sbuf(f"l3f_ebr_{tag}", [1, 8], F32)
    ctx.dma("SYNC", ebr, w["ebrow"].tv())
    ebb = ctx.sbuf(f"l3f_ebb_{tag}", [128, 8], F32)
    mm_acc(ctx, ps[2][:, 0:8], [(ln.ones_1, ebr)])
    copy(ctx, "ACT", ebb, ps[2][:, 0:8])
    gtok = ctx.sbuf(f"l3f_gtok_{tag}", [128, 16 * 8], F32)
    rsc = ctx.sbuf(f"l3f_rsc_{tag}", [128, 4 * 8], F32)
    rcol = ctx.sbuf(f"l3f_rcol_{tag}", [128, 8], F32)
    sc_, is1_, is2_, ee_ = rsc[:, 0:8], rsc[:, 8:16], rsc[:, 16:24], rsc[:, 24:32]
    t1c, t2c, e2c, sic = rcol[:, 0:1], rcol[:, 1:2], rcol[:, 2:3], rcol[:, 3:4]
    nt1c = rcol[:, 4:5]
    for ti in range(16):
        pp = ps[2][:, 0:8]
        mm_acc(ctx, pp, [(xm[j][:, ti * 128:(ti + 1) * 128], wrt[j]) for j in range(8)])
        tt(ctx, "DVE", sc_, pp, ebb, ALU.add)
        reduce(ctx, "DVE", t1c, sc_, ALU.max)
        ts(ctx, "DVE", nt1c, t1c, -1.0, ALU.mult)
        act(ctx, sc_, sc_, AF.Exp, bias=nt1c, accum_out=sic)
        recip(ctx, sic, sic)
        ts(ctx, "DVE", sc_, sc_, sic, ALU.mult)
        reduce(ctx, "DVE", t1c, sc_, ALU.max)
        ts(ctx, "DVE", is1_, sc_, t1c, ALU.is_equal)
        stt(ctx, ee_, is1_, -1e30, ALU.mult, sc_, ALU.add)
        reduce(ctx, "DVE", t2c, ee_, ALU.max)
        ts(ctx, "DVE", is2_, ee_, t2c, ALU.is_equal)
        ts(ctx, "DVE", nt1c, t1c, -1.0, ALU.mult)
        act(ctx, e2c, t2c, AF.Exp, bias=nt1c)
        ts(ctx, "DVE", sic, e2c, 1.0, ALU.add)
        recip(ctx, sic, sic)
        tt(ctx, "DVE", e2c, e2c, sic, ALU.mult)
        g = gtok[:, ti * 8:(ti + 1) * 8]
        ts(ctx, "DVE", is1_, is1_, sic, ALU.mult)
        ts(ctx, "DVE", is2_, is2_, e2c, ALU.mult)
        tt(ctx, "DVE", g, is1_, is2_, ALU.add)
    if dbg is not None:
        ctx.dma("SYNC", dbg["gtok"].tv(), gtok)
        ctx.dma("SYNC", dbg["xm0"].tv(), xm[0])
    gbeb = [ctx.sbuf(f"l3f_gbe{i}_{tag}", [128, 8 * 512], BF16) for i in range(4)]
    gbet = [[gbeb[tc][:, e * 512:(e + 1) * 512] for tc in range(4)] for e in range(8)]
    grow = ctx.sbuf(f"l3f_grow_{tag}", [1, 512], F32)
    for e in range(8):
        for tc in range(4):
            gp = ctx.mm_open(ps[3])
            for c4 in range(4):
                gcolv = gtok[:, (tc * 4 + c4) * 8 + e:(tc * 4 + c4) * 8 + e + 1]
                gp.add(lambda e_, o=ps[3][0:1, 0:512], i=gcolv, id=ln.ident,
                       st=(c4 == 0), sp2=(c4 == 3), c4=c4: e_.matmul(
                           o[:, c4 * 128:(c4 + 1) * 128].ap(), i.ap(), id.ap(),
                           start=st, stop=sp2, is_transpose=True), [gcolv, ln.ident])
            gp.close()
            copy(ctx, "ACT", grow, ps[3][0:1, 0:512])
            mm_acc(ctx, ps[3][:, 0:512], [(ln.ones_1, grow)])
            copy(ctx, "ACT", gbet[e][tc], ps[3][:, 0:512])
    w1s = [ctx.sbuf(f"l3f_w1s{i}_{tag}", [128, 8 * 512], BF16) for i in range(2)]
    w3s = [ctx.sbuf(f"l3f_w3s{i}_{tag}", [128, 8 * 512], BF16) for i in range(2)]
    w2s = [ctx.sbuf(f"l3f_w2s{i}_{tag}", [128, 4 * 1024], BF16) for i in range(2)]
    s1b = ctx.sbuf(f"l3f_s1b_{tag}", [128, 4 * 512], BF16)
    h3b = ctx.sbuf(f"l3f_h3b_{tag}", [128, 4 * 512], BF16)
    si = 0
    units = [("s", None)] + [("e", e) for e in range(8)]
    for unm, e in units:
        for fg in range(2):
            sl1, sl3, sl2 = w1s[si % 2], w3s[si % 2], w2s[si % 2]
            si += 1
            for srcn, slot in (("W1", sl1), ("W3", sl3)):
                dsrc = w["s" + srcn + "T"].apobj if unm == "s" else w["e" + srcn + "T"].apobj

                def eml(e_, s=slot, d=dsrc, e=e, fg=fg, unm=unm):
                    base = d if unm == "s" else d[e]
                    return e_.dma_start(out=s.ap(), in_=base[:, fg * 512:(fg + 1) * 512]
                                        .rearrange("(k p) c -> p k c", p=128))
                ctx.dma("POOL", slot, (w["s" + srcn + "T"] if unm == "s" else w["e" + srcn + "T"]).tv(), emit=eml)
            dsrc2 = w["sW2T"].apobj if unm == "s" else w["eW2T"].apobj

            def eml2(e_, s=sl2, d=dsrc2, e=e, fg=fg, unm=unm):
                base = d if unm == "s" else d[e]
                return e_.dma_start(out=s.ap(), in_=base[fg * 512:(fg + 1) * 512, :]
                                    .rearrange("(k p) c -> p k c", p=128))
            ctx.dma("POOL", sl2, (w["sW2T"] if unm == "s" else w["eW2T"]).tv(), emit=eml2)
            for tc in range(4):
                t0 = tc * 512
                for fsub in range(4):
                    pp1 = ps[4 + fsub % 2]
                    mm_acc(ctx, pp1, [(sl1[:, k * 512 + fsub * 128:k * 512 + (fsub + 1) * 128],
                                       xm[k][:, t0:t0 + 512]) for k in range(8)])
                    act(ctx, s1b[:, fsub * 512:(fsub + 1) * 512], pp1, AF.Silu)
                    pp3 = ps[6 + fsub % 2]
                    mm_acc(ctx, pp3, [(sl3[:, k * 512 + fsub * 128:k * 512 + (fsub + 1) * 128],
                                       xm[k][:, t0:t0 + 512]) for k in range(8)])
                    if unm == "e":
                        stt(ctx, h3b[:, fsub * 512:(fsub + 1) * 512], pp3, None, ALU.bypass,
                            gbet[e][tc], ALU.mult)
                    else:
                        copy(ctx, "DVE", h3b[:, fsub * 512:(fsub + 1) * 512], pp3)
                    tt(ctx, "DVE", s1b[:, fsub * 512:(fsub + 1) * 512],
                       s1b[:, fsub * 512:(fsub + 1) * 512],
                       h3b[:, fsub * 512:(fsub + 1) * 512], ALU.mult)
                for ho in range(8):
                    pp = ps[ho % 2]
                    mm_acc(ctx, pp, [(sl2[:, k * 1024 + ho * 128:k * 1024 + (ho + 1) * 128],
                                      s1b[:, k * 512:(k + 1) * 512]) for k in range(4)])
                    yv = xs[ho][:, t0:t0 + 512]
                    stt(ctx, yv, pp, None, ALU.bypass, yv, ALU.add)
    ctx.free_to(mk_moe)
    xstg = ctx.sbuf(f"l3f_xstg_{tag}", [128, T], F32)
    for j in range(8):
        def emL(e, j=j, d=dxs, dst=xstg):
            return e.dma_start(out=dst.ap(), in_=d[j])
        ctx.dma("SYNC", xstg, xsave.tv(), emit=emL)
        tt(ctx, "DVE", xs[j], xs[j], xstg, ALU.add)


from concourse.bass_utils import run_bass_kernel_spmd


def _build_program():
    nc = bass.Bass("TRN2", target_bir_lowering=False, debug=False)
    c = Ctx(nc)
    xT = c.dram("xT", [2, 1024, 2048], F32, "ExternalInput")
    yT = c.dram("yT", [2, 1024, 2048], F32, "ExternalOutput")
    w0a = {"Mq": c.dram("l0_Mq", [1024, 512], F32, "ExternalInput"),
           "Mk": c.dram("l0_Mk", [1024, 512], F32, "ExternalInput"),
           "Mv": c.dram("l0_Mv", [1024, 1024], F32, "ExternalInput")}
    w0f = {"W1": c.dram("l0f_W1", [1024, 4096], F32, "ExternalInput"),
           "b1": c.dram("l0f_b1", [4096], F32, "ExternalInput"),
           "qp": c.dram("l0f_qp", [4096], F32, "ExternalInput"),
           "qwr": c.dram("l0f_qwr", [4096], F32, "ExternalInput"),
           "qwi": c.dram("l0f_qwi", [4096], F32, "ExternalInput"),
           "W2": c.dram("l0f_W2", [4096, 1024], F32, "ExternalInput"),
           "b2": c.dram("l0f_b2", [1024], F32, "ExternalInput")}
    w1a = {**{nm: c.dram(f"l1a_{nm}", [1024, 1024], F32, "ExternalInput")
              for nm in ("WqT", "WkT", "WvT", "Wq2T", "Wk2T", "WoT")},
           "mixture": c.dram("l1a_mixture", [1, 1], F32, "ExternalInput"),
           "qs": c.dram("l1a_qs", [1, 1], F32, "ExternalInput")}
    w1f = {nm: c.dram(f"l1f_{nm}", list(shp), F32, "ExternalInput") for nm, shp in (
        ("cn_w1", (1024, 1024)), ("cn_b1", (1024,)), ("cn_w2", (1024, 512)),
        ("cn_b2", (512,)), ("cn_w3", (512, 256)), ("cn_b3", (256,)),
        ("cn_w4", (256, 1)), ("cn_b4", (1, 1)), ("gc_w1", (1024, 512)),
        ("gc_b1", (512,)), ("gc_w2", (512, 256)), ("gc_b2", (256,)),
        ("gc_w3", (256, 1)), ("gc_b3", (1, 1)), ("ar", (1, 1)))}
    w2a = {**{nm: c.dram(f"l2a_{nm}", [1024, 1024], F32, "ExternalInput")
              for nm in ("WqT", "WkT", "WvT", "WoT")},
           "bq": c.dram("l2a_bq", [16, 64], F32, "ExternalInput"),
           "bk": c.dram("l2a_bk", [16, 64], F32, "ExternalInput"),
           "bv": c.dram("l2a_bv", [1, 1024], F32, "ExternalInput"),
           "bo": c.dram("l2a_bo", [1024], F32, "ExternalInput")}
    w2f = {"gwT": c.dram("l2f_gwT", [1024, 8], F32, "ExternalInput"),
           "w1T": c.dram("l2f_w1T", [8, 1024, 3584], F32, "ExternalInput"),
           "w3T": c.dram("l2f_w3T", [8, 1024, 3584], F32, "ExternalInput"),
           "w2T": c.dram("l2f_w2T", [8, 3584, 1024], F32, "ExternalInput")}
    w3a = {"ipwT": c.dram("l3a_ipwT", [1024, 3072], F32, "ExternalInput"),
           "ipb48": c.dram("l3a_ipb48", [48, 64], F32, "ExternalInput"),
           "ipbrow": c.dram("l3a_ipbrow", [1, 3072], F32, "ExternalInput"),
           "owT": c.dram("l3a_owT", [1024, 1024], F32, "ExternalInput"),
           "ob": c.dram("l3a_ob", [1024], F32, "ExternalInput"),
           "cosb": c.dram("l3a_cosb", [128, 2048], F32, "ExternalInput"),
           "sinb2": c.dram("l3a_sinb2", [128, 2048], F32, "ExternalInput"),
           "permT": c.dram("l3a_permT", [128, 128], F32, "ExternalInput"),
           "maskadd": c.dram("l3a_maskadd", [2048, 2048], BF16, "ExternalInput")}
    w3f = {"g1": c.dram("l3f_g1", [1024], F32, "ExternalInput"),
           "g2": c.dram("l3f_g2", [1024], F32, "ExternalInput"),
           "WqT": c.dram("l3f_WqT", [1024, 1024], F32, "ExternalInput"),
           "WkcT": c.dram("l3f_WkcT", [1024, 256], F32, "ExternalInput"),
           "WvcT": c.dram("l3f_WvcT", [1024, 256], F32, "ExternalInput"),
           "WoT": c.dram("l3f_WoT", [1024, 1024], F32, "ExternalInput"),
           "WrT": c.dram("l3f_WrT", [1024, 8], F32, "ExternalInput"),
           "ebrow": c.dram("l3f_ebrow", [1, 8], F32, "ExternalInput"),
           "sW1T": c.dram("l3f_sW1T", [1024, 1024], F32, "ExternalInput"),
           "sW2T": c.dram("l3f_sW2T", [1024, 1024], F32, "ExternalInput"),
           "sW3T": c.dram("l3f_sW3T", [1024, 1024], F32, "ExternalInput"),
           "eW1T": c.dram("l3f_eW1T", [8, 1024, 1024], F32, "ExternalInput"),
           "eW2T": c.dram("l3f_eW2T", [8, 1024, 1024], F32, "ExternalInput"),
           "eW3T": c.dram("l3f_eW3T", [8, 1024, 1024], F32, "ExternalInput")}
    ps = [c.psum(f"ps{i}", [128, 512]) for i in range(8)]
    xs = [c.sbuf(f"x{j}", [128, 2048], F32) for j in range(8)]
    ln = LNScratch(c, 2048)
    for b in range(2):
        for j in range(8):
            d = xT.apobj

            def emit(e, j=j, b=b, d=d):
                return e.dma_start(out=xs[j].ap(), in_=d[b, j * 128:(j + 1) * 128, :])
            c.dma("SYNC", xs[j], xT.tv(), emit=emit)
        for fn, wd, kw in ((emit_l0_attn, w0a, {}), (emit_l0_ffn, w0f, {}),
                           (emit_l1_attn, w1a, {}), (emit_l1_ffn, w1f, {}),
                           (emit_l2_attn, w2a, {}), (emit_l2_ffn, w2f, {}),
                           (emit_l3_attn, w3a, {})):
            mk = c.mark()
            fn(c, xs, wd, ln, ps, f"{fn.__name__}_b{b}", **kw)
            c.free_to(mk)
        mk = c.mark()

        def load_x_tile(cx, j, dst, b=b, xT=xT):
            d = xT.apobj

            def em(e, j=j, b=b, d=d, dst=dst):
                return e.dma_start(out=dst.ap(), in_=d[b, j * 128:(j + 1) * 128, :])
            cx.dma("SYNC", dst, xT.tv(), emit=em)
        emit_l3_ffn(c, xs, w3f, ln, ps, f"l3f_b{b}", load_x_tile=load_x_tile)
        c.free_to(mk)
        for j in range(8):
            d = yT.apobj

            def emit(e, j=j, b=b, d=d):
                return e.dma_start(out=d[b, j * 128:(j + 1) * 128, :], in_=xs[j].ap())
            c.dma("SYNC", yT.tv()[j:j + 1, :], xs[j], emit=emit)
    c.final_wait("SYNC", [yT])
    c.build()
    c.close()
    return nc


def _shared_inputs(ins):
    import ml_dtypes
    d = {}
    for k in ("L0_attn_Mq", "L0_attn_Mk", "L0_attn_Mv"):
        d[k.replace("L0_attn_", "l0_")] = ins[k]
    for a, b2 in (("W1", "W1"), ("b1", "b1"), ("quantum_phases", "qp"),
                  ("quantum_weights_real", "qwr"), ("quantum_weights_imag", "qwi"),
                  ("W2", "W2"), ("b2", "b2")):
        d["l0f_" + b2] = ins["L0_ffn_" + a]
    for nm in ("Wq", "Wk", "Wv", "Wq2", "Wk2", "Wo"):
        d[f"l1a_{nm}T"] = np.ascontiguousarray(ins[f"L1_attn_{nm}"].T)
    d["l1a_mixture"] = np.asarray(ins["L1_attn_mixture"], np.float32).reshape(1, 1)
    d["l1a_qs"] = np.asarray(ins["L1_attn_quartet_scale"], np.float32).reshape(1, 1)
    for i in range(1, 5):
        d[f"l1f_cn_w{i}"] = ins[f"L1_ffn_cn_w{i}"]
        d[f"l1f_cn_b{i}"] = ins[f"L1_ffn_cn_b{i}"].reshape(1, 1) if i == 4 else ins[f"L1_ffn_cn_b{i}"]
    for i in range(1, 4):
        d[f"l1f_gc_w{i}"] = ins[f"L1_ffn_gc_w{i}"]
        d[f"l1f_gc_b{i}"] = ins[f"L1_ffn_gc_b{i}"].reshape(1, 1) if i == 3 else ins[f"L1_ffn_gc_b{i}"]
    d["l1f_ar"] = np.asarray(ins["L1_ffn_adaptive_range"], np.float32).reshape(1, 1)
    for nm in ("Wq", "Wk", "Wv", "Wo"):
        d[f"l2a_{nm}T"] = np.ascontiguousarray(ins[f"L2_attn_{nm}"].T)
    d["l2a_bq"] = ins["L2_attn_bq"].reshape(16, 64)
    d["l2a_bk"] = ins["L2_attn_bk"].reshape(16, 64)
    d["l2a_bv"] = ins["L2_attn_bv"].reshape(1, 1024)
    d["l2a_bo"] = ins["L2_attn_bo"]
    d["l2f_gwT"] = np.ascontiguousarray(ins["L2_ffn_gate_w"].T)
    d["l2f_w1T"] = np.ascontiguousarray(ins["L2_ffn_w1"].transpose(0, 2, 1))
    d["l2f_w3T"] = np.ascontiguousarray(ins["L2_ffn_w3"].transpose(0, 2, 1))
    d["l2f_w2T"] = np.ascontiguousarray(ins["L2_ffn_w2"].transpose(0, 2, 1))
    d["l3a_ipwT"] = np.ascontiguousarray(ins["L3_attn_in_proj_w"].T)
    d["l3a_ipb48"] = ins["L3_attn_in_proj_b"].reshape(48, 64)
    d["l3a_ipbrow"] = ins["L3_attn_in_proj_b"].reshape(1, 3072)
    d["l3a_owT"] = np.ascontiguousarray(ins["L3_attn_out_w"].T)
    d["l3a_ob"] = ins["L3_attn_out_b"]
    t = np.arange(2048, dtype=np.float64)
    pvec = np.arange(128)
    i_of_p = (pvec % 64) // 2
    inv = 1.0 / 10000.0 ** (np.arange(0, 64, 2, dtype=np.float64) / 64.0)
    fidx = (2 * i_of_p) % 32
    fr = t[None, :] * inv[fidx][:, None]
    d["l3a_cosb"] = np.cos(fr).astype(np.float32)
    sgn = np.where(pvec % 2 == 0, -1.0, 1.0)
    d["l3a_sinb2"] = (np.sin(fr) * sgn[:, None]).astype(np.float32)
    perm = np.zeros((128, 128), np.float32)
    perm[np.arange(128), np.arange(128) ^ 1] = 1.0
    d["l3a_permT"] = perm
    ii, jj = np.mgrid[0:2048, 0:2048]
    allowed = (np.abs(ii - jj) <= 128) | (jj % 256 == 0) | (jj == 0)
    d["l3a_maskadd"] = np.where(allowed, 0.0, -1e30).astype(ml_dtypes.bfloat16)
    d["l3f_g1"] = ins["L3_ffn_g1"]
    d["l3f_g2"] = ins["L3_ffn_g2"]
    for nm in ("Wq", "Wkc", "Wvc", "Wo", "Wr", "sW1", "sW2", "sW3"):
        d[f"l3f_{nm}T"] = np.ascontiguousarray(ins[f"L3_ffn_{nm}"].T)
    d["l3f_ebrow"] = ins["L3_ffn_expert_bias"].reshape(1, 8)
    for nm in ("eW1", "eW2", "eW3"):
        d[f"l3f_{nm}T"] = np.ascontiguousarray(ins[f"L3_ffn_{nm}"].transpose(0, 2, 1))
    return d


def kernel(**inputs):
    ins = {k: np.asarray(v) for k, v in inputs.items()}
    nc = _build_program()
    shared = _shared_inputs(ins)
    x = ins["L0_attn_x"]
    in_maps = []
    for core in range(8):
        m = dict(shared)
        m["xT"] = np.ascontiguousarray(x[2 * core:2 * core + 2].transpose(0, 2, 1))
        in_maps.append(m)
    res = run_bass_kernel_spmd(nc, in_maps, core_ids=list(range(8)))
    out = np.empty((16, 2048, 1024), np.float32)
    for core in range(8):
        out[2 * core:2 * core + 2] = res.results[core]["yT"].transpose(0, 2, 1)
    return out


# version 02
# speedup vs baseline: 1.4614x; 1.4614x over previous
import contextlib
import numpy as np
import concourse.bass as bass
import concourse.mybir as mybir

F32 = mybir.dt.float32
BF16 = mybir.dt.bfloat16
I32 = mybir.dt.int32
AF = mybir.ActivationFunctionType
ALU = mybir.AluOpType
AX = mybir.AxisListType

QUEUES = ("PE", "ACT", "DVE", "POOL", "SYNC")
_DT_SIZE = {F32: 4, BF16: 2, I32: 4, mybir.dt.uint8: 1, mybir.dt.int16: 2,
            mybir.dt.uint32: 4, mybir.dt.float16: 2, mybir.dt.uint16: 2}


class Access:
    __slots__ = ("p0", "p1", "b0", "b1", "deps")

    def __init__(self, p0, p1, b0, b1, deps):
        self.p0, self.p1, self.b0, self.b1 = p0, p1, b0, b1
        self.deps = deps


class Buf:
    def __init__(self, name, apobj, kind, dtype, shape):
        self.name = name
        self.apobj = apobj
        self.kind = kind
        self.dtype = dtype
        self.shape = shape
        self.writes = []
        self.reads = []
        self.stream_sem = None
        self.stream_count = 0
        self.stream_deps = None
        self.batch_open = False

    def tv(self):
        if self.kind == "dram":
            P = self.shape[0] if len(self.shape) > 1 else 1
            E = int(np.prod(self.shape[1:])) if len(self.shape) > 1 else self.shape[0]
            return TV(self, 0, P, 0, E)
        return TV(self, 0, self.shape[0], 0, int(np.prod(self.shape[1:])))


class TV:
    __slots__ = ("buf", "p0", "p1", "e0", "e1", "cast")

    def __init__(self, buf, p0, p1, e0, e1, cast=None):
        self.buf, self.p0, self.p1, self.e0, self.e1 = buf, p0, p1, e0, e1
        self.cast = cast

    @property
    def P(self):
        return self.p1 - self.p0

    @property
    def N(self):
        return self.e1 - self.e0

    def ap(self):
        h = self.buf.apobj
        if self.buf.kind == "dram":
            a = h
        elif self.cast is not None:
            a = h.bitcast(self.cast).ap()
        else:
            a = h.ap()
        return a[self.p0:self.p1, self.e0:self.e1]

    def bf16(self):
        assert self.cast is None and self.buf.dtype == F32
        return TV(self.buf, self.p0, self.p1, self.e0 * 2, self.e1 * 2, cast=BF16)

    def __getitem__(self, idx):
        ps, es = idx if isinstance(idx, tuple) else (idx, slice(None))
        if isinstance(ps, int):
            ps = slice(ps, ps + 1)
        if isinstance(es, int):
            es = slice(es, es + 1)
        p0, p1, _ = ps.indices(self.P)
        e0, e1, _ = es.indices(self.N)
        return TV(self.buf, self.p0 + p0, self.p0 + p1, self.e0 + e0,
                  self.e0 + e1, self.cast)

    def rect(self):
        if self.buf.kind != "sbuf":
            return (0, 128, 0, 1 << 40)
        s = _DT_SIZE[self.buf.dtype]
        return (self.p0, self.p1, self.e0 * s, self.e1 * s)


def _merge_into(dst: dict, deps: dict):
    for k, v in deps.items():
        if dst.get(k, -1) < v:
            dst[k] = v


def _prune(lst, maxlen=48):
    if len(lst) <= maxlen:
        return lst
    half = len(lst) // 2
    old, rest = lst[:half], lst[half:]
    p0 = min(a.p0 for a in old)
    p1 = max(a.p1 for a in old)
    b0 = min(a.b0 for a in old)
    b1 = max(a.b1 for a in old)
    deps = {}
    for a in old:
        _merge_into(deps, a.deps)
    return [Access(p0, p1, b0, b1, deps)] + rest


class OpRec:
    __slots__ = ("queue", "waits", "emit", "inc")

    def __init__(self, queue, waits, emit, inc):
        self.queue, self.waits, self.emit, self.inc = queue, waits, emit, inc


class _MMGroup:

    def __init__(self, ctx, psum_tv):
        self.ctx = ctx
        self.psum = psum_tv
        self.first = True
        self.last_val = None

    def add(self, emit_fn, reads):
        ctx = self.ctx
        deps = {}
        ctx._collect_deps(list(reads), [self.psum] if self.first else [], deps)
        waits = ctx._waits_for("PE", deps)
        ctx.qcount["PE"] += 1
        val = ctx.qcount["PE"]
        for tv in reads:
            p0, p1, b0, b1 = tv.rect()
            tv.buf.reads.append(Access(p0, p1, b0, b1, {"prog_PE": val}))
            tv.buf.reads = _prune(tv.buf.reads)
            tv.buf.batch_open = False
        ctx.ops.append(OpRec("PE", waits, emit_fn, ("prog_PE", 1)))
        self.first = False
        self.last_val = val

    def close(self):
        self.ctx._record_accesses([], [self.psum], ("prog_PE", self.last_val))


class Ctx:
    def __init__(self, nc, name="ctx"):
        self.nc = nc
        self.name = name
        self.ops = []
        self.qcount = {q: 0 for q in ("PE", "ACT", "DVE", "POOL")}
        self.clock = {q: {} for q in QUEUES}
        self.sem_names = ["prog_PE", "prog_ACT", "prog_DVE", "prog_POOL"]
        self.sems = {}
        self.bufs = {}
        self._cm = []
        self.n_waits = 0
        self.sem_pool = []

    def _new_sem(self, name):
        assert name not in self.sem_names, name
        self.sem_names.append(name)
        return name

    def sbuf(self, name, shape, dtype):
        cm = self.nc.sbuf_tensor(name, list(shape), dtype)
        t = cm.__enter__()
        self._cm.append((name, cm))
        b = Buf(name, t, "sbuf", dtype, tuple(shape))
        self.bufs[name] = b
        return b.tv()

    def psum(self, name, shape=(128, 512), dtype=F32):
        cm = self.nc.psum_tensor(name, list(shape), dtype)
        t = cm.__enter__()
        self._cm.append((name, cm))
        b = Buf(name, t, "psum", dtype, tuple(shape))
        self.bufs[name] = b
        return b.tv()

    def mark(self):
        return len(self._cm)

    def free_to(self, mark):
        self.barrier()
        while len(self._cm) > mark:
            name, cm = self._cm.pop()
            b = self.bufs.pop(name, None)
            if b is not None and b.stream_sem is not None:
                self.sem_pool.append((b.stream_sem, b.stream_count))
            cm.__exit__(None, None, None)

    def dram(self, name, shape, dtype, kind):
        t = self.nc.dram_tensor(name, list(shape), dtype, kind=kind)
        b = Buf(name, t.ap(), "dram", dtype, tuple(shape))
        self.bufs[name] = b
        return b

    def _collect_deps(self, reads, writes, deps):
        for tv in reads:
            p0, p1, b0, b1 = tv.rect()
            for a in tv.buf.writes:
                if not (a.p1 <= p0 or p1 <= a.p0 or a.b1 <= b0 or b1 <= a.b0):
                    _merge_into(deps, a.deps)
        for tv in writes:
            p0, p1, b0, b1 = tv.rect()
            for a in tv.buf.writes:
                if not (a.p1 <= p0 or p1 <= a.p0 or a.b1 <= b0 or b1 <= a.b0):
                    _merge_into(deps, a.deps)
            for a in tv.buf.reads:
                if not (a.p1 <= p0 or p1 <= a.p0 or a.b1 <= b0 or b1 <= a.b0):
                    _merge_into(deps, a.deps)

    def _waits_for(self, queue, deps):
        cl = self.clock[queue]
        waits = []
        for sem, val in deps.items():
            if cl.get(sem, 0) >= val:
                continue
            cl[sem] = val
            waits.append((sem, val))
        self.n_waits += len(waits)
        return waits

    def _record_accesses(self, reads, writes, dep, is_dma=False):
        depd = dep if isinstance(dep, dict) else {dep[0]: dep[1]}
        for tv in reads:
            p0, p1, b0, b1 = tv.rect()
            tv.buf.reads.append(Access(p0, p1, b0, b1, depd))
            tv.buf.reads = _prune(tv.buf.reads)
            tv.buf.batch_open = False
        for tv in writes:
            p0, p1, b0, b1 = tv.rect()
            tv.buf.writes = [a for a in tv.buf.writes
                             if not (p0 <= a.p0 and a.p1 <= p1 and b0 <= a.b0 and a.b1 <= b1)]
            tv.buf.reads = [a for a in tv.buf.reads
                            if not (p0 <= a.p0 and a.p1 <= p1 and b0 <= a.b0 and a.b1 <= b1)]
            tv.buf.writes.append(Access(p0, p1, b0, b1, depd))
            if not is_dma:
                tv.buf.batch_open = False

    def op(self, queue, emit, reads=(), writes=()):
        reads, writes = list(reads), list(writes)
        deps = {}
        self._collect_deps(reads, writes, deps)
        waits = self._waits_for(queue, deps)
        self.qcount[queue] += 1
        dep = (f"prog_{queue}", self.qcount[queue])
        self._record_accesses(reads, writes, dep)
        self.ops.append(OpRec(queue, waits, emit, (dep[0], 1)))

    def mm_group(self, psum_tv, items):
        deps = {}
        all_reads = []
        for _, rds in items:
            all_reads.extend(rds)
        self._collect_deps(all_reads, [psum_tv], deps)
        waits = self._waits_for("PE", deps)
        self.qcount["PE"] += 1
        dep = ("prog_PE", self.qcount["PE"])
        self._record_accesses(all_reads, [psum_tv], dep)
        n = len(items)
        for i, (fn, _) in enumerate(items):
            self.ops.append(OpRec("PE", waits if i == 0 else [],
                                  fn, (dep[0], 1) if i == n - 1 else None))

    def mm_seq(self, psums, seq):
        deps = {}
        all_reads = []
        for _, rds, _, _ in seq:
            all_reads.extend(rds)
        self._collect_deps(all_reads, list(psums), deps)
        waits = self._waits_for("PE", deps)
        incs = {}
        for fn, rds, pi, last in seq:
            if last:
                self.qcount["PE"] += 1
                incs[pi] = self.qcount["PE"]
        for tv in all_reads:
            p0, p1, b0, b1 = tv.rect()
            tv.buf.reads.append(Access(p0, p1, b0, b1, {"prog_PE": self.qcount["PE"]}))
            tv.buf.reads = _prune(tv.buf.reads)
            tv.buf.batch_open = False
        for pi, tv in enumerate(psums):
            self._record_accesses([], [tv], ("prog_PE", incs[pi]))
        first = True
        for fn, rds, pi, last in seq:
            self.ops.append(OpRec("PE", waits if first else [], fn,
                                  ("prog_PE", 1) if last else None))
            first = False

    def mm_open(self, psum_tv):
        return _MMGroup(self, psum_tv)

    def dma(self, queue, dst_tv, src_tv, emit=None):
        db = dst_tv.buf
        if db.stream_sem is None:
            if self.sem_pool:
                db.stream_sem, db.stream_count = self.sem_pool.pop()
            else:
                db.stream_sem = self._new_sem(f"dma_{db.name}")
            db.stream_deps = {}
        deps = {}
        self._collect_deps([src_tv], [dst_tv], deps)
        if not db.batch_open:
            if db.stream_count:
                deps[db.stream_sem] = max(deps.get(db.stream_sem, 0), db.stream_count)
            db.batch_open = True
        waits = self._waits_for(queue, deps)
        db.stream_count += 16
        db.stream_deps[db.stream_sem] = db.stream_count
        self._record_accesses([src_tv], [dst_tv], db.stream_deps, is_dma=True)
        if emit is None:
            d, s = dst_tv, src_tv

            def emit(eng):
                return eng.dma_start(out=d.ap(), in_=s.ap())
        self.ops.append(OpRec(queue, waits, emit, (db.stream_sem, 16)))

    def barrier(self):
        deps = {}
        for q in ("PE", "ACT", "DVE", "POOL"):
            if self.qcount[q]:
                deps[f"prog_{q}"] = self.qcount[q]
        for b in self.bufs.values():
            if b.stream_sem is not None:
                deps[b.stream_sem] = b.stream_count
        for q in QUEUES:
            waits = self._waits_for(q, dict(deps))
            if waits:
                self.ops.append(OpRec(q, waits, None, None))

    def final_wait(self, queue, bufs):
        waits = []
        for b in bufs:
            if b.stream_sem is not None:
                waits.append((b.stream_sem, b.stream_count))
        self.ops.append(OpRec(queue, waits, None, None))

    def build(self):
        st = contextlib.ExitStack()
        for name in self.sem_names:
            self.sems[name] = st.enter_context(self.nc.semaphore(name))
        per_q = {q: [] for q in QUEUES}
        for rec in self.ops:
            per_q[rec.queue].append(rec)
        sec_name = {"PE": "tensor", "ACT": "scalar", "DVE": "vector",
                    "POOL": "gpsimd", "SYNC": "sync"}
        with self.nc.Block(self.name) as block:
            for q in QUEUES:
                recs = per_q[q]
                if not recs:
                    continue

                def body(e, recs=recs):
                    for rec in recs:
                        for sem, val in rec.waits:
                            e.wait_ge(self.sems[sem], val)
                        if rec.emit is not None:
                            inst = rec.emit(e)
                            if rec.inc is not None:
                                inst.then_inc(self.sems[rec.inc[0]], rec.inc[1])

                getattr(block, sec_name[q])(body)
        st.close()

    def close(self):
        for _, cm in reversed(self._cm):
            cm.__exit__(None, None, None)


"""Common emission helpers on top of fw.Ctx for the mk_seq_preln_d4 kernel.

Conventions:
 - feature-major activation: list of d-tile TVs [128, T] (tile j = features j*128..)
 - weights host-pre-transposed where needed; streamed fp32->bf16 via POOL cast DMA.
 - LN: xm = x - mean (bf16, 1 DVE pass per tile); rstd (rb, [128,T] fp32 broadcast)
   folded into projection evacuations via scalar_tensor_tensor.
"""

PI = float(np.pi)


def f32(v):
    return np.float32(v)



def ts(ctx, q, out, in0, scalar1, op0, scalar2=None, op1=None, accum_out=None):
    s1 = scalar1.ap() if isinstance(scalar1, TV) else scalar1
    s2 = scalar2.ap() if isinstance(scalar2, TV) else scalar2
    reads = [in0] + [s for s in (scalar1, scalar2) if isinstance(s, TV)]
    writes = [out]
    if accum_out is not None:
        writes = writes + [accum_out]

    def emit(e, out=out, in0=in0, s1=s1, s2=s2, op0=op0, op1=op1, ao=accum_out):
        kw = {}
        if op1 is not None:
            kw["op1"] = op1
        if ao is not None:
            kw["accum_out"] = ao.ap()
        return e.tensor_scalar(out=out.ap(), in0=in0.ap(), scalar1=s1, scalar2=s2,
                               op0=op0, **kw)
    ctx.op(q, emit, reads=reads, writes=writes)


def tt(ctx, q, out, a, b, op):
    ctx.op(q, lambda e, out=out, a=a, b=b, op=op: e.tensor_tensor(
        out=out.ap(), in0=a.ap(), in1=b.ap(), op=op), reads=[a, b], writes=[out])


def stt(ctx, out, in0, scalar, op0, in1, op1, accum_out=None):
    s = scalar.ap() if isinstance(scalar, TV) else (0.0 if scalar is None else scalar)
    reads = [in0, in1] + ([scalar] if isinstance(scalar, TV) else [])
    writes = [out] + ([accum_out] if accum_out is not None else [])

    def emit(e, out=out, in0=in0, s=s, in1=in1, op0=op0, op1=op1, ao=accum_out):
        kw = {"accum_out": ao.ap()} if ao is not None else {}
        return e.scalar_tensor_tensor(out=out.ap(), in0=in0.ap(), scalar=s,
                                      in1=in1.ap(), op0=op0, op1=op1, **kw)
    ctx.op("DVE", emit, reads=reads, writes=writes)


def act(ctx, out, in_, func, bias=0.0, scale=1.0, accum_out=None):
    b = bias.ap() if isinstance(bias, TV) else f32(bias)
    s = scale.ap() if isinstance(scale, TV) else f32(scale)
    reads = [in_] + [v for v in (bias, scale) if isinstance(v, TV)]
    writes = [out] + ([accum_out] if accum_out is not None else [])

    def emit(e, out=out, in_=in_, func=func, b=b, s=s, ao=accum_out):
        kw = {"accum_out": ao.ap()} if ao is not None else {}
        return e.activation(out=out.ap(), in_=in_.ap(), func=func, bias=b, scale=s, **kw)
    ctx.op("ACT", emit, reads=reads, writes=writes)


def copy(ctx, q, out, in_):
    if q == "ACT":
        ctx.op(q, lambda e, o=out, i=in_: e.copy(o.ap(), i.ap()), reads=[in_], writes=[out])
    else:
        ctx.op(q, lambda e, o=out, i=in_: e.tensor_copy(o.ap(), i.ap()),
               reads=[in_], writes=[out])


def memset(ctx, q, out, val):
    ctx.op(q, lambda e, o=out, v=val: e.memset(o.ap(), v), writes=[out])


def reduce(ctx, q, out, in_, op, negate=False):
    ctx.op(q, lambda e, o=out, i=in_, op=op, n=negate: e.tensor_reduce(
        out=o.ap(), in_=i.ap(), axis=AX.X, op=op, negate=n), reads=[in_], writes=[out])


def recip(ctx, out, in_):
    ctx.op("DVE", lambda e, o=out, i=in_: e.reciprocal(o.ap(), i.ap()),
           reads=[in_], writes=[out])


def transpose(ctx, psum_out, in_, ident):
    ctx.mm_group(psum_out, [(lambda e, o=psum_out, i=in_, id=ident: e.transpose(
        o.ap(), i.ap(), id.ap()), [in_, ident])])


def _affsel_emit(e, o, i, p, c, f, b, m):
    oa = o.ap()
    ia = i.ap()
    return e.affine_select(out=oa, in_=ia, pattern=p, compare_op=c, fill=f,
                           base=b, channel_multiplier=m)


def affsel(ctx, out, in_, pattern, cmp, fill, base, chmul):
    ctx.op("POOL", lambda e, o=out, i=in_, p=pattern, c=cmp, f=fill, b=base, m=chmul:
           _affsel_emit(e, o, i, p, c, f, b, m),
           reads=[in_], writes=[out])


def mm_acc(ctx, psum_tv, pairs):
    items = []
    n = len(pairs)
    for i, (lt, rt) in enumerate(pairs):
        st, sp = i == 0, i == n - 1
        items.append((lambda e, lt=lt, rt=rt, st=st, sp=sp, o=psum_tv: e.matmul(
            o.ap(), lt.ap(), rt.ap(), start=st, stop=sp), [lt, rt]))
    ctx.mm_group(psum_tv, items)



def load_full_bf16(ctx, name, dram_buf, R, C, q="POOL"):
    nk = (R + 127) // 128
    sb = ctx.sbuf(name, [128, nk * C], BF16)
    d = dram_buf.apobj
    tvs = []
    for ch in range(nk):
        r0, r1 = ch * 128, min(R, (ch + 1) * 128)
        tv = sb[:r1 - r0, ch * C:(ch + 1) * C]

        def emit(e, tv=tv, d=d, r0=r0, r1=r1):
            return e.dma_start(out=tv.ap(), in_=d[r0:r1, :])
        ctx.dma(q, tv, dram_buf.tv(), emit=emit)
        tvs.append(tv)
    return tvs


class WSlots:

    def __init__(self, ctx, name, nslots, npart, ncols):
        self.ctx = ctx
        self.npart, self.ncols = npart, ncols
        self.slots = [ctx.sbuf(f"{name}_s{i}", [npart, ncols], BF16) for i in range(nslots)]
        self.i = 0

    def load(self, dram_buf, r0, nrows, c0, ncols, pre=None):
        tv = self.slots[self.i][:nrows, :ncols]
        self.i = (self.i + 1) % len(self.slots)
        d = dram_buf.apobj

        def emit(e, tv=tv, d=d, r0=r0, nrows=nrows, c0=c0, ncols=ncols, pre=pre):
            src = d[pre] if pre is not None else d
            return e.dma_start(out=tv.ap(), in_=src[r0:r0 + nrows, c0:c0 + ncols])
        self.ctx.dma("POOL", tv, dram_buf.tv(), emit=emit)
        return tv


def load_col_fp32(ctx, name, dram_buf, R, ln, ps, q="SYNC"):
    nk = (R + 127) // 128
    stage = ctx.sbuf(name + "_st", [nk, 128], F32)
    sb = ctx.sbuf(name, [128, nk], F32)
    d = dram_buf.apobj

    def emit(e, stage=stage, d=d, nk=nk):
        return e.dma_start(out=stage.ap(), in_=d.rearrange("(k p) -> k p", p=128))
    ctx.dma(q, stage, dram_buf.tv(), emit=emit)
    transpose(ctx, ps[:, 0:nk], stage, ln.ident[0:nk, 0:nk])
    copy(ctx, "ACT", sb, ps[:, 0:nk])
    return sb



class LNScratch:
    def __init__(self, ctx, T):
        self.T = T
        self.stat = ctx.sbuf("ln_stat", [1, 1024], F32)
        self.rb = ctx.sbuf("ln_rb", [128, T], F32)
        self.xsq = ctx.sbuf("ln_xsq", [128, 2 * 512], BF16)
        self.ones_k = ctx.sbuf("ln_ones_k", [128, 1], BF16)
        self.ones_1 = ctx.sbuf("ln_ones_1", [1, 128], F32)
        self.ident = ctx.sbuf("g_ident", [128, 128], F32)
        self.identb = ctx.sbuf("g_identb", [128, 128], BF16)
        self.cmask = ctx.sbuf("g_cmask", [128, 128], F32)
        memset(ctx, "POOL", self.cmask, 0.0)
        affsel(ctx, self.cmask, self.cmask, [[-1, 128]], ALU.is_ge, -1e30, 0, 1)
        memset(ctx, "DVE", self.ones_k, 1.0)
        memset(ctx, "DVE", self.ones_1, 1.0)
        memset(ctx, "POOL", self.ident, 1.0)
        affsel(ctx, self.ident, self.ident, [[-1, 128]], ALU.is_equal, 0.0, 0, 1)
        copy(ctx, "DVE", self.identb, self.ident)


def emit_ln(ctx, xs, tok0, T, ln, ps_a, ps_b, xm, mode="ln", pre_col=None, eps=1e-5):
    nd = len(xs)
    D = nd * 128
    for c in range((T + 511) // 512):
        c0, c1 = c * 512, min(T, (c + 1) * 512)
        W = c1 - c0
        mean = ln.stat[:, 0:W]
        sd = ln.stat[:, 512:512 + W]
        g2 = ctx.mm_open(ps_b[0:1, 0:W])
        for j in range(nd):
            xc = xs[j][:, tok0 + c0:tok0 + c1]
            xq = ln.xsq[:, (j % 2) * 512:(j % 2) * 512 + W]
            if mode == "rms":
                if pre_col is not None:
                    ts(ctx, "DVE", xm[j][:, c0:c1], xc, pre_col[j], ALU.mult)
                else:
                    copy(ctx, "DVE", xm[j][:, c0:c1], xc)
                act(ctx, xq, xc, AF.Square)
            else:
                copy(ctx, "DVE", xm[j][:, c0:c1], xc)
                act(ctx, xq, xm[j][:, c0:c1], AF.Square)
            g2.add(lambda e, o=ps_b[0:1, 0:W], l=ln.ones_k, r=xq, st=(j == 0), sp=(j == nd - 1):
                   e.matmul(o.ap(), l.ap(), r.ap(), start=st, stop=sp), [ln.ones_k, xq])
        g2.close()
        ts(ctx, "DVE", sd, ps_b[0:1, 0:W], 1.0 / D, ALU.mult)
        if mode == "ln":
            mm_acc(ctx, ps_a[0:1, 0:W], [(ln.ones_k, xm[j][:, c0:c1]) for j in range(nd)])
            ts(ctx, "DVE", mean, ps_a[0:1, 0:W], 1.0 / D, ALU.mult)
            mm_acc(ctx, ps_a[:, 0:W], [(ln.ones_1, mean)])
            for j in range(nd):
                xc = xs[j][:, tok0 + c0:tok0 + c1]
                stt(ctx, xm[j][:, c0:c1], xc, None, ALU.bypass,
                    ps_a[:, 0:W], ALU.subtract)
            tt(ctx, "DVE", mean, mean, mean, ALU.mult)
            tt(ctx, "DVE", sd, sd, mean, ALU.subtract)
        act(ctx, sd, sd, AF.Sqrt, bias=eps)
        recip(ctx, sd, sd)
        mm_acc(ctx, ps_b[:, 0:W], [(ln.ones_1, sd)])
        copy(ctx, "ACT", ln.rb[:, c0:c1], ps_b[:, 0:W])


def emit_ln_inplace(ctx, xs, ln, ps_a, ps_b, T, eps=1e-5):
    nd = len(xs)
    D = nd * 128
    for c in range((T + 511) // 512):
        c0, c1 = c * 512, min(T, (c + 1) * 512)
        W = c1 - c0
        mean = ln.stat[:, 0:W]
        sd = ln.stat[:, 512:512 + W]
        g1 = ctx.mm_open(ps_a[0:1, 0:W])
        g2 = ctx.mm_open(ps_b[0:1, 0:W])
        for j in range(nd):
            xc = xs[j][:, c0:c1]
            xb = ln.xsq[:, (j % 2) * 512:(j % 2) * 512 + W]
            copy(ctx, "DVE", xb, xc)
            g1.add(lambda e, o=ps_a[0:1, 0:W], l=ln.ones_k, r=xb, st=(j == 0),
                   sp=(j == nd - 1): e.matmul(o.ap(), l.ap(), r.ap(), start=st, stop=sp),
                   [ln.ones_k, xb])
            act(ctx, xb, xb, AF.Square)
            g2.add(lambda e, o=ps_b[0:1, 0:W], l=ln.ones_k, r=xb, st=(j == 0),
                   sp=(j == nd - 1): e.matmul(o.ap(), l.ap(), r.ap(), start=st, stop=sp),
                   [ln.ones_k, xb])
        g1.close()
        g2.close()
        ts(ctx, "DVE", mean, ps_a[0:1, 0:W], 1.0 / D, ALU.mult)
        ts(ctx, "DVE", sd, ps_b[0:1, 0:W], 1.0 / D, ALU.mult)
        mm_acc(ctx, ps_a[:, 0:W], [(ln.ones_1, mean)])
        for j in range(nd):
            stt(ctx, xs[j][:, c0:c1], xs[j][:, c0:c1], None, ALU.bypass,
                ps_a[:, 0:W], ALU.subtract)
        tt(ctx, "DVE", mean, mean, mean, ALU.mult)
        tt(ctx, "DVE", sd, sd, mean, ALU.subtract)
        act(ctx, sd, sd, AF.Sqrt, bias=eps)
        recip(ctx, sd, sd)
        mm_acc(ctx, ps_b[:, 0:W], [(ln.ones_1, sd)])
        copy(ctx, "ACT", ln.rb[:, c0:c1], ps_b[:, 0:W])
        for j in range(nd):
            tt(ctx, "DVE", xs[j][:, c0:c1], xs[j][:, c0:c1], ln.rb[:, c0:c1], ALU.mult)



def proj_feat(ctx, out_tiles, wT_tiles, xm, rb, ps_banks, T, n_out,
              bias_col=None, func=None, out_chunks=512, extra=None):
    nf = n_out // 128
    nk = len(xm)
    bi = 0
    for f in range(nf):
        for c0 in range(0, T, out_chunks):
            c1 = min(T, c0 + out_chunks)
            ps = ps_banks[bi % len(ps_banks)][:, 0:c1 - c0]
            bi += 1
            mm_acc(ctx, ps, [(wT_tiles[j][:, f * 128:(f + 1) * 128], xm[j][:, c0:c1])
                             for j in range(nk)])
            dst = out_tiles[f][:, c0:c1]
            if rb is not None:
                stt(ctx, dst, ps, None, ALU.bypass, rb[:, c0:c1], ALU.mult)
            else:
                copy(ctx, "DVE", dst, ps)
            if func is not None:
                act(ctx, dst, dst, func,
                    bias=bias_col[:, f:f + 1] if bias_col is not None else 0.0)
            elif bias_col is not None:
                ts(ctx, "DVE", dst, dst, bias_col[:, f:f + 1], ALU.add)
            if extra is not None:
                extra(ctx, f, c0, c1, dst)



def softmax_rows(ctx, s_tv, p_out, scale, tmp_col, do_recip_mult=True):
    mx = tmp_col[:, 0:1]
    bias = tmp_col[:, 1:2]
    sm = tmp_col[:, 2:3]
    reduce(ctx, "DVE", mx, s_tv, ALU.max)
    ts(ctx, "DVE", bias, mx, -scale, ALU.mult)
    act(ctx, p_out, s_tv, AF.Exp, bias=bias, scale=scale, accum_out=sm)
    recip(ctx, sm, sm)
    if do_recip_mult:
        ts(ctx, "DVE", p_out, p_out, sm, ALU.mult)
    return sm


"""Block emitters for mk_seq_preln_d4. Each operates on one batch (T=2048 tokens)
of the feature-major fp32 residual stream xs (8 TVs [128, 2048])."""

T = 2048


class L0AttnScratch:
    def __init__(self, ctx, tag):
        self.xm = [ctx.sbuf(f"l0a_xm{j}_{tag}", [128, T], BF16) for j in range(8)]
        self.V = [ctx.sbuf(f"l0a_v{i}_{tag}", [128, 1024], BF16) for i in range(16)]
        self.A = ctx.sbuf(f"l0a_A_{tag}", [64, 8 * 256], BF16)
        self.B = ctx.sbuf(f"l0a_B_{tag}", [64, 8 * 256], BF16)
        self.s = ctx.sbuf(f"l0a_s_{tag}", [128, T], F32)
        self.p = ctx.sbuf(f"l0a_p_{tag}", [128, T], BF16)
        self.rtok = ctx.sbuf(f"l0a_rtok_{tag}", [128, 16], F32)
        self.col = ctx.sbuf(f"l0a_col_{tag}", [128, 8], F32)


def emit_l0_attn(ctx, xs, w, ln, ps, tag):
    sc = L0AttnScratch(ctx, tag)
    emit_ln(ctx, xs, 0, T, ln, ps[0], ps[1], sc.xm, mode="ln")
    for i in range(16):
        transpose(ctx, ps[2][:, 0:1], ln.rb[0:1, i * 128:(i + 1) * 128], ln.ones_1[0:1, 0:1])
        copy(ctx, "ACT", sc.rtok[:, i:i + 1], ps[2][:, 0:1])
    mq = load_full_bf16(ctx, f"l0_mq_{tag}", w["Mq"], 1024, 512)
    mk = load_full_bf16(ctx, f"l0_mk_{tag}", w["Mk"], 1024, 512)
    mv = load_full_bf16(ctx, f"l0_mv_{tag}", w["Mv"], 1024, 1024)
    for i in range(16):
        for n in range(2):
            pp = ps[2 + (i * 2 + n) % 2][:, 0:512]
            mm_acc(ctx, pp, [(sc.xm[j][:, i * 128:(i + 1) * 128], mv[j][:, n * 512:(n + 1) * 512])
                             for j in range(8)])
            ts(ctx, "DVE", sc.V[i][:, n * 512:(n + 1) * 512], pp, sc.rtok[:, i:i + 1], ALU.mult)
    for m in range(8):
        t0 = m * 256
        for (dst, wt) in ((sc.A, mq), (sc.B, mk)):
            for jb in range(8):
                pp = ps[2 + jb % 2][0:64, 0:256]
                mm_acc(ctx, pp, [(wt[j][:, jb * 64:(jb + 1) * 64],
                                  sc.xm[j][:, t0:t0 + 256]) for j in range(8)])
                d = dst[:, jb * 256:(jb + 1) * 256]
                stt(ctx, d, pp, None, ALU.bypass, ln.rb[0:64, t0:t0 + 256], ALU.mult)
        outg = [ctx.mm_open(ps[4 + gi]) for gi in range(4)]
        started = [False] * 4
        for jp in range(8):
            for tc in range(2):
                for jj in range(4):
                    pp = ps[2 + jj % 2]
                    g = ctx.mm_open(pp)
                    for j2 in range(2):
                        j = jj * 2 + j2
                        lt = sc.B[:, jp * 256 + tc * 128:jp * 256 + (tc + 1) * 128]
                        rt = sc.A[:, j * 256:(j + 1) * 256]
                        g.add(lambda e, o=pp, l=lt, r=rt, st=(j2 == 0), sp=(j2 == 1), j2=j2:
                              e.matmul(o[:, j2 * 256:(j2 + 1) * 256].ap(), l.ap(), r.ap(),
                                       start=st, stop=sp), [lt, rt])
                    g.close()
                    copy(ctx, "DVE", sc.s[:, jj * 512:(jj + 1) * 512], pp[:, 0:512])
                softmax_rows(ctx, sc.s, sc.p, 1.0 / 8.0, sc.col[:, 0:4])
                for hp in range(8):
                    lt = sc.V[hp * 2 + tc][:, jp * 128:(jp + 1) * 128]
                    rt = sc.p[:, hp * 256:(hp + 1) * 256]
                    gi = hp // 2
                    st = not started[gi]
                    outg[gi].add(lambda e, o=ps[4 + gi], l=lt, r=rt, st=st, c0=(hp % 2) * 256:
                                 e.matmul(o[:, c0:c0 + 256].ap(), l.ap(), r.ap(),
                                          start=st, stop=False), [lt, rt])
                    started[gi] = True
        for gi in range(4):
            outg[gi].close()
        for hp in range(8):
            xsl = xs[hp][:, t0:t0 + 256]
            stt(ctx, xsl, ps[4 + hp // 2][:, (hp % 2) * 256:(hp % 2) * 256 + 256],
                None, ALU.bypass, xsl, ALU.add)


def emit_l0_ffn(ctx, xs, w, ln, ps, tag):
    TWO_PI = 2 * PI
    xm = [ctx.sbuf(f"l0f_xm{j}_{tag}", [128, T], BF16) for j in range(8)]
    emit_ln(ctx, xs, 0, T, ln, ps[0], ps[1], xm, mode="ln")
    b1c = load_col_fp32(ctx, f"l0f_b1_{tag}", w["b1"], 4096, ln, ps[0])
    b2c = load_col_fp32(ctx, f"l0f_b2_{tag}", w["b2"], 1024, ln, ps[1])
    thc = load_col_fp32(ctx, f"l0f_th_{tag}", w["qp"], 4096, ln, ps[2])
    wrc = load_col_fp32(ctx, f"l0f_wr_{tag}", w["qwr"], 4096, ln, ps[3])
    wic = load_col_fp32(ctx, f"l0f_wi_{tag}", w["qwi"], 4096, ln, ps[4])
    cc = ctx.sbuf(f"l0f_cc_{tag}", [128, 6 * 32], F32)
    sth, cth, c1c = cc[:, 0:32], cc[:, 32:64], cc[:, 64:96]
    b110, b110p, tmpc = cc[:, 96:128], cc[:, 128:160], cc[:, 160:192]
    copy(ctx, "DVE", tmpc, thc)
    for _ in range(8):
        ts(ctx, "DVE", sth, tmpc, PI, ALU.is_gt)
        stt(ctx, tmpc, sth, -TWO_PI, ALU.mult, tmpc, ALU.add)
        ts(ctx, "DVE", sth, tmpc, -PI, ALU.is_lt)
        stt(ctx, tmpc, sth, TWO_PI, ALU.mult, tmpc, ALU.add)
    ts(ctx, "DVE", cth, tmpc, PI / 2, ALU.add)
    ts(ctx, "DVE", sth, cth, PI, ALU.is_gt)
    stt(ctx, cth, sth, -TWO_PI, ALU.mult, cth, ALU.add)
    act(ctx, cth, cth, AF.Sin)
    act(ctx, sth, tmpc, AF.Sin)
    tt(ctx, "DVE", c1c, cth, wrc, ALU.mult)
    tt(ctx, "DVE", tmpc, sth, wic, ALU.mult)
    tt(ctx, "DVE", c1c, c1c, tmpc, ALU.add)
    tt(ctx, "DVE", c1c, c1c, wic, ALU.mult)
    ts(ctx, "DVE", c1c, c1c, 0.1, ALU.mult)
    ts(ctx, "DVE", b110, b1c, 0.1, ALU.mult)
    ts(ctx, "DVE", b110p, b110, PI / 2, ALU.add)
    w1s = [ctx.sbuf(f"l0f_w1s{i}_{tag}", [128, 1024], BF16) for i in range(3)]
    w2s = [ctx.sbuf(f"l0f_w2s{i}_{tag}", [128, 1024], BF16) for i in range(3)]
    hsl = [ctx.sbuf(f"l0f_h{i}_{tag}", [128, 512], F32) for i in range(2)]
    ssl = [ctx.sbuf(f"l0f_s{i}_{tag}", [128, 512], F32) for i in range(2)]
    csl = [ctx.sbuf(f"l0f_c{i}_{tag}", [128, 512], F32) for i in range(2)]
    asb = [ctx.sbuf(f"l0f_a{i}_{tag}", [128, 512], BF16) for i in range(32)]
    d1 = w["W1"].apobj
    for c in range(4):
        t0 = c * 512
        rbch = ln.rb[:, t0:t0 + 512]
        for f in range(32):
            slot = w1s[f % 3]
            def em(e, slot=slot, d1=d1, f=f):
                return e.dma_start(out=slot.ap(),
                                   in_=d1[:, f * 128:(f + 1) * 128].rearrange(
                                       "(k p) c -> p k c", p=128))
            ctx.dma("POOL", slot, w["W1"].tv(), emit=em)
            pp = ps[f % 2]
            mm_acc(ctx, pp, [(slot[:, j * 128:(j + 1) * 128], xm[j][:, t0:t0 + 512])
                             for j in range(8)])
            h0 = hsl[f % 2]
            sg = ssl[f % 2]
            cg = csl[f % 2]
            fc = slice(f, f + 1)
            stt(ctx, h0, pp, None, ALU.bypass, rbch, ALU.mult)
            act(ctx, sg, h0, AF.Sin, bias=b110[:, fc], scale=0.1)
            act(ctx, cg, h0, AF.Sin, bias=b110p[:, fc], scale=0.1)
            ts(ctx, "DVE", sg, sg, cth[:, fc], ALU.mult)
            stt(ctx, sg, cg, sth[:, fc], ALU.mult, sg, ALU.add)
            stt(ctx, h0, sg, c1c[:, fc], ALU.mult, h0, ALU.add)
            act(ctx, asb[f], h0, AF.Relu, bias=b1c[:, fc])
        groups = [ctx.mm_open(ps[fo]) for fo in range(8)]
        d2 = w["W2"].apobj
        for k in range(32):
            slot = w2s[k % 3]
            def em2(e, slot=slot, d2=d2, k=k):
                return e.dma_start(out=slot.ap(), in_=d2[k * 128:(k + 1) * 128, :])
            ctx.dma("POOL", slot, w["W2"].tv(), emit=em2)
            for fo in range(8):
                lt = slot[:, fo * 128:(fo + 1) * 128]
                rt = asb[k]
                groups[fo].add(lambda e, o=ps[fo], l=lt, r=rt, st=(k == 0):
                               e.matmul(o.ap(), l.ap(), r.ap(), start=st, stop=False),
                               [lt, rt])
        for fo in range(8):
            groups[fo].close()
            xsl = xs[fo][:, t0:t0 + 512]
            stt(ctx, xsl, ps[fo], b2c[:, fo:fo + 1], ALU.add, xsl, ALU.add)


def tr_mm(ctx, g, out_tv, in_tv, ident_tv, st, sp):
    g.add(lambda e, o=out_tv, i=in_tv, id=ident_tv, st=st, sp=sp:
          e.matmul(o.ap(), i.ap(), id.ap(), start=st, stop=sp, is_transpose=True),
          [in_tv, ident_tv])


def emit_l1_attn(ctx, xs, w, ln, ps, tag, dbg=None):
    H = 16
    xm = [ctx.sbuf(f"l1a_xm{j}_{tag}", [128, T], BF16) for j in range(8)]
    emit_ln(ctx, xs, 0, T, ln, ps[0], ps[1], xm, mode="ln")
    rtok = ctx.sbuf(f"l1a_rtok_{tag}", [128, 16], F32)
    for i in range(16):
        transpose(ctx, ps[2][:, 0:1], ln.rb[0:1, i * 128:(i + 1) * 128], ln.ones_1[0:1, 0:1])
        copy(ctx, "ACT", rtok[:, i:i + 1], ps[2][:, 0:1])
    mval = ctx.sbuf(f"l1a_mval_{tag}", [1, 4], F32)
    mcol = ctx.sbuf(f"l1a_mcol_{tag}", [128, 8], F32)
    c_1m, c_mqs = mcol[:, 0:1], mcol[:, 1:2]
    maxc, sumc, biasc = mcol[:, 2:3], mcol[:, 3:4], mcol[:, 4:5]
    ctx.dma("SYNC", mval[:, 0:1], w["mixture"].tv())
    ctx.dma("SYNC", mval[:, 1:2], w["qs"].tv())
    act(ctx, mval[:, 2:3], mval[:, 0:1], AF.Sigmoid)
    tt(ctx, "DVE", mval[:, 3:4], mval[:, 2:3], mval[:, 1:2], ALU.mult)
    ts(ctx, "DVE", mval[:, 2:3], mval[:, 2:3], -1.0, ALU.mult, 1.0, ALU.add)
    mm_acc(ctx, ps[2][:, 0:2], [(ln.ones_1, mval[:, 2:4])])
    copy(ctx, "ACT", mcol[:, 0:2], ps[2][:, 0:2])
    ao = [ctx.sbuf(f"l1a_ao{j}_{tag}", [128, T], BF16) for j in range(8)]
    pt = ctx.sbuf(f"l1a_pt_{tag}", [128, 16 * 512], BF16)
    p = ctx.sbuf(f"l1a_p_{tag}", [128, T], BF16)
    u = ctx.sbuf(f"l1a_u_{tag}", [128, T], BF16)
    tq = ctx.sbuf(f"l1a_tq_{tag}", [128, T], F32)
    mk0 = ctx.mark()
    qh = ctx.sbuf(f"l1a_qh_{tag}", [64, T], BF16)
    kh = ctx.sbuf(f"l1a_kh_{tag}", [64, T], BF16)
    q2h = ctx.sbuf(f"l1a_q2h_{tag}", [64, T], BF16)
    k2h = ctx.sbuf(f"l1a_k2h_{tag}", [64, T], BF16)
    vtok = ctx.sbuf(f"l1a_vtok_{tag}", [128, 16 * 64], BF16)
    ktok = ctx.sbuf(f"l1a_ktok_{tag}", [128, 16 * 64], BF16)
    k2tok = ctx.sbuf(f"l1a_k2tok_{tag}", [128, 16 * 64], BF16)
    gsb = ctx.sbuf(f"l1a_g_{tag}", [64, 128], BF16)
    hq = ctx.sbuf(f"l1a_hq_{tag}", [64, 512], BF16)
    kcol = ctx.sbuf(f"l1a_kcol_{tag}", [64, 4], F32)
    kcolb = ctx.sbuf(f"l1a_kcolb_{tag}", [64, 2], BF16)
    stg = ctx.sbuf(f"l1a_stg_{tag}", [1, 512], F32)
    cols = ctx.sbuf(f"l1a_cols_{tag}", [128, 8 * 16], F32)
    nmu1, ss1 = cols[:, 0:16], cols[:, 16:32]
    nmu2, ss2 = cols[:, 32:48], cols[:, 48:64]
    rs1, a2c = cols[:, 64:80], cols[:, 80:96]
    b2c, ctmp = cols[:, 96:112], cols[:, 112:128]
    wsl = [ctx.sbuf(f"l1a_ws{i}_{tag}", [128, 512], BF16) for i in range(5)]
    SC8 = 1.0 / 2048.0

    def stat_rows(qsrc, gview, kb, nmucols, sscols):
        for half, dst in ((0, nmucols), (1, sscols)):
            gcol = ctx.mm_open(ps[5][:, 0:16])
            first = True
            for c in range(4):
                if half == 0:
                    mm_acc(ctx, ps[4][0:1, 0:512], [(kb, qsrc[:, c * 512:(c + 1) * 512])])
                    ts(ctx, "DVE", stg, ps[4][0:1, 0:512], -SC8, ALU.mult)
                else:
                    mm_acc(ctx, ps[4][:64, 0:512], [(gview, qsrc[:, c * 512:(c + 1) * 512])])
                    tt(ctx, "DVE", hq, ps[4][:64, 0:512], qsrc[:, c * 512:(c + 1) * 512], ALU.mult)
                    mm_acc(ctx, ps[3][0:1, 0:512], [(ln.ones_k[0:64], hq)])
                    ts(ctx, "DVE", stg, ps[3][0:1, 0:512], 1.0 / 64.0, ALU.mult)
                for i in range(4):
                    tr_mm(ctx, gcol, ps[5][:, c * 4 + i:c * 4 + i + 1],
                          stg[:, i * 128:(i + 1) * 128], ln.ones_1[0:1, 0:1],
                          st=first, sp=(c == 3 and i == 3))
                    first = False
            gcol.close()
            copy(ctx, "ACT", dst, ps[5][:, 0:16])

    def rsig_from(nmu, ss, dst):
        tt(ctx, "DVE", ctmp, nmu, nmu, ALU.mult)
        stt(ctx, dst, ctmp, -32.0, ALU.mult, ss, ALU.add)
        ts(ctx, "DVE", dst, dst, 1.0 / 2047.0, ALU.mult)
        act(ctx, dst, dst, AF.Sqrt)
        ts(ctx, "DVE", dst, dst, 1e-5, ALU.add)
        recip(ctx, dst, dst)

    for h in range(H):
        hb = (h % 2) * 64
        for i, nm in enumerate(("WqT", "WkT", "Wq2T", "Wk2T", "WvT")):
            d = w[nm].apobj

            def em(e, s=wsl[i], d=d, h=h):
                return e.dma_start(out=s.ap(), in_=d[:, h * 64:(h + 1) * 64].rearrange(
                    "(k p) c -> p k c", p=128))
            ctx.dma("POOL", wsl[i], w[nm].tv(), emit=em)
        for mi, dst in ((0, qh), (1, kh), (2, q2h), (3, k2h)):
            for n in range(4):
                pp = ps[n % 2][0:64, 0:512]
                mm_acc(ctx, pp, [(wsl[mi][:, j * 64:(j + 1) * 64],
                                  xm[j][:, n * 512:(n + 1) * 512]) for j in range(8)])
                stt(ctx, dst[:, n * 512:(n + 1) * 512], pp, None, ALU.bypass,
                    ln.rb[0:64, n * 512:(n + 1) * 512], ALU.mult)
        for tt_i in range(16):
            pp = ps[2 + tt_i % 2][:, 0:64]
            mm_acc(ctx, pp, [(xm[j][:, tt_i * 128:(tt_i + 1) * 128],
                              wsl[4][:, j * 64:(j + 1) * 64]) for j in range(8)])
            ts(ctx, "DVE", vtok[:, tt_i * 64:(tt_i + 1) * 64], pp, rtok[:, tt_i:tt_i + 1], ALU.mult)
        for src, dst in ((kh, ktok), (k2h, k2tok)):
            for gq in range(2):
                bank = ps[6].bf16()[:, 0:512]
                gg = ctx.mm_open(ps[6])
                for i2 in range(8):
                    kc = gq * 8 + i2
                    tr_mm(ctx, gg, bank[:, i2 * 64:(i2 + 1) * 64],
                          src[:, kc * 128:(kc + 1) * 128], ln.identb[0:64, 0:64],
                          st=(i2 == 0), sp=(i2 == 7))
                gg.close()
                copy(ctx, "DVE", dst[:, gq * 512:(gq + 1) * 512], bank)
        for gi, (ktk, gslice) in enumerate(((ktok, gsb[:, 0:64]), (k2tok, gsb[:, 64:128]))):
            mm_acc(ctx, ps[3][0:64, 0:64], [(ktk[:, kc * 64:(kc + 1) * 64],
                                             ktk[:, kc * 64:(kc + 1) * 64]) for kc in range(16)])
            copy(ctx, "DVE", gslice, ps[3][0:64, 0:64])
        reduce(ctx, "DVE", kcol[:, 0:1], kh, ALU.add)
        reduce(ctx, "DVE", kcol[:, 1:2], k2h, ALU.add)
        copy(ctx, "DVE", kcolb, kcol[:, 0:2])
        stat_rows(qh, gsb[:, 0:64], kcolb[:, 0:1], nmu1, ss1)
        stat_rows(q2h, gsb[:, 64:128], kcolb[:, 1:2], nmu2, ss2)
        rsig_from(nmu1, ss1, rs1)
        ts(ctx, "DVE", rs1, rs1, 0.125, ALU.mult)
        rsig_from(nmu2, ss2, a2c)
        ts(ctx, "DVE", a2c, a2c, c_mqs, ALU.mult, 0.125, ALU.mult)
        tt(ctx, "DVE", ctmp, nmu2, a2c, ALU.mult)
        ts(ctx, "DVE", b2c, ctmp, c_1m, ALU.add)
        if dbg is not None and h == 0:
            for nm, tvv in (("qh", qh), ("kh", kh), ("q2h", q2h), ("k2h", k2h)):
                ctx.dma("SYNC", dbg[nm].tv(), tvv)
            ctx.dma("SYNC", dbg["vtok"].tv(), vtok)
            ctx.dma("SYNC", dbg["ktok"].tv(), ktok)
            ctx.dma("SYNC", dbg["gsb"].tv(), gsb)
            ctx.dma("SYNC", dbg["cols"].tv(), cols)
        for qc in range(16):
            W = (qc + 1) * 128
            if qc % 4 == 0:
                for kc in range(qc, min(qc + 4, 16)):
                    memset(ctx, "POOL", pt[:, kc * 512:(kc + 1) * 512], 0.0)
            nch = (W + 511) // 512
            for c in range(nch):
                c0, c1 = c * 512, min(W, (c + 1) * 512)
                pp = ps[c % 2][:, 0:c1 - c0]
                mm_acc(ctx, pp, [(q2h[:, qc * 128:(qc + 1) * 128], k2h[:, c0:c1])])
                ts(ctx, "DVE", u[:, c0:c1], pp, a2c[:, qc:qc + 1], ALU.mult,
                   b2c[:, qc:qc + 1], ALU.add)
            for c in range(nch):
                c0, c1 = c * 512, min(W, (c + 1) * 512)
                pp = ps[c % 2][:, 0:c1 - c0]
                mm_acc(ctx, pp, [(qh[:, qc * 128:(qc + 1) * 128], kh[:, c0:c1])])
                stt(ctx, tq[:, c0:c1], pp, nmu1[:, qc:qc + 1], ALU.add, u[:, c0:c1], ALU.mult)
            tt(ctx, "DVE", tq[:, qc * 128:W], tq[:, qc * 128:W], ln.cmask, ALU.add)
            reduce(ctx, "DVE", maxc, tq[:, 0:W], ALU.max)
            ts(ctx, "DVE", biasc, maxc, rs1[:, qc:qc + 1], ALU.mult, -1.0, ALU.mult)
            act(ctx, p[:, 0:W], tq[:, 0:W], AF.Exp, bias=biasc, scale=rs1[:, qc:qc + 1],
                accum_out=sumc)
            recip(ctx, sumc, sumc)
            ts(ctx, "DVE", p[:, 0:W], p[:, 0:W], sumc, ALU.mult)
            for kc in range(qc + 1):
                bank = ps[6 + kc % 2]
                gg = ctx.mm_open(bank)
                tr_mm(ctx, gg, bank.bf16()[:, 0:128], p[:, kc * 128:(kc + 1) * 128],
                      ln.identb, st=True, sp=True)
                gg.close()
                dst = pt[:, kc * 512 + (qc % 4) * 128:kc * 512 + (qc % 4) * 128 + 128]
                copy(ctx, "DVE" if kc % 2 == 0 else "ACT", dst, bank.bf16()[:, 0:128])
            if dbg is not None and h == 0 and qc == 9:
                ctx.dma("SYNC", dbg["u"].tv(), u)
                ctx.dma("SYNC", dbg["tq"].tv(), tq)
                ctx.dma("SYNC", dbg["p"].tv(), p)
            if qc % 4 == 3:
                sp_i = qc // 4
                po = ps[2]
                out_tv = po[hb:hb + 64, 0:512]
                gpv = ctx.mm_open(po)
                for kc in range(qc + 1):
                    lt = vtok[:, kc * 64:(kc + 1) * 64]
                    rt = pt[:, kc * 512:(kc + 1) * 512]
                    tp = (0, hb)
                    gpv.add(lambda e, o=out_tv, l=lt, r=rt, st=(kc == 0), sp2=(kc == qc),
                            tp=tp: e.matmul(o.ap(), l.ap(), r.ap(), start=st, stop=sp2,
                                            tile_position=tp), [lt, rt])
                gpv.close()
                copy(ctx, "DVE", ao[h // 2][hb:hb + 64, sp_i * 512:(sp_i + 1) * 512], out_tv)
    if dbg is not None:
        for j in range(8):
            ctx.dma("SYNC", dbg[f"ao{j}"].tv(), ao[j])
    ctx.free_to(mk0)
    wot = load_full_bf16(ctx, f"l1a_wot_{tag}", w["WoT"], 1024, 1024)
    for f in range(8):
        for n in range(4):
            pp = ps[f % 2][:, 0:512]
            mm_acc(ctx, pp, [(wot[j][:, f * 128:(f + 1) * 128],
                              ao[j][:, n * 512:(n + 1) * 512]) for j in range(8)])
            xsl = xs[f][:, n * 512:(n + 1) * 512]
            stt(ctx, xsl, pp, None, ALU.bypass, xsl, ALU.add)


def emit_l1_ffn(ctx, xs, w, ln, ps, tag):
    xm = [ctx.sbuf(f"l1f_xm{j}_{tag}", [128, T], BF16) for j in range(8)]
    emit_ln(ctx, xs, 0, T, ln, ps[0], ps[1], xm, mode="ln")
    trash = ctx.sbuf(f"l1f_trash_{tag}", [128, T], BF16)
    gcol = ctx.sbuf(f"l1f_gcol_{tag}", [128, 20], F32)
    gcolb = ctx.sbuf(f"l1f_gcolb_{tag}", [128, 16], BF16)
    sval = ctx.sbuf(f"l1f_sval_{tag}", [1, 8], F32)
    for j in range(8):
        tt(ctx, "DVE", trash, xm[j], ln.rb, ALU.mult)
        reduce(ctx, "DVE", gcol[:, j:j + 1], trash, ALU.add)
    ts(ctx, "DVE", gcolb[:, 0:8], gcol[:, 0:8], 1.0 / 2048.0, ALU.mult)
    gw1 = load_full_bf16(ctx, f"l1f_gw1_{tag}", w["gc_w1"], 1024, 512)
    gb1 = load_col_fp32(ctx, f"l1f_gb1_{tag}", w["gc_b1"], 512, ln, ps[2])
    gw2 = load_full_bf16(ctx, f"l1f_gw2_{tag}", w["gc_w2"], 512, 256)
    gb2 = load_col_fp32(ctx, f"l1f_gb2_{tag}", w["gc_b2"], 256, ln, ps[3])
    gw3 = load_full_bf16(ctx, f"l1f_gw3_{tag}", w["gc_w3"], 256, 1)
    ctx.dma("SYNC", sval[:, 1:2], w["ar"].tv())
    ctx.dma("SYNC", sval[:, 5:6], w["gc_b3"].tv())
    ctx.dma("SYNC", sval[:, 6:7], w["cn_b4"].tv())
    for ft in range(4):
        mm_acc(ctx, ps[0][:, 0:1], [(gw1[j][:, ft * 128:(ft + 1) * 128], gcolb[:, j:j + 1])
                                    for j in range(8)])
        act(ctx, gcolb[:, 8 + ft:9 + ft], ps[0][:, 0:1], AF.Gelu, bias=gb1[:, ft:ft + 1])
    for ft in range(2):
        mm_acc(ctx, ps[0][:, 0:1], [(gw2[k][:, ft * 128:(ft + 1) * 128], gcolb[:, 8 + k:9 + k])
                                    for k in range(4)])
        act(ctx, gcolb[:, 12 + ft:13 + ft], ps[0][:, 0:1], AF.Gelu, bias=gb2[:, ft:ft + 1])
    mm_acc(ctx, ps[0][0:1, 0:1], [(gw3[k][:, 0:1], gcolb[:, 12 + k:13 + k]) for k in range(2)])
    act(ctx, sval[:, 0:1], ps[0][0:1, 0:1], AF.Sigmoid, bias=sval[:, 5:6])
    ts(ctx, "DVE", sval[:, 2:3], sval[:, 1:2], 0.6, ALU.mult)
    ts(ctx, "DVE", sval[:, 4:5], sval[:, 0:1], 1.4, ALU.mult, -1.0, ALU.add)
    tt(ctx, "DVE", sval[:, 4:5], sval[:, 4:5], sval[:, 1:2], ALU.mult)
    ts(ctx, "DVE", sval[:, 3:4], sval[:, 4:5], 0.1, ALU.add)
    w1t = load_full_bf16(ctx, f"l1f_w1_{tag}", w["cn_w1"], 1024, 1024)
    b1c = load_col_fp32(ctx, f"l1f_b1_{tag}", w["cn_b1"], 1024, ln, ps[2])
    w2t = load_full_bf16(ctx, f"l1f_w2_{tag}", w["cn_w2"], 1024, 512)
    b2c = load_col_fp32(ctx, f"l1f_b2_{tag}", w["cn_b2"], 512, ln, ps[3])
    w3t = load_full_bf16(ctx, f"l1f_w3_{tag}", w["cn_w3"], 512, 256)
    b3c = load_col_fp32(ctx, f"l1f_b3_{tag}", w["cn_b3"], 256, ln, ps[4])
    w4t = load_full_bf16(ctx, f"l1f_w4_{tag}", w["cn_w4"], 256, 1)
    out = [ctx.sbuf(f"l1f_out{j}_{tag}", [128, 512], F32) for j in range(8)]
    outb = [ctx.sbuf(f"l1f_outb{j}_{tag}", [128, 512], BF16) for j in range(8)]
    h1b = ctx.sbuf(f"l1f_h1b_{tag}", [128, 8 * 512], BF16)
    h2b = ctx.sbuf(f"l1f_h2b_{tag}", [128, 4 * 512], BF16)
    h3b = ctx.sbuf(f"l1f_h3b_{tag}", [128, 2 * 512], BF16)
    tfr = ctx.sbuf(f"l1f_tfr_{tag}", [1, 512], F32)
    cplb = ctx.sbuf(f"l1f_cplb_{tag}", [128, 2 * 512], F32)
    tmp = ctx.sbuf(f"l1f_tmp_{tag}", [128, 512], F32)
    PERM = [4, 5, 6, 7, 0, 1, 2, 3]
    SGN = [1.0, 1.0, -1.0, -1.0, -1.0, -1.0, 1.0, 1.0]
    for c in range(4):
        t0 = c * 512
        for j in range(8):
            stt(ctx, out[j], xs[j][:, t0:t0 + 512] if False else xm[j][:, t0:t0 + 512],
                None, ALU.bypass, ln.rb[:, t0:t0 + 512], ALU.mult)
        for cyc in range(3):
            for j in range(8):
                copy(ctx, "DVE" if j % 2 == 0 else "POOL", outb[j], out[j])
            for f in range(8):
                mm_acc(ctx, ps[f % 2][:, 0:512], [(w1t[j][:, f * 128:(f + 1) * 128], outb[j])
                                                  for j in range(8)])
                act(ctx, h1b[:, f * 512:(f + 1) * 512], ps[f % 2][:, 0:512], AF.Gelu,
                    bias=b1c[:, f:f + 1])
            for f in range(4):
                mm_acc(ctx, ps[f % 2][:, 0:512], [(w2t[j][:, f * 128:(f + 1) * 128],
                                                   h1b[:, j * 512:(j + 1) * 512]) for j in range(8)])
                act(ctx, h2b[:, f * 512:(f + 1) * 512], ps[f % 2][:, 0:512], AF.Gelu,
                    bias=b2c[:, f:f + 1])
            for f in range(2):
                mm_acc(ctx, ps[f % 2][:, 0:512], [(w3t[k][:, f * 128:(f + 1) * 128],
                                                   h2b[:, k * 512:(k + 1) * 512]) for k in range(4)])
                act(ctx, h3b[:, f * 512:(f + 1) * 512], ps[f % 2][:, 0:512], AF.Gelu,
                    bias=b3c[:, f:f + 1])
            mm_acc(ctx, ps[2][0:1, 0:512], [(w4t[k][:, 0:1], h3b[:, k * 512:(k + 1) * 512])
                                            for k in range(2)])
            act(ctx, tfr, ps[2][0:1, 0:512], AF.Sigmoid, bias=sval[:, 6:7])
            ts(ctx, "DVE", tfr, tfr, sval[:, 2:3], ALU.mult, sval[:, 3:4], ALU.add)
            mm_acc(ctx, ps[3][:, 0:512], [(ln.ones_1, tfr)])
            copy(ctx, "ACT", cplb[:, 0:512], ps[3][:, 0:512])
            ts(ctx, "DVE", cplb[:, 512:1024], cplb[:, 0:512], -1.0, ALU.mult)
            for j in range(8):
                cp = cplb[:, 0:512] if SGN[j] > 0 else cplb[:, 512:1024]
                stt(ctx, tmp, outb[PERM[j]], None, ALU.bypass, cp, ALU.mult)
                tt(ctx, "DVE", out[j], out[j], tmp, ALU.add)
        for j in range(8):
            tt(ctx, "DVE", xs[j][:, t0:t0 + 512], xs[j][:, t0:t0 + 512], out[j], ALU.add)


def emit_l2_attn(ctx, xs, w, ln, ps, tag):
    H = 16
    C = 128
    xm = [ctx.sbuf(f"l2a_xm{j}_{tag}", [128, T], BF16) for j in range(8)]
    emit_ln(ctx, xs, 0, T, ln, ps[0], ps[1], xm, mode="ln")
    rtok = ctx.sbuf(f"l2a_rtok_{tag}", [128, 16], F32)
    for i in range(16):
        transpose(ctx, ps[2][:, 0:1], ln.rb[0:1, i * 128:(i + 1) * 128], ln.ones_1[0:1, 0:1])
        copy(ctx, "ACT", rtok[:, i:i + 1], ps[2][:, 0:1])
    bqk = ctx.sbuf(f"l2a_bqk_{tag}", [64, 32], F32)
    stg = ctx.sbuf(f"l2a_bstg_{tag}", [16, 64], F32)
    for i, nm in enumerate(("bq", "bk")):
        ctx.dma("SYNC", stg, w[nm].tv())
        transpose(ctx, ps[2][0:64, 0:16], stg, ln.ident[0:16, 0:16])
        copy(ctx, "ACT", bqk[:, i * 16:(i + 1) * 16], ps[2][0:64, 0:16])
    bvrow = ctx.sbuf(f"l2a_bv_{tag}", [1, 1024], F32)
    ctx.dma("SYNC", bvrow, w["bv"].tv())
    tmulT = ctx.sbuf(f"l2a_tmulT_{tag}", [128, 128], F32)
    memset(ctx, "POOL", tmulT, 1.0)
    affsel(ctx, tmulT, tmulT, [[1, 128]], ALU.is_ge, 0.0, 0, -1)
    ao = [ctx.sbuf(f"l2a_ao{j}_{tag}", [128, T], BF16) for j in range(8)]
    NH = 2
    qf = [ctx.sbuf(f"l2a_q{i}_{tag}", [64, T], BF16) for i in range(NH)]
    kf = [ctx.sbuf(f"l2a_k{i}_{tag}", [64, T], BF16) for i in range(NH)]
    vt = [ctx.sbuf(f"l2a_v{i}_{tag}", [128, 16 * 64], BF16) for i in range(NH)]
    ssb = [ctx.sbuf(f"l2a_S{i}_{tag}", [64, 64], F32) for i in range(NH)]
    ssbb = [ctx.sbuf(f"l2a_Sb{i}_{tag}", [64, 64], BF16) for i in range(NH)]
    zcol = [ctx.sbuf(f"l2a_z{i}_{tag}", [64, 2], F32) for i in range(NH)]
    zcolb = [ctx.sbuf(f"l2a_zb{i}_{tag}", [64, 1], BF16) for i in range(NH)]
    ktc = [ctx.sbuf(f"l2a_kt{i}_{tag}", [128, 64], BF16) for i in range(NH)]
    sct = [ctx.sbuf(f"l2a_sct{i}_{tag}", [128, 128], BF16) for i in range(NH)]
    denr = [ctx.sbuf(f"l2a_den{i}_{tag}", [1, 128], F32) for i in range(NH)]
    denb = [ctx.sbuf(f"l2a_denb{i}_{tag}", [64, 128], F32) for i in range(NH)]
    bvb = [ctx.sbuf(f"l2a_bvb{i}_{tag}", [128, 64], F32) for i in range(NH)]
    zq = [ctx.sbuf(f"l2a_zq{i}_{tag}", [64, 512], F32) for i in range(NH)]
    zq2 = [ctx.sbuf(f"l2a_zq2{i}_{tag}", [64, 512], F32) for i in range(NH)]
    wsl = [ctx.sbuf(f"l2a_ws{i}_{tag}", [128, 512], BF16) for i in range(3 * NH)]
    ones1b = ctx.sbuf(f"l2a_ones1b_{tag}", [128, 1], BF16)
    memset(ctx, "DVE", ones1b, 1.0)
    for hg in range(H // NH):
        heads = [hg * NH + i for i in range(NH)]
        for ii, h in enumerate(heads):
            hb = (h % 2) * 64
            for mi, nm in enumerate(("WqT", "WkT", "WvT")):
                d = w[nm].apobj

                def em(e, s=wsl[3 * ii + mi], d=d, h=h):
                    return e.dma_start(out=s.ap(), in_=d[:, h * 64:(h + 1) * 64].rearrange(
                        "(k p) c -> p k c", p=128))
                ctx.dma("POOL", wsl[3 * ii + mi], w[nm].tv(), emit=em)
            for mi, dst in ((0, qf[ii]), (1, kf[ii])):
                bcol = bqk[:, mi * 16 + h:mi * 16 + h + 1]
                for n in range(4):
                    pp = ps[n % 2][0:64, 0:512]
                    mm_acc(ctx, pp, [(wsl[3 * ii + mi][:, j * 64:(j + 1) * 64],
                                      xm[j][:, n * 512:(n + 1) * 512]) for j in range(8)])
                    z0 = zq[ii]
                    stt(ctx, z0, pp, None, ALU.bypass, ln.rb[0:64, n * 512:(n + 1) * 512], ALU.mult)
                    ts(ctx, "DVE", zq2[ii], z0, bcol, ALU.add, 0.0, ALU.min)
                    act(ctx, zq2[ii], zq2[ii], AF.Exp)
                    ts(ctx, "DVE", z0, z0, bcol, ALU.add, 0.0, ALU.max)
                    tt(ctx, "DVE", dst[:, n * 512:(n + 1) * 512], zq2[ii], z0, ALU.add)
            mm_acc(ctx, ps[2][:, 0:64], [(ln.ones_1, bvrow[:, h * 64:(h + 1) * 64])])
            copy(ctx, "ACT", bvb[ii], ps[2][:, 0:64])
            for tt_i in range(16):
                pp = ps[2 + tt_i % 2][:, 0:64]
                mm_acc(ctx, pp, [(xm[j][:, tt_i * 128:(tt_i + 1) * 128],
                                  wsl[3 * ii + 2][:, j * 64:(j + 1) * 64]) for j in range(8)])
                stt(ctx, vt[ii][:, tt_i * 64:(tt_i + 1) * 64], pp, rtok[:, tt_i:tt_i + 1],
                    ALU.mult, bvb[ii], ALU.add)
            memset(ctx, "DVE", ssb[ii], 0.0)
            memset(ctx, "DVE", ssbb[ii], 0.0)
            memset(ctx, "DVE", zcol[ii], 0.0)
            memset(ctx, "DVE", zcolb[ii], 0.0)
        for i in range(16):
            for ii, h in enumerate(heads):
                hb = (h % 2) * 64
                qi = qf[ii][:, i * 128:(i + 1) * 128]
                ki = kf[ii][:, i * 128:(i + 1) * 128]
                vi = vt[ii][:, i * 64:(i + 1) * 64]
                pp = ps[ii]
                mm_acc(ctx, pp[:, 0:128], [(ki, qi)])
                stt(ctx, sct[ii], pp[:, 0:128], None, ALU.bypass, tmulT, ALU.mult)
                pd = ps[2 + ii]
                gd = ctx.mm_open(pd)
                gd.add(lambda e, o=pd[0:1, 0:128], l=ones1b, r=sct[ii]: e.matmul(
                    o.ap(), l.ap(), r.ap(), start=True, stop=False), [ones1b, sct[ii]])
                gd.add(lambda e, o=pd[0:1, 0:128], l=zcolb[ii], r=qi: e.matmul(
                    o.ap(), l.ap(), r.ap(), start=False, stop=True), [zcolb[ii], qi])
                gd.close()
                copy(ctx, "DVE", denr[ii], pd[0:1, 0:128])
                recip(ctx, denr[ii], denr[ii])
                mm_acc(ctx, ps[4 + ii][0:64, 0:128], [(ln.ones_1[:, 0:64], denr[ii])])
                copy(ctx, "ACT", denb[ii], ps[4 + ii][0:64, 0:128])
                pn = ps[6]
                gn = ctx.mm_open(pn)
                out_tv = pn[hb:hb + 64, 0:128]
                gn.add(lambda e, o=out_tv, l=vi, r=sct[ii], tp=(0, hb): e.matmul(
                    o.ap(), l.ap(), r.ap(), start=True, stop=False, tile_position=tp),
                    [vi, sct[ii]])
                gn.add(lambda e, o=out_tv, l=ssbb[ii], r=qi, tp=(0, hb): e.matmul(
                    o.ap(), l.ap(), r.ap(), start=False, stop=True, tile_position=tp),
                    [ssbb[ii], qi])
                gn.close()
                stt(ctx, ao[h // 2][hb:hb + 64, i * 128:(i + 1) * 128], out_tv,
                    None, ALU.bypass, denb[ii], ALU.mult)
                pk = ps[7]
                gk = ctx.mm_open(pk)
                tr_mm(ctx, gk, pk.bf16()[:, 0:64], ki, ln.identb[0:64, 0:64], True, True)
                gk.close()
                copy(ctx, "DVE", ktc[ii], pk.bf16()[:, 0:64])
                mm_acc(ctx, ps[4 + ii][0:64, 0:64], [(ktc[ii], vi)])
                tt(ctx, "DVE", ssb[ii], ssb[ii], ps[4 + ii][0:64, 0:64], ALU.add)
                copy(ctx, "DVE", ssbb[ii], ssb[ii])
                reduce(ctx, "DVE", zcol[ii][:, 1:2], ki, ALU.add)
                tt(ctx, "DVE", zcol[ii][:, 0:1], zcol[ii][:, 0:1], zcol[ii][:, 1:2], ALU.add)
                copy(ctx, "DVE", zcolb[ii], zcol[ii][:, 0:1])
    wot = load_full_bf16(ctx, f"l2a_wot_{tag}", w["WoT"], 1024, 1024)
    boc = load_col_fp32(ctx, f"l2a_bo_{tag}", w["bo"], 1024, ln, ps[2])
    for f in range(8):
        for n in range(4):
            pp = ps[f % 2][:, 0:512]
            mm_acc(ctx, pp, [(wot[j][:, f * 128:(f + 1) * 128],
                              ao[j][:, n * 512:(n + 1) * 512]) for j in range(8)])
            xsl = xs[f][:, n * 512:(n + 1) * 512]
            stt(ctx, xsl, pp, boc[:, f:f + 1], ALU.add, xsl, ALU.add)


def emit_l2_ffn(ctx, xs, w, ln, ps, tag):
    E, F = 8, 3584
    xm = [ctx.sbuf(f"l2f_xm{j}_{tag}", [128, T], BF16) for j in range(8)]
    emit_ln(ctx, xs, 0, T, ln, ps[0], ps[1], xm, mode="ln")
    for j in range(8):
        stt(ctx, xm[j], xm[j], None, ALU.bypass, ln.rb, ALU.mult)
    gwt = load_full_bf16(ctx, f"l2f_gw_{tag}", w["gwT"], 1024, 8)
    gtok = ctx.sbuf(f"l2f_gtok_{tag}", [128, 16 * 8], F32)
    rsc = ctx.sbuf(f"l2f_rsc_{tag}", [128, 6 * 8], F32)
    rcol = ctx.sbuf(f"l2f_rcol_{tag}", [128, 8], F32)
    sc_, ab_, nu_ = rsc[:, 0:8], rsc[:, 8:16], rsc[:, 16:24]
    ke_, ee_, g1_ = rsc[:, 24:32], rsc[:, 32:40], rsc[:, 40:48]
    t1c, s1c, t2c, s2c = rcol[:, 0:1], rcol[:, 1:2], rcol[:, 2:3], rcol[:, 3:4]
    for ti in range(16):
        pp = ps[2][:, 0:8]
        mm_acc(ctx, pp, [(xm[j][:, ti * 128:(ti + 1) * 128], gwt[j]) for j in range(8)])
        copy(ctx, "DVE", sc_, pp)
        g = gtok[:, ti * 8:(ti + 1) * 8]
        reduce(ctx, "DVE", t1c, sc_, ALU.max)
        act(ctx, ab_, sc_, AF.Abs)
        ts(ctx, "DVE", nu_, sc_, -1.0, ALU.mult, t1c, ALU.add)
        ts(ctx, "DVE", ee_, ab_, t1c, ALU.max, 0.02, ALU.mult)
        tt(ctx, "DVE", ke_, nu_, ee_, ALU.is_le)
        ts(ctx, "DVE", t2c, t1c, -1.0, ALU.mult)
        act(ctx, ee_, sc_, AF.Exp, bias=t2c)
        tt(ctx, "DVE", ee_, ee_, ke_, ALU.mult)
        reduce(ctx, "DVE", s1c, ee_, ALU.add)
        recip(ctx, s1c, s1c)
        ts(ctx, "DVE", g1_, sc_, t1c, ALU.is_equal)
        ts(ctx, "DVE", g1_, g1_, s1c, ALU.mult)
        ts(ctx, "DVE", ke_, sc_, t1c, ALU.is_equal)
        stt(ctx, ee_, ke_, -1e30, ALU.mult, sc_, ALU.add)
        reduce(ctx, "DVE", t2c, ee_, ALU.max)
        ts(ctx, "DVE", nu_, sc_, -1.0, ALU.mult, t2c, ALU.add)
        ts(ctx, "DVE", ab_, ab_, t2c, ALU.max, 0.02, ALU.mult)
        tt(ctx, "DVE", ke_, nu_, ab_, ALU.is_le)
        ts(ctx, "DVE", s2c, t2c, -1.0, ALU.mult)
        act(ctx, ee_, ee_, AF.Exp, bias=s2c)
        tt(ctx, "DVE", ee_, ee_, ke_, ALU.mult)
        reduce(ctx, "DVE", s2c, ee_, ALU.add)
        recip(ctx, s2c, s2c)
        ts(ctx, "DVE", nu_, sc_, t1c, ALU.is_equal)
        stt(ctx, ee_, nu_, -1e30, ALU.mult, sc_, ALU.add)
        ts(ctx, "DVE", ee_, ee_, t2c, ALU.is_equal)
        ts(ctx, "DVE", ee_, ee_, s2c, ALU.mult)
        tt(ctx, "DVE", g, g1_, ee_, ALU.add)
    grow = ctx.sbuf(f"l2f_grow_{tag}", [1, 512], F32)
    w1s = [ctx.sbuf(f"l2f_w1s{i}_{tag}", [128, 8 * 512], BF16) for i in range(2)]
    w3s = [ctx.sbuf(f"l2f_w3s{i}_{tag}", [128, 8 * 512], BF16) for i in range(2)]
    w2s = [ctx.sbuf(f"l2f_w2s{i}_{tag}", [128, 4 * 1024], BF16) for i in range(2)]
    s1b = ctx.sbuf(f"l2f_s1b_{tag}", [128, 4 * 512], BF16)
    h3b = ctx.sbuf(f"l2f_h3b_{tag}", [128, 4 * 512], BF16)
    gbeb = [ctx.sbuf(f"l2f_gbe{i}_{tag}", [128, 8 * 512], BF16) for i in range(4)]
    gbet = [[gbeb[tc][:, e * 512:(e + 1) * 512] for tc in range(4)] for e in range(8)]
    d1, d2, d3 = w["w1T"].apobj, w["w2T"].apobj, w["w3T"].apobj
    si = 0
    for e in range(E):
        for fg in range(7):
            sl1, sl3, sl2 = w1s[si % 2], w3s[si % 2], w2s[si % 2]
            si += 1

            def em1(e_, s=sl1, d=d1, e=e, fg=fg):
                return e_.dma_start(out=s.ap(), in_=d[e][:, fg * 512:(fg + 1) * 512]
                                    .rearrange("(k p) c -> p k c", p=128))
            ctx.dma("POOL", sl1, w["w1T"].tv(), emit=em1)

            def em3(e_, s=sl3, d=d3, e=e, fg=fg):
                return e_.dma_start(out=s.ap(), in_=d[e][:, fg * 512:(fg + 1) * 512]
                                    .rearrange("(k p) c -> p k c", p=128))
            ctx.dma("POOL", sl3, w["w3T"].tv(), emit=em3)

            def em2(e_, s=sl2, d=d2, e=e, fg=fg):
                return e_.dma_start(out=s.ap(), in_=d[e][fg * 512:(fg + 1) * 512, :]
                                    .rearrange("(k p) c -> p k c", p=128))
            ctx.dma("POOL", sl2, w["w2T"].tv(), emit=em2)
            for tc in range(4):
                t0 = tc * 512
                if fg == 0:
                    gp = ctx.mm_open(ps[3])
                    for c4 in range(4):
                        gcolv = gtok[:, (tc * 4 + c4) * 8 + e:(tc * 4 + c4) * 8 + e + 1]
                        gp.add(lambda e_, o=ps[3][0:1, 0:512], i=gcolv, id=ln.ident,
                               st=(c4 == 0), sp2=(c4 == 3), c4=c4: e_.matmul(
                                   o[:, c4 * 128:(c4 + 1) * 128].ap(), i.ap(), id.ap(),
                                   start=st, stop=sp2, is_transpose=True), [gcolv, ln.ident])
                    gp.close()
                    copy(ctx, "ACT", grow, ps[3][0:1, 0:512])
                    mm_acc(ctx, ps[3][:, 0:512], [(ln.ones_1, grow)])
                    copy(ctx, "ACT", gbet[e][tc], ps[3][:, 0:512])
                for fsub in range(4):
                    pp1 = ps[4 + fsub % 2]
                    mm_acc(ctx, pp1, [(sl1[:, k * 512 + fsub * 128:k * 512 + (fsub + 1) * 128],
                                       xm[k][:, t0:t0 + 512]) for k in range(8)])
                    act(ctx, s1b[:, fsub * 512:(fsub + 1) * 512], pp1, AF.Silu)
                    pp3 = ps[6 + fsub % 2]
                    mm_acc(ctx, pp3, [(sl3[:, k * 512 + fsub * 128:k * 512 + (fsub + 1) * 128],
                                       xm[k][:, t0:t0 + 512]) for k in range(8)])
                    stt(ctx, h3b[:, fsub * 512:(fsub + 1) * 512], pp3, None, ALU.bypass,
                        gbet[e][tc], ALU.mult)
                    tt(ctx, "DVE", s1b[:, fsub * 512:(fsub + 1) * 512],
                       s1b[:, fsub * 512:(fsub + 1) * 512],
                       h3b[:, fsub * 512:(fsub + 1) * 512], ALU.mult)
                for ho in range(8):
                    pp = ps[ho % 2]
                    mm_acc(ctx, pp, [(sl2[:, k * 1024 + ho * 128:k * 1024 + (ho + 1) * 128],
                                      s1b[:, k * 512:(k + 1) * 512]) for k in range(4)])
                    yv = xs[ho][:, t0:t0 + 512]
                    stt(ctx, yv, pp, None, ALU.bypass, yv, ALU.add)


def emit_l3_attn(ctx, xs, w, ln, ps, tag):
    H = 16
    xm = [ctx.sbuf(f"l3a_xm{j}_{tag}", [128, T], BF16) for j in range(8)]
    emit_ln(ctx, xs, 0, T, ln, ps[0], ps[1], xm, mode="ln")
    t1s = ctx.sbuf(f"l3a_t1_{tag}", [128, T], F32)
    mkr = ctx.mark()
    cosb = ctx.sbuf(f"l3a_cosb_{tag}", [128, T], F32)
    sinb = ctx.sbuf(f"l3a_sinb_{tag}", [128, T], F32)
    ctx.dma("SYNC", cosb, w["cosb"].tv())
    ctx.dma("SYNC", sinb, w["sinb2"].tv())
    permb = ctx.sbuf(f"l3a_perm_{tag}", [128, 128], BF16)
    ctx.dma("POOL", permb, w["permT"].tv())
    t2s = ctx.sbuf(f"l3a_t2_{tag}", [128, T], F32)
    for j in range(8):
        for n in range(4):
            pp = ps[2 + n % 2][:, 0:512]
            sl = slice(n * 512, (n + 1) * 512)
            mm_acc(ctx, pp, [(permb, xm[j][:, sl])])
            stt(ctx, t1s[:, sl], pp, None, ALU.bypass, sinb[:, sl], ALU.mult)
        tt(ctx, "POOL", t2s, xm[j], cosb, ALU.mult)
        tt(ctx, "DVE", xm[j], t1s, t2s, ALU.add)
    ctx.free_to(mkr)
    ao = [ctx.sbuf(f"l3a_ao{j}_{tag}", [128, T], BF16) for j in range(8)]
    mkh = ctx.mark()
    qstage = ctx.sbuf(f"l3a_qstg_{tag}", [64, 512], F32)
    bqk = ctx.sbuf(f"l3a_bqk_{tag}", [64, 48], F32)
    stg = ctx.sbuf(f"l3a_bstg_{tag}", [48, 64], F32)
    ctx.dma("SYNC", stg, w["ipb48"].tv())
    transpose(ctx, ps[2][0:64, 0:48], stg, ln.ident[0:48, 0:48])
    copy(ctx, "ACT", bqk, ps[2][0:64, 0:48])
    bvrow = ctx.sbuf(f"l3a_bvrow_{tag}", [1, 3072], F32)
    ctx.dma("SYNC", bvrow, w["ipbrow"].tv())
    rtok = ctx.sbuf(f"l3a_rtok_{tag}", [128, 16], F32)
    for i in range(16):
        transpose(ctx, ps[2][:, 0:1], ln.rb[0:1, i * 128:(i + 1) * 128], ln.ones_1[0:1, 0:1])
        copy(ctx, "ACT", rtok[:, i:i + 1], ps[2][:, 0:1])
    pt = ctx.sbuf(f"l3a_pt_{tag}", [128, 16 * 512], BF16)
    p = ctx.sbuf(f"l3a_p_{tag}", [128, T], BF16)
    tq = t1s
    mcol = ctx.sbuf(f"l3a_mcol_{tag}", [128, 4], F32)
    maxc, sumc, biasc = mcol[:, 0:1], mcol[:, 1:2], mcol[:, 2:3]
    qh = ctx.sbuf(f"l3a_qh_{tag}", [64, T], BF16)
    kh = ctx.sbuf(f"l3a_kh_{tag}", [64, T], BF16)
    vtok = ctx.sbuf(f"l3a_vtok_{tag}", [128, 16 * 64], BF16)
    bvb = ctx.sbuf(f"l3a_bvb_{tag}", [128, 64], F32)
    msk = [ctx.sbuf(f"l3a_msk{i}_{tag}", [128, T], BF16) for i in range(2)]
    wsl = [ctx.sbuf(f"l3a_ws{i}_{tag}", [128, 512], BF16) for i in range(3)]
    dw = w["ipwT"].apobj
    dm = w["maskadd"].apobj
    for h in range(H):
        hb = (h % 2) * 64
        for mi, col0 in ((0, h * 64), (1, 1024 + h * 64), (2, 2048 + h * 64)):
            def em(e, s=wsl[mi], d=dw, col0=col0):
                return e.dma_start(out=s.ap(), in_=d[:, col0:col0 + 64].rearrange(
                    "(k p) c -> p k c", p=128))
            ctx.dma("POOL", wsl[mi], w["ipwT"].tv(), emit=em)
        for mi, dst in ((0, qh), (1, kh)):
            bcol = bqk[:, (mi * 16 + h):(mi * 16 + h) + 1]
            for n in range(4):
                pp = ps[n % 2][0:64, 0:512]
                sl = slice(n * 512, (n + 1) * 512)
                mm_acc(ctx, pp, [(wsl[mi][:, j * 64:(j + 1) * 64], xm[j][:, sl])
                                 for j in range(8)])
                stt(ctx, qstage, pp, None, ALU.bypass, ln.rb[0:64, sl], ALU.mult)
                ts(ctx, "DVE", dst[:, sl], qstage, bcol, ALU.add)
        mm_acc(ctx, ps[2][:, 0:64], [(ln.ones_1, bvrow[:, 2048 + h * 64:2048 + (h + 1) * 64])])
        copy(ctx, "ACT", bvb, ps[2][:, 0:64])
        for tt_i in range(16):
            pp = ps[2 + tt_i % 2][:, 0:64]
            mm_acc(ctx, pp, [(xm[j][:, tt_i * 128:(tt_i + 1) * 128],
                              wsl[2][:, j * 64:(j + 1) * 64]) for j in range(8)])
            stt(ctx, vtok[:, tt_i * 64:(tt_i + 1) * 64], pp, rtok[:, tt_i:tt_i + 1],
                ALU.mult, bvb, ALU.add)
        for qc in range(16):
            mslot = msk[qc % 2]

            def emm(e, s=mslot, d=dm, qc=qc):
                return e.dma_start(out=s.ap(), in_=d[qc * 128:(qc + 1) * 128, :])
            ctx.dma("SYNC", mslot, w["maskadd"].tv(), emit=emm)
            for c in range(4):
                c0 = c * 512
                pp = ps[c % 2][:, 0:512]
                mm_acc(ctx, pp, [(qh[:, qc * 128:(qc + 1) * 128], kh[:, c0:c0 + 512])])
                stt(ctx, tq[:, c0:c0 + 512], pp, 0.125, ALU.mult, mslot[:, c0:c0 + 512], ALU.add)
            reduce(ctx, "DVE", maxc, tq, ALU.max)
            ts(ctx, "DVE", biasc, maxc, -1.0, ALU.mult)
            act(ctx, p, tq, AF.Exp, bias=biasc, accum_out=sumc)
            recip(ctx, sumc, sumc)
            ts(ctx, "DVE", p, p, sumc, ALU.mult)
            for kc in range(16):
                bank = ps[6 + kc % 2]
                gg = ctx.mm_open(bank)
                tr_mm(ctx, gg, bank.bf16()[:, 0:128], p[:, kc * 128:(kc + 1) * 128],
                      ln.identb, st=True, sp=True)
                gg.close()
                dst = pt[:, kc * 512 + (qc % 4) * 128:kc * 512 + (qc % 4) * 128 + 128]
                copy(ctx, "DVE" if kc % 2 == 0 else "ACT", dst, bank.bf16()[:, 0:128])
            if qc % 4 == 3:
                sp_i = qc // 4
                po = ps[2]
                out_tv = po[hb:hb + 64, 0:512]
                gpv = ctx.mm_open(po)
                for kc in range(16):
                    lt = vtok[:, kc * 64:(kc + 1) * 64]
                    rt = pt[:, kc * 512:(kc + 1) * 512]
                    gpv.add(lambda e, o=out_tv, l=lt, r=rt, st=(kc == 0), sp2=(kc == 15),
                            tp=(0, hb): e.matmul(o.ap(), l.ap(), r.ap(), start=st, stop=sp2,
                                                 tile_position=tp), [lt, rt])
                gpv.close()
                copy(ctx, "DVE", ao[h // 2][hb:hb + 64, sp_i * 512:(sp_i + 1) * 512], out_tv)
    ctx.free_to(mkh)
    wot = load_full_bf16(ctx, f"l3a_owt_{tag}", w["owT"], 1024, 1024)
    obc = load_col_fp32(ctx, f"l3a_ob_{tag}", w["ob"], 1024, ln, ps[2])
    for f in range(8):
        for n in range(4):
            pp = ps[f % 2][:, 0:512]
            mm_acc(ctx, pp, [(wot[j][:, f * 128:(f + 1) * 128],
                              ao[j][:, n * 512:(n + 1) * 512]) for j in range(8)])
            xsl = xs[f][:, n * 512:(n + 1) * 512]
            stt(ctx, xsl, pp, obc[:, f:f + 1], ALU.add, xsl, ALU.add)


def emit_l3_ffn(ctx, xs, w, ln, ps, tag, load_x_tile=None, dbg=None):
    xsave = ctx.dram(f"l3f_xsave_{tag}", [8, 128, T], F32, "Internal")
    dxs = xsave.apobj
    for j in range(8):
        def emS(e, j=j, d=dxs, s=xs[j]):
            return e.dma_start(out=d[j], in_=s.ap())
        ctx.dma("SYNC", xsave.tv(), xs[j], emit=emS)
    """MLA-ish attn + shared+top2 MoE. w: g1c/g2c via cols, WqT [1024,1024],
    WkcT/WvcT [1024,256], WoT [1024,1024], WrT [1024,8], ebrow [1,8],
    sW1T/sW2T/sW3T [1024,1024], eW1T/eW2T/eW3T [8,1024,1024]."""
    emit_ln_inplace(ctx, xs, ln, ps[0], ps[1], T)
    g1c = load_col_fp32(ctx, f"l3f_g1_{tag}", w["g1"], 1024, ln, ps[0])
    g2c = load_col_fp32(ctx, f"l3f_g2_{tag}", w["g2"], 1024, ln, ps[1])
    g1cols = [g1c[:, j:j + 1] for j in range(8)]
    g2cols = [g2c[:, j:j + 1] for j in range(8)]
    xm = [ctx.sbuf(f"l3f_xm{j}_{tag}", [128, T], BF16) for j in range(8)]
    emit_ln(ctx, xs, 0, T, ln, ps[0], ps[1], xm, mode="rms", pre_col=g1cols, eps=1e-6)
    rtok = ctx.sbuf(f"l3f_rtok_{tag}", [128, 16], F32)
    for i in range(16):
        transpose(ctx, ps[2][:, 0:1], ln.rb[0:1, i * 128:(i + 1) * 128], ln.ones_1[0:1, 0:1])
        copy(ctx, "ACT", rtok[:, i:i + 1], ps[2][:, 0:1])
    mk_ao = ctx.mark()
    ao = [ctx.sbuf(f"l3f_ao{j}_{tag}", [128, T], BF16) for j in range(8)]
    mka = ctx.mark()
    wqsl = [ctx.sbuf(f"l3f_wqs{i}_{tag}", [128, 8 * 256], BF16) for i in range(2)]
    wkt = load_full_bf16(ctx, f"l3f_wkc_{tag}", w["WkcT"], 1024, 256)
    wvt = load_full_bf16(ctx, f"l3f_wvc_{tag}", w["WvcT"], 1024, 256)
    qf = [ctx.sbuf(f"l3f_q{i}_{tag}", [128, T], BF16) for i in range(2)]
    kf = [ctx.sbuf(f"l3f_k{i}_{tag}", [128, T], BF16) for i in range(2)]
    vtok = ctx.sbuf(f"l3f_vtok_{tag}", [128, 16 * 256], BF16)
    for f in range(2):
        for n in range(4):
            pp = ps[f % 2][:, 0:512]
            sl = slice(n * 512, (n + 1) * 512)
            mm_acc(ctx, pp, [(wkt[j][:, f * 128:(f + 1) * 128], xm[j][:, sl]) for j in range(8)])
            stt(ctx, kf[f][:, sl], pp, None, ALU.bypass, ln.rb[:, sl], ALU.mult)
    for ti in range(16):
        for f2 in range(2):
            pp = ps[2 + f2][:, 0:128]
            mm_acc(ctx, pp, [(xm[j][:, ti * 128:(ti + 1) * 128],
                              wvt[j][:, f2 * 128:(f2 + 1) * 128]) for j in range(8)])
            ts(ctx, "DVE", vtok[:, ti * 256 + f2 * 128:ti * 256 + (f2 + 1) * 128], pp,
               rtok[:, ti:ti + 1], ALU.mult)
    pt = ctx.sbuf(f"l3f_pt_{tag}", [128, 16 * 512], BF16)
    p = ctx.sbuf(f"l3f_p_{tag}", [128, T], BF16)
    mcol = ctx.sbuf(f"l3f_mcol_{tag}", [128, 12], F32)
    maxc, sumc, biasc = mcol[:, 0:1], mcol[:, 1:2], mcol[:, 2:3]
    dwq = w["WqT"].apobj
    for h in range(4):
        wq = wqsl[h % 2]

        def emq(e, s=wq, d=dwq, h=h):
            return e.dma_start(out=s.ap(), in_=d[:, h * 256:(h + 1) * 256].rearrange(
                "(k p) c -> p k c", p=128))
        ctx.dma("POOL", wq, w["WqT"].tv(), emit=emq)
        for f in range(2):
            for n in range(4):
                pp = ps[f % 2][:, 0:512]
                sl = slice(n * 512, (n + 1) * 512)
                mm_acc(ctx, pp, [(wq[:, j * 256 + f * 128:j * 256 + (f + 1) * 128],
                                  xm[j][:, sl]) for j in range(8)])
                stt(ctx, qf[f][:, sl], pp, None, ALU.bypass, ln.rb[:, sl], ALU.mult)
        for qc in range(16):
            for c in range(4):
                c0 = c * 512
                pp = ps[c][:, 0:512]
                mm_acc(ctx, pp, [(qf[f2][:, qc * 128:(qc + 1) * 128],
                                  kf[f2][:, c0:c0 + 512]) for f2 in range(2)])
                reduce(ctx, "DVE", mcol[:, 4 + c:5 + c], pp, ALU.max)
            tt(ctx, "DVE", mcol[:, 8:9], mcol[:, 4:5], mcol[:, 5:6], ALU.max)
            tt(ctx, "DVE", mcol[:, 9:10], mcol[:, 6:7], mcol[:, 7:8], ALU.max)
            tt(ctx, "DVE", maxc, mcol[:, 8:9], mcol[:, 9:10], ALU.max)
            ts(ctx, "DVE", biasc, maxc, -1.0 / 16.0, ALU.mult)
            for c in range(4):
                c0 = c * 512
                act(ctx, p[:, c0:c0 + 512], ps[c][:, 0:512], AF.Exp, bias=biasc,
                    scale=1.0 / 16.0, accum_out=mcol[:, 4 + c:5 + c])
            tt(ctx, "DVE", mcol[:, 8:9], mcol[:, 4:5], mcol[:, 5:6], ALU.add)
            tt(ctx, "DVE", mcol[:, 9:10], mcol[:, 6:7], mcol[:, 7:8], ALU.add)
            tt(ctx, "DVE", sumc, mcol[:, 8:9], mcol[:, 9:10], ALU.add)
            recip(ctx, sumc, sumc)
            ts(ctx, "DVE", p, p, sumc, ALU.mult)
            for kc in range(16):
                bank = ps[6 + kc % 2]
                gg = ctx.mm_open(bank)
                tr_mm(ctx, gg, bank.bf16()[:, 0:128], p[:, kc * 128:(kc + 1) * 128],
                      ln.identb, st=True, sp=True)
                gg.close()
                dst = pt[:, kc * 512 + (qc % 4) * 128:kc * 512 + (qc % 4) * 128 + 128]
                copy(ctx, "DVE" if kc % 2 == 0 else "ACT", dst, bank.bf16()[:, 0:128])
            if qc % 4 == 3:
                sp_i = qc // 4
                for f2 in range(2):
                    po = ps[2 + f2]
                    gpv = ctx.mm_open(po)
                    for kc in range(16):
                        lt = vtok[:, kc * 256 + f2 * 128:kc * 256 + (f2 + 1) * 128]
                        rt = pt[:, kc * 512:(kc + 1) * 512]
                        gpv.add(lambda e, o=po[:, 0:512], l=lt, r=rt, st=(kc == 0),
                                sp2=(kc == 15): e.matmul(o.ap(), l.ap(), r.ap(),
                                                         start=st, stop=sp2), [lt, rt])
                    gpv.close()
                    copy(ctx, "DVE", ao[2 * h + f2][:, sp_i * 512:(sp_i + 1) * 512], po[:, 0:512])
    ctx.free_to(mka)
    wot = load_full_bf16(ctx, f"l3f_wo_{tag}", w["WoT"], 1024, 1024)
    for f in range(8):
        for n in range(4):
            pp = ps[f % 2][:, 0:512]
            mm_acc(ctx, pp, [(wot[j][:, f * 128:(f + 1) * 128],
                              ao[j][:, n * 512:(n + 1) * 512]) for j in range(8)])
            xsl = xs[f][:, n * 512:(n + 1) * 512]
            stt(ctx, xsl, pp, None, ALU.bypass, xsl, ALU.add)
    if dbg is not None:
        for j in range(8):
            ctx.dma("SYNC", dbg[f"xa{j}"].tv(), xs[j])
    ctx.free_to(mk_ao)
    ctx.barrier()
    emit_ln(ctx, xs, 0, T, ln, ps[0], ps[1], xm, mode="rms", pre_col=g2cols, eps=1e-6)
    for j in range(8):
        stt(ctx, xm[j], xm[j], None, ALU.bypass, ln.rb, ALU.mult)
    mk_moe = ctx.mark()
    wrt = load_full_bf16(ctx, f"l3f_wr_{tag}", w["WrT"], 1024, 8)
    ebr = ctx.sbuf(f"l3f_ebr_{tag}", [1, 8], F32)
    ctx.dma("SYNC", ebr, w["ebrow"].tv())
    ebb = ctx.sbuf(f"l3f_ebb_{tag}", [128, 8], F32)
    mm_acc(ctx, ps[2][:, 0:8], [(ln.ones_1, ebr)])
    copy(ctx, "ACT", ebb, ps[2][:, 0:8])
    gtok = ctx.sbuf(f"l3f_gtok_{tag}", [128, 16 * 8], F32)
    rsc = ctx.sbuf(f"l3f_rsc_{tag}", [128, 4 * 8], F32)
    rcol = ctx.sbuf(f"l3f_rcol_{tag}", [128, 8], F32)
    sc_, is1_, is2_, ee_ = rsc[:, 0:8], rsc[:, 8:16], rsc[:, 16:24], rsc[:, 24:32]
    t1c, t2c, e2c, sic = rcol[:, 0:1], rcol[:, 1:2], rcol[:, 2:3], rcol[:, 3:4]
    nt1c = rcol[:, 4:5]
    for ti in range(16):
        pp = ps[2][:, 0:8]
        mm_acc(ctx, pp, [(xm[j][:, ti * 128:(ti + 1) * 128], wrt[j]) for j in range(8)])
        tt(ctx, "DVE", sc_, pp, ebb, ALU.add)
        reduce(ctx, "DVE", t1c, sc_, ALU.max)
        ts(ctx, "DVE", nt1c, t1c, -1.0, ALU.mult)
        act(ctx, sc_, sc_, AF.Exp, bias=nt1c, accum_out=sic)
        recip(ctx, sic, sic)
        ts(ctx, "DVE", sc_, sc_, sic, ALU.mult)
        reduce(ctx, "DVE", t1c, sc_, ALU.max)
        ts(ctx, "DVE", is1_, sc_, t1c, ALU.is_equal)
        stt(ctx, ee_, is1_, -1e30, ALU.mult, sc_, ALU.add)
        reduce(ctx, "DVE", t2c, ee_, ALU.max)
        ts(ctx, "DVE", is2_, ee_, t2c, ALU.is_equal)
        ts(ctx, "DVE", nt1c, t1c, -1.0, ALU.mult)
        act(ctx, e2c, t2c, AF.Exp, bias=nt1c)
        ts(ctx, "DVE", sic, e2c, 1.0, ALU.add)
        recip(ctx, sic, sic)
        tt(ctx, "DVE", e2c, e2c, sic, ALU.mult)
        g = gtok[:, ti * 8:(ti + 1) * 8]
        ts(ctx, "DVE", is1_, is1_, sic, ALU.mult)
        ts(ctx, "DVE", is2_, is2_, e2c, ALU.mult)
        tt(ctx, "DVE", g, is1_, is2_, ALU.add)
    if dbg is not None:
        ctx.dma("SYNC", dbg["gtok"].tv(), gtok)
        ctx.dma("SYNC", dbg["xm0"].tv(), xm[0])
    gbeb = [ctx.sbuf(f"l3f_gbe{i}_{tag}", [128, 8 * 512], BF16) for i in range(4)]
    gbet = [[gbeb[tc][:, e * 512:(e + 1) * 512] for tc in range(4)] for e in range(8)]
    grow = ctx.sbuf(f"l3f_grow_{tag}", [1, 512], F32)
    for e in range(8):
        for tc in range(4):
            gp = ctx.mm_open(ps[3])
            for c4 in range(4):
                gcolv = gtok[:, (tc * 4 + c4) * 8 + e:(tc * 4 + c4) * 8 + e + 1]
                gp.add(lambda e_, o=ps[3][0:1, 0:512], i=gcolv, id=ln.ident,
                       st=(c4 == 0), sp2=(c4 == 3), c4=c4: e_.matmul(
                           o[:, c4 * 128:(c4 + 1) * 128].ap(), i.ap(), id.ap(),
                           start=st, stop=sp2, is_transpose=True), [gcolv, ln.ident])
            gp.close()
            copy(ctx, "ACT", grow, ps[3][0:1, 0:512])
            mm_acc(ctx, ps[3][:, 0:512], [(ln.ones_1, grow)])
            copy(ctx, "ACT", gbet[e][tc], ps[3][:, 0:512])
    w1s = [ctx.sbuf(f"l3f_w1s{i}_{tag}", [128, 8 * 512], BF16) for i in range(2)]
    w3s = [ctx.sbuf(f"l3f_w3s{i}_{tag}", [128, 8 * 512], BF16) for i in range(2)]
    w2s = [ctx.sbuf(f"l3f_w2s{i}_{tag}", [128, 4 * 1024], BF16) for i in range(2)]
    s1b = ctx.sbuf(f"l3f_s1b_{tag}", [128, 4 * 512], BF16)
    h3b = ctx.sbuf(f"l3f_h3b_{tag}", [128, 4 * 512], BF16)
    si = 0
    units = [("s", None)] + [("e", e) for e in range(8)]
    for unm, e in units:
        for fg in range(2):
            sl1, sl3, sl2 = w1s[si % 2], w3s[si % 2], w2s[si % 2]
            si += 1
            for srcn, slot in (("W1", sl1), ("W3", sl3)):
                dsrc = w["s" + srcn + "T"].apobj if unm == "s" else w["e" + srcn + "T"].apobj

                def eml(e_, s=slot, d=dsrc, e=e, fg=fg, unm=unm):
                    base = d if unm == "s" else d[e]
                    return e_.dma_start(out=s.ap(), in_=base[:, fg * 512:(fg + 1) * 512]
                                        .rearrange("(k p) c -> p k c", p=128))
                ctx.dma("POOL", slot, (w["s" + srcn + "T"] if unm == "s" else w["e" + srcn + "T"]).tv(), emit=eml)
            dsrc2 = w["sW2T"].apobj if unm == "s" else w["eW2T"].apobj

            def eml2(e_, s=sl2, d=dsrc2, e=e, fg=fg, unm=unm):
                base = d if unm == "s" else d[e]
                return e_.dma_start(out=s.ap(), in_=base[fg * 512:(fg + 1) * 512, :]
                                    .rearrange("(k p) c -> p k c", p=128))
            ctx.dma("POOL", sl2, (w["sW2T"] if unm == "s" else w["eW2T"]).tv(), emit=eml2)
            for tc in range(4):
                t0 = tc * 512
                for fsub in range(4):
                    pp1 = ps[4 + fsub % 2]
                    mm_acc(ctx, pp1, [(sl1[:, k * 512 + fsub * 128:k * 512 + (fsub + 1) * 128],
                                       xm[k][:, t0:t0 + 512]) for k in range(8)])
                    act(ctx, s1b[:, fsub * 512:(fsub + 1) * 512], pp1, AF.Silu)
                    pp3 = ps[6 + fsub % 2]
                    mm_acc(ctx, pp3, [(sl3[:, k * 512 + fsub * 128:k * 512 + (fsub + 1) * 128],
                                       xm[k][:, t0:t0 + 512]) for k in range(8)])
                    if unm == "e":
                        stt(ctx, h3b[:, fsub * 512:(fsub + 1) * 512], pp3, None, ALU.bypass,
                            gbet[e][tc], ALU.mult)
                    else:
                        copy(ctx, "DVE", h3b[:, fsub * 512:(fsub + 1) * 512], pp3)
                    tt(ctx, "DVE", s1b[:, fsub * 512:(fsub + 1) * 512],
                       s1b[:, fsub * 512:(fsub + 1) * 512],
                       h3b[:, fsub * 512:(fsub + 1) * 512], ALU.mult)
                for ho in range(8):
                    pp = ps[ho % 2]
                    mm_acc(ctx, pp, [(sl2[:, k * 1024 + ho * 128:k * 1024 + (ho + 1) * 128],
                                      s1b[:, k * 512:(k + 1) * 512]) for k in range(4)])
                    yv = xs[ho][:, t0:t0 + 512]
                    stt(ctx, yv, pp, None, ALU.bypass, yv, ALU.add)
    ctx.free_to(mk_moe)
    xstg = ctx.sbuf(f"l3f_xstg_{tag}", [128, T], F32)
    for j in range(8):
        def emL(e, j=j, d=dxs, dst=xstg):
            return e.dma_start(out=dst.ap(), in_=d[j])
        ctx.dma("SYNC", xstg, xsave.tv(), emit=emL)
        tt(ctx, "DVE", xs[j], xs[j], xstg, ALU.add)


from concourse.bass_utils import run_bass_kernel_spmd


def _build_program():
    nc = bass.Bass("TRN2", target_bir_lowering=False, debug=False)
    c = Ctx(nc)
    xT = c.dram("xT", [2, 1024, 2048], F32, "ExternalInput")
    yT = c.dram("yT", [2, 1024, 2048], F32, "ExternalOutput")
    w0a = {"Mq": c.dram("l0_Mq", [1024, 512], F32, "ExternalInput"),
           "Mk": c.dram("l0_Mk", [1024, 512], F32, "ExternalInput"),
           "Mv": c.dram("l0_Mv", [1024, 1024], F32, "ExternalInput")}
    w0f = {"W1": c.dram("l0f_W1", [1024, 4096], F32, "ExternalInput"),
           "b1": c.dram("l0f_b1", [4096], F32, "ExternalInput"),
           "qp": c.dram("l0f_qp", [4096], F32, "ExternalInput"),
           "qwr": c.dram("l0f_qwr", [4096], F32, "ExternalInput"),
           "qwi": c.dram("l0f_qwi", [4096], F32, "ExternalInput"),
           "W2": c.dram("l0f_W2", [4096, 1024], F32, "ExternalInput"),
           "b2": c.dram("l0f_b2", [1024], F32, "ExternalInput")}
    w1a = {**{nm: c.dram(f"l1a_{nm}", [1024, 1024], F32, "ExternalInput")
              for nm in ("WqT", "WkT", "WvT", "Wq2T", "Wk2T", "WoT")},
           "mixture": c.dram("l1a_mixture", [1, 1], F32, "ExternalInput"),
           "qs": c.dram("l1a_qs", [1, 1], F32, "ExternalInput")}
    w1f = {nm: c.dram(f"l1f_{nm}", list(shp), F32, "ExternalInput") for nm, shp in (
        ("cn_w1", (1024, 1024)), ("cn_b1", (1024,)), ("cn_w2", (1024, 512)),
        ("cn_b2", (512,)), ("cn_w3", (512, 256)), ("cn_b3", (256,)),
        ("cn_w4", (256, 1)), ("cn_b4", (1, 1)), ("gc_w1", (1024, 512)),
        ("gc_b1", (512,)), ("gc_w2", (512, 256)), ("gc_b2", (256,)),
        ("gc_w3", (256, 1)), ("gc_b3", (1, 1)), ("ar", (1, 1)))}
    w2a = {**{nm: c.dram(f"l2a_{nm}", [1024, 1024], F32, "ExternalInput")
              for nm in ("WqT", "WkT", "WvT", "WoT")},
           "bq": c.dram("l2a_bq", [16, 64], F32, "ExternalInput"),
           "bk": c.dram("l2a_bk", [16, 64], F32, "ExternalInput"),
           "bv": c.dram("l2a_bv", [1, 1024], F32, "ExternalInput"),
           "bo": c.dram("l2a_bo", [1024], F32, "ExternalInput")}
    w2f = {"gwT": c.dram("l2f_gwT", [1024, 8], F32, "ExternalInput"),
           "w1T": c.dram("l2f_w1T", [8, 1024, 3584], F32, "ExternalInput"),
           "w3T": c.dram("l2f_w3T", [8, 1024, 3584], F32, "ExternalInput"),
           "w2T": c.dram("l2f_w2T", [8, 3584, 1024], F32, "ExternalInput")}
    w3a = {"ipwT": c.dram("l3a_ipwT", [1024, 3072], F32, "ExternalInput"),
           "ipb48": c.dram("l3a_ipb48", [48, 64], F32, "ExternalInput"),
           "ipbrow": c.dram("l3a_ipbrow", [1, 3072], F32, "ExternalInput"),
           "owT": c.dram("l3a_owT", [1024, 1024], F32, "ExternalInput"),
           "ob": c.dram("l3a_ob", [1024], F32, "ExternalInput"),
           "cosb": c.dram("l3a_cosb", [128, 2048], F32, "ExternalInput"),
           "sinb2": c.dram("l3a_sinb2", [128, 2048], F32, "ExternalInput"),
           "permT": c.dram("l3a_permT", [128, 128], F32, "ExternalInput"),
           "maskadd": c.dram("l3a_maskadd", [2048, 2048], BF16, "ExternalInput")}
    w3f = {"g1": c.dram("l3f_g1", [1024], F32, "ExternalInput"),
           "g2": c.dram("l3f_g2", [1024], F32, "ExternalInput"),
           "WqT": c.dram("l3f_WqT", [1024, 1024], F32, "ExternalInput"),
           "WkcT": c.dram("l3f_WkcT", [1024, 256], F32, "ExternalInput"),
           "WvcT": c.dram("l3f_WvcT", [1024, 256], F32, "ExternalInput"),
           "WoT": c.dram("l3f_WoT", [1024, 1024], F32, "ExternalInput"),
           "WrT": c.dram("l3f_WrT", [1024, 8], F32, "ExternalInput"),
           "ebrow": c.dram("l3f_ebrow", [1, 8], F32, "ExternalInput"),
           "sW1T": c.dram("l3f_sW1T", [1024, 1024], F32, "ExternalInput"),
           "sW2T": c.dram("l3f_sW2T", [1024, 1024], F32, "ExternalInput"),
           "sW3T": c.dram("l3f_sW3T", [1024, 1024], F32, "ExternalInput"),
           "eW1T": c.dram("l3f_eW1T", [8, 1024, 1024], F32, "ExternalInput"),
           "eW2T": c.dram("l3f_eW2T", [8, 1024, 1024], F32, "ExternalInput"),
           "eW3T": c.dram("l3f_eW3T", [8, 1024, 1024], F32, "ExternalInput")}
    ps = [c.psum(f"ps{i}", [128, 512]) for i in range(8)]
    xs = [c.sbuf(f"x{j}", [128, 2048], F32) for j in range(8)]
    ln = LNScratch(c, 2048)
    for b in range(2):
        for j in range(8):
            d = xT.apobj

            def emit(e, j=j, b=b, d=d):
                return e.dma_start(out=xs[j].ap(), in_=d[b, j * 128:(j + 1) * 128, :])
            c.dma("SYNC", xs[j], xT.tv(), emit=emit)
        for fn, wd, kw in ((emit_l0_attn, w0a, {}), (emit_l0_ffn, w0f, {}),
                           (emit_l1_attn, w1a, {}), (emit_l1_ffn, w1f, {}),
                           (emit_l2_attn, w2a, {}), (emit_l2_ffn, w2f, {}),
                           (emit_l3_attn, w3a, {})):
            mk = c.mark()
            fn(c, xs, wd, ln, ps, f"{fn.__name__}_b{b}", **kw)
            c.free_to(mk)
        mk = c.mark()

        def load_x_tile(cx, j, dst, b=b, xT=xT):
            d = xT.apobj

            def em(e, j=j, b=b, d=d, dst=dst):
                return e.dma_start(out=dst.ap(), in_=d[b, j * 128:(j + 1) * 128, :])
            cx.dma("SYNC", dst, xT.tv(), emit=em)
        emit_l3_ffn(c, xs, w3f, ln, ps, f"l3f_b{b}", load_x_tile=load_x_tile)
        c.free_to(mk)
        for j in range(8):
            d = yT.apobj

            def emit(e, j=j, b=b, d=d):
                return e.dma_start(out=d[b, j * 128:(j + 1) * 128, :], in_=xs[j].ap())
            c.dma("SYNC", yT.tv()[j:j + 1, :], xs[j], emit=emit)
    c.final_wait("SYNC", [yT])
    c.build()
    c.close()
    return nc


def _shared_inputs(ins):
    import ml_dtypes
    d = {}
    for k in ("L0_attn_Mq", "L0_attn_Mk", "L0_attn_Mv"):
        d[k.replace("L0_attn_", "l0_")] = ins[k]
    for a, b2 in (("W1", "W1"), ("b1", "b1"), ("quantum_phases", "qp"),
                  ("quantum_weights_real", "qwr"), ("quantum_weights_imag", "qwi"),
                  ("W2", "W2"), ("b2", "b2")):
        d["l0f_" + b2] = ins["L0_ffn_" + a]
    for nm in ("Wq", "Wk", "Wv", "Wq2", "Wk2", "Wo"):
        d[f"l1a_{nm}T"] = np.ascontiguousarray(ins[f"L1_attn_{nm}"].T)
    d["l1a_mixture"] = np.asarray(ins["L1_attn_mixture"], np.float32).reshape(1, 1)
    d["l1a_qs"] = np.asarray(ins["L1_attn_quartet_scale"], np.float32).reshape(1, 1)
    for i in range(1, 5):
        d[f"l1f_cn_w{i}"] = ins[f"L1_ffn_cn_w{i}"]
        d[f"l1f_cn_b{i}"] = ins[f"L1_ffn_cn_b{i}"].reshape(1, 1) if i == 4 else ins[f"L1_ffn_cn_b{i}"]
    for i in range(1, 4):
        d[f"l1f_gc_w{i}"] = ins[f"L1_ffn_gc_w{i}"]
        d[f"l1f_gc_b{i}"] = ins[f"L1_ffn_gc_b{i}"].reshape(1, 1) if i == 3 else ins[f"L1_ffn_gc_b{i}"]
    d["l1f_ar"] = np.asarray(ins["L1_ffn_adaptive_range"], np.float32).reshape(1, 1)
    for nm in ("Wq", "Wk", "Wv", "Wo"):
        d[f"l2a_{nm}T"] = np.ascontiguousarray(ins[f"L2_attn_{nm}"].T)
    d["l2a_bq"] = ins["L2_attn_bq"].reshape(16, 64)
    d["l2a_bk"] = ins["L2_attn_bk"].reshape(16, 64)
    d["l2a_bv"] = ins["L2_attn_bv"].reshape(1, 1024)
    d["l2a_bo"] = ins["L2_attn_bo"]
    d["l2f_gwT"] = np.ascontiguousarray(ins["L2_ffn_gate_w"].T)
    d["l2f_w1T"] = np.ascontiguousarray(ins["L2_ffn_w1"].transpose(0, 2, 1))
    d["l2f_w3T"] = np.ascontiguousarray(ins["L2_ffn_w3"].transpose(0, 2, 1))
    d["l2f_w2T"] = np.ascontiguousarray(ins["L2_ffn_w2"].transpose(0, 2, 1))
    d["l3a_ipwT"] = np.ascontiguousarray(ins["L3_attn_in_proj_w"].T)
    d["l3a_ipb48"] = ins["L3_attn_in_proj_b"].reshape(48, 64)
    d["l3a_ipbrow"] = ins["L3_attn_in_proj_b"].reshape(1, 3072)
    d["l3a_owT"] = np.ascontiguousarray(ins["L3_attn_out_w"].T)
    d["l3a_ob"] = ins["L3_attn_out_b"]
    t = np.arange(2048, dtype=np.float64)
    pvec = np.arange(128)
    i_of_p = (pvec % 64) // 2
    inv = 1.0 / 10000.0 ** (np.arange(0, 64, 2, dtype=np.float64) / 64.0)
    fidx = (2 * i_of_p) % 32
    fr = t[None, :] * inv[fidx][:, None]
    d["l3a_cosb"] = np.cos(fr).astype(np.float32)
    sgn = np.where(pvec % 2 == 0, -1.0, 1.0)
    d["l3a_sinb2"] = (np.sin(fr) * sgn[:, None]).astype(np.float32)
    perm = np.zeros((128, 128), np.float32)
    perm[np.arange(128), np.arange(128) ^ 1] = 1.0
    d["l3a_permT"] = perm
    ii, jj = np.mgrid[0:2048, 0:2048]
    allowed = (np.abs(ii - jj) <= 128) | (jj % 256 == 0) | (jj == 0)
    d["l3a_maskadd"] = np.where(allowed, 0.0, -1e30).astype(ml_dtypes.bfloat16)
    d["l3f_g1"] = ins["L3_ffn_g1"]
    d["l3f_g2"] = ins["L3_ffn_g2"]
    for nm in ("Wq", "Wkc", "Wvc", "Wo", "Wr", "sW1", "sW2", "sW3"):
        d[f"l3f_{nm}T"] = np.ascontiguousarray(ins[f"L3_ffn_{nm}"].T)
    d["l3f_ebrow"] = ins["L3_ffn_expert_bias"].reshape(1, 8)
    for nm in ("eW1", "eW2", "eW3"):
        d[f"l3f_{nm}T"] = np.ascontiguousarray(ins[f"L3_ffn_{nm}"].transpose(0, 2, 1))
    return d


def kernel(**inputs):
    ins = {k: np.asarray(v) for k, v in inputs.items()}
    nc = _build_program()
    shared = _shared_inputs(ins)
    x = ins["L0_attn_x"]
    in_maps = []
    for core in range(8):
        m = dict(shared)
        m["xT"] = np.ascontiguousarray(x[2 * core:2 * core + 2].transpose(0, 2, 1))
        in_maps.append(m)
    res = run_bass_kernel_spmd(nc, in_maps, core_ids=list(range(8)))
    out = np.empty((16, 2048, 1024), np.float32)
    for core in range(8):
        out[2 * core:2 * core + 2] = res.results[core]["yT"].transpose(0, 2, 1)
    return out


# version 03
# speedup vs baseline: 1.4903x; 1.0197x over previous
import contextlib
import numpy as np
import concourse.bass as bass
import concourse.mybir as mybir

F32 = mybir.dt.float32
BF16 = mybir.dt.bfloat16
I32 = mybir.dt.int32
AF = mybir.ActivationFunctionType
ALU = mybir.AluOpType
AX = mybir.AxisListType

QUEUES = ("PE", "ACT", "DVE", "POOL", "SYNC")
_DT_SIZE = {F32: 4, BF16: 2, I32: 4, mybir.dt.uint8: 1, mybir.dt.int16: 2,
            mybir.dt.uint32: 4, mybir.dt.float16: 2, mybir.dt.uint16: 2}


class Access:
    __slots__ = ("p0", "p1", "b0", "b1", "deps")

    def __init__(self, p0, p1, b0, b1, deps):
        self.p0, self.p1, self.b0, self.b1 = p0, p1, b0, b1
        self.deps = deps


class Buf:
    def __init__(self, name, apobj, kind, dtype, shape):
        self.name = name
        self.apobj = apobj
        self.kind = kind
        self.dtype = dtype
        self.shape = shape
        self.writes = []
        self.reads = []
        self.stream_sem = None
        self.stream_count = 0
        self.stream_deps = None
        self.batch_open = False

    def tv(self):
        if self.kind == "dram":
            P = self.shape[0] if len(self.shape) > 1 else 1
            E = int(np.prod(self.shape[1:])) if len(self.shape) > 1 else self.shape[0]
            return TV(self, 0, P, 0, E)
        return TV(self, 0, self.shape[0], 0, int(np.prod(self.shape[1:])))


class TV:
    __slots__ = ("buf", "p0", "p1", "e0", "e1", "cast")

    def __init__(self, buf, p0, p1, e0, e1, cast=None):
        self.buf, self.p0, self.p1, self.e0, self.e1 = buf, p0, p1, e0, e1
        self.cast = cast

    @property
    def P(self):
        return self.p1 - self.p0

    @property
    def N(self):
        return self.e1 - self.e0

    def ap(self):
        h = self.buf.apobj
        if self.buf.kind == "dram":
            a = h
        elif self.cast is not None:
            a = h.bitcast(self.cast).ap()
        else:
            a = h.ap()
        return a[self.p0:self.p1, self.e0:self.e1]

    def bf16(self):
        assert self.cast is None and self.buf.dtype == F32
        return TV(self.buf, self.p0, self.p1, self.e0 * 2, self.e1 * 2, cast=BF16)

    def __getitem__(self, idx):
        ps, es = idx if isinstance(idx, tuple) else (idx, slice(None))
        if isinstance(ps, int):
            ps = slice(ps, ps + 1)
        if isinstance(es, int):
            es = slice(es, es + 1)
        p0, p1, _ = ps.indices(self.P)
        e0, e1, _ = es.indices(self.N)
        return TV(self.buf, self.p0 + p0, self.p0 + p1, self.e0 + e0,
                  self.e0 + e1, self.cast)

    def rect(self):
        if self.buf.kind != "sbuf":
            return (0, 128, 0, 1 << 40)
        s = _DT_SIZE[self.buf.dtype]
        return (self.p0, self.p1, self.e0 * s, self.e1 * s)


def _merge_into(dst: dict, deps: dict):
    for k, v in deps.items():
        if dst.get(k, -1) < v:
            dst[k] = v


def _prune(lst, maxlen=48):
    if len(lst) <= maxlen:
        return lst
    half = len(lst) // 2
    old, rest = lst[:half], lst[half:]
    p0 = min(a.p0 for a in old)
    p1 = max(a.p1 for a in old)
    b0 = min(a.b0 for a in old)
    b1 = max(a.b1 for a in old)
    deps = {}
    for a in old:
        _merge_into(deps, a.deps)
    return [Access(p0, p1, b0, b1, deps)] + rest


class OpRec:
    __slots__ = ("queue", "waits", "emit", "inc")

    def __init__(self, queue, waits, emit, inc):
        self.queue, self.waits, self.emit, self.inc = queue, waits, emit, inc


class _MMGroup:

    def __init__(self, ctx, psum_tv):
        self.ctx = ctx
        self.psum = psum_tv
        self.first = True
        self.last_val = None

    def add(self, emit_fn, reads):
        ctx = self.ctx
        deps = {}
        ctx._collect_deps(list(reads), [self.psum] if self.first else [], deps)
        waits = ctx._waits_for("PE", deps)
        ctx.qcount["PE"] += 1
        val = ctx.qcount["PE"]
        for tv in reads:
            p0, p1, b0, b1 = tv.rect()
            tv.buf.reads.append(Access(p0, p1, b0, b1, {"prog_PE": val}))
            tv.buf.reads = _prune(tv.buf.reads)
            tv.buf.batch_open = False
        ctx.ops.append(OpRec("PE", waits, emit_fn, ("prog_PE", 1)))
        self.first = False
        self.last_val = val

    def close(self):
        self.ctx._record_accesses([], [self.psum], ("prog_PE", self.last_val))


class Ctx:
    def __init__(self, nc, name="ctx"):
        self.nc = nc
        self.name = name
        self.ops = []
        self.qcount = {q: 0 for q in ("PE", "ACT", "DVE", "POOL")}
        self.clock = {q: {} for q in QUEUES}
        self.sem_names = ["prog_PE", "prog_ACT", "prog_DVE", "prog_POOL"]
        self.sems = {}
        self.bufs = {}
        self._cm = []
        self.n_waits = 0
        self.sem_pool = []

    def _new_sem(self, name):
        assert name not in self.sem_names, name
        self.sem_names.append(name)
        return name

    def sbuf(self, name, shape, dtype):
        cm = self.nc.sbuf_tensor(name, list(shape), dtype)
        t = cm.__enter__()
        self._cm.append((name, cm))
        b = Buf(name, t, "sbuf", dtype, tuple(shape))
        self.bufs[name] = b
        return b.tv()

    def psum(self, name, shape=(128, 512), dtype=F32):
        cm = self.nc.psum_tensor(name, list(shape), dtype)
        t = cm.__enter__()
        self._cm.append((name, cm))
        b = Buf(name, t, "psum", dtype, tuple(shape))
        self.bufs[name] = b
        return b.tv()

    def mark(self):
        return len(self._cm)

    def free_to(self, mark):
        self.barrier()
        while len(self._cm) > mark:
            name, cm = self._cm.pop()
            b = self.bufs.pop(name, None)
            if b is not None and b.stream_sem is not None:
                self.sem_pool.append((b.stream_sem, b.stream_count))
            cm.__exit__(None, None, None)

    def dram(self, name, shape, dtype, kind):
        t = self.nc.dram_tensor(name, list(shape), dtype, kind=kind)
        b = Buf(name, t.ap(), "dram", dtype, tuple(shape))
        self.bufs[name] = b
        return b

    def _collect_deps(self, reads, writes, deps):
        for tv in reads:
            p0, p1, b0, b1 = tv.rect()
            for a in tv.buf.writes:
                if not (a.p1 <= p0 or p1 <= a.p0 or a.b1 <= b0 or b1 <= a.b0):
                    _merge_into(deps, a.deps)
        for tv in writes:
            p0, p1, b0, b1 = tv.rect()
            for a in tv.buf.writes:
                if not (a.p1 <= p0 or p1 <= a.p0 or a.b1 <= b0 or b1 <= a.b0):
                    _merge_into(deps, a.deps)
            for a in tv.buf.reads:
                if not (a.p1 <= p0 or p1 <= a.p0 or a.b1 <= b0 or b1 <= a.b0):
                    _merge_into(deps, a.deps)

    def _waits_for(self, queue, deps):
        cl = self.clock[queue]
        waits = []
        for sem, val in deps.items():
            if cl.get(sem, 0) >= val:
                continue
            cl[sem] = val
            waits.append((sem, val))
        self.n_waits += len(waits)
        return waits

    def _record_accesses(self, reads, writes, dep, is_dma=False):
        depd = dep if isinstance(dep, dict) else {dep[0]: dep[1]}
        for tv in reads:
            p0, p1, b0, b1 = tv.rect()
            tv.buf.reads.append(Access(p0, p1, b0, b1, depd))
            tv.buf.reads = _prune(tv.buf.reads)
            tv.buf.batch_open = False
        for tv in writes:
            p0, p1, b0, b1 = tv.rect()
            tv.buf.writes = [a for a in tv.buf.writes
                             if not (p0 <= a.p0 and a.p1 <= p1 and b0 <= a.b0 and a.b1 <= b1)]
            tv.buf.reads = [a for a in tv.buf.reads
                            if not (p0 <= a.p0 and a.p1 <= p1 and b0 <= a.b0 and a.b1 <= b1)]
            tv.buf.writes.append(Access(p0, p1, b0, b1, depd))
            if not is_dma:
                tv.buf.batch_open = False

    def op(self, queue, emit, reads=(), writes=()):
        reads, writes = list(reads), list(writes)
        deps = {}
        self._collect_deps(reads, writes, deps)
        waits = self._waits_for(queue, deps)
        self.qcount[queue] += 1
        dep = (f"prog_{queue}", self.qcount[queue])
        self._record_accesses(reads, writes, dep)
        self.ops.append(OpRec(queue, waits, emit, (dep[0], 1)))

    def mm_group(self, psum_tv, items):
        deps = {}
        all_reads = []
        for _, rds in items:
            all_reads.extend(rds)
        self._collect_deps(all_reads, [psum_tv], deps)
        waits = self._waits_for("PE", deps)
        self.qcount["PE"] += 1
        dep = ("prog_PE", self.qcount["PE"])
        self._record_accesses(all_reads, [psum_tv], dep)
        n = len(items)
        for i, (fn, _) in enumerate(items):
            self.ops.append(OpRec("PE", waits if i == 0 else [],
                                  fn, (dep[0], 1) if i == n - 1 else None))

    def mm_seq(self, psums, seq):
        deps = {}
        all_reads = []
        for _, rds, _, _ in seq:
            all_reads.extend(rds)
        self._collect_deps(all_reads, list(psums), deps)
        waits = self._waits_for("PE", deps)
        incs = {}
        for fn, rds, pi, last in seq:
            if last:
                self.qcount["PE"] += 1
                incs[pi] = self.qcount["PE"]
        for tv in all_reads:
            p0, p1, b0, b1 = tv.rect()
            tv.buf.reads.append(Access(p0, p1, b0, b1, {"prog_PE": self.qcount["PE"]}))
            tv.buf.reads = _prune(tv.buf.reads)
            tv.buf.batch_open = False
        for pi, tv in enumerate(psums):
            self._record_accesses([], [tv], ("prog_PE", incs[pi]))
        first = True
        for fn, rds, pi, last in seq:
            self.ops.append(OpRec("PE", waits if first else [], fn,
                                  ("prog_PE", 1) if last else None))
            first = False

    def mm_open(self, psum_tv):
        return _MMGroup(self, psum_tv)

    def dma(self, queue, dst_tv, src_tv, emit=None):
        db = dst_tv.buf
        if db.stream_sem is None:
            if self.sem_pool:
                db.stream_sem, db.stream_count = self.sem_pool.pop()
            else:
                db.stream_sem = self._new_sem(f"dma_{db.name}")
            db.stream_deps = {}
        deps = {}
        self._collect_deps([src_tv], [dst_tv], deps)
        if not db.batch_open:
            if db.stream_count:
                deps[db.stream_sem] = max(deps.get(db.stream_sem, 0), db.stream_count)
            db.batch_open = True
        waits = self._waits_for(queue, deps)
        db.stream_count += 16
        db.stream_deps[db.stream_sem] = db.stream_count
        self._record_accesses([src_tv], [dst_tv], db.stream_deps, is_dma=True)
        if emit is None:
            d, s = dst_tv, src_tv

            def emit(eng):
                return eng.dma_start(out=d.ap(), in_=s.ap())
        self.ops.append(OpRec(queue, waits, emit, (db.stream_sem, 16)))

    def barrier(self):
        deps = {}
        for q in ("PE", "ACT", "DVE", "POOL"):
            if self.qcount[q]:
                deps[f"prog_{q}"] = self.qcount[q]
        for b in self.bufs.values():
            if b.stream_sem is not None:
                deps[b.stream_sem] = b.stream_count
        for q in QUEUES:
            waits = self._waits_for(q, dict(deps))
            if waits:
                self.ops.append(OpRec(q, waits, None, None))

    def final_wait(self, queue, bufs):
        waits = []
        for b in bufs:
            if b.stream_sem is not None:
                waits.append((b.stream_sem, b.stream_count))
        self.ops.append(OpRec(queue, waits, None, None))

    def build(self):
        st = contextlib.ExitStack()
        for name in self.sem_names:
            self.sems[name] = st.enter_context(self.nc.semaphore(name))
        per_q = {q: [] for q in QUEUES}
        for rec in self.ops:
            per_q[rec.queue].append(rec)
        sec_name = {"PE": "tensor", "ACT": "scalar", "DVE": "vector",
                    "POOL": "gpsimd", "SYNC": "sync"}
        with self.nc.Block(self.name) as block:
            for q in QUEUES:
                recs = per_q[q]
                if not recs:
                    continue

                def body(e, recs=recs):
                    for rec in recs:
                        for sem, val in rec.waits:
                            e.wait_ge(self.sems[sem], val)
                        if rec.emit is not None:
                            inst = rec.emit(e)
                            if rec.inc is not None:
                                inst.then_inc(self.sems[rec.inc[0]], rec.inc[1])

                getattr(block, sec_name[q])(body)
        st.close()

    def close(self):
        for _, cm in reversed(self._cm):
            cm.__exit__(None, None, None)


"""Common emission helpers on top of fw.Ctx for the mk_seq_preln_d4 kernel.

Conventions:
 - feature-major activation: list of d-tile TVs [128, T] (tile j = features j*128..)
 - weights host-pre-transposed where needed; streamed fp32->bf16 via POOL cast DMA.
 - LN: xm = x - mean (bf16, 1 DVE pass per tile); rstd (rb, [128,T] fp32 broadcast)
   folded into projection evacuations via scalar_tensor_tensor.
"""

PI = float(np.pi)


def f32(v):
    return np.float32(v)



def ts(ctx, q, out, in0, scalar1, op0, scalar2=None, op1=None, accum_out=None):
    s1 = scalar1.ap() if isinstance(scalar1, TV) else scalar1
    s2 = scalar2.ap() if isinstance(scalar2, TV) else scalar2
    reads = [in0] + [s for s in (scalar1, scalar2) if isinstance(s, TV)]
    writes = [out]
    if accum_out is not None:
        writes = writes + [accum_out]

    def emit(e, out=out, in0=in0, s1=s1, s2=s2, op0=op0, op1=op1, ao=accum_out):
        kw = {}
        if op1 is not None:
            kw["op1"] = op1
        if ao is not None:
            kw["accum_out"] = ao.ap()
        return e.tensor_scalar(out=out.ap(), in0=in0.ap(), scalar1=s1, scalar2=s2,
                               op0=op0, **kw)
    ctx.op(q, emit, reads=reads, writes=writes)


def tt(ctx, q, out, a, b, op):
    ctx.op(q, lambda e, out=out, a=a, b=b, op=op: e.tensor_tensor(
        out=out.ap(), in0=a.ap(), in1=b.ap(), op=op), reads=[a, b], writes=[out])


def stt(ctx, out, in0, scalar, op0, in1, op1, accum_out=None):
    s = scalar.ap() if isinstance(scalar, TV) else (0.0 if scalar is None else scalar)
    reads = [in0, in1] + ([scalar] if isinstance(scalar, TV) else [])
    writes = [out] + ([accum_out] if accum_out is not None else [])

    def emit(e, out=out, in0=in0, s=s, in1=in1, op0=op0, op1=op1, ao=accum_out):
        kw = {"accum_out": ao.ap()} if ao is not None else {}
        return e.scalar_tensor_tensor(out=out.ap(), in0=in0.ap(), scalar=s,
                                      in1=in1.ap(), op0=op0, op1=op1, **kw)
    ctx.op("DVE", emit, reads=reads, writes=writes)


def act(ctx, out, in_, func, bias=0.0, scale=1.0, accum_out=None):
    b = bias.ap() if isinstance(bias, TV) else f32(bias)
    s = scale.ap() if isinstance(scale, TV) else f32(scale)
    reads = [in_] + [v for v in (bias, scale) if isinstance(v, TV)]
    writes = [out] + ([accum_out] if accum_out is not None else [])

    def emit(e, out=out, in_=in_, func=func, b=b, s=s, ao=accum_out):
        kw = {"accum_out": ao.ap()} if ao is not None else {}
        return e.activation(out=out.ap(), in_=in_.ap(), func=func, bias=b, scale=s, **kw)
    ctx.op("ACT", emit, reads=reads, writes=writes)


def copy(ctx, q, out, in_):
    if q == "ACT":
        ctx.op(q, lambda e, o=out, i=in_: e.copy(o.ap(), i.ap()), reads=[in_], writes=[out])
    else:
        ctx.op(q, lambda e, o=out, i=in_: e.tensor_copy(o.ap(), i.ap()),
               reads=[in_], writes=[out])


def memset(ctx, q, out, val):
    ctx.op(q, lambda e, o=out, v=val: e.memset(o.ap(), v), writes=[out])


def reduce(ctx, q, out, in_, op, negate=False):
    ctx.op(q, lambda e, o=out, i=in_, op=op, n=negate: e.tensor_reduce(
        out=o.ap(), in_=i.ap(), axis=AX.X, op=op, negate=n), reads=[in_], writes=[out])


def recip(ctx, out, in_):
    ctx.op("DVE", lambda e, o=out, i=in_: e.reciprocal(o.ap(), i.ap()),
           reads=[in_], writes=[out])


def transpose(ctx, psum_out, in_, ident):
    ctx.mm_group(psum_out, [(lambda e, o=psum_out, i=in_, id=ident: e.transpose(
        o.ap(), i.ap(), id.ap()), [in_, ident])])


def _affsel_emit(e, o, i, p, c, f, b, m):
    oa = o.ap()
    ia = i.ap()
    return e.affine_select(out=oa, in_=ia, pattern=p, compare_op=c, fill=f,
                           base=b, channel_multiplier=m)


def affsel(ctx, out, in_, pattern, cmp, fill, base, chmul):
    ctx.op("POOL", lambda e, o=out, i=in_, p=pattern, c=cmp, f=fill, b=base, m=chmul:
           _affsel_emit(e, o, i, p, c, f, b, m),
           reads=[in_], writes=[out])


def mm_acc(ctx, psum_tv, pairs):
    items = []
    n = len(pairs)
    for i, (lt, rt) in enumerate(pairs):
        st, sp = i == 0, i == n - 1
        items.append((lambda e, lt=lt, rt=rt, st=st, sp=sp, o=psum_tv: e.matmul(
            o.ap(), lt.ap(), rt.ap(), start=st, stop=sp), [lt, rt]))
    ctx.mm_group(psum_tv, items)



def load_full_bf16(ctx, name, dram_buf, R, C, q="POOL"):
    nk = (R + 127) // 128
    sb = ctx.sbuf(name, [128, nk * C], BF16)
    d = dram_buf.apobj
    tvs = []
    for ch in range(nk):
        r0, r1 = ch * 128, min(R, (ch + 1) * 128)
        tv = sb[:r1 - r0, ch * C:(ch + 1) * C]

        def emit(e, tv=tv, d=d, r0=r0, r1=r1):
            return e.dma_start(out=tv.ap(), in_=d[r0:r1, :])
        ctx.dma(q, tv, dram_buf.tv(), emit=emit)
        tvs.append(tv)
    return tvs


class WSlots:

    def __init__(self, ctx, name, nslots, npart, ncols):
        self.ctx = ctx
        self.npart, self.ncols = npart, ncols
        self.slots = [ctx.sbuf(f"{name}_s{i}", [npart, ncols], BF16) for i in range(nslots)]
        self.i = 0

    def load(self, dram_buf, r0, nrows, c0, ncols, pre=None):
        tv = self.slots[self.i][:nrows, :ncols]
        self.i = (self.i + 1) % len(self.slots)
        d = dram_buf.apobj

        def emit(e, tv=tv, d=d, r0=r0, nrows=nrows, c0=c0, ncols=ncols, pre=pre):
            src = d[pre] if pre is not None else d
            return e.dma_start(out=tv.ap(), in_=src[r0:r0 + nrows, c0:c0 + ncols])
        self.ctx.dma("POOL", tv, dram_buf.tv(), emit=emit)
        return tv


def load_col_fp32(ctx, name, dram_buf, R, ln, ps, q="SYNC"):
    nk = (R + 127) // 128
    stage = ctx.sbuf(name + "_st", [nk, 128], F32)
    sb = ctx.sbuf(name, [128, nk], F32)
    d = dram_buf.apobj

    def emit(e, stage=stage, d=d, nk=nk):
        return e.dma_start(out=stage.ap(), in_=d.rearrange("(k p) -> k p", p=128))
    ctx.dma(q, stage, dram_buf.tv(), emit=emit)
    transpose(ctx, ps[:, 0:nk], stage, ln.ident[0:nk, 0:nk])
    copy(ctx, "ACT", sb, ps[:, 0:nk])
    return sb



class LNScratch:
    def __init__(self, ctx, T):
        self.T = T
        self.stat = ctx.sbuf("ln_stat", [1, 1024], F32)
        self.rb = ctx.sbuf("ln_rb", [128, T], F32)
        self.xsq = ctx.sbuf("ln_xsq", [128, 2 * 512], BF16)
        self.ones_k = ctx.sbuf("ln_ones_k", [128, 1], BF16)
        self.ones_1 = ctx.sbuf("ln_ones_1", [1, 128], F32)
        self.ident = ctx.sbuf("g_ident", [128, 128], F32)
        self.identb = ctx.sbuf("g_identb", [128, 128], BF16)
        self.cmask = ctx.sbuf("g_cmask", [128, 128], F32)
        memset(ctx, "POOL", self.cmask, 0.0)
        affsel(ctx, self.cmask, self.cmask, [[-1, 128]], ALU.is_ge, -1e30, 0, 1)
        memset(ctx, "DVE", self.ones_k, 1.0)
        memset(ctx, "DVE", self.ones_1, 1.0)
        memset(ctx, "POOL", self.ident, 1.0)
        affsel(ctx, self.ident, self.ident, [[-1, 128]], ALU.is_equal, 0.0, 0, 1)
        copy(ctx, "DVE", self.identb, self.ident)


def emit_ln(ctx, xs, tok0, T, ln, ps_a, ps_b, xm, mode="ln", pre_col=None, eps=1e-5):
    nd = len(xs)
    D = nd * 128
    for c in range((T + 511) // 512):
        c0, c1 = c * 512, min(T, (c + 1) * 512)
        W = c1 - c0
        mean = ln.stat[:, 0:W]
        sd = ln.stat[:, 512:512 + W]
        g2 = ctx.mm_open(ps_b[0:1, 0:W])
        for j in range(nd):
            xc = xs[j][:, tok0 + c0:tok0 + c1]
            xq = ln.xsq[:, (j % 2) * 512:(j % 2) * 512 + W]
            if mode == "rms":
                if pre_col is not None:
                    ts(ctx, "DVE", xm[j][:, c0:c1], xc, pre_col[j], ALU.mult)
                else:
                    copy(ctx, "DVE", xm[j][:, c0:c1], xc)
                act(ctx, xq, xc, AF.Square)
            else:
                copy(ctx, "DVE", xm[j][:, c0:c1], xc)
                act(ctx, xq, xm[j][:, c0:c1], AF.Square)
            g2.add(lambda e, o=ps_b[0:1, 0:W], l=ln.ones_k, r=xq, st=(j == 0), sp=(j == nd - 1):
                   e.matmul(o.ap(), l.ap(), r.ap(), start=st, stop=sp), [ln.ones_k, xq])
        g2.close()
        ts(ctx, "DVE", sd, ps_b[0:1, 0:W], 1.0 / D, ALU.mult)
        if mode == "ln":
            mm_acc(ctx, ps_a[0:1, 0:W], [(ln.ones_k, xm[j][:, c0:c1]) for j in range(nd)])
            ts(ctx, "DVE", mean, ps_a[0:1, 0:W], 1.0 / D, ALU.mult)
            mm_acc(ctx, ps_a[:, 0:W], [(ln.ones_1, mean)])
            for j in range(nd):
                xc = xs[j][:, tok0 + c0:tok0 + c1]
                stt(ctx, xm[j][:, c0:c1], xc, None, ALU.bypass,
                    ps_a[:, 0:W], ALU.subtract)
            tt(ctx, "DVE", mean, mean, mean, ALU.mult)
            tt(ctx, "DVE", sd, sd, mean, ALU.subtract)
        act(ctx, sd, sd, AF.Sqrt, bias=eps)
        recip(ctx, sd, sd)
        mm_acc(ctx, ps_b[:, 0:W], [(ln.ones_1, sd)])
        copy(ctx, "ACT", ln.rb[:, c0:c1], ps_b[:, 0:W])


def emit_ln_inplace(ctx, xs, ln, ps_a, ps_b, T, eps=1e-5):
    nd = len(xs)
    D = nd * 128
    for c in range((T + 511) // 512):
        c0, c1 = c * 512, min(T, (c + 1) * 512)
        W = c1 - c0
        mean = ln.stat[:, 0:W]
        sd = ln.stat[:, 512:512 + W]
        g1 = ctx.mm_open(ps_a[0:1, 0:W])
        g2 = ctx.mm_open(ps_b[0:1, 0:W])
        for j in range(nd):
            xc = xs[j][:, c0:c1]
            xb = ln.xsq[:, (j % 2) * 512:(j % 2) * 512 + W]
            copy(ctx, "DVE", xb, xc)
            g1.add(lambda e, o=ps_a[0:1, 0:W], l=ln.ones_k, r=xb, st=(j == 0),
                   sp=(j == nd - 1): e.matmul(o.ap(), l.ap(), r.ap(), start=st, stop=sp),
                   [ln.ones_k, xb])
            act(ctx, xb, xb, AF.Square)
            g2.add(lambda e, o=ps_b[0:1, 0:W], l=ln.ones_k, r=xb, st=(j == 0),
                   sp=(j == nd - 1): e.matmul(o.ap(), l.ap(), r.ap(), start=st, stop=sp),
                   [ln.ones_k, xb])
        g1.close()
        g2.close()
        ts(ctx, "DVE", mean, ps_a[0:1, 0:W], 1.0 / D, ALU.mult)
        ts(ctx, "DVE", sd, ps_b[0:1, 0:W], 1.0 / D, ALU.mult)
        mm_acc(ctx, ps_a[:, 0:W], [(ln.ones_1, mean)])
        for j in range(nd):
            stt(ctx, xs[j][:, c0:c1], xs[j][:, c0:c1], None, ALU.bypass,
                ps_a[:, 0:W], ALU.subtract)
        tt(ctx, "DVE", mean, mean, mean, ALU.mult)
        tt(ctx, "DVE", sd, sd, mean, ALU.subtract)
        act(ctx, sd, sd, AF.Sqrt, bias=eps)
        recip(ctx, sd, sd)
        mm_acc(ctx, ps_b[:, 0:W], [(ln.ones_1, sd)])
        copy(ctx, "ACT", ln.rb[:, c0:c1], ps_b[:, 0:W])
        for j in range(nd):
            tt(ctx, "DVE", xs[j][:, c0:c1], xs[j][:, c0:c1], ln.rb[:, c0:c1], ALU.mult)



def proj_feat(ctx, out_tiles, wT_tiles, xm, rb, ps_banks, T, n_out,
              bias_col=None, func=None, out_chunks=512, extra=None):
    nf = n_out // 128
    nk = len(xm)
    bi = 0
    for f in range(nf):
        for c0 in range(0, T, out_chunks):
            c1 = min(T, c0 + out_chunks)
            ps = ps_banks[bi % len(ps_banks)][:, 0:c1 - c0]
            bi += 1
            mm_acc(ctx, ps, [(wT_tiles[j][:, f * 128:(f + 1) * 128], xm[j][:, c0:c1])
                             for j in range(nk)])
            dst = out_tiles[f][:, c0:c1]
            if rb is not None:
                stt(ctx, dst, ps, None, ALU.bypass, rb[:, c0:c1], ALU.mult)
            else:
                copy(ctx, "DVE", dst, ps)
            if func is not None:
                act(ctx, dst, dst, func,
                    bias=bias_col[:, f:f + 1] if bias_col is not None else 0.0)
            elif bias_col is not None:
                ts(ctx, "DVE", dst, dst, bias_col[:, f:f + 1], ALU.add)
            if extra is not None:
                extra(ctx, f, c0, c1, dst)



def softmax_rows(ctx, s_tv, p_out, scale, tmp_col, do_recip_mult=True):
    mx = tmp_col[:, 0:1]
    bias = tmp_col[:, 1:2]
    sm = tmp_col[:, 2:3]
    reduce(ctx, "DVE", mx, s_tv, ALU.max)
    ts(ctx, "DVE", bias, mx, -scale, ALU.mult)
    act(ctx, p_out, s_tv, AF.Exp, bias=bias, scale=scale, accum_out=sm)
    recip(ctx, sm, sm)
    if do_recip_mult:
        ts(ctx, "DVE", p_out, p_out, sm, ALU.mult)
    return sm


"""Block emitters for mk_seq_preln_d4. Each operates on one batch (T=2048 tokens)
of the feature-major fp32 residual stream xs (8 TVs [128, 2048])."""

T = 2048


class L0AttnScratch:
    def __init__(self, ctx, tag):
        self.xm = [ctx.sbuf(f"l0a_xm{j}_{tag}", [128, T], BF16) for j in range(8)]
        self.V = [ctx.sbuf(f"l0a_v{i}_{tag}", [128, 1024], BF16) for i in range(16)]
        self.A = ctx.sbuf(f"l0a_A_{tag}", [64, 8 * 256], BF16)
        self.B = ctx.sbuf(f"l0a_B_{tag}", [64, 8 * 256], BF16)
        self.s = ctx.sbuf(f"l0a_s_{tag}", [128, T], F32)
        self.p = ctx.sbuf(f"l0a_p_{tag}", [128, T], BF16)
        self.rtok = ctx.sbuf(f"l0a_rtok_{tag}", [128, 16], F32)
        self.col = ctx.sbuf(f"l0a_col_{tag}", [128, 8], F32)


def emit_l0_attn(ctx, xs, w, ln, ps, tag):
    sc = L0AttnScratch(ctx, tag)
    emit_ln(ctx, xs, 0, T, ln, ps[0], ps[1], sc.xm, mode="ln")
    for i in range(16):
        transpose(ctx, ps[2][:, 0:1], ln.rb[0:1, i * 128:(i + 1) * 128], ln.ones_1[0:1, 0:1])
        copy(ctx, "ACT", sc.rtok[:, i:i + 1], ps[2][:, 0:1])
    mq = load_full_bf16(ctx, f"l0_mq_{tag}", w["Mq"], 1024, 512)
    mk = load_full_bf16(ctx, f"l0_mk_{tag}", w["Mk"], 1024, 512)
    mv = load_full_bf16(ctx, f"l0_mv_{tag}", w["Mv"], 1024, 1024)
    for i in range(16):
        for n in range(2):
            pp = ps[2 + (i * 2 + n) % 2][:, 0:512]
            mm_acc(ctx, pp, [(sc.xm[j][:, i * 128:(i + 1) * 128], mv[j][:, n * 512:(n + 1) * 512])
                             for j in range(8)])
            ts(ctx, "DVE", sc.V[i][:, n * 512:(n + 1) * 512], pp, sc.rtok[:, i:i + 1], ALU.mult)
    for m in range(8):
        t0 = m * 256
        for (dst, wt) in ((sc.A, mq), (sc.B, mk)):
            for jb in range(8):
                pp = ps[2 + jb % 2][0:64, 0:256]
                mm_acc(ctx, pp, [(wt[j][:, jb * 64:(jb + 1) * 64],
                                  sc.xm[j][:, t0:t0 + 256]) for j in range(8)])
                d = dst[:, jb * 256:(jb + 1) * 256]
                stt(ctx, d, pp, None, ALU.bypass, ln.rb[0:64, t0:t0 + 256], ALU.mult)
        outg = [ctx.mm_open(ps[4 + gi]) for gi in range(4)]
        started = [False] * 4
        for jp in range(8):
            for tc in range(2):
                for jj in range(4):
                    pp = ps[2 + jj % 2]
                    g = ctx.mm_open(pp)
                    for j2 in range(2):
                        j = jj * 2 + j2
                        lt = sc.B[:, jp * 256 + tc * 128:jp * 256 + (tc + 1) * 128]
                        rt = sc.A[:, j * 256:(j + 1) * 256]
                        g.add(lambda e, o=pp, l=lt, r=rt, st=(j2 == 0), sp=(j2 == 1), j2=j2:
                              e.matmul(o[:, j2 * 256:(j2 + 1) * 256].ap(), l.ap(), r.ap(),
                                       start=st, stop=sp), [lt, rt])
                    g.close()
                    copy(ctx, "DVE", sc.s[:, jj * 512:(jj + 1) * 512], pp[:, 0:512])
                softmax_rows(ctx, sc.s, sc.p, 1.0 / 8.0, sc.col[:, 0:4])
                for hp in range(8):
                    lt = sc.V[hp * 2 + tc][:, jp * 128:(jp + 1) * 128]
                    rt = sc.p[:, hp * 256:(hp + 1) * 256]
                    gi = hp // 2
                    st = not started[gi]
                    outg[gi].add(lambda e, o=ps[4 + gi], l=lt, r=rt, st=st, c0=(hp % 2) * 256:
                                 e.matmul(o[:, c0:c0 + 256].ap(), l.ap(), r.ap(),
                                          start=st, stop=False), [lt, rt])
                    started[gi] = True
        for gi in range(4):
            outg[gi].close()
        for hp in range(8):
            xsl = xs[hp][:, t0:t0 + 256]
            stt(ctx, xsl, ps[4 + hp // 2][:, (hp % 2) * 256:(hp % 2) * 256 + 256],
                None, ALU.bypass, xsl, ALU.add)


def emit_l0_ffn(ctx, xs, w, ln, ps, tag):
    TWO_PI = 2 * PI
    xm = [ctx.sbuf(f"l0f_xm{j}_{tag}", [128, T], BF16) for j in range(8)]
    emit_ln(ctx, xs, 0, T, ln, ps[0], ps[1], xm, mode="ln")
    b1c = load_col_fp32(ctx, f"l0f_b1_{tag}", w["b1"], 4096, ln, ps[0])
    b2c = load_col_fp32(ctx, f"l0f_b2_{tag}", w["b2"], 1024, ln, ps[1])
    thc = load_col_fp32(ctx, f"l0f_th_{tag}", w["qp"], 4096, ln, ps[2])
    wrc = load_col_fp32(ctx, f"l0f_wr_{tag}", w["qwr"], 4096, ln, ps[3])
    wic = load_col_fp32(ctx, f"l0f_wi_{tag}", w["qwi"], 4096, ln, ps[4])
    cc = ctx.sbuf(f"l0f_cc_{tag}", [128, 6 * 32], F32)
    sth, cth, c1c = cc[:, 0:32], cc[:, 32:64], cc[:, 64:96]
    b110, b110p, tmpc = cc[:, 96:128], cc[:, 128:160], cc[:, 160:192]
    copy(ctx, "DVE", tmpc, thc)
    for _ in range(8):
        ts(ctx, "DVE", sth, tmpc, PI, ALU.is_gt)
        stt(ctx, tmpc, sth, -TWO_PI, ALU.mult, tmpc, ALU.add)
        ts(ctx, "DVE", sth, tmpc, -PI, ALU.is_lt)
        stt(ctx, tmpc, sth, TWO_PI, ALU.mult, tmpc, ALU.add)
    ts(ctx, "DVE", cth, tmpc, PI / 2, ALU.add)
    ts(ctx, "DVE", sth, cth, PI, ALU.is_gt)
    stt(ctx, cth, sth, -TWO_PI, ALU.mult, cth, ALU.add)
    act(ctx, cth, cth, AF.Sin)
    act(ctx, sth, tmpc, AF.Sin)
    tt(ctx, "DVE", c1c, cth, wrc, ALU.mult)
    tt(ctx, "DVE", tmpc, sth, wic, ALU.mult)
    tt(ctx, "DVE", c1c, c1c, tmpc, ALU.add)
    tt(ctx, "DVE", c1c, c1c, wic, ALU.mult)
    ts(ctx, "DVE", c1c, c1c, 0.1, ALU.mult)
    ts(ctx, "DVE", b110, b1c, 0.1, ALU.mult)
    ts(ctx, "DVE", b110p, b110, PI / 2, ALU.add)
    w1s = [ctx.sbuf(f"l0f_w1s{i}_{tag}", [128, 1024], BF16) for i in range(3)]
    w2s = [ctx.sbuf(f"l0f_w2s{i}_{tag}", [128, 1024], BF16) for i in range(3)]
    hsl = [ctx.sbuf(f"l0f_h{i}_{tag}", [128, 512], F32) for i in range(2)]
    ssl = [ctx.sbuf(f"l0f_s{i}_{tag}", [128, 512], F32) for i in range(2)]
    csl = [ctx.sbuf(f"l0f_c{i}_{tag}", [128, 512], F32) for i in range(2)]
    asb = [ctx.sbuf(f"l0f_a{i}_{tag}", [128, 512], BF16) for i in range(32)]
    d1 = w["W1"].apobj
    for c in range(4):
        t0 = c * 512
        rbch = ln.rb[:, t0:t0 + 512]
        for f in range(32):
            slot = w1s[f % 3]
            def em(e, slot=slot, d1=d1, f=f):
                return e.dma_start(out=slot.ap(),
                                   in_=d1[:, f * 128:(f + 1) * 128].rearrange(
                                       "(k p) c -> p k c", p=128))
            ctx.dma("POOL", slot, w["W1"].tv(), emit=em)
            pp = ps[f % 2]
            mm_acc(ctx, pp, [(slot[:, j * 128:(j + 1) * 128], xm[j][:, t0:t0 + 512])
                             for j in range(8)])
            h0 = hsl[f % 2]
            sg = ssl[f % 2]
            cg = csl[f % 2]
            fc = slice(f, f + 1)
            stt(ctx, h0, pp, None, ALU.bypass, rbch, ALU.mult)
            act(ctx, sg, h0, AF.Sin, bias=b110[:, fc], scale=0.1)
            act(ctx, cg, h0, AF.Sin, bias=b110p[:, fc], scale=0.1)
            ts(ctx, "DVE", sg, sg, cth[:, fc], ALU.mult)
            stt(ctx, sg, cg, sth[:, fc], ALU.mult, sg, ALU.add)
            stt(ctx, h0, sg, c1c[:, fc], ALU.mult, h0, ALU.add)
            act(ctx, asb[f], h0, AF.Relu, bias=b1c[:, fc])
        groups = [ctx.mm_open(ps[fo]) for fo in range(8)]
        d2 = w["W2"].apobj
        for k in range(32):
            slot = w2s[k % 3]
            def em2(e, slot=slot, d2=d2, k=k):
                return e.dma_start(out=slot.ap(), in_=d2[k * 128:(k + 1) * 128, :])
            ctx.dma("POOL", slot, w["W2"].tv(), emit=em2)
            for fo in range(8):
                lt = slot[:, fo * 128:(fo + 1) * 128]
                rt = asb[k]
                groups[fo].add(lambda e, o=ps[fo], l=lt, r=rt, st=(k == 0):
                               e.matmul(o.ap(), l.ap(), r.ap(), start=st, stop=False),
                               [lt, rt])
        for fo in range(8):
            groups[fo].close()
            xsl = xs[fo][:, t0:t0 + 512]
            stt(ctx, xsl, ps[fo], b2c[:, fo:fo + 1], ALU.add, xsl, ALU.add)


def tr_mm(ctx, g, out_tv, in_tv, ident_tv, st, sp):
    g.add(lambda e, o=out_tv, i=in_tv, id=ident_tv, st=st, sp=sp:
          e.matmul(o.ap(), i.ap(), id.ap(), start=st, stop=sp, is_transpose=True),
          [in_tv, ident_tv])


def emit_l1_attn(ctx, xs, w, ln, ps, tag, dbg=None):
    H = 16
    xm = [ctx.sbuf(f"l1a_xm{j}_{tag}", [128, T], BF16) for j in range(8)]
    emit_ln(ctx, xs, 0, T, ln, ps[0], ps[1], xm, mode="ln")
    rtok = ctx.sbuf(f"l1a_rtok_{tag}", [128, 16], F32)
    for i in range(16):
        transpose(ctx, ps[2][:, 0:1], ln.rb[0:1, i * 128:(i + 1) * 128], ln.ones_1[0:1, 0:1])
        copy(ctx, "ACT", rtok[:, i:i + 1], ps[2][:, 0:1])
    mval = ctx.sbuf(f"l1a_mval_{tag}", [1, 4], F32)
    mcol = ctx.sbuf(f"l1a_mcol_{tag}", [128, 8], F32)
    c_1m, c_mqs = mcol[:, 0:1], mcol[:, 1:2]
    maxc, sumc, biasc = mcol[:, 2:3], mcol[:, 3:4], mcol[:, 4:5]
    ctx.dma("SYNC", mval[:, 0:1], w["mixture"].tv())
    ctx.dma("SYNC", mval[:, 1:2], w["qs"].tv())
    act(ctx, mval[:, 2:3], mval[:, 0:1], AF.Sigmoid)
    tt(ctx, "DVE", mval[:, 3:4], mval[:, 2:3], mval[:, 1:2], ALU.mult)
    ts(ctx, "DVE", mval[:, 2:3], mval[:, 2:3], -1.0, ALU.mult, 1.0, ALU.add)
    mm_acc(ctx, ps[2][:, 0:2], [(ln.ones_1, mval[:, 2:4])])
    copy(ctx, "ACT", mcol[:, 0:2], ps[2][:, 0:2])
    ao = [ctx.sbuf(f"l1a_ao{j}_{tag}", [128, T], BF16) for j in range(8)]
    pt = ctx.sbuf(f"l1a_pt_{tag}", [128, 16 * 512], BF16)
    p = ctx.sbuf(f"l1a_p_{tag}", [128, T], BF16)
    u = ctx.sbuf(f"l1a_u_{tag}", [128, T], BF16)
    tq = ctx.sbuf(f"l1a_tq_{tag}", [128, T], F32)
    mk0 = ctx.mark()
    qh = ctx.sbuf(f"l1a_qh_{tag}", [64, T], BF16)
    kh = ctx.sbuf(f"l1a_kh_{tag}", [64, T], BF16)
    q2h = ctx.sbuf(f"l1a_q2h_{tag}", [64, T], BF16)
    k2h = ctx.sbuf(f"l1a_k2h_{tag}", [64, T], BF16)
    vtok = ctx.sbuf(f"l1a_vtok_{tag}", [128, 16 * 64], BF16)
    ktok = ctx.sbuf(f"l1a_ktok_{tag}", [128, 16 * 64], BF16)
    k2tok = ctx.sbuf(f"l1a_k2tok_{tag}", [128, 16 * 64], BF16)
    gsb = ctx.sbuf(f"l1a_g_{tag}", [64, 128], BF16)
    hq = ctx.sbuf(f"l1a_hq_{tag}", [64, 512], BF16)
    kcol = ctx.sbuf(f"l1a_kcol_{tag}", [64, 4], F32)
    kcolb = ctx.sbuf(f"l1a_kcolb_{tag}", [64, 2], BF16)
    stg = ctx.sbuf(f"l1a_stg_{tag}", [1, 512], F32)
    cols = ctx.sbuf(f"l1a_cols_{tag}", [128, 8 * 16], F32)
    nmu1, ss1 = cols[:, 0:16], cols[:, 16:32]
    nmu2, ss2 = cols[:, 32:48], cols[:, 48:64]
    rs1, a2c = cols[:, 64:80], cols[:, 80:96]
    b2c, ctmp = cols[:, 96:112], cols[:, 112:128]
    wsl = [ctx.sbuf(f"l1a_ws{i}_{tag}", [128, 512], BF16) for i in range(5)]
    SC8 = 1.0 / 2048.0

    def stat_rows(qsrc, gview, kb, nmucols, sscols):
        for half, dst in ((0, nmucols), (1, sscols)):
            gcol = ctx.mm_open(ps[5][:, 0:16])
            first = True
            for c in range(4):
                if half == 0:
                    mm_acc(ctx, ps[4][0:1, 0:512], [(kb, qsrc[:, c * 512:(c + 1) * 512])])
                    ts(ctx, "DVE", stg, ps[4][0:1, 0:512], -SC8, ALU.mult)
                else:
                    mm_acc(ctx, ps[4][:64, 0:512], [(gview, qsrc[:, c * 512:(c + 1) * 512])])
                    tt(ctx, "DVE", hq, ps[4][:64, 0:512], qsrc[:, c * 512:(c + 1) * 512], ALU.mult)
                    mm_acc(ctx, ps[3][0:1, 0:512], [(ln.ones_k[0:64], hq)])
                    ts(ctx, "DVE", stg, ps[3][0:1, 0:512], 1.0 / 64.0, ALU.mult)
                for i in range(4):
                    tr_mm(ctx, gcol, ps[5][:, c * 4 + i:c * 4 + i + 1],
                          stg[:, i * 128:(i + 1) * 128], ln.ones_1[0:1, 0:1],
                          st=first, sp=(c == 3 and i == 3))
                    first = False
            gcol.close()
            copy(ctx, "ACT", dst, ps[5][:, 0:16])

    def rsig_from(nmu, ss, dst):
        tt(ctx, "DVE", ctmp, nmu, nmu, ALU.mult)
        stt(ctx, dst, ctmp, -32.0, ALU.mult, ss, ALU.add)
        ts(ctx, "DVE", dst, dst, 1.0 / 2047.0, ALU.mult)
        act(ctx, dst, dst, AF.Sqrt)
        ts(ctx, "DVE", dst, dst, 1e-5, ALU.add)
        recip(ctx, dst, dst)

    for h in range(H):
        hb = (h % 2) * 64
        for i, nm in enumerate(("WqT", "WkT", "Wq2T", "Wk2T", "WvT")):
            d = w[nm].apobj

            def em(e, s=wsl[i], d=d, h=h):
                return e.dma_start(out=s.ap(), in_=d[:, h * 64:(h + 1) * 64].rearrange(
                    "(k p) c -> p k c", p=128))
            ctx.dma("POOL", wsl[i], w[nm].tv(), emit=em)
        for mi, dst in ((0, qh), (1, kh), (2, q2h), (3, k2h)):
            for n in range(4):
                pp = ps[n % 2][0:64, 0:512]
                mm_acc(ctx, pp, [(wsl[mi][:, j * 64:(j + 1) * 64],
                                  xm[j][:, n * 512:(n + 1) * 512]) for j in range(8)])
                stt(ctx, dst[:, n * 512:(n + 1) * 512], pp, None, ALU.bypass,
                    ln.rb[0:64, n * 512:(n + 1) * 512], ALU.mult)
        for tt_i in range(16):
            pp = ps[2 + tt_i % 2][:, 0:64]
            mm_acc(ctx, pp, [(xm[j][:, tt_i * 128:(tt_i + 1) * 128],
                              wsl[4][:, j * 64:(j + 1) * 64]) for j in range(8)])
            ts(ctx, "DVE", vtok[:, tt_i * 64:(tt_i + 1) * 64], pp, rtok[:, tt_i:tt_i + 1], ALU.mult)
        for src, dst in ((kh, ktok), (k2h, k2tok)):
            for gq in range(2):
                bank = ps[6].bf16()[:, 0:512]
                gg = ctx.mm_open(ps[6])
                for i2 in range(8):
                    kc = gq * 8 + i2
                    tr_mm(ctx, gg, bank[:, i2 * 64:(i2 + 1) * 64],
                          src[:, kc * 128:(kc + 1) * 128], ln.identb[0:64, 0:64],
                          st=(i2 == 0), sp=(i2 == 7))
                gg.close()
                copy(ctx, "DVE", dst[:, gq * 512:(gq + 1) * 512], bank)
        for gi, (ktk, gslice) in enumerate(((ktok, gsb[:, 0:64]), (k2tok, gsb[:, 64:128]))):
            mm_acc(ctx, ps[3][0:64, 0:64], [(ktk[:, kc * 64:(kc + 1) * 64],
                                             ktk[:, kc * 64:(kc + 1) * 64]) for kc in range(16)])
            copy(ctx, "DVE", gslice, ps[3][0:64, 0:64])
        reduce(ctx, "DVE", kcol[:, 0:1], kh, ALU.add)
        reduce(ctx, "DVE", kcol[:, 1:2], k2h, ALU.add)
        copy(ctx, "DVE", kcolb, kcol[:, 0:2])
        stat_rows(qh, gsb[:, 0:64], kcolb[:, 0:1], nmu1, ss1)
        stat_rows(q2h, gsb[:, 64:128], kcolb[:, 1:2], nmu2, ss2)
        rsig_from(nmu1, ss1, rs1)
        ts(ctx, "DVE", rs1, rs1, 0.125, ALU.mult)
        rsig_from(nmu2, ss2, a2c)
        ts(ctx, "DVE", a2c, a2c, c_mqs, ALU.mult, 0.125, ALU.mult)
        tt(ctx, "DVE", ctmp, nmu2, a2c, ALU.mult)
        ts(ctx, "DVE", b2c, ctmp, c_1m, ALU.add)
        if dbg is not None and h == 0:
            for nm, tvv in (("qh", qh), ("kh", kh), ("q2h", q2h), ("k2h", k2h)):
                ctx.dma("SYNC", dbg[nm].tv(), tvv)
            ctx.dma("SYNC", dbg["vtok"].tv(), vtok)
            ctx.dma("SYNC", dbg["ktok"].tv(), ktok)
            ctx.dma("SYNC", dbg["gsb"].tv(), gsb)
            ctx.dma("SYNC", dbg["cols"].tv(), cols)
        for qc in range(16):
            W = (qc + 1) * 128
            if qc % 4 == 0:
                for kc in range(qc, min(qc + 4, 16)):
                    memset(ctx, "POOL", pt[:, kc * 512:(kc + 1) * 512], 0.0)
            nch = (W + 511) // 512
            for c in range(nch):
                c0, c1 = c * 512, min(W, (c + 1) * 512)
                pp = ps[c % 2][:, 0:c1 - c0]
                mm_acc(ctx, pp, [(q2h[:, qc * 128:(qc + 1) * 128], k2h[:, c0:c1])])
                ts(ctx, "DVE", u[:, c0:c1], pp, a2c[:, qc:qc + 1], ALU.mult,
                   b2c[:, qc:qc + 1], ALU.add)
            for c in range(nch):
                c0, c1 = c * 512, min(W, (c + 1) * 512)
                pp = ps[c % 2][:, 0:c1 - c0]
                mm_acc(ctx, pp, [(qh[:, qc * 128:(qc + 1) * 128], kh[:, c0:c1])])
                stt(ctx, tq[:, c0:c1], pp, nmu1[:, qc:qc + 1], ALU.add, u[:, c0:c1], ALU.mult)
            tt(ctx, "DVE", tq[:, qc * 128:W], tq[:, qc * 128:W], ln.cmask, ALU.add)
            reduce(ctx, "DVE", maxc, tq[:, 0:W], ALU.max)
            ts(ctx, "DVE", biasc, maxc, rs1[:, qc:qc + 1], ALU.mult, -1.0, ALU.mult)
            act(ctx, p[:, 0:W], tq[:, 0:W], AF.Exp, bias=biasc, scale=rs1[:, qc:qc + 1],
                accum_out=sumc)
            recip(ctx, sumc, sumc)
            ts(ctx, "DVE", p[:, 0:W], p[:, 0:W], sumc, ALU.mult)
            for kc in range(qc + 1):
                bank = ps[6 + kc % 2]
                gg = ctx.mm_open(bank)
                tr_mm(ctx, gg, bank.bf16()[:, 0:128], p[:, kc * 128:(kc + 1) * 128],
                      ln.identb, st=True, sp=True)
                gg.close()
                dst = pt[:, kc * 512 + (qc % 4) * 128:kc * 512 + (qc % 4) * 128 + 128]
                copy(ctx, "DVE" if kc % 2 == 0 else "ACT", dst, bank.bf16()[:, 0:128])
            if dbg is not None and h == 0 and qc == 9:
                ctx.dma("SYNC", dbg["u"].tv(), u)
                ctx.dma("SYNC", dbg["tq"].tv(), tq)
                ctx.dma("SYNC", dbg["p"].tv(), p)
            if qc % 4 == 3:
                sp_i = qc // 4
                po = ps[2]
                out_tv = po[hb:hb + 64, 0:512]
                gpv = ctx.mm_open(po)
                for kc in range(qc + 1):
                    lt = vtok[:, kc * 64:(kc + 1) * 64]
                    rt = pt[:, kc * 512:(kc + 1) * 512]
                    tp = (0, hb)
                    gpv.add(lambda e, o=out_tv, l=lt, r=rt, st=(kc == 0), sp2=(kc == qc),
                            tp=tp: e.matmul(o.ap(), l.ap(), r.ap(), start=st, stop=sp2,
                                            tile_position=tp), [lt, rt])
                gpv.close()
                copy(ctx, "DVE", ao[h // 2][hb:hb + 64, sp_i * 512:(sp_i + 1) * 512], out_tv)
    if dbg is not None:
        for j in range(8):
            ctx.dma("SYNC", dbg[f"ao{j}"].tv(), ao[j])
    ctx.free_to(mk0)
    wot = load_full_bf16(ctx, f"l1a_wot_{tag}", w["WoT"], 1024, 1024)
    for f in range(8):
        for n in range(4):
            pp = ps[f % 2][:, 0:512]
            mm_acc(ctx, pp, [(wot[j][:, f * 128:(f + 1) * 128],
                              ao[j][:, n * 512:(n + 1) * 512]) for j in range(8)])
            xsl = xs[f][:, n * 512:(n + 1) * 512]
            stt(ctx, xsl, pp, None, ALU.bypass, xsl, ALU.add)


def emit_l1_ffn(ctx, xs, w, ln, ps, tag):
    xm = [ctx.sbuf(f"l1f_xm{j}_{tag}", [128, T], BF16) for j in range(8)]
    emit_ln(ctx, xs, 0, T, ln, ps[0], ps[1], xm, mode="ln")
    trash = ctx.sbuf(f"l1f_trash_{tag}", [128, T], BF16)
    gcol = ctx.sbuf(f"l1f_gcol_{tag}", [128, 20], F32)
    gcolb = ctx.sbuf(f"l1f_gcolb_{tag}", [128, 16], BF16)
    sval = ctx.sbuf(f"l1f_sval_{tag}", [1, 8], F32)
    for j in range(8):
        tt(ctx, "DVE", trash, xm[j], ln.rb, ALU.mult)
        reduce(ctx, "DVE", gcol[:, j:j + 1], trash, ALU.add)
    ts(ctx, "DVE", gcolb[:, 0:8], gcol[:, 0:8], 1.0 / 2048.0, ALU.mult)
    gw1 = load_full_bf16(ctx, f"l1f_gw1_{tag}", w["gc_w1"], 1024, 512)
    gb1 = load_col_fp32(ctx, f"l1f_gb1_{tag}", w["gc_b1"], 512, ln, ps[2])
    gw2 = load_full_bf16(ctx, f"l1f_gw2_{tag}", w["gc_w2"], 512, 256)
    gb2 = load_col_fp32(ctx, f"l1f_gb2_{tag}", w["gc_b2"], 256, ln, ps[3])
    gw3 = load_full_bf16(ctx, f"l1f_gw3_{tag}", w["gc_w3"], 256, 1)
    ctx.dma("SYNC", sval[:, 1:2], w["ar"].tv())
    ctx.dma("SYNC", sval[:, 5:6], w["gc_b3"].tv())
    ctx.dma("SYNC", sval[:, 6:7], w["cn_b4"].tv())
    for ft in range(4):
        mm_acc(ctx, ps[0][:, 0:1], [(gw1[j][:, ft * 128:(ft + 1) * 128], gcolb[:, j:j + 1])
                                    for j in range(8)])
        act(ctx, gcolb[:, 8 + ft:9 + ft], ps[0][:, 0:1], AF.Gelu, bias=gb1[:, ft:ft + 1])
    for ft in range(2):
        mm_acc(ctx, ps[0][:, 0:1], [(gw2[k][:, ft * 128:(ft + 1) * 128], gcolb[:, 8 + k:9 + k])
                                    for k in range(4)])
        act(ctx, gcolb[:, 12 + ft:13 + ft], ps[0][:, 0:1], AF.Gelu, bias=gb2[:, ft:ft + 1])
    mm_acc(ctx, ps[0][0:1, 0:1], [(gw3[k][:, 0:1], gcolb[:, 12 + k:13 + k]) for k in range(2)])
    act(ctx, sval[:, 0:1], ps[0][0:1, 0:1], AF.Sigmoid, bias=sval[:, 5:6])
    ts(ctx, "DVE", sval[:, 2:3], sval[:, 1:2], 0.6, ALU.mult)
    ts(ctx, "DVE", sval[:, 4:5], sval[:, 0:1], 1.4, ALU.mult, -1.0, ALU.add)
    tt(ctx, "DVE", sval[:, 4:5], sval[:, 4:5], sval[:, 1:2], ALU.mult)
    ts(ctx, "DVE", sval[:, 3:4], sval[:, 4:5], 0.1, ALU.add)
    w1t = load_full_bf16(ctx, f"l1f_w1_{tag}", w["cn_w1"], 1024, 1024)
    b1c = load_col_fp32(ctx, f"l1f_b1_{tag}", w["cn_b1"], 1024, ln, ps[2])
    w2t = load_full_bf16(ctx, f"l1f_w2_{tag}", w["cn_w2"], 1024, 512)
    b2c = load_col_fp32(ctx, f"l1f_b2_{tag}", w["cn_b2"], 512, ln, ps[3])
    w3t = load_full_bf16(ctx, f"l1f_w3_{tag}", w["cn_w3"], 512, 256)
    b3c = load_col_fp32(ctx, f"l1f_b3_{tag}", w["cn_b3"], 256, ln, ps[4])
    w4t = load_full_bf16(ctx, f"l1f_w4_{tag}", w["cn_w4"], 256, 1)
    out = [ctx.sbuf(f"l1f_out{j}_{tag}", [128, 512], F32) for j in range(8)]
    outb = [ctx.sbuf(f"l1f_outb{j}_{tag}", [128, 512], BF16) for j in range(8)]
    h1b = ctx.sbuf(f"l1f_h1b_{tag}", [128, 8 * 512], BF16)
    h2b = ctx.sbuf(f"l1f_h2b_{tag}", [128, 4 * 512], BF16)
    h3b = ctx.sbuf(f"l1f_h3b_{tag}", [128, 2 * 512], BF16)
    tfr = ctx.sbuf(f"l1f_tfr_{tag}", [1, 512], F32)
    cplb = ctx.sbuf(f"l1f_cplb_{tag}", [128, 2 * 512], F32)
    tmp = ctx.sbuf(f"l1f_tmp_{tag}", [128, 512], F32)
    PERM = [4, 5, 6, 7, 0, 1, 2, 3]
    SGN = [1.0, 1.0, -1.0, -1.0, -1.0, -1.0, 1.0, 1.0]
    for c in range(4):
        t0 = c * 512
        for j in range(8):
            stt(ctx, out[j], xs[j][:, t0:t0 + 512] if False else xm[j][:, t0:t0 + 512],
                None, ALU.bypass, ln.rb[:, t0:t0 + 512], ALU.mult)
        for cyc in range(3):
            for j in range(8):
                copy(ctx, "DVE" if j % 2 == 0 else "POOL", outb[j], out[j])
            for f in range(8):
                mm_acc(ctx, ps[f % 2][:, 0:512], [(w1t[j][:, f * 128:(f + 1) * 128], outb[j])
                                                  for j in range(8)])
                act(ctx, h1b[:, f * 512:(f + 1) * 512], ps[f % 2][:, 0:512], AF.Gelu,
                    bias=b1c[:, f:f + 1])
            for f in range(4):
                mm_acc(ctx, ps[f % 2][:, 0:512], [(w2t[j][:, f * 128:(f + 1) * 128],
                                                   h1b[:, j * 512:(j + 1) * 512]) for j in range(8)])
                act(ctx, h2b[:, f * 512:(f + 1) * 512], ps[f % 2][:, 0:512], AF.Gelu,
                    bias=b2c[:, f:f + 1])
            for f in range(2):
                mm_acc(ctx, ps[f % 2][:, 0:512], [(w3t[k][:, f * 128:(f + 1) * 128],
                                                   h2b[:, k * 512:(k + 1) * 512]) for k in range(4)])
                act(ctx, h3b[:, f * 512:(f + 1) * 512], ps[f % 2][:, 0:512], AF.Gelu,
                    bias=b3c[:, f:f + 1])
            mm_acc(ctx, ps[2][0:1, 0:512], [(w4t[k][:, 0:1], h3b[:, k * 512:(k + 1) * 512])
                                            for k in range(2)])
            act(ctx, tfr, ps[2][0:1, 0:512], AF.Sigmoid, bias=sval[:, 6:7])
            ts(ctx, "DVE", tfr, tfr, sval[:, 2:3], ALU.mult, sval[:, 3:4], ALU.add)
            mm_acc(ctx, ps[3][:, 0:512], [(ln.ones_1, tfr)])
            copy(ctx, "ACT", cplb[:, 0:512], ps[3][:, 0:512])
            ts(ctx, "DVE", cplb[:, 512:1024], cplb[:, 0:512], -1.0, ALU.mult)
            for j in range(8):
                cp = cplb[:, 0:512] if SGN[j] > 0 else cplb[:, 512:1024]
                stt(ctx, tmp, outb[PERM[j]], None, ALU.bypass, cp, ALU.mult)
                tt(ctx, "DVE", out[j], out[j], tmp, ALU.add)
        for j in range(8):
            tt(ctx, "DVE", xs[j][:, t0:t0 + 512], xs[j][:, t0:t0 + 512], out[j], ALU.add)


def emit_l2_attn(ctx, xs, w, ln, ps, tag):
    H = 16
    C = 128
    xm = [ctx.sbuf(f"l2a_xm{j}_{tag}", [128, T], BF16) for j in range(8)]
    emit_ln(ctx, xs, 0, T, ln, ps[0], ps[1], xm, mode="ln")
    rtok = ctx.sbuf(f"l2a_rtok_{tag}", [128, 16], F32)
    for i in range(16):
        transpose(ctx, ps[2][:, 0:1], ln.rb[0:1, i * 128:(i + 1) * 128], ln.ones_1[0:1, 0:1])
        copy(ctx, "ACT", rtok[:, i:i + 1], ps[2][:, 0:1])
    bqk = ctx.sbuf(f"l2a_bqk_{tag}", [64, 32], F32)
    stg = ctx.sbuf(f"l2a_bstg_{tag}", [16, 64], F32)
    for i, nm in enumerate(("bq", "bk")):
        ctx.dma("SYNC", stg, w[nm].tv())
        transpose(ctx, ps[2][0:64, 0:16], stg, ln.ident[0:16, 0:16])
        copy(ctx, "ACT", bqk[:, i * 16:(i + 1) * 16], ps[2][0:64, 0:16])
    bvrow = ctx.sbuf(f"l2a_bv_{tag}", [1, 1024], F32)
    ctx.dma("SYNC", bvrow, w["bv"].tv())
    tmulT = ctx.sbuf(f"l2a_tmulT_{tag}", [128, 128], F32)
    memset(ctx, "POOL", tmulT, 1.0)
    affsel(ctx, tmulT, tmulT, [[1, 128]], ALU.is_ge, 0.0, 0, -1)
    ao = [ctx.sbuf(f"l2a_ao{j}_{tag}", [128, T], BF16) for j in range(8)]
    NH = 2
    qf = [ctx.sbuf(f"l2a_q{i}_{tag}", [64, T], BF16) for i in range(NH)]
    kf = [ctx.sbuf(f"l2a_k{i}_{tag}", [64, T], BF16) for i in range(NH)]
    vt = [ctx.sbuf(f"l2a_v{i}_{tag}", [128, 16 * 64], BF16) for i in range(NH)]
    ssb = [ctx.sbuf(f"l2a_S{i}_{tag}", [64, 64], F32) for i in range(NH)]
    ssbb = [ctx.sbuf(f"l2a_Sb{i}_{tag}", [64, 64], BF16) for i in range(NH)]
    zcol = [ctx.sbuf(f"l2a_z{i}_{tag}", [64, 2], F32) for i in range(NH)]
    zcolb = [ctx.sbuf(f"l2a_zb{i}_{tag}", [64, 1], BF16) for i in range(NH)]
    ktc = [ctx.sbuf(f"l2a_kt{i}_{tag}", [128, 64], BF16) for i in range(NH)]
    sct = [ctx.sbuf(f"l2a_sct{i}_{tag}", [128, 128], BF16) for i in range(NH)]
    denr = [ctx.sbuf(f"l2a_den{i}_{tag}", [1, 128], F32) for i in range(NH)]
    denb = [ctx.sbuf(f"l2a_denb{i}_{tag}", [64, 128], F32) for i in range(NH)]
    bvb = [ctx.sbuf(f"l2a_bvb{i}_{tag}", [128, 64], F32) for i in range(NH)]
    zq = [ctx.sbuf(f"l2a_zq{i}_{tag}", [64, 512], F32) for i in range(NH)]
    zq2 = [ctx.sbuf(f"l2a_zq2{i}_{tag}", [64, 512], F32) for i in range(NH)]
    wsl = [ctx.sbuf(f"l2a_ws{i}_{tag}", [128, 512], BF16) for i in range(3 * NH)]
    ones1b = ctx.sbuf(f"l2a_ones1b_{tag}", [128, 1], BF16)
    memset(ctx, "DVE", ones1b, 1.0)
    for hg in range(H // NH):
        heads = [hg * NH + i for i in range(NH)]
        for ii, h in enumerate(heads):
            hb = (h % 2) * 64
            for mi, nm in enumerate(("WqT", "WkT", "WvT")):
                d = w[nm].apobj

                def em(e, s=wsl[3 * ii + mi], d=d, h=h):
                    return e.dma_start(out=s.ap(), in_=d[:, h * 64:(h + 1) * 64].rearrange(
                        "(k p) c -> p k c", p=128))
                ctx.dma("POOL", wsl[3 * ii + mi], w[nm].tv(), emit=em)
            for mi, dst in ((0, qf[ii]), (1, kf[ii])):
                bcol = bqk[:, mi * 16 + h:mi * 16 + h + 1]
                for n in range(4):
                    pp = ps[n % 2][0:64, 0:512]
                    mm_acc(ctx, pp, [(wsl[3 * ii + mi][:, j * 64:(j + 1) * 64],
                                      xm[j][:, n * 512:(n + 1) * 512]) for j in range(8)])
                    z0 = zq[ii]
                    stt(ctx, z0, pp, None, ALU.bypass, ln.rb[0:64, n * 512:(n + 1) * 512], ALU.mult)
                    ts(ctx, "DVE", zq2[ii], z0, bcol, ALU.add, 0.0, ALU.min)
                    act(ctx, zq2[ii], zq2[ii], AF.Exp)
                    ts(ctx, "DVE", z0, z0, bcol, ALU.add, 0.0, ALU.max)
                    tt(ctx, "DVE", dst[:, n * 512:(n + 1) * 512], zq2[ii], z0, ALU.add)
            mm_acc(ctx, ps[2][:, 0:64], [(ln.ones_1, bvrow[:, h * 64:(h + 1) * 64])])
            copy(ctx, "ACT", bvb[ii], ps[2][:, 0:64])
            for tt_i in range(16):
                pp = ps[2 + tt_i % 2][:, 0:64]
                mm_acc(ctx, pp, [(xm[j][:, tt_i * 128:(tt_i + 1) * 128],
                                  wsl[3 * ii + 2][:, j * 64:(j + 1) * 64]) for j in range(8)])
                stt(ctx, vt[ii][:, tt_i * 64:(tt_i + 1) * 64], pp, rtok[:, tt_i:tt_i + 1],
                    ALU.mult, bvb[ii], ALU.add)
            memset(ctx, "DVE", ssb[ii], 0.0)
            memset(ctx, "DVE", ssbb[ii], 0.0)
            memset(ctx, "DVE", zcol[ii], 0.0)
            memset(ctx, "DVE", zcolb[ii], 0.0)
        for i in range(16):
            for ii, h in enumerate(heads):
                hb = (h % 2) * 64
                qi = qf[ii][:, i * 128:(i + 1) * 128]
                ki = kf[ii][:, i * 128:(i + 1) * 128]
                vi = vt[ii][:, i * 64:(i + 1) * 64]
                pp = ps[ii]
                mm_acc(ctx, pp[:, 0:128], [(ki, qi)])
                stt(ctx, sct[ii], pp[:, 0:128], None, ALU.bypass, tmulT, ALU.mult)
                pd = ps[2 + ii]
                gd = ctx.mm_open(pd)
                gd.add(lambda e, o=pd[0:1, 0:128], l=ones1b, r=sct[ii]: e.matmul(
                    o.ap(), l.ap(), r.ap(), start=True, stop=False), [ones1b, sct[ii]])
                gd.add(lambda e, o=pd[0:1, 0:128], l=zcolb[ii], r=qi: e.matmul(
                    o.ap(), l.ap(), r.ap(), start=False, stop=True), [zcolb[ii], qi])
                gd.close()
                copy(ctx, "DVE", denr[ii], pd[0:1, 0:128])
                recip(ctx, denr[ii], denr[ii])
                mm_acc(ctx, ps[4 + ii][0:64, 0:128], [(ln.ones_1[:, 0:64], denr[ii])])
                copy(ctx, "ACT", denb[ii], ps[4 + ii][0:64, 0:128])
                pn = ps[6]
                gn = ctx.mm_open(pn)
                out_tv = pn[hb:hb + 64, 0:128]
                gn.add(lambda e, o=out_tv, l=vi, r=sct[ii], tp=(0, hb): e.matmul(
                    o.ap(), l.ap(), r.ap(), start=True, stop=False, tile_position=tp),
                    [vi, sct[ii]])
                gn.add(lambda e, o=out_tv, l=ssbb[ii], r=qi, tp=(0, hb): e.matmul(
                    o.ap(), l.ap(), r.ap(), start=False, stop=True, tile_position=tp),
                    [ssbb[ii], qi])
                gn.close()
                stt(ctx, ao[h // 2][hb:hb + 64, i * 128:(i + 1) * 128], out_tv,
                    None, ALU.bypass, denb[ii], ALU.mult)
                pk = ps[7]
                gk = ctx.mm_open(pk)
                tr_mm(ctx, gk, pk.bf16()[:, 0:64], ki, ln.identb[0:64, 0:64], True, True)
                gk.close()
                copy(ctx, "DVE", ktc[ii], pk.bf16()[:, 0:64])
                mm_acc(ctx, ps[4 + ii][0:64, 0:64], [(ktc[ii], vi)])
                tt(ctx, "DVE", ssb[ii], ssb[ii], ps[4 + ii][0:64, 0:64], ALU.add)
                copy(ctx, "DVE", ssbb[ii], ssb[ii])
                reduce(ctx, "DVE", zcol[ii][:, 1:2], ki, ALU.add)
                tt(ctx, "DVE", zcol[ii][:, 0:1], zcol[ii][:, 0:1], zcol[ii][:, 1:2], ALU.add)
                copy(ctx, "DVE", zcolb[ii], zcol[ii][:, 0:1])
    wot = load_full_bf16(ctx, f"l2a_wot_{tag}", w["WoT"], 1024, 1024)
    boc = load_col_fp32(ctx, f"l2a_bo_{tag}", w["bo"], 1024, ln, ps[2])
    for f in range(8):
        for n in range(4):
            pp = ps[f % 2][:, 0:512]
            mm_acc(ctx, pp, [(wot[j][:, f * 128:(f + 1) * 128],
                              ao[j][:, n * 512:(n + 1) * 512]) for j in range(8)])
            xsl = xs[f][:, n * 512:(n + 1) * 512]
            stt(ctx, xsl, pp, boc[:, f:f + 1], ALU.add, xsl, ALU.add)


def emit_l2_ffn(ctx, xs, w, ln, ps, tag):
    E, F = 8, 3584
    xm = [ctx.sbuf(f"l2f_xm{j}_{tag}", [128, T], BF16) for j in range(8)]
    emit_ln(ctx, xs, 0, T, ln, ps[0], ps[1], xm, mode="ln")
    for j in range(8):
        stt(ctx, xm[j], xm[j], None, ALU.bypass, ln.rb, ALU.mult)
    gwt = load_full_bf16(ctx, f"l2f_gw_{tag}", w["gwT"], 1024, 8)
    gtok = ctx.sbuf(f"l2f_gtok_{tag}", [128, 16 * 8], F32)
    rsc = ctx.sbuf(f"l2f_rsc_{tag}", [128, 6 * 8], F32)
    rcol = ctx.sbuf(f"l2f_rcol_{tag}", [128, 8], F32)
    sc_, ab_, nu_ = rsc[:, 0:8], rsc[:, 8:16], rsc[:, 16:24]
    ke_, ee_, g1_ = rsc[:, 24:32], rsc[:, 32:40], rsc[:, 40:48]
    t1c, s1c, t2c, s2c = rcol[:, 0:1], rcol[:, 1:2], rcol[:, 2:3], rcol[:, 3:4]
    for ti in range(16):
        pp = ps[2][:, 0:8]
        mm_acc(ctx, pp, [(xm[j][:, ti * 128:(ti + 1) * 128], gwt[j]) for j in range(8)])
        copy(ctx, "DVE", sc_, pp)
        g = gtok[:, ti * 8:(ti + 1) * 8]
        reduce(ctx, "DVE", t1c, sc_, ALU.max)
        act(ctx, ab_, sc_, AF.Abs)
        ts(ctx, "DVE", nu_, sc_, -1.0, ALU.mult, t1c, ALU.add)
        ts(ctx, "DVE", ee_, ab_, t1c, ALU.max, 0.02, ALU.mult)
        tt(ctx, "DVE", ke_, nu_, ee_, ALU.is_le)
        ts(ctx, "DVE", t2c, t1c, -1.0, ALU.mult)
        act(ctx, ee_, sc_, AF.Exp, bias=t2c)
        tt(ctx, "DVE", ee_, ee_, ke_, ALU.mult)
        reduce(ctx, "DVE", s1c, ee_, ALU.add)
        recip(ctx, s1c, s1c)
        ts(ctx, "DVE", g1_, sc_, t1c, ALU.is_equal)
        ts(ctx, "DVE", g1_, g1_, s1c, ALU.mult)
        ts(ctx, "DVE", ke_, sc_, t1c, ALU.is_equal)
        stt(ctx, ee_, ke_, -1e30, ALU.mult, sc_, ALU.add)
        reduce(ctx, "DVE", t2c, ee_, ALU.max)
        ts(ctx, "DVE", nu_, sc_, -1.0, ALU.mult, t2c, ALU.add)
        ts(ctx, "DVE", ab_, ab_, t2c, ALU.max, 0.02, ALU.mult)
        tt(ctx, "DVE", ke_, nu_, ab_, ALU.is_le)
        ts(ctx, "DVE", s2c, t2c, -1.0, ALU.mult)
        act(ctx, ee_, ee_, AF.Exp, bias=s2c)
        tt(ctx, "DVE", ee_, ee_, ke_, ALU.mult)
        reduce(ctx, "DVE", s2c, ee_, ALU.add)
        recip(ctx, s2c, s2c)
        ts(ctx, "DVE", nu_, sc_, t1c, ALU.is_equal)
        stt(ctx, ee_, nu_, -1e30, ALU.mult, sc_, ALU.add)
        ts(ctx, "DVE", ee_, ee_, t2c, ALU.is_equal)
        ts(ctx, "DVE", ee_, ee_, s2c, ALU.mult)
        tt(ctx, "DVE", g, g1_, ee_, ALU.add)
    grow = ctx.sbuf(f"l2f_grow_{tag}", [1, 512], F32)
    w1s = [ctx.sbuf(f"l2f_w1s{i}_{tag}", [128, 8 * 512], BF16) for i in range(2)]
    w3s = [ctx.sbuf(f"l2f_w3s{i}_{tag}", [128, 8 * 512], BF16) for i in range(2)]
    w2s = [ctx.sbuf(f"l2f_w2s{i}_{tag}", [128, 4 * 1024], BF16) for i in range(2)]
    s1b = ctx.sbuf(f"l2f_s1b_{tag}", [128, 4 * 512], BF16)
    h3b = ctx.sbuf(f"l2f_h3b_{tag}", [128, 4 * 512], BF16)
    gbeb = [ctx.sbuf(f"l2f_gbe{i}_{tag}", [128, 8 * 512], BF16) for i in range(4)]
    gbet = [[gbeb[tc][:, e * 512:(e + 1) * 512] for tc in range(4)] for e in range(8)]
    d1, d2, d3 = w["w1T"].apobj, w["w2T"].apobj, w["w3T"].apobj
    si = 0
    for e in range(E):
        for fg in range(7):
            sl1, sl3, sl2 = w1s[si % 2], w3s[si % 2], w2s[si % 2]
            si += 1

            def em1(e_, s=sl1, d=d1, e=e, fg=fg):
                return e_.dma_start(out=s.ap(), in_=d[e][:, fg * 512:(fg + 1) * 512]
                                    .rearrange("(k p) c -> p k c", p=128))
            ctx.dma("POOL", sl1, w["w1T"].tv(), emit=em1)

            def em3(e_, s=sl3, d=d3, e=e, fg=fg):
                return e_.dma_start(out=s.ap(), in_=d[e][:, fg * 512:(fg + 1) * 512]
                                    .rearrange("(k p) c -> p k c", p=128))
            ctx.dma("POOL", sl3, w["w3T"].tv(), emit=em3)

            def em2(e_, s=sl2, d=d2, e=e, fg=fg):
                return e_.dma_start(out=s.ap(), in_=d[e][fg * 512:(fg + 1) * 512, :]
                                    .rearrange("(k p) c -> p k c", p=128))
            ctx.dma("POOL", sl2, w["w2T"].tv(), emit=em2)
            for tc in range(4):
                t0 = tc * 512
                if fg == 0:
                    gp = ctx.mm_open(ps[3])
                    for c4 in range(4):
                        gcolv = gtok[:, (tc * 4 + c4) * 8 + e:(tc * 4 + c4) * 8 + e + 1]
                        gp.add(lambda e_, o=ps[3][0:1, 0:512], i=gcolv, id=ln.ident,
                               st=(c4 == 0), sp2=(c4 == 3), c4=c4: e_.matmul(
                                   o[:, c4 * 128:(c4 + 1) * 128].ap(), i.ap(), id.ap(),
                                   start=st, stop=sp2, is_transpose=True), [gcolv, ln.ident])
                    gp.close()
                    copy(ctx, "ACT", grow, ps[3][0:1, 0:512])
                    mm_acc(ctx, ps[3][:, 0:512], [(ln.ones_1, grow)])
                    copy(ctx, "ACT", gbet[e][tc], ps[3][:, 0:512])
                for fsub in range(4):
                    pp1 = ps[4 + fsub % 2]
                    mm_acc(ctx, pp1, [(sl1[:, k * 512 + fsub * 128:k * 512 + (fsub + 1) * 128],
                                       xm[k][:, t0:t0 + 512]) for k in range(8)])
                    act(ctx, s1b[:, fsub * 512:(fsub + 1) * 512], pp1, AF.Silu)
                    pp3 = ps[6 + fsub % 2]
                    mm_acc(ctx, pp3, [(sl3[:, k * 512 + fsub * 128:k * 512 + (fsub + 1) * 128],
                                       xm[k][:, t0:t0 + 512]) for k in range(8)])
                    stt(ctx, h3b[:, fsub * 512:(fsub + 1) * 512], pp3, None, ALU.bypass,
                        gbet[e][tc], ALU.mult)
                    tt(ctx, "DVE", s1b[:, fsub * 512:(fsub + 1) * 512],
                       s1b[:, fsub * 512:(fsub + 1) * 512],
                       h3b[:, fsub * 512:(fsub + 1) * 512], ALU.mult)
                for ho in range(8):
                    pp = ps[ho % 2]
                    mm_acc(ctx, pp, [(sl2[:, k * 1024 + ho * 128:k * 1024 + (ho + 1) * 128],
                                      s1b[:, k * 512:(k + 1) * 512]) for k in range(4)])
                    yv = xs[ho][:, t0:t0 + 512]
                    stt(ctx, yv, pp, None, ALU.bypass, yv, ALU.add)


def emit_l3_attn(ctx, xs, w, ln, ps, tag):
    H = 16
    xm = [ctx.sbuf(f"l3a_xm{j}_{tag}", [128, T], BF16) for j in range(8)]
    emit_ln(ctx, xs, 0, T, ln, ps[0], ps[1], xm, mode="ln")
    t1s = ctx.sbuf(f"l3a_t1_{tag}", [128, T], F32)
    mkr = ctx.mark()
    cosb = ctx.sbuf(f"l3a_cosb_{tag}", [128, T], F32)
    sinb = ctx.sbuf(f"l3a_sinb_{tag}", [128, T], F32)
    ctx.dma("SYNC", cosb, w["cosb"].tv())
    ctx.dma("SYNC", sinb, w["sinb2"].tv())
    permb = ctx.sbuf(f"l3a_perm_{tag}", [128, 128], BF16)
    ctx.dma("POOL", permb, w["permT"].tv())
    t2s = ctx.sbuf(f"l3a_t2_{tag}", [128, T], F32)
    for j in range(8):
        for n in range(4):
            pp = ps[2 + n % 2][:, 0:512]
            sl = slice(n * 512, (n + 1) * 512)
            mm_acc(ctx, pp, [(permb, xm[j][:, sl])])
            stt(ctx, t1s[:, sl], pp, None, ALU.bypass, sinb[:, sl], ALU.mult)
        tt(ctx, "POOL", t2s, xm[j], cosb, ALU.mult)
        tt(ctx, "DVE", xm[j], t1s, t2s, ALU.add)
    ctx.free_to(mkr)
    ao = [ctx.sbuf(f"l3a_ao{j}_{tag}", [128, T], BF16) for j in range(8)]
    mkh = ctx.mark()
    qstage = ctx.sbuf(f"l3a_qstg_{tag}", [64, 512], F32)
    bqk = ctx.sbuf(f"l3a_bqk_{tag}", [64, 48], F32)
    stg = ctx.sbuf(f"l3a_bstg_{tag}", [48, 64], F32)
    ctx.dma("SYNC", stg, w["ipb48"].tv())
    transpose(ctx, ps[2][0:64, 0:48], stg, ln.ident[0:48, 0:48])
    copy(ctx, "ACT", bqk, ps[2][0:64, 0:48])
    bvh = [ctx.sbuf(f"l3a_bvh{i}_{tag}", [1, 64], F32) for i in range(2)]
    rtok = ctx.sbuf(f"l3a_rtok_{tag}", [128, 16], F32)
    for i in range(16):
        transpose(ctx, ps[2][:, 0:1], ln.rb[0:1, i * 128:(i + 1) * 128], ln.ones_1[0:1, 0:1])
        copy(ctx, "ACT", rtok[:, i:i + 1], ps[2][:, 0:1])
    pt = ctx.sbuf(f"l3a_pt_{tag}", [128, 16 * 512], BF16)
    pB = [ctx.sbuf(f"l3a_p{i}_{tag}", [128, T], BF16) for i in range(2)]
    tqB = [t1s, t1s]
    mcol = ctx.sbuf(f"l3a_mcol_{tag}", [128, 8], F32)
    qh = ctx.sbuf(f"l3a_qh_{tag}", [64, T], BF16)
    kh = ctx.sbuf(f"l3a_kh_{tag}", [64, T], BF16)
    vtok = ctx.sbuf(f"l3a_vtok_{tag}", [128, 16 * 64], BF16)
    bvb = ctx.sbuf(f"l3a_bvb_{tag}", [128, 64], F32)
    msk = [ctx.sbuf(f"l3a_msk{i}_{tag}", [128, T], BF16) for i in range(2)]
    wsl = [ctx.sbuf(f"l3a_ws{i}_{tag}", [128, 512], BF16) for i in range(3)]
    dw = w["ipwT"].apobj
    dm = w["maskadd"].apobj
    for h in range(H):
        hb = (h % 2) * 64
        for mi, col0 in ((0, h * 64), (1, 1024 + h * 64), (2, 2048 + h * 64)):
            def em(e, s=wsl[mi], d=dw, col0=col0):
                return e.dma_start(out=s.ap(), in_=d[:, col0:col0 + 64].rearrange(
                    "(k p) c -> p k c", p=128))
            ctx.dma("POOL", wsl[mi], w["ipwT"].tv(), emit=em)
        for mi, dst in ((0, qh), (1, kh)):
            bcol = bqk[:, (mi * 16 + h):(mi * 16 + h) + 1]
            for n in range(4):
                pp = ps[n % 2][0:64, 0:512]
                sl = slice(n * 512, (n + 1) * 512)
                mm_acc(ctx, pp, [(wsl[mi][:, j * 64:(j + 1) * 64], xm[j][:, sl])
                                 for j in range(8)])
                stt(ctx, qstage, pp, None, ALU.bypass, ln.rb[0:64, sl], ALU.mult)
                ts(ctx, "DVE", dst[:, sl], qstage, bcol, ALU.add)
        bslot = bvh[h % 2]

        def emb(e, s=bslot, d=w["ipbrow"].apobj, h=h):
            return e.dma_start(out=s.ap(), in_=d[:, 2048 + h * 64:2048 + (h + 1) * 64])
        ctx.dma("SYNC", bslot, w["ipbrow"].tv(), emit=emb)
        mm_acc(ctx, ps[2][:, 0:64], [(ln.ones_1, bslot)])
        copy(ctx, "ACT", bvb, ps[2][:, 0:64])
        for tt_i in range(16):
            pp = ps[2 + tt_i % 2][:, 0:64]
            mm_acc(ctx, pp, [(xm[j][:, tt_i * 128:(tt_i + 1) * 128],
                              wsl[2][:, j * 64:(j + 1) * 64]) for j in range(8)])
            stt(ctx, vtok[:, tt_i * 64:(tt_i + 1) * 64], pp, rtok[:, tt_i:tt_i + 1],
                ALU.mult, bvb, ALU.add)
        def stageA(qc):
            par = qc % 2
            mslot = msk[par]
            tq, p = tqB[par], pB[par]
            maxc, sumc, biasc = (mcol[:, par * 4:par * 4 + 1],
                                 mcol[:, par * 4 + 1:par * 4 + 2],
                                 mcol[:, par * 4 + 2:par * 4 + 3])

            def emm(e, s=mslot, d=dm, qc=qc):
                return e.dma_start(out=s.ap(), in_=d[qc * 128:(qc + 1) * 128, :])
            ctx.dma("SYNC", mslot, w["maskadd"].tv(), emit=emm)
            for c in range(4):
                c0 = c * 512
                pp = ps[c % 2][:, 0:512]
                mm_acc(ctx, pp, [(qh[:, qc * 128:(qc + 1) * 128], kh[:, c0:c0 + 512])])
                stt(ctx, tq[:, c0:c0 + 512], pp, 0.125, ALU.mult, mslot[:, c0:c0 + 512], ALU.add)
            reduce(ctx, "DVE", maxc, tq, ALU.max)
            ts(ctx, "DVE", biasc, maxc, -1.0, ALU.mult)
            act(ctx, p, tq, AF.Exp, bias=biasc, accum_out=sumc)
            recip(ctx, sumc, sumc)
            ts(ctx, "DVE", p, p, sumc, ALU.mult)

        def stageB(qc, h=h, hb=hb):
            p = pB[qc % 2]
            for kc in range(16):
                bank = ps[6 + kc % 2]
                gg = ctx.mm_open(bank)
                tr_mm(ctx, gg, bank.bf16()[:, 0:128], p[:, kc * 128:(kc + 1) * 128],
                      ln.identb, st=True, sp=True)
                gg.close()
                dst = pt[:, kc * 512 + (qc % 4) * 128:kc * 512 + (qc % 4) * 128 + 128]
                copy(ctx, "DVE" if kc % 2 == 0 else "ACT", dst, bank.bf16()[:, 0:128])
            if qc % 4 == 3:
                sp_i = qc // 4
                po = ps[2]
                out_tv = po[hb:hb + 64, 0:512]
                gpv = ctx.mm_open(po)
                for kc in range(16):
                    lt = vtok[:, kc * 64:(kc + 1) * 64]
                    rt = pt[:, kc * 512:(kc + 1) * 512]
                    gpv.add(lambda e, o=out_tv, l=lt, r=rt, st=(kc == 0), sp2=(kc == 15),
                            tp=(0, hb): e.matmul(o.ap(), l.ap(), r.ap(), start=st, stop=sp2,
                                                 tile_position=tp), [lt, rt])
                gpv.close()
                copy(ctx, "DVE", ao[h // 2][hb:hb + 64, sp_i * 512:(sp_i + 1) * 512], out_tv)

        stageA(0)
        for qc in range(16):
            if qc < 15:
                stageA(qc + 1)
            stageB(qc)
    ctx.free_to(mkh)
    wot = load_full_bf16(ctx, f"l3a_owt_{tag}", w["owT"], 1024, 1024)
    obc = load_col_fp32(ctx, f"l3a_ob_{tag}", w["ob"], 1024, ln, ps[2])
    for f in range(8):
        for n in range(4):
            pp = ps[f % 2][:, 0:512]
            mm_acc(ctx, pp, [(wot[j][:, f * 128:(f + 1) * 128],
                              ao[j][:, n * 512:(n + 1) * 512]) for j in range(8)])
            xsl = xs[f][:, n * 512:(n + 1) * 512]
            stt(ctx, xsl, pp, obc[:, f:f + 1], ALU.add, xsl, ALU.add)


def emit_l3_ffn(ctx, xs, w, ln, ps, tag, load_x_tile=None, dbg=None):
    xsave = ctx.dram(f"l3f_xsave_{tag}", [8, 128, T], F32, "Internal")
    dxs = xsave.apobj
    for j in range(8):
        def emS(e, j=j, d=dxs, s=xs[j]):
            return e.dma_start(out=d[j], in_=s.ap())
        ctx.dma("SYNC", xsave.tv(), xs[j], emit=emS)
    """MLA-ish attn + shared+top2 MoE. w: g1c/g2c via cols, WqT [1024,1024],
    WkcT/WvcT [1024,256], WoT [1024,1024], WrT [1024,8], ebrow [1,8],
    sW1T/sW2T/sW3T [1024,1024], eW1T/eW2T/eW3T [8,1024,1024]."""
    emit_ln_inplace(ctx, xs, ln, ps[0], ps[1], T)
    g1c = load_col_fp32(ctx, f"l3f_g1_{tag}", w["g1"], 1024, ln, ps[0])
    g2c = load_col_fp32(ctx, f"l3f_g2_{tag}", w["g2"], 1024, ln, ps[1])
    g1cols = [g1c[:, j:j + 1] for j in range(8)]
    g2cols = [g2c[:, j:j + 1] for j in range(8)]
    xm = [ctx.sbuf(f"l3f_xm{j}_{tag}", [128, T], BF16) for j in range(8)]
    emit_ln(ctx, xs, 0, T, ln, ps[0], ps[1], xm, mode="rms", pre_col=g1cols, eps=1e-6)
    rtok = ctx.sbuf(f"l3f_rtok_{tag}", [128, 16], F32)
    for i in range(16):
        transpose(ctx, ps[2][:, 0:1], ln.rb[0:1, i * 128:(i + 1) * 128], ln.ones_1[0:1, 0:1])
        copy(ctx, "ACT", rtok[:, i:i + 1], ps[2][:, 0:1])
    mk_ao = ctx.mark()
    ao = [ctx.sbuf(f"l3f_ao{j}_{tag}", [128, T], BF16) for j in range(8)]
    mka = ctx.mark()
    wqsl = [ctx.sbuf(f"l3f_wqs{i}_{tag}", [128, 8 * 256], BF16) for i in range(2)]
    qf = [ctx.sbuf(f"l3f_q{i}_{tag}", [128, T], BF16) for i in range(2)]
    kf = [ctx.sbuf(f"l3f_k{i}_{tag}", [128, T], BF16) for i in range(2)]
    vtok = ctx.sbuf(f"l3f_vtok_{tag}", [128, 16 * 256], BF16)
    mk_kv = ctx.mark()
    wkt = load_full_bf16(ctx, f"l3f_wkc_{tag}", w["WkcT"], 1024, 256)
    wvt = load_full_bf16(ctx, f"l3f_wvc_{tag}", w["WvcT"], 1024, 256)
    for f in range(2):
        for n in range(4):
            pp = ps[f % 2][:, 0:512]
            sl = slice(n * 512, (n + 1) * 512)
            mm_acc(ctx, pp, [(wkt[j][:, f * 128:(f + 1) * 128], xm[j][:, sl]) for j in range(8)])
            stt(ctx, kf[f][:, sl], pp, None, ALU.bypass, ln.rb[:, sl], ALU.mult)
    for ti in range(16):
        for f2 in range(2):
            pp = ps[2 + f2][:, 0:128]
            mm_acc(ctx, pp, [(xm[j][:, ti * 128:(ti + 1) * 128],
                              wvt[j][:, f2 * 128:(f2 + 1) * 128]) for j in range(8)])
            ts(ctx, "DVE", vtok[:, ti * 256 + f2 * 128:ti * 256 + (f2 + 1) * 128], pp,
               rtok[:, ti:ti + 1], ALU.mult)
    ctx.free_to(mk_kv)
    pt = ctx.sbuf(f"l3f_pt_{tag}", [128, 16 * 512], BF16)
    pB = [ctx.sbuf(f"l3f_p{i}_{tag}", [128, T], BF16) for i in range(2)]
    mcol = ctx.sbuf(f"l3f_mcol_{tag}", [128, 24], F32)
    dwq = w["WqT"].apobj
    for h in range(4):
        wq = wqsl[h % 2]

        def emq(e, s=wq, d=dwq, h=h):
            return e.dma_start(out=s.ap(), in_=d[:, h * 256:(h + 1) * 256].rearrange(
                "(k p) c -> p k c", p=128))
        ctx.dma("POOL", wq, w["WqT"].tv(), emit=emq)
        for f in range(2):
            for n in range(4):
                pp = ps[f % 2][:, 0:512]
                sl = slice(n * 512, (n + 1) * 512)
                mm_acc(ctx, pp, [(wq[:, j * 256 + f * 128:j * 256 + (f + 1) * 128],
                                  xm[j][:, sl]) for j in range(8)])
                stt(ctx, qf[f][:, sl], pp, None, ALU.bypass, ln.rb[:, sl], ALU.mult)
        def stageA(qc):
            par = qc % 2
            p = pB[par]
            mc = mcol[:, par * 12:par * 12 + 12]
            maxc, sumc, biasc = mc[:, 0:1], mc[:, 1:2], mc[:, 2:3]
            for c in range(4):
                c0 = c * 512
                pp = ps[c][:, 0:512]
                mm_acc(ctx, pp, [(qf[f2][:, qc * 128:(qc + 1) * 128],
                                  kf[f2][:, c0:c0 + 512]) for f2 in range(2)])
                reduce(ctx, "DVE", mc[:, 4 + c:5 + c], pp, ALU.max)
            tt(ctx, "DVE", mc[:, 8:9], mc[:, 4:5], mc[:, 5:6], ALU.max)
            tt(ctx, "DVE", mc[:, 9:10], mc[:, 6:7], mc[:, 7:8], ALU.max)
            tt(ctx, "DVE", maxc, mc[:, 8:9], mc[:, 9:10], ALU.max)
            ts(ctx, "DVE", biasc, maxc, -1.0 / 16.0, ALU.mult)
            for c in range(4):
                c0 = c * 512
                act(ctx, p[:, c0:c0 + 512], ps[c][:, 0:512], AF.Exp, bias=biasc,
                    scale=1.0 / 16.0, accum_out=mc[:, 4 + c:5 + c])
            tt(ctx, "DVE", mc[:, 8:9], mc[:, 4:5], mc[:, 5:6], ALU.add)
            tt(ctx, "DVE", mc[:, 9:10], mc[:, 6:7], mc[:, 7:8], ALU.add)
            tt(ctx, "DVE", sumc, mc[:, 8:9], mc[:, 9:10], ALU.add)
            recip(ctx, sumc, sumc)
            ts(ctx, "DVE", p, p, sumc, ALU.mult)

        def stageB(qc, h=h):
            p = pB[qc % 2]
            for kc in range(16):
                bank = ps[6 + kc % 2]
                gg = ctx.mm_open(bank)
                tr_mm(ctx, gg, bank.bf16()[:, 0:128], p[:, kc * 128:(kc + 1) * 128],
                      ln.identb, st=True, sp=True)
                gg.close()
                dst = pt[:, kc * 512 + (qc % 4) * 128:kc * 512 + (qc % 4) * 128 + 128]
                copy(ctx, "DVE" if kc % 2 == 0 else "ACT", dst, bank.bf16()[:, 0:128])
            if qc % 4 == 3:
                sp_i = qc // 4
                for f2 in range(2):
                    po = ps[4 + f2]
                    gpv = ctx.mm_open(po)
                    for kc in range(16):
                        lt = vtok[:, kc * 256 + f2 * 128:kc * 256 + (f2 + 1) * 128]
                        rt = pt[:, kc * 512:(kc + 1) * 512]
                        gpv.add(lambda e, o=po[:, 0:512], l=lt, r=rt, st=(kc == 0),
                                sp2=(kc == 15): e.matmul(o.ap(), l.ap(), r.ap(),
                                                         start=st, stop=sp2), [lt, rt])
                    gpv.close()
                    copy(ctx, "DVE", ao[2 * h + f2][:, sp_i * 512:(sp_i + 1) * 512], po[:, 0:512])

        stageA(0)
        for qc in range(16):
            if qc < 15:
                stageA(qc + 1)
            stageB(qc)
    ctx.free_to(mka)
    wot = load_full_bf16(ctx, f"l3f_wo_{tag}", w["WoT"], 1024, 1024)
    for f in range(8):
        for n in range(4):
            pp = ps[f % 2][:, 0:512]
            mm_acc(ctx, pp, [(wot[j][:, f * 128:(f + 1) * 128],
                              ao[j][:, n * 512:(n + 1) * 512]) for j in range(8)])
            xsl = xs[f][:, n * 512:(n + 1) * 512]
            stt(ctx, xsl, pp, None, ALU.bypass, xsl, ALU.add)
    if dbg is not None:
        for j in range(8):
            ctx.dma("SYNC", dbg[f"xa{j}"].tv(), xs[j])
    ctx.free_to(mk_ao)
    ctx.barrier()
    emit_ln(ctx, xs, 0, T, ln, ps[0], ps[1], xm, mode="rms", pre_col=g2cols, eps=1e-6)
    for j in range(8):
        stt(ctx, xm[j], xm[j], None, ALU.bypass, ln.rb, ALU.mult)
    mk_moe = ctx.mark()
    wrt = load_full_bf16(ctx, f"l3f_wr_{tag}", w["WrT"], 1024, 8)
    ebr = ctx.sbuf(f"l3f_ebr_{tag}", [1, 8], F32)
    ctx.dma("SYNC", ebr, w["ebrow"].tv())
    ebb = ctx.sbuf(f"l3f_ebb_{tag}", [128, 8], F32)
    mm_acc(ctx, ps[2][:, 0:8], [(ln.ones_1, ebr)])
    copy(ctx, "ACT", ebb, ps[2][:, 0:8])
    gtok = ctx.sbuf(f"l3f_gtok_{tag}", [128, 16 * 8], F32)
    rsc = ctx.sbuf(f"l3f_rsc_{tag}", [128, 4 * 8], F32)
    rcol = ctx.sbuf(f"l3f_rcol_{tag}", [128, 8], F32)
    sc_, is1_, is2_, ee_ = rsc[:, 0:8], rsc[:, 8:16], rsc[:, 16:24], rsc[:, 24:32]
    t1c, t2c, e2c, sic = rcol[:, 0:1], rcol[:, 1:2], rcol[:, 2:3], rcol[:, 3:4]
    nt1c = rcol[:, 4:5]
    for ti in range(16):
        pp = ps[2][:, 0:8]
        mm_acc(ctx, pp, [(xm[j][:, ti * 128:(ti + 1) * 128], wrt[j]) for j in range(8)])
        tt(ctx, "DVE", sc_, pp, ebb, ALU.add)
        reduce(ctx, "DVE", t1c, sc_, ALU.max)
        ts(ctx, "DVE", nt1c, t1c, -1.0, ALU.mult)
        act(ctx, sc_, sc_, AF.Exp, bias=nt1c, accum_out=sic)
        recip(ctx, sic, sic)
        ts(ctx, "DVE", sc_, sc_, sic, ALU.mult)
        reduce(ctx, "DVE", t1c, sc_, ALU.max)
        ts(ctx, "DVE", is1_, sc_, t1c, ALU.is_equal)
        stt(ctx, ee_, is1_, -1e30, ALU.mult, sc_, ALU.add)
        reduce(ctx, "DVE", t2c, ee_, ALU.max)
        ts(ctx, "DVE", is2_, ee_, t2c, ALU.is_equal)
        ts(ctx, "DVE", nt1c, t1c, -1.0, ALU.mult)
        act(ctx, e2c, t2c, AF.Exp, bias=nt1c)
        ts(ctx, "DVE", sic, e2c, 1.0, ALU.add)
        recip(ctx, sic, sic)
        tt(ctx, "DVE", e2c, e2c, sic, ALU.mult)
        g = gtok[:, ti * 8:(ti + 1) * 8]
        ts(ctx, "DVE", is1_, is1_, sic, ALU.mult)
        ts(ctx, "DVE", is2_, is2_, e2c, ALU.mult)
        tt(ctx, "DVE", g, is1_, is2_, ALU.add)
    if dbg is not None:
        ctx.dma("SYNC", dbg["gtok"].tv(), gtok)
        ctx.dma("SYNC", dbg["xm0"].tv(), xm[0])
    gbeb = [ctx.sbuf(f"l3f_gbe{i}_{tag}", [128, 8 * 512], BF16) for i in range(4)]
    gbet = [[gbeb[tc][:, e * 512:(e + 1) * 512] for tc in range(4)] for e in range(8)]
    grow = ctx.sbuf(f"l3f_grow_{tag}", [1, 512], F32)
    for e in range(8):
        for tc in range(4):
            gp = ctx.mm_open(ps[3])
            for c4 in range(4):
                gcolv = gtok[:, (tc * 4 + c4) * 8 + e:(tc * 4 + c4) * 8 + e + 1]
                gp.add(lambda e_, o=ps[3][0:1, 0:512], i=gcolv, id=ln.ident,
                       st=(c4 == 0), sp2=(c4 == 3), c4=c4: e_.matmul(
                           o[:, c4 * 128:(c4 + 1) * 128].ap(), i.ap(), id.ap(),
                           start=st, stop=sp2, is_transpose=True), [gcolv, ln.ident])
            gp.close()
            copy(ctx, "ACT", grow, ps[3][0:1, 0:512])
            mm_acc(ctx, ps[3][:, 0:512], [(ln.ones_1, grow)])
            copy(ctx, "ACT", gbet[e][tc], ps[3][:, 0:512])
    w1s = [ctx.sbuf(f"l3f_w1s{i}_{tag}", [128, 8 * 512], BF16) for i in range(2)]
    w3s = [ctx.sbuf(f"l3f_w3s{i}_{tag}", [128, 8 * 512], BF16) for i in range(2)]
    w2s = [ctx.sbuf(f"l3f_w2s{i}_{tag}", [128, 4 * 1024], BF16) for i in range(2)]
    s1b = ctx.sbuf(f"l3f_s1b_{tag}", [128, 4 * 512], BF16)
    h3b = ctx.sbuf(f"l3f_h3b_{tag}", [128, 4 * 512], BF16)
    si = 0
    units = [("s", None)] + [("e", e) for e in range(8)]
    for unm, e in units:
        for fg in range(2):
            sl1, sl3, sl2 = w1s[si % 2], w3s[si % 2], w2s[si % 2]
            si += 1
            for srcn, slot in (("W1", sl1), ("W3", sl3)):
                dsrc = w["s" + srcn + "T"].apobj if unm == "s" else w["e" + srcn + "T"].apobj

                def eml(e_, s=slot, d=dsrc, e=e, fg=fg, unm=unm):
                    base = d if unm == "s" else d[e]
                    return e_.dma_start(out=s.ap(), in_=base[:, fg * 512:(fg + 1) * 512]
                                        .rearrange("(k p) c -> p k c", p=128))
                ctx.dma("POOL", slot, (w["s" + srcn + "T"] if unm == "s" else w["e" + srcn + "T"]).tv(), emit=eml)
            dsrc2 = w["sW2T"].apobj if unm == "s" else w["eW2T"].apobj

            def eml2(e_, s=sl2, d=dsrc2, e=e, fg=fg, unm=unm):
                base = d if unm == "s" else d[e]
                return e_.dma_start(out=s.ap(), in_=base[fg * 512:(fg + 1) * 512, :]
                                    .rearrange("(k p) c -> p k c", p=128))
            ctx.dma("POOL", sl2, (w["sW2T"] if unm == "s" else w["eW2T"]).tv(), emit=eml2)
            for tc in range(4):
                t0 = tc * 512
                for fsub in range(4):
                    pp1 = ps[4 + fsub % 2]
                    mm_acc(ctx, pp1, [(sl1[:, k * 512 + fsub * 128:k * 512 + (fsub + 1) * 128],
                                       xm[k][:, t0:t0 + 512]) for k in range(8)])
                    act(ctx, s1b[:, fsub * 512:(fsub + 1) * 512], pp1, AF.Silu)
                    pp3 = ps[6 + fsub % 2]
                    mm_acc(ctx, pp3, [(sl3[:, k * 512 + fsub * 128:k * 512 + (fsub + 1) * 128],
                                       xm[k][:, t0:t0 + 512]) for k in range(8)])
                    if unm == "e":
                        stt(ctx, h3b[:, fsub * 512:(fsub + 1) * 512], pp3, None, ALU.bypass,
                            gbet[e][tc], ALU.mult)
                    else:
                        copy(ctx, "DVE", h3b[:, fsub * 512:(fsub + 1) * 512], pp3)
                    tt(ctx, "DVE", s1b[:, fsub * 512:(fsub + 1) * 512],
                       s1b[:, fsub * 512:(fsub + 1) * 512],
                       h3b[:, fsub * 512:(fsub + 1) * 512], ALU.mult)
                for ho in range(8):
                    pp = ps[ho % 2]
                    mm_acc(ctx, pp, [(sl2[:, k * 1024 + ho * 128:k * 1024 + (ho + 1) * 128],
                                      s1b[:, k * 512:(k + 1) * 512]) for k in range(4)])
                    yv = xs[ho][:, t0:t0 + 512]
                    stt(ctx, yv, pp, None, ALU.bypass, yv, ALU.add)
    ctx.free_to(mk_moe)
    xstg = ctx.sbuf(f"l3f_xstg_{tag}", [128, T], F32)
    for j in range(8):
        def emL(e, j=j, d=dxs, dst=xstg):
            return e.dma_start(out=dst.ap(), in_=d[j])
        ctx.dma("SYNC", xstg, xsave.tv(), emit=emL)
        tt(ctx, "DVE", xs[j], xs[j], xstg, ALU.add)


from concourse.bass_utils import run_bass_kernel_spmd


def _build_program():
    nc = bass.Bass("TRN2", target_bir_lowering=False, debug=False)
    c = Ctx(nc)
    xT = c.dram("xT", [2, 1024, 2048], F32, "ExternalInput")
    yT = c.dram("yT", [2, 1024, 2048], F32, "ExternalOutput")
    w0a = {"Mq": c.dram("l0_Mq", [1024, 512], F32, "ExternalInput"),
           "Mk": c.dram("l0_Mk", [1024, 512], F32, "ExternalInput"),
           "Mv": c.dram("l0_Mv", [1024, 1024], F32, "ExternalInput")}
    w0f = {"W1": c.dram("l0f_W1", [1024, 4096], F32, "ExternalInput"),
           "b1": c.dram("l0f_b1", [4096], F32, "ExternalInput"),
           "qp": c.dram("l0f_qp", [4096], F32, "ExternalInput"),
           "qwr": c.dram("l0f_qwr", [4096], F32, "ExternalInput"),
           "qwi": c.dram("l0f_qwi", [4096], F32, "ExternalInput"),
           "W2": c.dram("l0f_W2", [4096, 1024], F32, "ExternalInput"),
           "b2": c.dram("l0f_b2", [1024], F32, "ExternalInput")}
    w1a = {**{nm: c.dram(f"l1a_{nm}", [1024, 1024], F32, "ExternalInput")
              for nm in ("WqT", "WkT", "WvT", "Wq2T", "Wk2T", "WoT")},
           "mixture": c.dram("l1a_mixture", [1, 1], F32, "ExternalInput"),
           "qs": c.dram("l1a_qs", [1, 1], F32, "ExternalInput")}
    w1f = {nm: c.dram(f"l1f_{nm}", list(shp), F32, "ExternalInput") for nm, shp in (
        ("cn_w1", (1024, 1024)), ("cn_b1", (1024,)), ("cn_w2", (1024, 512)),
        ("cn_b2", (512,)), ("cn_w3", (512, 256)), ("cn_b3", (256,)),
        ("cn_w4", (256, 1)), ("cn_b4", (1, 1)), ("gc_w1", (1024, 512)),
        ("gc_b1", (512,)), ("gc_w2", (512, 256)), ("gc_b2", (256,)),
        ("gc_w3", (256, 1)), ("gc_b3", (1, 1)), ("ar", (1, 1)))}
    w2a = {**{nm: c.dram(f"l2a_{nm}", [1024, 1024], F32, "ExternalInput")
              for nm in ("WqT", "WkT", "WvT", "WoT")},
           "bq": c.dram("l2a_bq", [16, 64], F32, "ExternalInput"),
           "bk": c.dram("l2a_bk", [16, 64], F32, "ExternalInput"),
           "bv": c.dram("l2a_bv", [1, 1024], F32, "ExternalInput"),
           "bo": c.dram("l2a_bo", [1024], F32, "ExternalInput")}
    w2f = {"gwT": c.dram("l2f_gwT", [1024, 8], F32, "ExternalInput"),
           "w1T": c.dram("l2f_w1T", [8, 1024, 3584], F32, "ExternalInput"),
           "w3T": c.dram("l2f_w3T", [8, 1024, 3584], F32, "ExternalInput"),
           "w2T": c.dram("l2f_w2T", [8, 3584, 1024], F32, "ExternalInput")}
    w3a = {"ipwT": c.dram("l3a_ipwT", [1024, 3072], F32, "ExternalInput"),
           "ipb48": c.dram("l3a_ipb48", [48, 64], F32, "ExternalInput"),
           "ipbrow": c.dram("l3a_ipbrow", [1, 3072], F32, "ExternalInput"),
           "owT": c.dram("l3a_owT", [1024, 1024], F32, "ExternalInput"),
           "ob": c.dram("l3a_ob", [1024], F32, "ExternalInput"),
           "cosb": c.dram("l3a_cosb", [128, 2048], F32, "ExternalInput"),
           "sinb2": c.dram("l3a_sinb2", [128, 2048], F32, "ExternalInput"),
           "permT": c.dram("l3a_permT", [128, 128], F32, "ExternalInput"),
           "maskadd": c.dram("l3a_maskadd", [2048, 2048], BF16, "ExternalInput")}
    w3f = {"g1": c.dram("l3f_g1", [1024], F32, "ExternalInput"),
           "g2": c.dram("l3f_g2", [1024], F32, "ExternalInput"),
           "WqT": c.dram("l3f_WqT", [1024, 1024], F32, "ExternalInput"),
           "WkcT": c.dram("l3f_WkcT", [1024, 256], F32, "ExternalInput"),
           "WvcT": c.dram("l3f_WvcT", [1024, 256], F32, "ExternalInput"),
           "WoT": c.dram("l3f_WoT", [1024, 1024], F32, "ExternalInput"),
           "WrT": c.dram("l3f_WrT", [1024, 8], F32, "ExternalInput"),
           "ebrow": c.dram("l3f_ebrow", [1, 8], F32, "ExternalInput"),
           "sW1T": c.dram("l3f_sW1T", [1024, 1024], F32, "ExternalInput"),
           "sW2T": c.dram("l3f_sW2T", [1024, 1024], F32, "ExternalInput"),
           "sW3T": c.dram("l3f_sW3T", [1024, 1024], F32, "ExternalInput"),
           "eW1T": c.dram("l3f_eW1T", [8, 1024, 1024], F32, "ExternalInput"),
           "eW2T": c.dram("l3f_eW2T", [8, 1024, 1024], F32, "ExternalInput"),
           "eW3T": c.dram("l3f_eW3T", [8, 1024, 1024], F32, "ExternalInput")}
    ps = [c.psum(f"ps{i}", [128, 512]) for i in range(8)]
    xs = [c.sbuf(f"x{j}", [128, 2048], F32) for j in range(8)]
    ln = LNScratch(c, 2048)
    for b in range(2):
        for j in range(8):
            d = xT.apobj

            def emit(e, j=j, b=b, d=d):
                return e.dma_start(out=xs[j].ap(), in_=d[b, j * 128:(j + 1) * 128, :])
            c.dma("SYNC", xs[j], xT.tv(), emit=emit)
        for fn, wd, kw in ((emit_l0_attn, w0a, {}), (emit_l0_ffn, w0f, {}),
                           (emit_l1_attn, w1a, {}), (emit_l1_ffn, w1f, {}),
                           (emit_l2_attn, w2a, {}), (emit_l2_ffn, w2f, {}),
                           (emit_l3_attn, w3a, {})):
            mk = c.mark()
            fn(c, xs, wd, ln, ps, f"{fn.__name__}_b{b}", **kw)
            c.free_to(mk)
        mk = c.mark()

        def load_x_tile(cx, j, dst, b=b, xT=xT):
            d = xT.apobj

            def em(e, j=j, b=b, d=d, dst=dst):
                return e.dma_start(out=dst.ap(), in_=d[b, j * 128:(j + 1) * 128, :])
            cx.dma("SYNC", dst, xT.tv(), emit=em)
        emit_l3_ffn(c, xs, w3f, ln, ps, f"l3f_b{b}", load_x_tile=load_x_tile)
        c.free_to(mk)
        for j in range(8):
            d = yT.apobj

            def emit(e, j=j, b=b, d=d):
                return e.dma_start(out=d[b, j * 128:(j + 1) * 128, :], in_=xs[j].ap())
            c.dma("SYNC", yT.tv()[j:j + 1, :], xs[j], emit=emit)
    c.final_wait("SYNC", [yT])
    c.build()
    c.close()
    return nc


def _shared_inputs(ins):
    import ml_dtypes
    d = {}
    for k in ("L0_attn_Mq", "L0_attn_Mk", "L0_attn_Mv"):
        d[k.replace("L0_attn_", "l0_")] = ins[k]
    for a, b2 in (("W1", "W1"), ("b1", "b1"), ("quantum_phases", "qp"),
                  ("quantum_weights_real", "qwr"), ("quantum_weights_imag", "qwi"),
                  ("W2", "W2"), ("b2", "b2")):
        d["l0f_" + b2] = ins["L0_ffn_" + a]
    for nm in ("Wq", "Wk", "Wv", "Wq2", "Wk2", "Wo"):
        d[f"l1a_{nm}T"] = np.ascontiguousarray(ins[f"L1_attn_{nm}"].T)
    d["l1a_mixture"] = np.asarray(ins["L1_attn_mixture"], np.float32).reshape(1, 1)
    d["l1a_qs"] = np.asarray(ins["L1_attn_quartet_scale"], np.float32).reshape(1, 1)
    for i in range(1, 5):
        d[f"l1f_cn_w{i}"] = ins[f"L1_ffn_cn_w{i}"]
        d[f"l1f_cn_b{i}"] = ins[f"L1_ffn_cn_b{i}"].reshape(1, 1) if i == 4 else ins[f"L1_ffn_cn_b{i}"]
    for i in range(1, 4):
        d[f"l1f_gc_w{i}"] = ins[f"L1_ffn_gc_w{i}"]
        d[f"l1f_gc_b{i}"] = ins[f"L1_ffn_gc_b{i}"].reshape(1, 1) if i == 3 else ins[f"L1_ffn_gc_b{i}"]
    d["l1f_ar"] = np.asarray(ins["L1_ffn_adaptive_range"], np.float32).reshape(1, 1)
    for nm in ("Wq", "Wk", "Wv", "Wo"):
        d[f"l2a_{nm}T"] = np.ascontiguousarray(ins[f"L2_attn_{nm}"].T)
    d["l2a_bq"] = ins["L2_attn_bq"].reshape(16, 64)
    d["l2a_bk"] = ins["L2_attn_bk"].reshape(16, 64)
    d["l2a_bv"] = ins["L2_attn_bv"].reshape(1, 1024)
    d["l2a_bo"] = ins["L2_attn_bo"]
    d["l2f_gwT"] = np.ascontiguousarray(ins["L2_ffn_gate_w"].T)
    d["l2f_w1T"] = np.ascontiguousarray(ins["L2_ffn_w1"].transpose(0, 2, 1))
    d["l2f_w3T"] = np.ascontiguousarray(ins["L2_ffn_w3"].transpose(0, 2, 1))
    d["l2f_w2T"] = np.ascontiguousarray(ins["L2_ffn_w2"].transpose(0, 2, 1))
    d["l3a_ipwT"] = np.ascontiguousarray(ins["L3_attn_in_proj_w"].T)
    d["l3a_ipb48"] = ins["L3_attn_in_proj_b"].reshape(48, 64)
    d["l3a_ipbrow"] = ins["L3_attn_in_proj_b"].reshape(1, 3072)
    d["l3a_owT"] = np.ascontiguousarray(ins["L3_attn_out_w"].T)
    d["l3a_ob"] = ins["L3_attn_out_b"]
    t = np.arange(2048, dtype=np.float64)
    pvec = np.arange(128)
    i_of_p = (pvec % 64) // 2
    inv = 1.0 / 10000.0 ** (np.arange(0, 64, 2, dtype=np.float64) / 64.0)
    fidx = (2 * i_of_p) % 32
    fr = t[None, :] * inv[fidx][:, None]
    d["l3a_cosb"] = np.cos(fr).astype(np.float32)
    sgn = np.where(pvec % 2 == 0, -1.0, 1.0)
    d["l3a_sinb2"] = (np.sin(fr) * sgn[:, None]).astype(np.float32)
    perm = np.zeros((128, 128), np.float32)
    perm[np.arange(128), np.arange(128) ^ 1] = 1.0
    d["l3a_permT"] = perm
    ii, jj = np.mgrid[0:2048, 0:2048]
    allowed = (np.abs(ii - jj) <= 128) | (jj % 256 == 0) | (jj == 0)
    d["l3a_maskadd"] = np.where(allowed, 0.0, -1e30).astype(ml_dtypes.bfloat16)
    d["l3f_g1"] = ins["L3_ffn_g1"]
    d["l3f_g2"] = ins["L3_ffn_g2"]
    for nm in ("Wq", "Wkc", "Wvc", "Wo", "Wr", "sW1", "sW2", "sW3"):
        d[f"l3f_{nm}T"] = np.ascontiguousarray(ins[f"L3_ffn_{nm}"].T)
    d["l3f_ebrow"] = ins["L3_ffn_expert_bias"].reshape(1, 8)
    for nm in ("eW1", "eW2", "eW3"):
        d[f"l3f_{nm}T"] = np.ascontiguousarray(ins[f"L3_ffn_{nm}"].transpose(0, 2, 1))
    return d


def kernel(**inputs):
    ins = {k: np.asarray(v) for k, v in inputs.items()}
    nc = _build_program()
    shared = _shared_inputs(ins)
    x = ins["L0_attn_x"]
    in_maps = []
    for core in range(8):
        m = dict(shared)
        m["xT"] = np.ascontiguousarray(x[2 * core:2 * core + 2].transpose(0, 2, 1))
        in_maps.append(m)
    res = run_bass_kernel_spmd(nc, in_maps, core_ids=list(range(8)))
    out = np.empty((16, 2048, 1024), np.float32)
    for core in range(8):
        out[2 * core:2 * core + 2] = res.results[core]["yT"].transpose(0, 2, 1)
    return out


# version 04 (final)
# speedup vs baseline: 1.5008x; 1.0070x over previous
import contextlib
import numpy as np
import concourse.bass as bass
import concourse.mybir as mybir

F32 = mybir.dt.float32
BF16 = mybir.dt.bfloat16
I32 = mybir.dt.int32
AF = mybir.ActivationFunctionType
ALU = mybir.AluOpType
AX = mybir.AxisListType

QUEUES = ("PE", "ACT", "DVE", "POOL", "SYNC")
_DT_SIZE = {F32: 4, BF16: 2, I32: 4, mybir.dt.uint8: 1, mybir.dt.int16: 2,
            mybir.dt.uint32: 4, mybir.dt.float16: 2, mybir.dt.uint16: 2}


class Access:
    __slots__ = ("p0", "p1", "b0", "b1", "deps")

    def __init__(self, p0, p1, b0, b1, deps):
        self.p0, self.p1, self.b0, self.b1 = p0, p1, b0, b1
        self.deps = deps


class Buf:
    def __init__(self, name, apobj, kind, dtype, shape):
        self.name = name
        self.apobj = apobj
        self.kind = kind
        self.dtype = dtype
        self.shape = shape
        self.writes = []
        self.reads = []
        self.stream_sem = None
        self.stream_count = 0
        self.stream_deps = None
        self.batch_open = False

    def tv(self):
        if self.kind == "dram":
            P = self.shape[0] if len(self.shape) > 1 else 1
            E = int(np.prod(self.shape[1:])) if len(self.shape) > 1 else self.shape[0]
            return TV(self, 0, P, 0, E)
        return TV(self, 0, self.shape[0], 0, int(np.prod(self.shape[1:])))


class TV:
    __slots__ = ("buf", "p0", "p1", "e0", "e1", "cast")

    def __init__(self, buf, p0, p1, e0, e1, cast=None):
        self.buf, self.p0, self.p1, self.e0, self.e1 = buf, p0, p1, e0, e1
        self.cast = cast

    @property
    def P(self):
        return self.p1 - self.p0

    @property
    def N(self):
        return self.e1 - self.e0

    def ap(self):
        h = self.buf.apobj
        if self.buf.kind == "dram":
            a = h
        elif self.cast is not None:
            a = h.bitcast(self.cast).ap()
        else:
            a = h.ap()
        return a[self.p0:self.p1, self.e0:self.e1]

    def bf16(self):
        assert self.cast is None and self.buf.dtype == F32
        return TV(self.buf, self.p0, self.p1, self.e0 * 2, self.e1 * 2, cast=BF16)

    def __getitem__(self, idx):
        ps, es = idx if isinstance(idx, tuple) else (idx, slice(None))
        if isinstance(ps, int):
            ps = slice(ps, ps + 1)
        if isinstance(es, int):
            es = slice(es, es + 1)
        p0, p1, _ = ps.indices(self.P)
        e0, e1, _ = es.indices(self.N)
        return TV(self.buf, self.p0 + p0, self.p0 + p1, self.e0 + e0,
                  self.e0 + e1, self.cast)

    def rect(self):
        if self.buf.kind != "sbuf":
            return (0, 128, 0, 1 << 40)
        s = _DT_SIZE[self.buf.dtype]
        return (self.p0, self.p1, self.e0 * s, self.e1 * s)


def _merge_into(dst: dict, deps: dict):
    for k, v in deps.items():
        if dst.get(k, -1) < v:
            dst[k] = v


def _prune(lst, maxlen=48):
    if len(lst) <= maxlen:
        return lst
    half = len(lst) // 2
    old, rest = lst[:half], lst[half:]
    p0 = min(a.p0 for a in old)
    p1 = max(a.p1 for a in old)
    b0 = min(a.b0 for a in old)
    b1 = max(a.b1 for a in old)
    deps = {}
    for a in old:
        _merge_into(deps, a.deps)
    return [Access(p0, p1, b0, b1, deps)] + rest


class OpRec:
    __slots__ = ("queue", "waits", "emit", "inc")

    def __init__(self, queue, waits, emit, inc):
        self.queue, self.waits, self.emit, self.inc = queue, waits, emit, inc


class _MMGroup:

    def __init__(self, ctx, psum_tv):
        self.ctx = ctx
        self.psum = psum_tv
        self.first = True
        self.last_val = None

    def add(self, emit_fn, reads):
        ctx = self.ctx
        deps = {}
        ctx._collect_deps(list(reads), [self.psum] if self.first else [], deps)
        waits = ctx._waits_for("PE", deps)
        ctx.qcount["PE"] += 1
        val = ctx.qcount["PE"]
        for tv in reads:
            p0, p1, b0, b1 = tv.rect()
            tv.buf.reads.append(Access(p0, p1, b0, b1, {"prog_PE": val}))
            tv.buf.reads = _prune(tv.buf.reads)
            tv.buf.batch_open = False
        ctx.ops.append(OpRec("PE", waits, emit_fn, ("prog_PE", 1)))
        self.first = False
        self.last_val = val

    def close(self):
        self.ctx._record_accesses([], [self.psum], ("prog_PE", self.last_val))


class Ctx:
    def __init__(self, nc, name="ctx"):
        self.nc = nc
        self.name = name
        self.ops = []
        self.qcount = {q: 0 for q in ("PE", "ACT", "DVE", "POOL")}
        self.clock = {q: {} for q in QUEUES}
        self.sem_names = ["prog_PE", "prog_ACT", "prog_DVE", "prog_POOL"]
        self.sems = {}
        self.bufs = {}
        self._cm = []
        self.n_waits = 0
        self.sem_pool = []

    def _new_sem(self, name):
        assert name not in self.sem_names, name
        self.sem_names.append(name)
        return name

    def sbuf(self, name, shape, dtype):
        cm = self.nc.sbuf_tensor(name, list(shape), dtype)
        t = cm.__enter__()
        self._cm.append((name, cm))
        b = Buf(name, t, "sbuf", dtype, tuple(shape))
        self.bufs[name] = b
        return b.tv()

    def psum(self, name, shape=(128, 512), dtype=F32):
        cm = self.nc.psum_tensor(name, list(shape), dtype)
        t = cm.__enter__()
        self._cm.append((name, cm))
        b = Buf(name, t, "psum", dtype, tuple(shape))
        self.bufs[name] = b
        return b.tv()

    def mark(self):
        return len(self._cm)

    def free_to(self, mark):
        self.barrier()
        while len(self._cm) > mark:
            name, cm = self._cm.pop()
            b = self.bufs.pop(name, None)
            if b is not None and b.stream_sem is not None:
                self.sem_pool.append((b.stream_sem, b.stream_count))
            cm.__exit__(None, None, None)

    def dram(self, name, shape, dtype, kind):
        t = self.nc.dram_tensor(name, list(shape), dtype, kind=kind)
        b = Buf(name, t.ap(), "dram", dtype, tuple(shape))
        self.bufs[name] = b
        return b

    def _collect_deps(self, reads, writes, deps):
        for tv in reads:
            p0, p1, b0, b1 = tv.rect()
            for a in tv.buf.writes:
                if not (a.p1 <= p0 or p1 <= a.p0 or a.b1 <= b0 or b1 <= a.b0):
                    _merge_into(deps, a.deps)
        for tv in writes:
            p0, p1, b0, b1 = tv.rect()
            for a in tv.buf.writes:
                if not (a.p1 <= p0 or p1 <= a.p0 or a.b1 <= b0 or b1 <= a.b0):
                    _merge_into(deps, a.deps)
            for a in tv.buf.reads:
                if not (a.p1 <= p0 or p1 <= a.p0 or a.b1 <= b0 or b1 <= a.b0):
                    _merge_into(deps, a.deps)

    def _waits_for(self, queue, deps):
        cl = self.clock[queue]
        waits = []
        for sem, val in deps.items():
            if cl.get(sem, 0) >= val:
                continue
            cl[sem] = val
            waits.append((sem, val))
        self.n_waits += len(waits)
        return waits

    def _record_accesses(self, reads, writes, dep, is_dma=False):
        depd = dep if isinstance(dep, dict) else {dep[0]: dep[1]}
        for tv in reads:
            p0, p1, b0, b1 = tv.rect()
            tv.buf.reads.append(Access(p0, p1, b0, b1, depd))
            tv.buf.reads = _prune(tv.buf.reads)
            tv.buf.batch_open = False
        for tv in writes:
            p0, p1, b0, b1 = tv.rect()
            tv.buf.writes = [a for a in tv.buf.writes
                             if not (p0 <= a.p0 and a.p1 <= p1 and b0 <= a.b0 and a.b1 <= b1)]
            tv.buf.reads = [a for a in tv.buf.reads
                            if not (p0 <= a.p0 and a.p1 <= p1 and b0 <= a.b0 and a.b1 <= b1)]
            tv.buf.writes.append(Access(p0, p1, b0, b1, depd))
            if not is_dma:
                tv.buf.batch_open = False

    def op(self, queue, emit, reads=(), writes=()):
        reads, writes = list(reads), list(writes)
        deps = {}
        self._collect_deps(reads, writes, deps)
        waits = self._waits_for(queue, deps)
        self.qcount[queue] += 1
        dep = (f"prog_{queue}", self.qcount[queue])
        self._record_accesses(reads, writes, dep)
        self.ops.append(OpRec(queue, waits, emit, (dep[0], 1)))

    def mm_group(self, psum_tv, items):
        deps = {}
        all_reads = []
        for _, rds in items:
            all_reads.extend(rds)
        self._collect_deps(all_reads, [psum_tv], deps)
        waits = self._waits_for("PE", deps)
        self.qcount["PE"] += 1
        dep = ("prog_PE", self.qcount["PE"])
        self._record_accesses(all_reads, [psum_tv], dep)
        n = len(items)
        for i, (fn, _) in enumerate(items):
            self.ops.append(OpRec("PE", waits if i == 0 else [],
                                  fn, (dep[0], 1) if i == n - 1 else None))

    def mm_seq(self, psums, seq):
        deps = {}
        all_reads = []
        for _, rds, _, _ in seq:
            all_reads.extend(rds)
        self._collect_deps(all_reads, list(psums), deps)
        waits = self._waits_for("PE", deps)
        incs = {}
        for fn, rds, pi, last in seq:
            if last:
                self.qcount["PE"] += 1
                incs[pi] = self.qcount["PE"]
        for tv in all_reads:
            p0, p1, b0, b1 = tv.rect()
            tv.buf.reads.append(Access(p0, p1, b0, b1, {"prog_PE": self.qcount["PE"]}))
            tv.buf.reads = _prune(tv.buf.reads)
            tv.buf.batch_open = False
        for pi, tv in enumerate(psums):
            self._record_accesses([], [tv], ("prog_PE", incs[pi]))
        first = True
        for fn, rds, pi, last in seq:
            self.ops.append(OpRec("PE", waits if first else [], fn,
                                  ("prog_PE", 1) if last else None))
            first = False

    def mm_open(self, psum_tv):
        return _MMGroup(self, psum_tv)

    def dma(self, queue, dst_tv, src_tv, emit=None):
        db = dst_tv.buf
        if db.stream_sem is None:
            if self.sem_pool:
                db.stream_sem, db.stream_count = self.sem_pool.pop()
            else:
                db.stream_sem = self._new_sem(f"dma_{db.name}")
            db.stream_deps = {}
        deps = {}
        self._collect_deps([src_tv], [dst_tv], deps)
        if not db.batch_open:
            if db.stream_count:
                deps[db.stream_sem] = max(deps.get(db.stream_sem, 0), db.stream_count)
            db.batch_open = True
        waits = self._waits_for(queue, deps)
        db.stream_count += 16
        db.stream_deps[db.stream_sem] = db.stream_count
        self._record_accesses([src_tv], [dst_tv], db.stream_deps, is_dma=True)
        if emit is None:
            d, s = dst_tv, src_tv

            def emit(eng):
                return eng.dma_start(out=d.ap(), in_=s.ap())
        self.ops.append(OpRec(queue, waits, emit, (db.stream_sem, 16)))

    def barrier(self):
        deps = {}
        for q in ("PE", "ACT", "DVE", "POOL"):
            if self.qcount[q]:
                deps[f"prog_{q}"] = self.qcount[q]
        for b in self.bufs.values():
            if b.stream_sem is not None:
                deps[b.stream_sem] = b.stream_count
        for q in QUEUES:
            waits = self._waits_for(q, dict(deps))
            if waits:
                self.ops.append(OpRec(q, waits, None, None))

    def final_wait(self, queue, bufs):
        waits = []
        for b in bufs:
            if b.stream_sem is not None:
                waits.append((b.stream_sem, b.stream_count))
        self.ops.append(OpRec(queue, waits, None, None))

    def build(self):
        st = contextlib.ExitStack()
        for name in self.sem_names:
            self.sems[name] = st.enter_context(self.nc.semaphore(name))
        per_q = {q: [] for q in QUEUES}
        for rec in self.ops:
            per_q[rec.queue].append(rec)
        sec_name = {"PE": "tensor", "ACT": "scalar", "DVE": "vector",
                    "POOL": "gpsimd", "SYNC": "sync"}
        with self.nc.Block(self.name) as block:
            for q in QUEUES:
                recs = per_q[q]
                if not recs:
                    continue

                def body(e, recs=recs):
                    for rec in recs:
                        for sem, val in rec.waits:
                            e.wait_ge(self.sems[sem], val)
                        if rec.emit is not None:
                            inst = rec.emit(e)
                            if rec.inc is not None:
                                inst.then_inc(self.sems[rec.inc[0]], rec.inc[1])

                getattr(block, sec_name[q])(body)
        st.close()

    def close(self):
        for _, cm in reversed(self._cm):
            cm.__exit__(None, None, None)


"""Common emission helpers on top of fw.Ctx for the mk_seq_preln_d4 kernel.

Conventions:
 - feature-major activation: list of d-tile TVs [128, T] (tile j = features j*128..)
 - weights host-pre-transposed where needed; streamed fp32->bf16 via POOL cast DMA.
 - LN: xm = x - mean (bf16, 1 DVE pass per tile); rstd (rb, [128,T] fp32 broadcast)
   folded into projection evacuations via scalar_tensor_tensor.
"""

PI = float(np.pi)


def f32(v):
    return np.float32(v)



def ts(ctx, q, out, in0, scalar1, op0, scalar2=None, op1=None, accum_out=None):
    s1 = scalar1.ap() if isinstance(scalar1, TV) else scalar1
    s2 = scalar2.ap() if isinstance(scalar2, TV) else scalar2
    reads = [in0] + [s for s in (scalar1, scalar2) if isinstance(s, TV)]
    writes = [out]
    if accum_out is not None:
        writes = writes + [accum_out]

    def emit(e, out=out, in0=in0, s1=s1, s2=s2, op0=op0, op1=op1, ao=accum_out):
        kw = {}
        if op1 is not None:
            kw["op1"] = op1
        if ao is not None:
            kw["accum_out"] = ao.ap()
        return e.tensor_scalar(out=out.ap(), in0=in0.ap(), scalar1=s1, scalar2=s2,
                               op0=op0, **kw)
    ctx.op(q, emit, reads=reads, writes=writes)


def tt(ctx, q, out, a, b, op):
    ctx.op(q, lambda e, out=out, a=a, b=b, op=op: e.tensor_tensor(
        out=out.ap(), in0=a.ap(), in1=b.ap(), op=op), reads=[a, b], writes=[out])


def stt(ctx, out, in0, scalar, op0, in1, op1, accum_out=None):
    s = scalar.ap() if isinstance(scalar, TV) else (0.0 if scalar is None else scalar)
    reads = [in0, in1] + ([scalar] if isinstance(scalar, TV) else [])
    writes = [out] + ([accum_out] if accum_out is not None else [])

    def emit(e, out=out, in0=in0, s=s, in1=in1, op0=op0, op1=op1, ao=accum_out):
        kw = {"accum_out": ao.ap()} if ao is not None else {}
        return e.scalar_tensor_tensor(out=out.ap(), in0=in0.ap(), scalar=s,
                                      in1=in1.ap(), op0=op0, op1=op1, **kw)
    ctx.op("DVE", emit, reads=reads, writes=writes)


def act(ctx, out, in_, func, bias=0.0, scale=1.0, accum_out=None):
    b = bias.ap() if isinstance(bias, TV) else f32(bias)
    s = scale.ap() if isinstance(scale, TV) else f32(scale)
    reads = [in_] + [v for v in (bias, scale) if isinstance(v, TV)]
    writes = [out] + ([accum_out] if accum_out is not None else [])

    def emit(e, out=out, in_=in_, func=func, b=b, s=s, ao=accum_out):
        kw = {"accum_out": ao.ap()} if ao is not None else {}
        return e.activation(out=out.ap(), in_=in_.ap(), func=func, bias=b, scale=s, **kw)
    ctx.op("ACT", emit, reads=reads, writes=writes)


def copy(ctx, q, out, in_):
    if q == "ACT":
        ctx.op(q, lambda e, o=out, i=in_: e.copy(o.ap(), i.ap()), reads=[in_], writes=[out])
    else:
        ctx.op(q, lambda e, o=out, i=in_: e.tensor_copy(o.ap(), i.ap()),
               reads=[in_], writes=[out])


def memset(ctx, q, out, val):
    ctx.op(q, lambda e, o=out, v=val: e.memset(o.ap(), v), writes=[out])


def reduce(ctx, q, out, in_, op, negate=False):
    ctx.op(q, lambda e, o=out, i=in_, op=op, n=negate: e.tensor_reduce(
        out=o.ap(), in_=i.ap(), axis=AX.X, op=op, negate=n), reads=[in_], writes=[out])


def recip(ctx, out, in_):
    ctx.op("DVE", lambda e, o=out, i=in_: e.reciprocal(o.ap(), i.ap()),
           reads=[in_], writes=[out])


def transpose(ctx, psum_out, in_, ident):
    ctx.mm_group(psum_out, [(lambda e, o=psum_out, i=in_, id=ident: e.transpose(
        o.ap(), i.ap(), id.ap()), [in_, ident])])


def _affsel_emit(e, o, i, p, c, f, b, m):
    oa = o.ap()
    ia = i.ap()
    return e.affine_select(out=oa, in_=ia, pattern=p, compare_op=c, fill=f,
                           base=b, channel_multiplier=m)


def affsel(ctx, out, in_, pattern, cmp, fill, base, chmul):
    ctx.op("POOL", lambda e, o=out, i=in_, p=pattern, c=cmp, f=fill, b=base, m=chmul:
           _affsel_emit(e, o, i, p, c, f, b, m),
           reads=[in_], writes=[out])


def mm_acc(ctx, psum_tv, pairs):
    items = []
    n = len(pairs)
    for i, (lt, rt) in enumerate(pairs):
        st, sp = i == 0, i == n - 1
        items.append((lambda e, lt=lt, rt=rt, st=st, sp=sp, o=psum_tv: e.matmul(
            o.ap(), lt.ap(), rt.ap(), start=st, stop=sp), [lt, rt]))
    ctx.mm_group(psum_tv, items)



def load_full_bf16(ctx, name, dram_buf, R, C, q="POOL"):
    nk = (R + 127) // 128
    sb = ctx.sbuf(name, [128, nk * C], BF16)
    d = dram_buf.apobj
    tvs = []
    for ch in range(nk):
        r0, r1 = ch * 128, min(R, (ch + 1) * 128)
        tv = sb[:r1 - r0, ch * C:(ch + 1) * C]

        def emit(e, tv=tv, d=d, r0=r0, r1=r1):
            return e.dma_start(out=tv.ap(), in_=d[r0:r1, :])
        ctx.dma(q, tv, dram_buf.tv(), emit=emit)
        tvs.append(tv)
    return tvs


class WSlots:

    def __init__(self, ctx, name, nslots, npart, ncols):
        self.ctx = ctx
        self.npart, self.ncols = npart, ncols
        self.slots = [ctx.sbuf(f"{name}_s{i}", [npart, ncols], BF16) for i in range(nslots)]
        self.i = 0

    def load(self, dram_buf, r0, nrows, c0, ncols, pre=None):
        tv = self.slots[self.i][:nrows, :ncols]
        self.i = (self.i + 1) % len(self.slots)
        d = dram_buf.apobj

        def emit(e, tv=tv, d=d, r0=r0, nrows=nrows, c0=c0, ncols=ncols, pre=pre):
            src = d[pre] if pre is not None else d
            return e.dma_start(out=tv.ap(), in_=src[r0:r0 + nrows, c0:c0 + ncols])
        self.ctx.dma("POOL", tv, dram_buf.tv(), emit=emit)
        return tv


def load_col_fp32(ctx, name, dram_buf, R, ln, ps, q="SYNC"):
    nk = (R + 127) // 128
    stage = ctx.sbuf(name + "_st", [nk, 128], F32)
    sb = ctx.sbuf(name, [128, nk], F32)
    d = dram_buf.apobj

    def emit(e, stage=stage, d=d, nk=nk):
        return e.dma_start(out=stage.ap(), in_=d.rearrange("(k p) -> k p", p=128))
    ctx.dma(q, stage, dram_buf.tv(), emit=emit)
    transpose(ctx, ps[:, 0:nk], stage, ln.ident[0:nk, 0:nk])
    copy(ctx, "ACT", sb, ps[:, 0:nk])
    return sb



class LNScratch:
    def __init__(self, ctx, T):
        self.T = T
        self.stat = ctx.sbuf("ln_stat", [1, 1024], F32)
        self.rb = ctx.sbuf("ln_rb", [128, T], F32)
        self.xsq = ctx.sbuf("ln_xsq", [128, 2 * 512], BF16)
        self.ones_k = ctx.sbuf("ln_ones_k", [128, 1], BF16)
        self.ones_1 = ctx.sbuf("ln_ones_1", [1, 128], F32)
        self.ident = ctx.sbuf("g_ident", [128, 128], F32)
        self.identb = ctx.sbuf("g_identb", [128, 128], BF16)
        self.cmask = ctx.sbuf("g_cmask", [128, 128], F32)
        memset(ctx, "POOL", self.cmask, 0.0)
        affsel(ctx, self.cmask, self.cmask, [[-1, 128]], ALU.is_ge, -1e30, 0, 1)
        memset(ctx, "DVE", self.ones_k, 1.0)
        memset(ctx, "DVE", self.ones_1, 1.0)
        memset(ctx, "POOL", self.ident, 1.0)
        affsel(ctx, self.ident, self.ident, [[-1, 128]], ALU.is_equal, 0.0, 0, 1)
        copy(ctx, "DVE", self.identb, self.ident)


def emit_ln(ctx, xs, tok0, T, ln, ps_a, ps_b, xm, mode="ln", pre_col=None, eps=1e-5):
    nd = len(xs)
    D = nd * 128
    for c in range((T + 511) // 512):
        c0, c1 = c * 512, min(T, (c + 1) * 512)
        W = c1 - c0
        mean = ln.stat[:, 0:W]
        sd = ln.stat[:, 512:512 + W]
        g2 = ctx.mm_open(ps_b[0:1, 0:W])
        for j in range(nd):
            xc = xs[j][:, tok0 + c0:tok0 + c1]
            xq = ln.xsq[:, (j % 2) * 512:(j % 2) * 512 + W]
            if mode == "rms":
                if pre_col is not None:
                    ts(ctx, "DVE", xm[j][:, c0:c1], xc, pre_col[j], ALU.mult)
                else:
                    copy(ctx, "DVE", xm[j][:, c0:c1], xc)
                act(ctx, xq, xc, AF.Square)
            else:
                copy(ctx, "DVE", xm[j][:, c0:c1], xc)
                act(ctx, xq, xm[j][:, c0:c1], AF.Square)
            g2.add(lambda e, o=ps_b[0:1, 0:W], l=ln.ones_k, r=xq, st=(j == 0), sp=(j == nd - 1):
                   e.matmul(o.ap(), l.ap(), r.ap(), start=st, stop=sp), [ln.ones_k, xq])
        g2.close()
        ts(ctx, "DVE", sd, ps_b[0:1, 0:W], 1.0 / D, ALU.mult)
        if mode == "ln":
            mm_acc(ctx, ps_a[0:1, 0:W], [(ln.ones_k, xm[j][:, c0:c1]) for j in range(nd)])
            ts(ctx, "DVE", mean, ps_a[0:1, 0:W], 1.0 / D, ALU.mult)
            mm_acc(ctx, ps_a[:, 0:W], [(ln.ones_1, mean)])
            for j in range(nd):
                xc = xs[j][:, tok0 + c0:tok0 + c1]
                stt(ctx, xm[j][:, c0:c1], xc, None, ALU.bypass,
                    ps_a[:, 0:W], ALU.subtract)
            tt(ctx, "DVE", mean, mean, mean, ALU.mult)
            tt(ctx, "DVE", sd, sd, mean, ALU.subtract)
        act(ctx, sd, sd, AF.Sqrt, bias=eps)
        recip(ctx, sd, sd)
        mm_acc(ctx, ps_b[:, 0:W], [(ln.ones_1, sd)])
        copy(ctx, "ACT", ln.rb[:, c0:c1], ps_b[:, 0:W])


def emit_ln_inplace(ctx, xs, ln, ps_a, ps_b, T, eps=1e-5):
    nd = len(xs)
    D = nd * 128
    for c in range((T + 511) // 512):
        c0, c1 = c * 512, min(T, (c + 1) * 512)
        W = c1 - c0
        mean = ln.stat[:, 0:W]
        sd = ln.stat[:, 512:512 + W]
        g1 = ctx.mm_open(ps_a[0:1, 0:W])
        g2 = ctx.mm_open(ps_b[0:1, 0:W])
        for j in range(nd):
            xc = xs[j][:, c0:c1]
            xb = ln.xsq[:, (j % 2) * 512:(j % 2) * 512 + W]
            copy(ctx, "DVE", xb, xc)
            g1.add(lambda e, o=ps_a[0:1, 0:W], l=ln.ones_k, r=xb, st=(j == 0),
                   sp=(j == nd - 1): e.matmul(o.ap(), l.ap(), r.ap(), start=st, stop=sp),
                   [ln.ones_k, xb])
            act(ctx, xb, xb, AF.Square)
            g2.add(lambda e, o=ps_b[0:1, 0:W], l=ln.ones_k, r=xb, st=(j == 0),
                   sp=(j == nd - 1): e.matmul(o.ap(), l.ap(), r.ap(), start=st, stop=sp),
                   [ln.ones_k, xb])
        g1.close()
        g2.close()
        ts(ctx, "DVE", mean, ps_a[0:1, 0:W], 1.0 / D, ALU.mult)
        ts(ctx, "DVE", sd, ps_b[0:1, 0:W], 1.0 / D, ALU.mult)
        mm_acc(ctx, ps_a[:, 0:W], [(ln.ones_1, mean)])
        for j in range(nd):
            stt(ctx, xs[j][:, c0:c1], xs[j][:, c0:c1], None, ALU.bypass,
                ps_a[:, 0:W], ALU.subtract)
        tt(ctx, "DVE", mean, mean, mean, ALU.mult)
        tt(ctx, "DVE", sd, sd, mean, ALU.subtract)
        act(ctx, sd, sd, AF.Sqrt, bias=eps)
        recip(ctx, sd, sd)
        mm_acc(ctx, ps_b[:, 0:W], [(ln.ones_1, sd)])
        copy(ctx, "ACT", ln.rb[:, c0:c1], ps_b[:, 0:W])
        for j in range(nd):
            tt(ctx, "DVE", xs[j][:, c0:c1], xs[j][:, c0:c1], ln.rb[:, c0:c1], ALU.mult)



def proj_feat(ctx, out_tiles, wT_tiles, xm, rb, ps_banks, T, n_out,
              bias_col=None, func=None, out_chunks=512, extra=None):
    nf = n_out // 128
    nk = len(xm)
    bi = 0
    for f in range(nf):
        for c0 in range(0, T, out_chunks):
            c1 = min(T, c0 + out_chunks)
            ps = ps_banks[bi % len(ps_banks)][:, 0:c1 - c0]
            bi += 1
            mm_acc(ctx, ps, [(wT_tiles[j][:, f * 128:(f + 1) * 128], xm[j][:, c0:c1])
                             for j in range(nk)])
            dst = out_tiles[f][:, c0:c1]
            if rb is not None:
                stt(ctx, dst, ps, None, ALU.bypass, rb[:, c0:c1], ALU.mult)
            else:
                copy(ctx, "DVE", dst, ps)
            if func is not None:
                act(ctx, dst, dst, func,
                    bias=bias_col[:, f:f + 1] if bias_col is not None else 0.0)
            elif bias_col is not None:
                ts(ctx, "DVE", dst, dst, bias_col[:, f:f + 1], ALU.add)
            if extra is not None:
                extra(ctx, f, c0, c1, dst)



def softmax_rows(ctx, s_tv, p_out, scale, tmp_col, do_recip_mult=True):
    mx = tmp_col[:, 0:1]
    bias = tmp_col[:, 1:2]
    sm = tmp_col[:, 2:3]
    reduce(ctx, "DVE", mx, s_tv, ALU.max)
    ts(ctx, "DVE", bias, mx, -scale, ALU.mult)
    act(ctx, p_out, s_tv, AF.Exp, bias=bias, scale=scale, accum_out=sm)
    recip(ctx, sm, sm)
    if do_recip_mult:
        ts(ctx, "DVE", p_out, p_out, sm, ALU.mult)
    return sm


"""Block emitters for mk_seq_preln_d4. Each operates on one batch (T=2048 tokens)
of the feature-major fp32 residual stream xs (8 TVs [128, 2048])."""

T = 2048


class L0AttnScratch:
    def __init__(self, ctx, tag):
        self.xm = [ctx.sbuf(f"l0a_xm{j}_{tag}", [128, T], BF16) for j in range(8)]
        self.V = [ctx.sbuf(f"l0a_v{i}_{tag}", [128, 1024], BF16) for i in range(16)]
        self.A = ctx.sbuf(f"l0a_A_{tag}", [64, 8 * 256], BF16)
        self.B = ctx.sbuf(f"l0a_B_{tag}", [64, 8 * 256], BF16)
        self.s = ctx.sbuf(f"l0a_s_{tag}", [128, T], F32)
        self.p = ctx.sbuf(f"l0a_p_{tag}", [128, T], BF16)
        self.rtok = ctx.sbuf(f"l0a_rtok_{tag}", [128, 16], F32)
        self.col = ctx.sbuf(f"l0a_col_{tag}", [128, 8], F32)


def emit_l0_attn(ctx, xs, w, ln, ps, tag):
    sc = L0AttnScratch(ctx, tag)
    emit_ln(ctx, xs, 0, T, ln, ps[0], ps[1], sc.xm, mode="ln")
    for i in range(16):
        transpose(ctx, ps[2][:, 0:1], ln.rb[0:1, i * 128:(i + 1) * 128], ln.ones_1[0:1, 0:1])
        copy(ctx, "ACT", sc.rtok[:, i:i + 1], ps[2][:, 0:1])
    mq = load_full_bf16(ctx, f"l0_mq_{tag}", w["Mq"], 1024, 512)
    mk = load_full_bf16(ctx, f"l0_mk_{tag}", w["Mk"], 1024, 512)
    mv = load_full_bf16(ctx, f"l0_mv_{tag}", w["Mv"], 1024, 1024)
    for i in range(16):
        for n in range(2):
            pp = ps[2 + (i * 2 + n) % 2][:, 0:512]
            mm_acc(ctx, pp, [(sc.xm[j][:, i * 128:(i + 1) * 128], mv[j][:, n * 512:(n + 1) * 512])
                             for j in range(8)])
            ts(ctx, "DVE", sc.V[i][:, n * 512:(n + 1) * 512], pp, sc.rtok[:, i:i + 1], ALU.mult)
    for m in range(8):
        t0 = m * 256
        for (dst, wt) in ((sc.A, mq), (sc.B, mk)):
            for jb in range(8):
                pp = ps[2 + jb % 2][0:64, 0:256]
                mm_acc(ctx, pp, [(wt[j][:, jb * 64:(jb + 1) * 64],
                                  sc.xm[j][:, t0:t0 + 256]) for j in range(8)])
                d = dst[:, jb * 256:(jb + 1) * 256]
                stt(ctx, d, pp, None, ALU.bypass, ln.rb[0:64, t0:t0 + 256], ALU.mult)
        outg = [ctx.mm_open(ps[4 + gi]) for gi in range(4)]
        started = [False] * 4
        for jp in range(8):
            for tc in range(2):
                for jj in range(4):
                    pp = ps[2 + jj % 2]
                    g = ctx.mm_open(pp)
                    for j2 in range(2):
                        j = jj * 2 + j2
                        lt = sc.B[:, jp * 256 + tc * 128:jp * 256 + (tc + 1) * 128]
                        rt = sc.A[:, j * 256:(j + 1) * 256]
                        g.add(lambda e, o=pp, l=lt, r=rt, st=(j2 == 0), sp=(j2 == 1), j2=j2:
                              e.matmul(o[:, j2 * 256:(j2 + 1) * 256].ap(), l.ap(), r.ap(),
                                       start=st, stop=sp), [lt, rt])
                    g.close()
                    copy(ctx, "DVE", sc.s[:, jj * 512:(jj + 1) * 512], pp[:, 0:512])
                softmax_rows(ctx, sc.s, sc.p, 1.0 / 8.0, sc.col[:, 0:4])
                for hp in range(8):
                    lt = sc.V[hp * 2 + tc][:, jp * 128:(jp + 1) * 128]
                    rt = sc.p[:, hp * 256:(hp + 1) * 256]
                    gi = hp // 2
                    st = not started[gi]
                    outg[gi].add(lambda e, o=ps[4 + gi], l=lt, r=rt, st=st, c0=(hp % 2) * 256:
                                 e.matmul(o[:, c0:c0 + 256].ap(), l.ap(), r.ap(),
                                          start=st, stop=False), [lt, rt])
                    started[gi] = True
        for gi in range(4):
            outg[gi].close()
        for hp in range(8):
            xsl = xs[hp][:, t0:t0 + 256]
            stt(ctx, xsl, ps[4 + hp // 2][:, (hp % 2) * 256:(hp % 2) * 256 + 256],
                None, ALU.bypass, xsl, ALU.add)


def emit_l0_ffn(ctx, xs, w, ln, ps, tag):
    TWO_PI = 2 * PI
    xm = [ctx.sbuf(f"l0f_xm{j}_{tag}", [128, T], BF16) for j in range(8)]
    emit_ln(ctx, xs, 0, T, ln, ps[0], ps[1], xm, mode="ln")
    b1c = load_col_fp32(ctx, f"l0f_b1_{tag}", w["b1"], 4096, ln, ps[0])
    b2c = load_col_fp32(ctx, f"l0f_b2_{tag}", w["b2"], 1024, ln, ps[1])
    thc = load_col_fp32(ctx, f"l0f_th_{tag}", w["qp"], 4096, ln, ps[2])
    wrc = load_col_fp32(ctx, f"l0f_wr_{tag}", w["qwr"], 4096, ln, ps[3])
    wic = load_col_fp32(ctx, f"l0f_wi_{tag}", w["qwi"], 4096, ln, ps[4])
    cc = ctx.sbuf(f"l0f_cc_{tag}", [128, 6 * 32], F32)
    sth, cth, c1c = cc[:, 0:32], cc[:, 32:64], cc[:, 64:96]
    b110, b110p, tmpc = cc[:, 96:128], cc[:, 128:160], cc[:, 160:192]
    copy(ctx, "DVE", tmpc, thc)
    for _ in range(8):
        ts(ctx, "DVE", sth, tmpc, PI, ALU.is_gt)
        stt(ctx, tmpc, sth, -TWO_PI, ALU.mult, tmpc, ALU.add)
        ts(ctx, "DVE", sth, tmpc, -PI, ALU.is_lt)
        stt(ctx, tmpc, sth, TWO_PI, ALU.mult, tmpc, ALU.add)
    ts(ctx, "DVE", cth, tmpc, PI / 2, ALU.add)
    ts(ctx, "DVE", sth, cth, PI, ALU.is_gt)
    stt(ctx, cth, sth, -TWO_PI, ALU.mult, cth, ALU.add)
    act(ctx, cth, cth, AF.Sin)
    act(ctx, sth, tmpc, AF.Sin)
    tt(ctx, "DVE", c1c, cth, wrc, ALU.mult)
    tt(ctx, "DVE", tmpc, sth, wic, ALU.mult)
    tt(ctx, "DVE", c1c, c1c, tmpc, ALU.add)
    tt(ctx, "DVE", c1c, c1c, wic, ALU.mult)
    ts(ctx, "DVE", c1c, c1c, 0.1, ALU.mult)
    ts(ctx, "DVE", b110, b1c, 0.1, ALU.mult)
    ts(ctx, "DVE", b110p, b110, PI / 2, ALU.add)
    w1s = [ctx.sbuf(f"l0f_w1s{i}_{tag}", [128, 1024], BF16) for i in range(3)]
    w2s = [ctx.sbuf(f"l0f_w2s{i}_{tag}", [128, 1024], BF16) for i in range(3)]
    hsl = [ctx.sbuf(f"l0f_h{i}_{tag}", [128, 512], F32) for i in range(2)]
    ssl = [ctx.sbuf(f"l0f_s{i}_{tag}", [128, 512], F32) for i in range(2)]
    csl = [ctx.sbuf(f"l0f_c{i}_{tag}", [128, 512], F32) for i in range(2)]
    asb = [ctx.sbuf(f"l0f_a{i}_{tag}", [128, 512], BF16) for i in range(32)]
    d1 = w["W1"].apobj
    for c in range(4):
        t0 = c * 512
        rbch = ln.rb[:, t0:t0 + 512]
        for f in range(32):
            slot = w1s[f % 3]
            def em(e, slot=slot, d1=d1, f=f):
                return e.dma_start(out=slot.ap(),
                                   in_=d1[:, f * 128:(f + 1) * 128].rearrange(
                                       "(k p) c -> p k c", p=128))
            ctx.dma("POOL", slot, w["W1"].tv(), emit=em)
            pp = ps[f % 2]
            mm_acc(ctx, pp, [(slot[:, j * 128:(j + 1) * 128], xm[j][:, t0:t0 + 512])
                             for j in range(8)])
            h0 = hsl[f % 2]
            sg = ssl[f % 2]
            cg = csl[f % 2]
            fc = slice(f, f + 1)
            stt(ctx, h0, pp, None, ALU.bypass, rbch, ALU.mult)
            act(ctx, sg, h0, AF.Sin, bias=b110[:, fc], scale=0.1)
            act(ctx, cg, h0, AF.Sin, bias=b110p[:, fc], scale=0.1)
            ts(ctx, "DVE", sg, sg, cth[:, fc], ALU.mult)
            stt(ctx, sg, cg, sth[:, fc], ALU.mult, sg, ALU.add)
            stt(ctx, h0, sg, c1c[:, fc], ALU.mult, h0, ALU.add)
            act(ctx, asb[f], h0, AF.Relu, bias=b1c[:, fc])
        groups = [ctx.mm_open(ps[fo]) for fo in range(8)]
        d2 = w["W2"].apobj
        for k in range(32):
            slot = w2s[k % 3]
            def em2(e, slot=slot, d2=d2, k=k):
                return e.dma_start(out=slot.ap(), in_=d2[k * 128:(k + 1) * 128, :])
            ctx.dma("POOL", slot, w["W2"].tv(), emit=em2)
            for fo in range(8):
                lt = slot[:, fo * 128:(fo + 1) * 128]
                rt = asb[k]
                groups[fo].add(lambda e, o=ps[fo], l=lt, r=rt, st=(k == 0):
                               e.matmul(o.ap(), l.ap(), r.ap(), start=st, stop=False),
                               [lt, rt])
        for fo in range(8):
            groups[fo].close()
            xsl = xs[fo][:, t0:t0 + 512]
            stt(ctx, xsl, ps[fo], b2c[:, fo:fo + 1], ALU.add, xsl, ALU.add)


def tr_mm(ctx, g, out_tv, in_tv, ident_tv, st, sp):
    g.add(lambda e, o=out_tv, i=in_tv, id=ident_tv, st=st, sp=sp:
          e.matmul(o.ap(), i.ap(), id.ap(), start=st, stop=sp, is_transpose=True),
          [in_tv, ident_tv])


def emit_l1_attn(ctx, xs, w, ln, ps, tag, dbg=None):
    H = 16
    xm = [ctx.sbuf(f"l1a_xm{j}_{tag}", [128, T], BF16) for j in range(8)]
    emit_ln(ctx, xs, 0, T, ln, ps[0], ps[1], xm, mode="ln")
    rtok = ctx.sbuf(f"l1a_rtok_{tag}", [128, 16], F32)
    for i in range(16):
        transpose(ctx, ps[2][:, 0:1], ln.rb[0:1, i * 128:(i + 1) * 128], ln.ones_1[0:1, 0:1])
        copy(ctx, "ACT", rtok[:, i:i + 1], ps[2][:, 0:1])
    mval = ctx.sbuf(f"l1a_mval_{tag}", [1, 4], F32)
    mcol = ctx.sbuf(f"l1a_mcol_{tag}", [128, 8], F32)
    c_1m, c_mqs = mcol[:, 0:1], mcol[:, 1:2]
    maxc, sumc, biasc = mcol[:, 2:3], mcol[:, 3:4], mcol[:, 4:5]
    ctx.dma("SYNC", mval[:, 0:1], w["mixture"].tv())
    ctx.dma("SYNC", mval[:, 1:2], w["qs"].tv())
    act(ctx, mval[:, 2:3], mval[:, 0:1], AF.Sigmoid)
    tt(ctx, "DVE", mval[:, 3:4], mval[:, 2:3], mval[:, 1:2], ALU.mult)
    ts(ctx, "DVE", mval[:, 2:3], mval[:, 2:3], -1.0, ALU.mult, 1.0, ALU.add)
    mm_acc(ctx, ps[2][:, 0:2], [(ln.ones_1, mval[:, 2:4])])
    copy(ctx, "ACT", mcol[:, 0:2], ps[2][:, 0:2])
    ao = [ctx.sbuf(f"l1a_ao{j}_{tag}", [128, T], BF16) for j in range(8)]
    pt = ctx.sbuf(f"l1a_pt_{tag}", [128, 16 * 512], BF16)
    pB = [ctx.sbuf(f"l1a_p{i}_{tag}", [128, T], BF16) for i in range(2)]
    u = ctx.sbuf(f"l1a_u_{tag}", [128, T], BF16)
    tq = ctx.sbuf(f"l1a_tq_{tag}", [128, T], F32)
    mk0 = ctx.mark()
    qh = ctx.sbuf(f"l1a_qh_{tag}", [64, T], BF16)
    kh = ctx.sbuf(f"l1a_kh_{tag}", [64, T], BF16)
    q2h = ctx.sbuf(f"l1a_q2h_{tag}", [64, T], BF16)
    k2h = ctx.sbuf(f"l1a_k2h_{tag}", [64, T], BF16)
    vtok = ctx.sbuf(f"l1a_vtok_{tag}", [128, 16 * 64], BF16)
    ktok = pt[:, 0:1024]
    k2tok = pt[:, 1024:2048]
    gsb = ctx.sbuf(f"l1a_g_{tag}", [64, 128], BF16)
    hq = ctx.sbuf(f"l1a_hq_{tag}", [64, 512], BF16)
    kcol = ctx.sbuf(f"l1a_kcol_{tag}", [64, 4], F32)
    kcolb = ctx.sbuf(f"l1a_kcolb_{tag}", [64, 2], BF16)
    stg = ctx.sbuf(f"l1a_stg_{tag}", [1, 512], F32)
    cols = ctx.sbuf(f"l1a_cols_{tag}", [128, 8 * 16], F32)
    nmu1, ss1 = cols[:, 0:16], cols[:, 16:32]
    nmu2, ss2 = cols[:, 32:48], cols[:, 48:64]
    rs1, a2c = cols[:, 64:80], cols[:, 80:96]
    b2c, ctmp = cols[:, 96:112], cols[:, 112:128]
    wsl = [ctx.sbuf(f"l1a_ws{i}_{tag}", [128, 512], BF16) for i in range(5)]
    SC8 = 1.0 / 2048.0

    def stat_rows(qsrc, gview, kb, nmucols, sscols):
        for half, dst in ((0, nmucols), (1, sscols)):
            gcol = ctx.mm_open(ps[5][:, 0:16])
            first = True
            for c in range(4):
                if half == 0:
                    mm_acc(ctx, ps[4][0:1, 0:512], [(kb, qsrc[:, c * 512:(c + 1) * 512])])
                    ts(ctx, "DVE", stg, ps[4][0:1, 0:512], -SC8, ALU.mult)
                else:
                    mm_acc(ctx, ps[4][:64, 0:512], [(gview, qsrc[:, c * 512:(c + 1) * 512])])
                    tt(ctx, "DVE", hq, ps[4][:64, 0:512], qsrc[:, c * 512:(c + 1) * 512], ALU.mult)
                    mm_acc(ctx, ps[3][0:1, 0:512], [(ln.ones_k[0:64], hq)])
                    ts(ctx, "DVE", stg, ps[3][0:1, 0:512], 1.0 / 64.0, ALU.mult)
                for i in range(4):
                    tr_mm(ctx, gcol, ps[5][:, c * 4 + i:c * 4 + i + 1],
                          stg[:, i * 128:(i + 1) * 128], ln.ones_1[0:1, 0:1],
                          st=first, sp=(c == 3 and i == 3))
                    first = False
            gcol.close()
            copy(ctx, "ACT", dst, ps[5][:, 0:16])

    def rsig_from(nmu, ss, dst):
        tt(ctx, "DVE", ctmp, nmu, nmu, ALU.mult)
        stt(ctx, dst, ctmp, -32.0, ALU.mult, ss, ALU.add)
        ts(ctx, "DVE", dst, dst, 1.0 / 2047.0, ALU.mult)
        act(ctx, dst, dst, AF.Sqrt)
        ts(ctx, "DVE", dst, dst, 1e-5, ALU.add)
        recip(ctx, dst, dst)

    for h in range(H):
        hb = (h % 2) * 64
        for i, nm in enumerate(("WqT", "WkT", "Wq2T", "Wk2T", "WvT")):
            d = w[nm].apobj

            def em(e, s=wsl[i], d=d, h=h):
                return e.dma_start(out=s.ap(), in_=d[:, h * 64:(h + 1) * 64].rearrange(
                    "(k p) c -> p k c", p=128))
            ctx.dma("POOL", wsl[i], w[nm].tv(), emit=em)
        for mi, dst in ((0, qh), (1, kh), (2, q2h), (3, k2h)):
            for n in range(4):
                pp = ps[n % 2][0:64, 0:512]
                mm_acc(ctx, pp, [(wsl[mi][:, j * 64:(j + 1) * 64],
                                  xm[j][:, n * 512:(n + 1) * 512]) for j in range(8)])
                stt(ctx, dst[:, n * 512:(n + 1) * 512], pp, None, ALU.bypass,
                    ln.rb[0:64, n * 512:(n + 1) * 512], ALU.mult)
        for tt_i in range(16):
            pp = ps[2 + tt_i % 2][:, 0:64]
            mm_acc(ctx, pp, [(xm[j][:, tt_i * 128:(tt_i + 1) * 128],
                              wsl[4][:, j * 64:(j + 1) * 64]) for j in range(8)])
            ts(ctx, "DVE", vtok[:, tt_i * 64:(tt_i + 1) * 64], pp, rtok[:, tt_i:tt_i + 1], ALU.mult)
        for src, dst in ((kh, ktok), (k2h, k2tok)):
            for gq in range(2):
                bank = ps[6].bf16()[:, 0:512]
                gg = ctx.mm_open(ps[6])
                for i2 in range(8):
                    kc = gq * 8 + i2
                    tr_mm(ctx, gg, bank[:, i2 * 64:(i2 + 1) * 64],
                          src[:, kc * 128:(kc + 1) * 128], ln.identb[0:64, 0:64],
                          st=(i2 == 0), sp=(i2 == 7))
                gg.close()
                copy(ctx, "DVE", dst[:, gq * 512:(gq + 1) * 512], bank)
        for gi, (ktk, gslice) in enumerate(((ktok, gsb[:, 0:64]), (k2tok, gsb[:, 64:128]))):
            mm_acc(ctx, ps[3][0:64, 0:64], [(ktk[:, kc * 64:(kc + 1) * 64],
                                             ktk[:, kc * 64:(kc + 1) * 64]) for kc in range(16)])
            copy(ctx, "DVE", gslice, ps[3][0:64, 0:64])
        reduce(ctx, "DVE", kcol[:, 0:1], kh, ALU.add)
        reduce(ctx, "DVE", kcol[:, 1:2], k2h, ALU.add)
        copy(ctx, "DVE", kcolb, kcol[:, 0:2])
        stat_rows(qh, gsb[:, 0:64], kcolb[:, 0:1], nmu1, ss1)
        stat_rows(q2h, gsb[:, 64:128], kcolb[:, 1:2], nmu2, ss2)
        rsig_from(nmu1, ss1, rs1)
        ts(ctx, "DVE", rs1, rs1, 0.125, ALU.mult)
        rsig_from(nmu2, ss2, a2c)
        ts(ctx, "DVE", a2c, a2c, c_mqs, ALU.mult, 0.125, ALU.mult)
        tt(ctx, "DVE", ctmp, nmu2, a2c, ALU.mult)
        ts(ctx, "DVE", b2c, ctmp, c_1m, ALU.add)
        if dbg is not None and h == 0:
            for nm, tvv in (("qh", qh), ("kh", kh), ("q2h", q2h), ("k2h", k2h)):
                ctx.dma("SYNC", dbg[nm].tv(), tvv)
            ctx.dma("SYNC", dbg["vtok"].tv(), vtok)
            ctx.dma("SYNC", dbg["ktok"].tv(), ktok)
            ctx.dma("SYNC", dbg["gsb"].tv(), gsb)
            ctx.dma("SYNC", dbg["cols"].tv(), cols)
        def stageA(qc):
            par = qc % 2
            p = pB[par]
            maxc, sumc, biasc = (mcol[:, 2 + par * 3:3 + par * 3],
                                 mcol[:, 3 + par * 3:4 + par * 3],
                                 mcol[:, 4 + par * 3:5 + par * 3])
            W = (qc + 1) * 128
            nch = (W + 511) // 512
            for c in range(nch):
                c0, c1 = c * 512, min(W, (c + 1) * 512)
                pp = ps[c % 2][:, 0:c1 - c0]
                mm_acc(ctx, pp, [(q2h[:, qc * 128:(qc + 1) * 128], k2h[:, c0:c1])])
                ts(ctx, "DVE", u[:, c0:c1], pp, a2c[:, qc:qc + 1], ALU.mult,
                   b2c[:, qc:qc + 1], ALU.add)
            for c in range(nch):
                c0, c1 = c * 512, min(W, (c + 1) * 512)
                pp = ps[c % 2][:, 0:c1 - c0]
                mm_acc(ctx, pp, [(qh[:, qc * 128:(qc + 1) * 128], kh[:, c0:c1])])
                stt(ctx, tq[:, c0:c1], pp, nmu1[:, qc:qc + 1], ALU.add, u[:, c0:c1], ALU.mult)
            tt(ctx, "DVE", tq[:, qc * 128:W], tq[:, qc * 128:W], ln.cmask, ALU.add)
            reduce(ctx, "DVE", maxc, tq[:, 0:W], ALU.max)
            ts(ctx, "DVE", biasc, maxc, rs1[:, qc:qc + 1], ALU.mult, -1.0, ALU.mult)
            act(ctx, p[:, 0:W], tq[:, 0:W], AF.Exp, bias=biasc, scale=rs1[:, qc:qc + 1],
                accum_out=sumc)
            recip(ctx, sumc, sumc)
            ts(ctx, "DVE", p[:, 0:W], p[:, 0:W], sumc, ALU.mult)

        def stageB(qc, h=h, hb=hb):
            p = pB[qc % 2]
            if qc % 4 == 0:
                for kc in range(qc, min(qc + 4, 16)):
                    memset(ctx, "POOL", pt[:, kc * 512:(kc + 1) * 512], 0.0)
            for kc in range(qc + 1):
                bank = ps[6 + kc % 2]
                gg = ctx.mm_open(bank)
                tr_mm(ctx, gg, bank.bf16()[:, 0:128], p[:, kc * 128:(kc + 1) * 128],
                      ln.identb, st=True, sp=True)
                gg.close()
                dst = pt[:, kc * 512 + (qc % 4) * 128:kc * 512 + (qc % 4) * 128 + 128]
                copy(ctx, "DVE" if kc % 2 == 0 else "ACT", dst, bank.bf16()[:, 0:128])
            if qc % 4 == 3:
                sp_i = qc // 4
                po = ps[2]
                out_tv = po[hb:hb + 64, 0:512]
                gpv = ctx.mm_open(po)
                for kc in range(qc + 1):
                    lt = vtok[:, kc * 64:(kc + 1) * 64]
                    rt = pt[:, kc * 512:(kc + 1) * 512]
                    tp = (0, hb)
                    gpv.add(lambda e, o=out_tv, l=lt, r=rt, st=(kc == 0), sp2=(kc == qc),
                            tp=tp: e.matmul(o.ap(), l.ap(), r.ap(), start=st, stop=sp2,
                                            tile_position=tp), [lt, rt])
                gpv.close()
                copy(ctx, "DVE", ao[h // 2][hb:hb + 64, sp_i * 512:(sp_i + 1) * 512], out_tv)

        stageA(0)
        for qc in range(16):
            if qc < 15:
                stageA(qc + 1)
            stageB(qc)
    if dbg is not None:
        for j in range(8):
            ctx.dma("SYNC", dbg[f"ao{j}"].tv(), ao[j])
    ctx.free_to(mk0)
    wot = load_full_bf16(ctx, f"l1a_wot_{tag}", w["WoT"], 1024, 1024)
    for f in range(8):
        for n in range(4):
            pp = ps[f % 2][:, 0:512]
            mm_acc(ctx, pp, [(wot[j][:, f * 128:(f + 1) * 128],
                              ao[j][:, n * 512:(n + 1) * 512]) for j in range(8)])
            xsl = xs[f][:, n * 512:(n + 1) * 512]
            stt(ctx, xsl, pp, None, ALU.bypass, xsl, ALU.add)


def emit_l1_ffn(ctx, xs, w, ln, ps, tag):
    xm = [ctx.sbuf(f"l1f_xm{j}_{tag}", [128, T], BF16) for j in range(8)]
    emit_ln(ctx, xs, 0, T, ln, ps[0], ps[1], xm, mode="ln")
    trash = ctx.sbuf(f"l1f_trash_{tag}", [128, T], BF16)
    gcol = ctx.sbuf(f"l1f_gcol_{tag}", [128, 20], F32)
    gcolb = ctx.sbuf(f"l1f_gcolb_{tag}", [128, 16], BF16)
    sval = ctx.sbuf(f"l1f_sval_{tag}", [1, 8], F32)
    for j in range(8):
        tt(ctx, "DVE", trash, xm[j], ln.rb, ALU.mult)
        reduce(ctx, "DVE", gcol[:, j:j + 1], trash, ALU.add)
    ts(ctx, "DVE", gcolb[:, 0:8], gcol[:, 0:8], 1.0 / 2048.0, ALU.mult)
    gw1 = load_full_bf16(ctx, f"l1f_gw1_{tag}", w["gc_w1"], 1024, 512)
    gb1 = load_col_fp32(ctx, f"l1f_gb1_{tag}", w["gc_b1"], 512, ln, ps[2])
    gw2 = load_full_bf16(ctx, f"l1f_gw2_{tag}", w["gc_w2"], 512, 256)
    gb2 = load_col_fp32(ctx, f"l1f_gb2_{tag}", w["gc_b2"], 256, ln, ps[3])
    gw3 = load_full_bf16(ctx, f"l1f_gw3_{tag}", w["gc_w3"], 256, 1)
    ctx.dma("SYNC", sval[:, 1:2], w["ar"].tv())
    ctx.dma("SYNC", sval[:, 5:6], w["gc_b3"].tv())
    ctx.dma("SYNC", sval[:, 6:7], w["cn_b4"].tv())
    for ft in range(4):
        mm_acc(ctx, ps[0][:, 0:1], [(gw1[j][:, ft * 128:(ft + 1) * 128], gcolb[:, j:j + 1])
                                    for j in range(8)])
        act(ctx, gcolb[:, 8 + ft:9 + ft], ps[0][:, 0:1], AF.Gelu, bias=gb1[:, ft:ft + 1])
    for ft in range(2):
        mm_acc(ctx, ps[0][:, 0:1], [(gw2[k][:, ft * 128:(ft + 1) * 128], gcolb[:, 8 + k:9 + k])
                                    for k in range(4)])
        act(ctx, gcolb[:, 12 + ft:13 + ft], ps[0][:, 0:1], AF.Gelu, bias=gb2[:, ft:ft + 1])
    mm_acc(ctx, ps[0][0:1, 0:1], [(gw3[k][:, 0:1], gcolb[:, 12 + k:13 + k]) for k in range(2)])
    act(ctx, sval[:, 0:1], ps[0][0:1, 0:1], AF.Sigmoid, bias=sval[:, 5:6])
    ts(ctx, "DVE", sval[:, 2:3], sval[:, 1:2], 0.6, ALU.mult)
    ts(ctx, "DVE", sval[:, 4:5], sval[:, 0:1], 1.4, ALU.mult, -1.0, ALU.add)
    tt(ctx, "DVE", sval[:, 4:5], sval[:, 4:5], sval[:, 1:2], ALU.mult)
    ts(ctx, "DVE", sval[:, 3:4], sval[:, 4:5], 0.1, ALU.add)
    w1t = load_full_bf16(ctx, f"l1f_w1_{tag}", w["cn_w1"], 1024, 1024)
    b1c = load_col_fp32(ctx, f"l1f_b1_{tag}", w["cn_b1"], 1024, ln, ps[2])
    w2t = load_full_bf16(ctx, f"l1f_w2_{tag}", w["cn_w2"], 1024, 512)
    b2c = load_col_fp32(ctx, f"l1f_b2_{tag}", w["cn_b2"], 512, ln, ps[3])
    w3t = load_full_bf16(ctx, f"l1f_w3_{tag}", w["cn_w3"], 512, 256)
    b3c = load_col_fp32(ctx, f"l1f_b3_{tag}", w["cn_b3"], 256, ln, ps[4])
    w4t = load_full_bf16(ctx, f"l1f_w4_{tag}", w["cn_w4"], 256, 1)
    out = [ctx.sbuf(f"l1f_out{j}_{tag}", [128, 512], F32) for j in range(8)]
    outb = [ctx.sbuf(f"l1f_outb{j}_{tag}", [128, 512], BF16) for j in range(8)]
    h1b = ctx.sbuf(f"l1f_h1b_{tag}", [128, 8 * 512], BF16)
    h2b = ctx.sbuf(f"l1f_h2b_{tag}", [128, 4 * 512], BF16)
    h3b = ctx.sbuf(f"l1f_h3b_{tag}", [128, 2 * 512], BF16)
    tfr = ctx.sbuf(f"l1f_tfr_{tag}", [1, 512], F32)
    cplb = ctx.sbuf(f"l1f_cplb_{tag}", [128, 2 * 512], F32)
    tmp = ctx.sbuf(f"l1f_tmp_{tag}", [128, 512], F32)
    PERM = [4, 5, 6, 7, 0, 1, 2, 3]
    SGN = [1.0, 1.0, -1.0, -1.0, -1.0, -1.0, 1.0, 1.0]
    for c in range(4):
        t0 = c * 512
        for j in range(8):
            stt(ctx, out[j], xs[j][:, t0:t0 + 512] if False else xm[j][:, t0:t0 + 512],
                None, ALU.bypass, ln.rb[:, t0:t0 + 512], ALU.mult)
        for cyc in range(3):
            for j in range(8):
                copy(ctx, "DVE" if j % 2 == 0 else "POOL", outb[j], out[j])
            for f in range(8):
                mm_acc(ctx, ps[f % 2][:, 0:512], [(w1t[j][:, f * 128:(f + 1) * 128], outb[j])
                                                  for j in range(8)])
                act(ctx, h1b[:, f * 512:(f + 1) * 512], ps[f % 2][:, 0:512], AF.Gelu,
                    bias=b1c[:, f:f + 1])
            for f in range(4):
                mm_acc(ctx, ps[f % 2][:, 0:512], [(w2t[j][:, f * 128:(f + 1) * 128],
                                                   h1b[:, j * 512:(j + 1) * 512]) for j in range(8)])
                act(ctx, h2b[:, f * 512:(f + 1) * 512], ps[f % 2][:, 0:512], AF.Gelu,
                    bias=b2c[:, f:f + 1])
            for f in range(2):
                mm_acc(ctx, ps[f % 2][:, 0:512], [(w3t[k][:, f * 128:(f + 1) * 128],
                                                   h2b[:, k * 512:(k + 1) * 512]) for k in range(4)])
                act(ctx, h3b[:, f * 512:(f + 1) * 512], ps[f % 2][:, 0:512], AF.Gelu,
                    bias=b3c[:, f:f + 1])
            mm_acc(ctx, ps[2][0:1, 0:512], [(w4t[k][:, 0:1], h3b[:, k * 512:(k + 1) * 512])
                                            for k in range(2)])
            act(ctx, tfr, ps[2][0:1, 0:512], AF.Sigmoid, bias=sval[:, 6:7])
            ts(ctx, "DVE", tfr, tfr, sval[:, 2:3], ALU.mult, sval[:, 3:4], ALU.add)
            mm_acc(ctx, ps[3][:, 0:512], [(ln.ones_1, tfr)])
            copy(ctx, "ACT", cplb[:, 0:512], ps[3][:, 0:512])
            ts(ctx, "DVE", cplb[:, 512:1024], cplb[:, 0:512], -1.0, ALU.mult)
            for j in range(8):
                cp = cplb[:, 0:512] if SGN[j] > 0 else cplb[:, 512:1024]
                stt(ctx, tmp, outb[PERM[j]], None, ALU.bypass, cp, ALU.mult)
                tt(ctx, "DVE", out[j], out[j], tmp, ALU.add)
        for j in range(8):
            tt(ctx, "DVE", xs[j][:, t0:t0 + 512], xs[j][:, t0:t0 + 512], out[j], ALU.add)


def emit_l2_attn(ctx, xs, w, ln, ps, tag):
    H = 16
    C = 128
    xm = [ctx.sbuf(f"l2a_xm{j}_{tag}", [128, T], BF16) for j in range(8)]
    emit_ln(ctx, xs, 0, T, ln, ps[0], ps[1], xm, mode="ln")
    rtok = ctx.sbuf(f"l2a_rtok_{tag}", [128, 16], F32)
    for i in range(16):
        transpose(ctx, ps[2][:, 0:1], ln.rb[0:1, i * 128:(i + 1) * 128], ln.ones_1[0:1, 0:1])
        copy(ctx, "ACT", rtok[:, i:i + 1], ps[2][:, 0:1])
    bqk = ctx.sbuf(f"l2a_bqk_{tag}", [64, 32], F32)
    stg = ctx.sbuf(f"l2a_bstg_{tag}", [16, 64], F32)
    for i, nm in enumerate(("bq", "bk")):
        ctx.dma("SYNC", stg, w[nm].tv())
        transpose(ctx, ps[2][0:64, 0:16], stg, ln.ident[0:16, 0:16])
        copy(ctx, "ACT", bqk[:, i * 16:(i + 1) * 16], ps[2][0:64, 0:16])
    bvrow = ctx.sbuf(f"l2a_bv_{tag}", [1, 1024], F32)
    ctx.dma("SYNC", bvrow, w["bv"].tv())
    tmulT = ctx.sbuf(f"l2a_tmulT_{tag}", [128, 128], F32)
    memset(ctx, "POOL", tmulT, 1.0)
    affsel(ctx, tmulT, tmulT, [[1, 128]], ALU.is_ge, 0.0, 0, -1)
    ao = [ctx.sbuf(f"l2a_ao{j}_{tag}", [128, T], BF16) for j in range(8)]
    NH = 2
    qf = [ctx.sbuf(f"l2a_q{i}_{tag}", [64, T], BF16) for i in range(NH)]
    kf = [ctx.sbuf(f"l2a_k{i}_{tag}", [64, T], BF16) for i in range(NH)]
    vt = [ctx.sbuf(f"l2a_v{i}_{tag}", [128, 16 * 64], BF16) for i in range(NH)]
    ssb = [ctx.sbuf(f"l2a_S{i}_{tag}", [64, 64], F32) for i in range(NH)]
    ssbb = [ctx.sbuf(f"l2a_Sb{i}_{tag}", [64, 64], BF16) for i in range(NH)]
    zcol = [ctx.sbuf(f"l2a_z{i}_{tag}", [64, 2], F32) for i in range(NH)]
    zcolb = [ctx.sbuf(f"l2a_zb{i}_{tag}", [64, 1], BF16) for i in range(NH)]
    ktc = [ctx.sbuf(f"l2a_kt{i}_{tag}", [128, 64], BF16) for i in range(NH)]
    sct = [ctx.sbuf(f"l2a_sct{i}_{tag}", [128, 128], BF16) for i in range(NH)]
    denr = [ctx.sbuf(f"l2a_den{i}_{tag}", [1, 128], F32) for i in range(NH)]
    denb = [ctx.sbuf(f"l2a_denb{i}_{tag}", [64, 128], F32) for i in range(NH)]
    bvb = [ctx.sbuf(f"l2a_bvb{i}_{tag}", [128, 64], F32) for i in range(NH)]
    zq = [ctx.sbuf(f"l2a_zq{i}_{tag}", [64, 512], F32) for i in range(NH)]
    zq2 = [ctx.sbuf(f"l2a_zq2{i}_{tag}", [64, 512], F32) for i in range(NH)]
    wsl = [ctx.sbuf(f"l2a_ws{i}_{tag}", [128, 512], BF16) for i in range(3 * NH)]
    ones1b = ctx.sbuf(f"l2a_ones1b_{tag}", [128, 1], BF16)
    memset(ctx, "DVE", ones1b, 1.0)
    for hg in range(H // NH):
        heads = [hg * NH + i for i in range(NH)]
        for ii, h in enumerate(heads):
            hb = (h % 2) * 64
            for mi, nm in enumerate(("WqT", "WkT", "WvT")):
                d = w[nm].apobj

                def em(e, s=wsl[3 * ii + mi], d=d, h=h):
                    return e.dma_start(out=s.ap(), in_=d[:, h * 64:(h + 1) * 64].rearrange(
                        "(k p) c -> p k c", p=128))
                ctx.dma("POOL", wsl[3 * ii + mi], w[nm].tv(), emit=em)
            for mi, dst in ((0, qf[ii]), (1, kf[ii])):
                bcol = bqk[:, mi * 16 + h:mi * 16 + h + 1]
                for n in range(4):
                    pp = ps[n % 2][0:64, 0:512]
                    mm_acc(ctx, pp, [(wsl[3 * ii + mi][:, j * 64:(j + 1) * 64],
                                      xm[j][:, n * 512:(n + 1) * 512]) for j in range(8)])
                    z0 = zq[ii]
                    stt(ctx, z0, pp, None, ALU.bypass, ln.rb[0:64, n * 512:(n + 1) * 512], ALU.mult)
                    ts(ctx, "DVE", zq2[ii], z0, bcol, ALU.add, 0.0, ALU.min)
                    act(ctx, zq2[ii], zq2[ii], AF.Exp)
                    ts(ctx, "DVE", z0, z0, bcol, ALU.add, 0.0, ALU.max)
                    tt(ctx, "DVE", dst[:, n * 512:(n + 1) * 512], zq2[ii], z0, ALU.add)
            mm_acc(ctx, ps[2][:, 0:64], [(ln.ones_1, bvrow[:, h * 64:(h + 1) * 64])])
            copy(ctx, "ACT", bvb[ii], ps[2][:, 0:64])
            for tt_i in range(16):
                pp = ps[2 + tt_i % 2][:, 0:64]
                mm_acc(ctx, pp, [(xm[j][:, tt_i * 128:(tt_i + 1) * 128],
                                  wsl[3 * ii + 2][:, j * 64:(j + 1) * 64]) for j in range(8)])
                stt(ctx, vt[ii][:, tt_i * 64:(tt_i + 1) * 64], pp, rtok[:, tt_i:tt_i + 1],
                    ALU.mult, bvb[ii], ALU.add)
            memset(ctx, "DVE", ssb[ii], 0.0)
            memset(ctx, "DVE", ssbb[ii], 0.0)
            memset(ctx, "DVE", zcol[ii], 0.0)
            memset(ctx, "DVE", zcolb[ii], 0.0)
        for i in range(16):
            for ii, h in enumerate(heads):
                hb = (h % 2) * 64
                qi = qf[ii][:, i * 128:(i + 1) * 128]
                ki = kf[ii][:, i * 128:(i + 1) * 128]
                vi = vt[ii][:, i * 64:(i + 1) * 64]
                pp = ps[ii]
                mm_acc(ctx, pp[:, 0:128], [(ki, qi)])
                stt(ctx, sct[ii], pp[:, 0:128], None, ALU.bypass, tmulT, ALU.mult)
                pd = ps[2 + ii]
                gd = ctx.mm_open(pd)
                gd.add(lambda e, o=pd[0:1, 0:128], l=ones1b, r=sct[ii]: e.matmul(
                    o.ap(), l.ap(), r.ap(), start=True, stop=False), [ones1b, sct[ii]])
                gd.add(lambda e, o=pd[0:1, 0:128], l=zcolb[ii], r=qi: e.matmul(
                    o.ap(), l.ap(), r.ap(), start=False, stop=True), [zcolb[ii], qi])
                gd.close()
                copy(ctx, "DVE", denr[ii], pd[0:1, 0:128])
                recip(ctx, denr[ii], denr[ii])
                mm_acc(ctx, ps[4 + ii][0:64, 0:128], [(ln.ones_1[:, 0:64], denr[ii])])
                copy(ctx, "ACT", denb[ii], ps[4 + ii][0:64, 0:128])
                pn = ps[6]
                gn = ctx.mm_open(pn)
                out_tv = pn[hb:hb + 64, 0:128]
                gn.add(lambda e, o=out_tv, l=vi, r=sct[ii], tp=(0, hb): e.matmul(
                    o.ap(), l.ap(), r.ap(), start=True, stop=False, tile_position=tp),
                    [vi, sct[ii]])
                gn.add(lambda e, o=out_tv, l=ssbb[ii], r=qi, tp=(0, hb): e.matmul(
                    o.ap(), l.ap(), r.ap(), start=False, stop=True, tile_position=tp),
                    [ssbb[ii], qi])
                gn.close()
                stt(ctx, ao[h // 2][hb:hb + 64, i * 128:(i + 1) * 128], out_tv,
                    None, ALU.bypass, denb[ii], ALU.mult)
                pk = ps[7]
                gk = ctx.mm_open(pk)
                tr_mm(ctx, gk, pk.bf16()[:, 0:64], ki, ln.identb[0:64, 0:64], True, True)
                gk.close()
                copy(ctx, "DVE", ktc[ii], pk.bf16()[:, 0:64])
                mm_acc(ctx, ps[4 + ii][0:64, 0:64], [(ktc[ii], vi)])
                tt(ctx, "DVE", ssb[ii], ssb[ii], ps[4 + ii][0:64, 0:64], ALU.add)
                copy(ctx, "DVE", ssbb[ii], ssb[ii])
                reduce(ctx, "DVE", zcol[ii][:, 1:2], ki, ALU.add)
                tt(ctx, "DVE", zcol[ii][:, 0:1], zcol[ii][:, 0:1], zcol[ii][:, 1:2], ALU.add)
                copy(ctx, "DVE", zcolb[ii], zcol[ii][:, 0:1])
    wot = load_full_bf16(ctx, f"l2a_wot_{tag}", w["WoT"], 1024, 1024)
    boc = load_col_fp32(ctx, f"l2a_bo_{tag}", w["bo"], 1024, ln, ps[2])
    for f in range(8):
        for n in range(4):
            pp = ps[f % 2][:, 0:512]
            mm_acc(ctx, pp, [(wot[j][:, f * 128:(f + 1) * 128],
                              ao[j][:, n * 512:(n + 1) * 512]) for j in range(8)])
            xsl = xs[f][:, n * 512:(n + 1) * 512]
            stt(ctx, xsl, pp, boc[:, f:f + 1], ALU.add, xsl, ALU.add)


def emit_l2_ffn(ctx, xs, w, ln, ps, tag):
    E, F = 8, 3584
    xm = [ctx.sbuf(f"l2f_xm{j}_{tag}", [128, T], BF16) for j in range(8)]
    emit_ln(ctx, xs, 0, T, ln, ps[0], ps[1], xm, mode="ln")
    for j in range(8):
        stt(ctx, xm[j], xm[j], None, ALU.bypass, ln.rb, ALU.mult)
    gwt = load_full_bf16(ctx, f"l2f_gw_{tag}", w["gwT"], 1024, 8)
    gtok = ctx.sbuf(f"l2f_gtok_{tag}", [128, 16 * 8], F32)
    rsc = ctx.sbuf(f"l2f_rsc_{tag}", [128, 6 * 8], F32)
    rcol = ctx.sbuf(f"l2f_rcol_{tag}", [128, 8], F32)
    sc_, ab_, nu_ = rsc[:, 0:8], rsc[:, 8:16], rsc[:, 16:24]
    ke_, ee_, g1_ = rsc[:, 24:32], rsc[:, 32:40], rsc[:, 40:48]
    t1c, s1c, t2c, s2c = rcol[:, 0:1], rcol[:, 1:2], rcol[:, 2:3], rcol[:, 3:4]
    for ti in range(16):
        pp = ps[2][:, 0:8]
        mm_acc(ctx, pp, [(xm[j][:, ti * 128:(ti + 1) * 128], gwt[j]) for j in range(8)])
        copy(ctx, "DVE", sc_, pp)
        g = gtok[:, ti * 8:(ti + 1) * 8]
        reduce(ctx, "DVE", t1c, sc_, ALU.max)
        act(ctx, ab_, sc_, AF.Abs)
        ts(ctx, "DVE", nu_, sc_, -1.0, ALU.mult, t1c, ALU.add)
        ts(ctx, "DVE", ee_, ab_, t1c, ALU.max, 0.02, ALU.mult)
        tt(ctx, "DVE", ke_, nu_, ee_, ALU.is_le)
        ts(ctx, "DVE", t2c, t1c, -1.0, ALU.mult)
        act(ctx, ee_, sc_, AF.Exp, bias=t2c)
        tt(ctx, "DVE", ee_, ee_, ke_, ALU.mult)
        reduce(ctx, "DVE", s1c, ee_, ALU.add)
        recip(ctx, s1c, s1c)
        ts(ctx, "DVE", g1_, sc_, t1c, ALU.is_equal)
        ts(ctx, "DVE", g1_, g1_, s1c, ALU.mult)
        ts(ctx, "DVE", ke_, sc_, t1c, ALU.is_equal)
        stt(ctx, ee_, ke_, -1e30, ALU.mult, sc_, ALU.add)
        reduce(ctx, "DVE", t2c, ee_, ALU.max)
        ts(ctx, "DVE", nu_, sc_, -1.0, ALU.mult, t2c, ALU.add)
        ts(ctx, "DVE", ab_, ab_, t2c, ALU.max, 0.02, ALU.mult)
        tt(ctx, "DVE", ke_, nu_, ab_, ALU.is_le)
        ts(ctx, "DVE", s2c, t2c, -1.0, ALU.mult)
        act(ctx, ee_, ee_, AF.Exp, bias=s2c)
        tt(ctx, "DVE", ee_, ee_, ke_, ALU.mult)
        reduce(ctx, "DVE", s2c, ee_, ALU.add)
        recip(ctx, s2c, s2c)
        ts(ctx, "DVE", nu_, sc_, t1c, ALU.is_equal)
        stt(ctx, ee_, nu_, -1e30, ALU.mult, sc_, ALU.add)
        ts(ctx, "DVE", ee_, ee_, t2c, ALU.is_equal)
        ts(ctx, "DVE", ee_, ee_, s2c, ALU.mult)
        tt(ctx, "DVE", g, g1_, ee_, ALU.add)
    grow = ctx.sbuf(f"l2f_grow_{tag}", [1, 512], F32)
    w1s = [ctx.sbuf(f"l2f_w1s{i}_{tag}", [128, 8 * 512], BF16) for i in range(2)]
    w3s = [ctx.sbuf(f"l2f_w3s{i}_{tag}", [128, 8 * 512], BF16) for i in range(2)]
    w2s = [ctx.sbuf(f"l2f_w2s{i}_{tag}", [128, 4 * 1024], BF16) for i in range(2)]
    s1b = ctx.sbuf(f"l2f_s1b_{tag}", [128, 4 * 512], BF16)
    h3b = ctx.sbuf(f"l2f_h3b_{tag}", [128, 4 * 512], BF16)
    gbeb = [ctx.sbuf(f"l2f_gbe{i}_{tag}", [128, 8 * 512], BF16) for i in range(4)]
    gbet = [[gbeb[tc][:, e * 512:(e + 1) * 512] for tc in range(4)] for e in range(8)]
    d1, d2, d3 = w["w1T"].apobj, w["w2T"].apobj, w["w3T"].apobj
    si = 0
    for e in range(E):
        for fg in range(7):
            sl1, sl3, sl2 = w1s[si % 2], w3s[si % 2], w2s[si % 2]
            si += 1

            def em1(e_, s=sl1, d=d1, e=e, fg=fg):
                return e_.dma_start(out=s.ap(), in_=d[e][:, fg * 512:(fg + 1) * 512]
                                    .rearrange("(k p) c -> p k c", p=128))
            ctx.dma("POOL", sl1, w["w1T"].tv(), emit=em1)

            def em3(e_, s=sl3, d=d3, e=e, fg=fg):
                return e_.dma_start(out=s.ap(), in_=d[e][:, fg * 512:(fg + 1) * 512]
                                    .rearrange("(k p) c -> p k c", p=128))
            ctx.dma("POOL", sl3, w["w3T"].tv(), emit=em3)

            def em2(e_, s=sl2, d=d2, e=e, fg=fg):
                return e_.dma_start(out=s.ap(), in_=d[e][fg * 512:(fg + 1) * 512, :]
                                    .rearrange("(k p) c -> p k c", p=128))
            ctx.dma("POOL", sl2, w["w2T"].tv(), emit=em2)
            for tc in range(4):
                t0 = tc * 512
                if fg == 0:
                    gp = ctx.mm_open(ps[3])
                    for c4 in range(4):
                        gcolv = gtok[:, (tc * 4 + c4) * 8 + e:(tc * 4 + c4) * 8 + e + 1]
                        gp.add(lambda e_, o=ps[3][0:1, 0:512], i=gcolv, id=ln.ident,
                               st=(c4 == 0), sp2=(c4 == 3), c4=c4: e_.matmul(
                                   o[:, c4 * 128:(c4 + 1) * 128].ap(), i.ap(), id.ap(),
                                   start=st, stop=sp2, is_transpose=True), [gcolv, ln.ident])
                    gp.close()
                    copy(ctx, "ACT", grow, ps[3][0:1, 0:512])
                    mm_acc(ctx, ps[3][:, 0:512], [(ln.ones_1, grow)])
                    copy(ctx, "ACT", gbet[e][tc], ps[3][:, 0:512])
                for fsub in range(4):
                    pp1 = ps[4 + fsub % 2]
                    mm_acc(ctx, pp1, [(sl1[:, k * 512 + fsub * 128:k * 512 + (fsub + 1) * 128],
                                       xm[k][:, t0:t0 + 512]) for k in range(8)])
                    act(ctx, s1b[:, fsub * 512:(fsub + 1) * 512], pp1, AF.Silu)
                    pp3 = ps[6 + fsub % 2]
                    mm_acc(ctx, pp3, [(sl3[:, k * 512 + fsub * 128:k * 512 + (fsub + 1) * 128],
                                       xm[k][:, t0:t0 + 512]) for k in range(8)])
                    stt(ctx, h3b[:, fsub * 512:(fsub + 1) * 512], pp3, None, ALU.bypass,
                        gbet[e][tc], ALU.mult)
                    tt(ctx, "DVE", s1b[:, fsub * 512:(fsub + 1) * 512],
                       s1b[:, fsub * 512:(fsub + 1) * 512],
                       h3b[:, fsub * 512:(fsub + 1) * 512], ALU.mult)
                for ho in range(8):
                    pp = ps[ho % 2]
                    mm_acc(ctx, pp, [(sl2[:, k * 1024 + ho * 128:k * 1024 + (ho + 1) * 128],
                                      s1b[:, k * 512:(k + 1) * 512]) for k in range(4)])
                    yv = xs[ho][:, t0:t0 + 512]
                    stt(ctx, yv, pp, None, ALU.bypass, yv, ALU.add)


def emit_l3_attn(ctx, xs, w, ln, ps, tag):
    H = 16
    xm = [ctx.sbuf(f"l3a_xm{j}_{tag}", [128, T], BF16) for j in range(8)]
    emit_ln(ctx, xs, 0, T, ln, ps[0], ps[1], xm, mode="ln")
    t1s = ctx.sbuf(f"l3a_t1_{tag}", [128, T], F32)
    mkr = ctx.mark()
    cosb = ctx.sbuf(f"l3a_cosb_{tag}", [128, T], F32)
    sinb = ctx.sbuf(f"l3a_sinb_{tag}", [128, T], F32)
    ctx.dma("SYNC", cosb, w["cosb"].tv())
    ctx.dma("SYNC", sinb, w["sinb2"].tv())
    permb = ctx.sbuf(f"l3a_perm_{tag}", [128, 128], BF16)
    ctx.dma("POOL", permb, w["permT"].tv())
    t2s = ctx.sbuf(f"l3a_t2_{tag}", [128, T], F32)
    for j in range(8):
        for n in range(4):
            pp = ps[2 + n % 2][:, 0:512]
            sl = slice(n * 512, (n + 1) * 512)
            mm_acc(ctx, pp, [(permb, xm[j][:, sl])])
            stt(ctx, t1s[:, sl], pp, None, ALU.bypass, sinb[:, sl], ALU.mult)
        tt(ctx, "POOL", t2s, xm[j], cosb, ALU.mult)
        tt(ctx, "DVE", xm[j], t1s, t2s, ALU.add)
    ctx.free_to(mkr)
    ao = [ctx.sbuf(f"l3a_ao{j}_{tag}", [128, T], BF16) for j in range(8)]
    mkh = ctx.mark()
    qstage = ctx.sbuf(f"l3a_qstg_{tag}", [64, 512], F32)
    bqk = ctx.sbuf(f"l3a_bqk_{tag}", [64, 48], F32)
    stg = ctx.sbuf(f"l3a_bstg_{tag}", [48, 64], F32)
    ctx.dma("SYNC", stg, w["ipb48"].tv())
    transpose(ctx, ps[2][0:64, 0:48], stg, ln.ident[0:48, 0:48])
    copy(ctx, "ACT", bqk, ps[2][0:64, 0:48])
    bvh = [ctx.sbuf(f"l3a_bvh{i}_{tag}", [1, 64], F32) for i in range(2)]
    rtok = ctx.sbuf(f"l3a_rtok_{tag}", [128, 16], F32)
    for i in range(16):
        transpose(ctx, ps[2][:, 0:1], ln.rb[0:1, i * 128:(i + 1) * 128], ln.ones_1[0:1, 0:1])
        copy(ctx, "ACT", rtok[:, i:i + 1], ps[2][:, 0:1])
    pt = ctx.sbuf(f"l3a_pt_{tag}", [128, 16 * 512], BF16)
    pB = [ctx.sbuf(f"l3a_p{i}_{tag}", [128, T], BF16) for i in range(2)]
    tqB = [t1s, t1s]
    mcol = ctx.sbuf(f"l3a_mcol_{tag}", [128, 8], F32)
    qh = ctx.sbuf(f"l3a_qh_{tag}", [64, T], BF16)
    kh = ctx.sbuf(f"l3a_kh_{tag}", [64, T], BF16)
    vtok = ctx.sbuf(f"l3a_vtok_{tag}", [128, 16 * 64], BF16)
    bvb = ctx.sbuf(f"l3a_bvb_{tag}", [128, 64], F32)
    msk = [ctx.sbuf(f"l3a_msk{i}_{tag}", [128, T], BF16) for i in range(2)]
    wsl = [ctx.sbuf(f"l3a_ws{i}_{tag}", [128, 512], BF16) for i in range(3)]
    dw = w["ipwT"].apobj
    dm = w["maskadd"].apobj
    for h in range(H):
        hb = (h % 2) * 64
        for mi, col0 in ((0, h * 64), (1, 1024 + h * 64), (2, 2048 + h * 64)):
            def em(e, s=wsl[mi], d=dw, col0=col0):
                return e.dma_start(out=s.ap(), in_=d[:, col0:col0 + 64].rearrange(
                    "(k p) c -> p k c", p=128))
            ctx.dma("POOL", wsl[mi], w["ipwT"].tv(), emit=em)
        for mi, dst in ((0, qh), (1, kh)):
            bcol = bqk[:, (mi * 16 + h):(mi * 16 + h) + 1]
            for n in range(4):
                pp = ps[n % 2][0:64, 0:512]
                sl = slice(n * 512, (n + 1) * 512)
                mm_acc(ctx, pp, [(wsl[mi][:, j * 64:(j + 1) * 64], xm[j][:, sl])
                                 for j in range(8)])
                stt(ctx, qstage, pp, None, ALU.bypass, ln.rb[0:64, sl], ALU.mult)
                ts(ctx, "DVE", dst[:, sl], qstage, bcol, ALU.add)
        bslot = bvh[h % 2]

        def emb(e, s=bslot, d=w["ipbrow"].apobj, h=h):
            return e.dma_start(out=s.ap(), in_=d[:, 2048 + h * 64:2048 + (h + 1) * 64])
        ctx.dma("SYNC", bslot, w["ipbrow"].tv(), emit=emb)
        mm_acc(ctx, ps[2][:, 0:64], [(ln.ones_1, bslot)])
        copy(ctx, "ACT", bvb, ps[2][:, 0:64])
        for tt_i in range(16):
            pp = ps[2 + tt_i % 2][:, 0:64]
            mm_acc(ctx, pp, [(xm[j][:, tt_i * 128:(tt_i + 1) * 128],
                              wsl[2][:, j * 64:(j + 1) * 64]) for j in range(8)])
            stt(ctx, vtok[:, tt_i * 64:(tt_i + 1) * 64], pp, rtok[:, tt_i:tt_i + 1],
                ALU.mult, bvb, ALU.add)
        def stageA(qc):
            par = qc % 2
            mslot = msk[par]
            tq, p = tqB[par], pB[par]
            maxc, sumc, biasc = (mcol[:, par * 4:par * 4 + 1],
                                 mcol[:, par * 4 + 1:par * 4 + 2],
                                 mcol[:, par * 4 + 2:par * 4 + 3])

            def emm(e, s=mslot, d=dm, qc=qc):
                return e.dma_start(out=s.ap(), in_=d[qc * 128:(qc + 1) * 128, :])
            ctx.dma("SYNC", mslot, w["maskadd"].tv(), emit=emm)
            for c in range(4):
                c0 = c * 512
                pp = ps[c % 2][:, 0:512]
                mm_acc(ctx, pp, [(qh[:, qc * 128:(qc + 1) * 128], kh[:, c0:c0 + 512])])
                stt(ctx, tq[:, c0:c0 + 512], pp, 0.125, ALU.mult, mslot[:, c0:c0 + 512], ALU.add)
            reduce(ctx, "DVE", maxc, tq, ALU.max)
            ts(ctx, "DVE", biasc, maxc, -1.0, ALU.mult)
            act(ctx, p, tq, AF.Exp, bias=biasc, accum_out=sumc)
            recip(ctx, sumc, sumc)
            ts(ctx, "DVE", p, p, sumc, ALU.mult)

        def stageB(qc, h=h, hb=hb):
            p = pB[qc % 2]
            for kc in range(16):
                bank = ps[6 + kc % 2]
                gg = ctx.mm_open(bank)
                tr_mm(ctx, gg, bank.bf16()[:, 0:128], p[:, kc * 128:(kc + 1) * 128],
                      ln.identb, st=True, sp=True)
                gg.close()
                dst = pt[:, kc * 512 + (qc % 4) * 128:kc * 512 + (qc % 4) * 128 + 128]
                copy(ctx, "DVE" if kc % 2 == 0 else "ACT", dst, bank.bf16()[:, 0:128])
            if qc % 4 == 3:
                sp_i = qc // 4
                po = ps[2]
                out_tv = po[hb:hb + 64, 0:512]
                gpv = ctx.mm_open(po)
                for kc in range(16):
                    lt = vtok[:, kc * 64:(kc + 1) * 64]
                    rt = pt[:, kc * 512:(kc + 1) * 512]
                    gpv.add(lambda e, o=out_tv, l=lt, r=rt, st=(kc == 0), sp2=(kc == 15),
                            tp=(0, hb): e.matmul(o.ap(), l.ap(), r.ap(), start=st, stop=sp2,
                                                 tile_position=tp), [lt, rt])
                gpv.close()
                copy(ctx, "DVE", ao[h // 2][hb:hb + 64, sp_i * 512:(sp_i + 1) * 512], out_tv)

        stageA(0)
        for qc in range(16):
            if qc < 15:
                stageA(qc + 1)
            stageB(qc)
    ctx.free_to(mkh)
    wot = load_full_bf16(ctx, f"l3a_owt_{tag}", w["owT"], 1024, 1024)
    obc = load_col_fp32(ctx, f"l3a_ob_{tag}", w["ob"], 1024, ln, ps[2])
    for f in range(8):
        for n in range(4):
            pp = ps[f % 2][:, 0:512]
            mm_acc(ctx, pp, [(wot[j][:, f * 128:(f + 1) * 128],
                              ao[j][:, n * 512:(n + 1) * 512]) for j in range(8)])
            xsl = xs[f][:, n * 512:(n + 1) * 512]
            stt(ctx, xsl, pp, obc[:, f:f + 1], ALU.add, xsl, ALU.add)


def emit_l3_ffn(ctx, xs, w, ln, ps, tag, load_x_tile=None, dbg=None):
    xsave = ctx.dram(f"l3f_xsave_{tag}", [8, 128, T], F32, "Internal")
    dxs = xsave.apobj
    for j in range(8):
        def emS(e, j=j, d=dxs, s=xs[j]):
            return e.dma_start(out=d[j], in_=s.ap())
        ctx.dma("SYNC", xsave.tv(), xs[j], emit=emS)
    """MLA-ish attn + shared+top2 MoE. w: g1c/g2c via cols, WqT [1024,1024],
    WkcT/WvcT [1024,256], WoT [1024,1024], WrT [1024,8], ebrow [1,8],
    sW1T/sW2T/sW3T [1024,1024], eW1T/eW2T/eW3T [8,1024,1024]."""
    emit_ln_inplace(ctx, xs, ln, ps[0], ps[1], T)
    g1c = load_col_fp32(ctx, f"l3f_g1_{tag}", w["g1"], 1024, ln, ps[0])
    g2c = load_col_fp32(ctx, f"l3f_g2_{tag}", w["g2"], 1024, ln, ps[1])
    g1cols = [g1c[:, j:j + 1] for j in range(8)]
    g2cols = [g2c[:, j:j + 1] for j in range(8)]
    xm = [ctx.sbuf(f"l3f_xm{j}_{tag}", [128, T], BF16) for j in range(8)]
    emit_ln(ctx, xs, 0, T, ln, ps[0], ps[1], xm, mode="rms", pre_col=g1cols, eps=1e-6)
    rtok = ctx.sbuf(f"l3f_rtok_{tag}", [128, 16], F32)
    for i in range(16):
        transpose(ctx, ps[2][:, 0:1], ln.rb[0:1, i * 128:(i + 1) * 128], ln.ones_1[0:1, 0:1])
        copy(ctx, "ACT", rtok[:, i:i + 1], ps[2][:, 0:1])
    mk_ao = ctx.mark()
    ao = [ctx.sbuf(f"l3f_ao{j}_{tag}", [128, T], BF16) for j in range(8)]
    mka = ctx.mark()
    wqsl = [ctx.sbuf(f"l3f_wqs{i}_{tag}", [128, 8 * 256], BF16) for i in range(2)]
    qf = [ctx.sbuf(f"l3f_q{i}_{tag}", [128, T], BF16) for i in range(2)]
    kf = [ctx.sbuf(f"l3f_k{i}_{tag}", [128, T], BF16) for i in range(2)]
    vtok = ctx.sbuf(f"l3f_vtok_{tag}", [128, 16 * 256], BF16)
    mk_kv = ctx.mark()
    wkt = load_full_bf16(ctx, f"l3f_wkc_{tag}", w["WkcT"], 1024, 256)
    wvt = load_full_bf16(ctx, f"l3f_wvc_{tag}", w["WvcT"], 1024, 256)
    for f in range(2):
        for n in range(4):
            pp = ps[f % 2][:, 0:512]
            sl = slice(n * 512, (n + 1) * 512)
            mm_acc(ctx, pp, [(wkt[j][:, f * 128:(f + 1) * 128], xm[j][:, sl]) for j in range(8)])
            stt(ctx, kf[f][:, sl], pp, None, ALU.bypass, ln.rb[:, sl], ALU.mult)
    for ti in range(16):
        for f2 in range(2):
            pp = ps[2 + f2][:, 0:128]
            mm_acc(ctx, pp, [(xm[j][:, ti * 128:(ti + 1) * 128],
                              wvt[j][:, f2 * 128:(f2 + 1) * 128]) for j in range(8)])
            ts(ctx, "DVE", vtok[:, ti * 256 + f2 * 128:ti * 256 + (f2 + 1) * 128], pp,
               rtok[:, ti:ti + 1], ALU.mult)
    ctx.free_to(mk_kv)
    pt = ctx.sbuf(f"l3f_pt_{tag}", [128, 16 * 512], BF16)
    pB = [ctx.sbuf(f"l3f_p{i}_{tag}", [128, T], BF16) for i in range(2)]
    mcol = ctx.sbuf(f"l3f_mcol_{tag}", [128, 24], F32)
    dwq = w["WqT"].apobj
    for h in range(4):
        wq = wqsl[h % 2]

        def emq(e, s=wq, d=dwq, h=h):
            return e.dma_start(out=s.ap(), in_=d[:, h * 256:(h + 1) * 256].rearrange(
                "(k p) c -> p k c", p=128))
        ctx.dma("POOL", wq, w["WqT"].tv(), emit=emq)
        for f in range(2):
            for n in range(4):
                pp = ps[f % 2][:, 0:512]
                sl = slice(n * 512, (n + 1) * 512)
                mm_acc(ctx, pp, [(wq[:, j * 256 + f * 128:j * 256 + (f + 1) * 128],
                                  xm[j][:, sl]) for j in range(8)])
                stt(ctx, qf[f][:, sl], pp, None, ALU.bypass, ln.rb[:, sl], ALU.mult)
        def stageA(qc):
            par = qc % 2
            p = pB[par]
            mc = mcol[:, par * 12:par * 12 + 12]
            maxc, sumc, biasc = mc[:, 0:1], mc[:, 1:2], mc[:, 2:3]
            for c in range(4):
                c0 = c * 512
                pp = ps[c][:, 0:512]
                mm_acc(ctx, pp, [(qf[f2][:, qc * 128:(qc + 1) * 128],
                                  kf[f2][:, c0:c0 + 512]) for f2 in range(2)])
                reduce(ctx, "DVE", mc[:, 4 + c:5 + c], pp, ALU.max)
            tt(ctx, "DVE", mc[:, 8:9], mc[:, 4:5], mc[:, 5:6], ALU.max)
            tt(ctx, "DVE", mc[:, 9:10], mc[:, 6:7], mc[:, 7:8], ALU.max)
            tt(ctx, "DVE", maxc, mc[:, 8:9], mc[:, 9:10], ALU.max)
            ts(ctx, "DVE", biasc, maxc, -1.0 / 16.0, ALU.mult)
            for c in range(4):
                c0 = c * 512
                act(ctx, p[:, c0:c0 + 512], ps[c][:, 0:512], AF.Exp, bias=biasc,
                    scale=1.0 / 16.0, accum_out=mc[:, 4 + c:5 + c])
            tt(ctx, "DVE", mc[:, 8:9], mc[:, 4:5], mc[:, 5:6], ALU.add)
            tt(ctx, "DVE", mc[:, 9:10], mc[:, 6:7], mc[:, 7:8], ALU.add)
            tt(ctx, "DVE", sumc, mc[:, 8:9], mc[:, 9:10], ALU.add)
            recip(ctx, sumc, sumc)
            ts(ctx, "DVE", p, p, sumc, ALU.mult)

        def stageB(qc, h=h):
            p = pB[qc % 2]
            for kc in range(16):
                bank = ps[6 + kc % 2]
                gg = ctx.mm_open(bank)
                tr_mm(ctx, gg, bank.bf16()[:, 0:128], p[:, kc * 128:(kc + 1) * 128],
                      ln.identb, st=True, sp=True)
                gg.close()
                dst = pt[:, kc * 512 + (qc % 4) * 128:kc * 512 + (qc % 4) * 128 + 128]
                copy(ctx, "DVE" if kc % 2 == 0 else "ACT", dst, bank.bf16()[:, 0:128])
            if qc % 4 == 3:
                sp_i = qc // 4
                for f2 in range(2):
                    po = ps[4 + f2]
                    gpv = ctx.mm_open(po)
                    for kc in range(16):
                        lt = vtok[:, kc * 256 + f2 * 128:kc * 256 + (f2 + 1) * 128]
                        rt = pt[:, kc * 512:(kc + 1) * 512]
                        gpv.add(lambda e, o=po[:, 0:512], l=lt, r=rt, st=(kc == 0),
                                sp2=(kc == 15): e.matmul(o.ap(), l.ap(), r.ap(),
                                                         start=st, stop=sp2), [lt, rt])
                    gpv.close()
                    copy(ctx, "DVE", ao[2 * h + f2][:, sp_i * 512:(sp_i + 1) * 512], po[:, 0:512])

        stageA(0)
        for qc in range(16):
            if qc < 15:
                stageA(qc + 1)
            stageB(qc)
    ctx.free_to(mka)
    wot = load_full_bf16(ctx, f"l3f_wo_{tag}", w["WoT"], 1024, 1024)
    for f in range(8):
        for n in range(4):
            pp = ps[f % 2][:, 0:512]
            mm_acc(ctx, pp, [(wot[j][:, f * 128:(f + 1) * 128],
                              ao[j][:, n * 512:(n + 1) * 512]) for j in range(8)])
            xsl = xs[f][:, n * 512:(n + 1) * 512]
            stt(ctx, xsl, pp, None, ALU.bypass, xsl, ALU.add)
    if dbg is not None:
        for j in range(8):
            ctx.dma("SYNC", dbg[f"xa{j}"].tv(), xs[j])
    ctx.free_to(mk_ao)
    ctx.barrier()
    emit_ln(ctx, xs, 0, T, ln, ps[0], ps[1], xm, mode="rms", pre_col=g2cols, eps=1e-6)
    for j in range(8):
        stt(ctx, xm[j], xm[j], None, ALU.bypass, ln.rb, ALU.mult)
    mk_moe = ctx.mark()
    wrt = load_full_bf16(ctx, f"l3f_wr_{tag}", w["WrT"], 1024, 8)
    ebr = ctx.sbuf(f"l3f_ebr_{tag}", [1, 8], F32)
    ctx.dma("SYNC", ebr, w["ebrow"].tv())
    ebb = ctx.sbuf(f"l3f_ebb_{tag}", [128, 8], F32)
    mm_acc(ctx, ps[2][:, 0:8], [(ln.ones_1, ebr)])
    copy(ctx, "ACT", ebb, ps[2][:, 0:8])
    gtok = ctx.sbuf(f"l3f_gtok_{tag}", [128, 16 * 8], F32)
    rsc = ctx.sbuf(f"l3f_rsc_{tag}", [128, 4 * 8], F32)
    rcol = ctx.sbuf(f"l3f_rcol_{tag}", [128, 8], F32)
    sc_, is1_, is2_, ee_ = rsc[:, 0:8], rsc[:, 8:16], rsc[:, 16:24], rsc[:, 24:32]
    t1c, t2c, e2c, sic = rcol[:, 0:1], rcol[:, 1:2], rcol[:, 2:3], rcol[:, 3:4]
    nt1c = rcol[:, 4:5]
    for ti in range(16):
        pp = ps[2][:, 0:8]
        mm_acc(ctx, pp, [(xm[j][:, ti * 128:(ti + 1) * 128], wrt[j]) for j in range(8)])
        tt(ctx, "DVE", sc_, pp, ebb, ALU.add)
        reduce(ctx, "DVE", t1c, sc_, ALU.max)
        ts(ctx, "DVE", nt1c, t1c, -1.0, ALU.mult)
        act(ctx, sc_, sc_, AF.Exp, bias=nt1c, accum_out=sic)
        recip(ctx, sic, sic)
        ts(ctx, "DVE", sc_, sc_, sic, ALU.mult)
        reduce(ctx, "DVE", t1c, sc_, ALU.max)
        ts(ctx, "DVE", is1_, sc_, t1c, ALU.is_equal)
        stt(ctx, ee_, is1_, -1e30, ALU.mult, sc_, ALU.add)
        reduce(ctx, "DVE", t2c, ee_, ALU.max)
        ts(ctx, "DVE", is2_, ee_, t2c, ALU.is_equal)
        ts(ctx, "DVE", nt1c, t1c, -1.0, ALU.mult)
        act(ctx, e2c, t2c, AF.Exp, bias=nt1c)
        ts(ctx, "DVE", sic, e2c, 1.0, ALU.add)
        recip(ctx, sic, sic)
        tt(ctx, "DVE", e2c, e2c, sic, ALU.mult)
        g = gtok[:, ti * 8:(ti + 1) * 8]
        ts(ctx, "DVE", is1_, is1_, sic, ALU.mult)
        ts(ctx, "DVE", is2_, is2_, e2c, ALU.mult)
        tt(ctx, "DVE", g, is1_, is2_, ALU.add)
    if dbg is not None:
        ctx.dma("SYNC", dbg["gtok"].tv(), gtok)
        ctx.dma("SYNC", dbg["xm0"].tv(), xm[0])
    gbeb = [ctx.sbuf(f"l3f_gbe{i}_{tag}", [128, 8 * 512], BF16) for i in range(4)]
    gbet = [[gbeb[tc][:, e * 512:(e + 1) * 512] for tc in range(4)] for e in range(8)]
    grow = ctx.sbuf(f"l3f_grow_{tag}", [1, 512], F32)
    for e in range(8):
        for tc in range(4):
            gp = ctx.mm_open(ps[3])
            for c4 in range(4):
                gcolv = gtok[:, (tc * 4 + c4) * 8 + e:(tc * 4 + c4) * 8 + e + 1]
                gp.add(lambda e_, o=ps[3][0:1, 0:512], i=gcolv, id=ln.ident,
                       st=(c4 == 0), sp2=(c4 == 3), c4=c4: e_.matmul(
                           o[:, c4 * 128:(c4 + 1) * 128].ap(), i.ap(), id.ap(),
                           start=st, stop=sp2, is_transpose=True), [gcolv, ln.ident])
            gp.close()
            copy(ctx, "ACT", grow, ps[3][0:1, 0:512])
            mm_acc(ctx, ps[3][:, 0:512], [(ln.ones_1, grow)])
            copy(ctx, "ACT", gbet[e][tc], ps[3][:, 0:512])
    w1s = [ctx.sbuf(f"l3f_w1s{i}_{tag}", [128, 8 * 512], BF16) for i in range(2)]
    w3s = [ctx.sbuf(f"l3f_w3s{i}_{tag}", [128, 8 * 512], BF16) for i in range(2)]
    w2s = [ctx.sbuf(f"l3f_w2s{i}_{tag}", [128, 4 * 1024], BF16) for i in range(2)]
    s1b = ctx.sbuf(f"l3f_s1b_{tag}", [128, 4 * 512], BF16)
    h3b = ctx.sbuf(f"l3f_h3b_{tag}", [128, 4 * 512], BF16)
    si = 0
    units = [("s", None)] + [("e", e) for e in range(8)]
    for unm, e in units:
        for fg in range(2):
            sl1, sl3, sl2 = w1s[si % 2], w3s[si % 2], w2s[si % 2]
            si += 1
            for srcn, slot in (("W1", sl1), ("W3", sl3)):
                dsrc = w["s" + srcn + "T"].apobj if unm == "s" else w["e" + srcn + "T"].apobj

                def eml(e_, s=slot, d=dsrc, e=e, fg=fg, unm=unm):
                    base = d if unm == "s" else d[e]
                    return e_.dma_start(out=s.ap(), in_=base[:, fg * 512:(fg + 1) * 512]
                                        .rearrange("(k p) c -> p k c", p=128))
                ctx.dma("POOL", slot, (w["s" + srcn + "T"] if unm == "s" else w["e" + srcn + "T"]).tv(), emit=eml)
            dsrc2 = w["sW2T"].apobj if unm == "s" else w["eW2T"].apobj

            def eml2(e_, s=sl2, d=dsrc2, e=e, fg=fg, unm=unm):
                base = d if unm == "s" else d[e]
                return e_.dma_start(out=s.ap(), in_=base[fg * 512:(fg + 1) * 512, :]
                                    .rearrange("(k p) c -> p k c", p=128))
            ctx.dma("POOL", sl2, (w["sW2T"] if unm == "s" else w["eW2T"]).tv(), emit=eml2)
            for tc in range(4):
                t0 = tc * 512
                for fsub in range(4):
                    pp1 = ps[4 + fsub % 2]
                    mm_acc(ctx, pp1, [(sl1[:, k * 512 + fsub * 128:k * 512 + (fsub + 1) * 128],
                                       xm[k][:, t0:t0 + 512]) for k in range(8)])
                    act(ctx, s1b[:, fsub * 512:(fsub + 1) * 512], pp1, AF.Silu)
                    pp3 = ps[6 + fsub % 2]
                    mm_acc(ctx, pp3, [(sl3[:, k * 512 + fsub * 128:k * 512 + (fsub + 1) * 128],
                                       xm[k][:, t0:t0 + 512]) for k in range(8)])
                    if unm == "e":
                        stt(ctx, h3b[:, fsub * 512:(fsub + 1) * 512], pp3, None, ALU.bypass,
                            gbet[e][tc], ALU.mult)
                    else:
                        copy(ctx, "DVE", h3b[:, fsub * 512:(fsub + 1) * 512], pp3)
                    tt(ctx, "DVE", s1b[:, fsub * 512:(fsub + 1) * 512],
                       s1b[:, fsub * 512:(fsub + 1) * 512],
                       h3b[:, fsub * 512:(fsub + 1) * 512], ALU.mult)
                for ho in range(8):
                    pp = ps[ho % 2]
                    mm_acc(ctx, pp, [(sl2[:, k * 1024 + ho * 128:k * 1024 + (ho + 1) * 128],
                                      s1b[:, k * 512:(k + 1) * 512]) for k in range(4)])
                    yv = xs[ho][:, t0:t0 + 512]
                    stt(ctx, yv, pp, None, ALU.bypass, yv, ALU.add)
    ctx.free_to(mk_moe)
    xstg = ctx.sbuf(f"l3f_xstg_{tag}", [128, T], F32)
    for j in range(8):
        def emL(e, j=j, d=dxs, dst=xstg):
            return e.dma_start(out=dst.ap(), in_=d[j])
        ctx.dma("SYNC", xstg, xsave.tv(), emit=emL)
        tt(ctx, "DVE", xs[j], xs[j], xstg, ALU.add)


from concourse.bass_utils import run_bass_kernel_spmd


def _build_program():
    nc = bass.Bass("TRN2", target_bir_lowering=False, debug=False)
    c = Ctx(nc)
    xT = c.dram("xT", [2, 1024, 2048], F32, "ExternalInput")
    yT = c.dram("yT", [2, 1024, 2048], F32, "ExternalOutput")
    w0a = {"Mq": c.dram("l0_Mq", [1024, 512], F32, "ExternalInput"),
           "Mk": c.dram("l0_Mk", [1024, 512], F32, "ExternalInput"),
           "Mv": c.dram("l0_Mv", [1024, 1024], F32, "ExternalInput")}
    w0f = {"W1": c.dram("l0f_W1", [1024, 4096], F32, "ExternalInput"),
           "b1": c.dram("l0f_b1", [4096], F32, "ExternalInput"),
           "qp": c.dram("l0f_qp", [4096], F32, "ExternalInput"),
           "qwr": c.dram("l0f_qwr", [4096], F32, "ExternalInput"),
           "qwi": c.dram("l0f_qwi", [4096], F32, "ExternalInput"),
           "W2": c.dram("l0f_W2", [4096, 1024], F32, "ExternalInput"),
           "b2": c.dram("l0f_b2", [1024], F32, "ExternalInput")}
    w1a = {**{nm: c.dram(f"l1a_{nm}", [1024, 1024], F32, "ExternalInput")
              for nm in ("WqT", "WkT", "WvT", "Wq2T", "Wk2T", "WoT")},
           "mixture": c.dram("l1a_mixture", [1, 1], F32, "ExternalInput"),
           "qs": c.dram("l1a_qs", [1, 1], F32, "ExternalInput")}
    w1f = {nm: c.dram(f"l1f_{nm}", list(shp), F32, "ExternalInput") for nm, shp in (
        ("cn_w1", (1024, 1024)), ("cn_b1", (1024,)), ("cn_w2", (1024, 512)),
        ("cn_b2", (512,)), ("cn_w3", (512, 256)), ("cn_b3", (256,)),
        ("cn_w4", (256, 1)), ("cn_b4", (1, 1)), ("gc_w1", (1024, 512)),
        ("gc_b1", (512,)), ("gc_w2", (512, 256)), ("gc_b2", (256,)),
        ("gc_w3", (256, 1)), ("gc_b3", (1, 1)), ("ar", (1, 1)))}
    w2a = {**{nm: c.dram(f"l2a_{nm}", [1024, 1024], F32, "ExternalInput")
              for nm in ("WqT", "WkT", "WvT", "WoT")},
           "bq": c.dram("l2a_bq", [16, 64], F32, "ExternalInput"),
           "bk": c.dram("l2a_bk", [16, 64], F32, "ExternalInput"),
           "bv": c.dram("l2a_bv", [1, 1024], F32, "ExternalInput"),
           "bo": c.dram("l2a_bo", [1024], F32, "ExternalInput")}
    w2f = {"gwT": c.dram("l2f_gwT", [1024, 8], F32, "ExternalInput"),
           "w1T": c.dram("l2f_w1T", [8, 1024, 3584], F32, "ExternalInput"),
           "w3T": c.dram("l2f_w3T", [8, 1024, 3584], F32, "ExternalInput"),
           "w2T": c.dram("l2f_w2T", [8, 3584, 1024], F32, "ExternalInput")}
    w3a = {"ipwT": c.dram("l3a_ipwT", [1024, 3072], F32, "ExternalInput"),
           "ipb48": c.dram("l3a_ipb48", [48, 64], F32, "ExternalInput"),
           "ipbrow": c.dram("l3a_ipbrow", [1, 3072], F32, "ExternalInput"),
           "owT": c.dram("l3a_owT", [1024, 1024], F32, "ExternalInput"),
           "ob": c.dram("l3a_ob", [1024], F32, "ExternalInput"),
           "cosb": c.dram("l3a_cosb", [128, 2048], F32, "ExternalInput"),
           "sinb2": c.dram("l3a_sinb2", [128, 2048], F32, "ExternalInput"),
           "permT": c.dram("l3a_permT", [128, 128], F32, "ExternalInput"),
           "maskadd": c.dram("l3a_maskadd", [2048, 2048], BF16, "ExternalInput")}
    w3f = {"g1": c.dram("l3f_g1", [1024], F32, "ExternalInput"),
           "g2": c.dram("l3f_g2", [1024], F32, "ExternalInput"),
           "WqT": c.dram("l3f_WqT", [1024, 1024], F32, "ExternalInput"),
           "WkcT": c.dram("l3f_WkcT", [1024, 256], F32, "ExternalInput"),
           "WvcT": c.dram("l3f_WvcT", [1024, 256], F32, "ExternalInput"),
           "WoT": c.dram("l3f_WoT", [1024, 1024], F32, "ExternalInput"),
           "WrT": c.dram("l3f_WrT", [1024, 8], F32, "ExternalInput"),
           "ebrow": c.dram("l3f_ebrow", [1, 8], F32, "ExternalInput"),
           "sW1T": c.dram("l3f_sW1T", [1024, 1024], F32, "ExternalInput"),
           "sW2T": c.dram("l3f_sW2T", [1024, 1024], F32, "ExternalInput"),
           "sW3T": c.dram("l3f_sW3T", [1024, 1024], F32, "ExternalInput"),
           "eW1T": c.dram("l3f_eW1T", [8, 1024, 1024], F32, "ExternalInput"),
           "eW2T": c.dram("l3f_eW2T", [8, 1024, 1024], F32, "ExternalInput"),
           "eW3T": c.dram("l3f_eW3T", [8, 1024, 1024], F32, "ExternalInput")}
    ps = [c.psum(f"ps{i}", [128, 512]) for i in range(8)]
    xs = [c.sbuf(f"x{j}", [128, 2048], F32) for j in range(8)]
    ln = LNScratch(c, 2048)
    for b in range(2):
        for j in range(8):
            d = xT.apobj

            def emit(e, j=j, b=b, d=d):
                return e.dma_start(out=xs[j].ap(), in_=d[b, j * 128:(j + 1) * 128, :])
            c.dma("SYNC", xs[j], xT.tv(), emit=emit)
        for fn, wd, kw in ((emit_l0_attn, w0a, {}), (emit_l0_ffn, w0f, {}),
                           (emit_l1_attn, w1a, {}), (emit_l1_ffn, w1f, {}),
                           (emit_l2_attn, w2a, {}), (emit_l2_ffn, w2f, {}),
                           (emit_l3_attn, w3a, {})):
            mk = c.mark()
            fn(c, xs, wd, ln, ps, f"{fn.__name__}_b{b}", **kw)
            c.free_to(mk)
        mk = c.mark()

        def load_x_tile(cx, j, dst, b=b, xT=xT):
            d = xT.apobj

            def em(e, j=j, b=b, d=d, dst=dst):
                return e.dma_start(out=dst.ap(), in_=d[b, j * 128:(j + 1) * 128, :])
            cx.dma("SYNC", dst, xT.tv(), emit=em)
        emit_l3_ffn(c, xs, w3f, ln, ps, f"l3f_b{b}", load_x_tile=load_x_tile)
        c.free_to(mk)
        for j in range(8):
            d = yT.apobj

            def emit(e, j=j, b=b, d=d):
                return e.dma_start(out=d[b, j * 128:(j + 1) * 128, :], in_=xs[j].ap())
            c.dma("SYNC", yT.tv()[j:j + 1, :], xs[j], emit=emit)
    c.final_wait("SYNC", [yT])
    c.build()
    c.close()
    return nc


def _shared_inputs(ins):
    import ml_dtypes
    d = {}
    for k in ("L0_attn_Mq", "L0_attn_Mk", "L0_attn_Mv"):
        d[k.replace("L0_attn_", "l0_")] = ins[k]
    for a, b2 in (("W1", "W1"), ("b1", "b1"), ("quantum_phases", "qp"),
                  ("quantum_weights_real", "qwr"), ("quantum_weights_imag", "qwi"),
                  ("W2", "W2"), ("b2", "b2")):
        d["l0f_" + b2] = ins["L0_ffn_" + a]
    for nm in ("Wq", "Wk", "Wv", "Wq2", "Wk2", "Wo"):
        d[f"l1a_{nm}T"] = np.ascontiguousarray(ins[f"L1_attn_{nm}"].T)
    d["l1a_mixture"] = np.asarray(ins["L1_attn_mixture"], np.float32).reshape(1, 1)
    d["l1a_qs"] = np.asarray(ins["L1_attn_quartet_scale"], np.float32).reshape(1, 1)
    for i in range(1, 5):
        d[f"l1f_cn_w{i}"] = ins[f"L1_ffn_cn_w{i}"]
        d[f"l1f_cn_b{i}"] = ins[f"L1_ffn_cn_b{i}"].reshape(1, 1) if i == 4 else ins[f"L1_ffn_cn_b{i}"]
    for i in range(1, 4):
        d[f"l1f_gc_w{i}"] = ins[f"L1_ffn_gc_w{i}"]
        d[f"l1f_gc_b{i}"] = ins[f"L1_ffn_gc_b{i}"].reshape(1, 1) if i == 3 else ins[f"L1_ffn_gc_b{i}"]
    d["l1f_ar"] = np.asarray(ins["L1_ffn_adaptive_range"], np.float32).reshape(1, 1)
    for nm in ("Wq", "Wk", "Wv", "Wo"):
        d[f"l2a_{nm}T"] = np.ascontiguousarray(ins[f"L2_attn_{nm}"].T)
    d["l2a_bq"] = ins["L2_attn_bq"].reshape(16, 64)
    d["l2a_bk"] = ins["L2_attn_bk"].reshape(16, 64)
    d["l2a_bv"] = ins["L2_attn_bv"].reshape(1, 1024)
    d["l2a_bo"] = ins["L2_attn_bo"]
    d["l2f_gwT"] = np.ascontiguousarray(ins["L2_ffn_gate_w"].T)
    d["l2f_w1T"] = np.ascontiguousarray(ins["L2_ffn_w1"].transpose(0, 2, 1))
    d["l2f_w3T"] = np.ascontiguousarray(ins["L2_ffn_w3"].transpose(0, 2, 1))
    d["l2f_w2T"] = np.ascontiguousarray(ins["L2_ffn_w2"].transpose(0, 2, 1))
    d["l3a_ipwT"] = np.ascontiguousarray(ins["L3_attn_in_proj_w"].T)
    d["l3a_ipb48"] = ins["L3_attn_in_proj_b"].reshape(48, 64)
    d["l3a_ipbrow"] = ins["L3_attn_in_proj_b"].reshape(1, 3072)
    d["l3a_owT"] = np.ascontiguousarray(ins["L3_attn_out_w"].T)
    d["l3a_ob"] = ins["L3_attn_out_b"]
    t = np.arange(2048, dtype=np.float64)
    pvec = np.arange(128)
    i_of_p = (pvec % 64) // 2
    inv = 1.0 / 10000.0 ** (np.arange(0, 64, 2, dtype=np.float64) / 64.0)
    fidx = (2 * i_of_p) % 32
    fr = t[None, :] * inv[fidx][:, None]
    d["l3a_cosb"] = np.cos(fr).astype(np.float32)
    sgn = np.where(pvec % 2 == 0, -1.0, 1.0)
    d["l3a_sinb2"] = (np.sin(fr) * sgn[:, None]).astype(np.float32)
    perm = np.zeros((128, 128), np.float32)
    perm[np.arange(128), np.arange(128) ^ 1] = 1.0
    d["l3a_permT"] = perm
    ii, jj = np.mgrid[0:2048, 0:2048]
    allowed = (np.abs(ii - jj) <= 128) | (jj % 256 == 0) | (jj == 0)
    d["l3a_maskadd"] = np.where(allowed, 0.0, -1e30).astype(ml_dtypes.bfloat16)
    d["l3f_g1"] = ins["L3_ffn_g1"]
    d["l3f_g2"] = ins["L3_ffn_g2"]
    for nm in ("Wq", "Wkc", "Wvc", "Wo", "Wr", "sW1", "sW2", "sW3"):
        d[f"l3f_{nm}T"] = np.ascontiguousarray(ins[f"L3_ffn_{nm}"].T)
    d["l3f_ebrow"] = ins["L3_ffn_expert_bias"].reshape(1, 8)
    for nm in ("eW1", "eW2", "eW3"):
        d[f"l3f_{nm}T"] = np.ascontiguousarray(ins[f"L3_ffn_{nm}"].transpose(0, 2, 1))
    return d


def kernel(**inputs):
    ins = {k: np.asarray(v) for k, v in inputs.items()}
    nc = _build_program()
    shared = _shared_inputs(ins)
    x = ins["L0_attn_x"]
    in_maps = []
    for core in range(8):
        m = dict(shared)
        m["xT"] = np.ascontiguousarray(x[2 * core:2 * core + 2].transpose(0, 2, 1))
        in_maps.append(m)
    res = run_bass_kernel_spmd(nc, in_maps, core_ids=list(range(8)))
    out = np.empty((16, 2048, 1024), np.float32)
    for core in range(8):
        out[2 * core:2 * core + 2] = res.results[core]["yT"].transpose(0, 2, 1)
    return out
